# version 09
# baseline (speedup 1.0000x reference)
import numpy as np
import ml_dtypes
import concourse.bass as bass
import concourse.mybir as mybir
import concourse.bass_utils as _bu
from concourse.bass_utils import run_bass_kernel_spmd

_LDWOPT_DISABLED = True

F32 = mybir.dt.float32
BF16 = mybir.dt.bfloat16
I32 = mybir.dt.int32
AF = mybir.ActivationFunctionType
ALU = mybir.AluOpType
AX = mybir.AxisListType

T = 1024
TH = 512
C = 16
NCH = T // C
NL = 4
D = 512
V = 4096
EPS = 1e-5


class Stream:

    def __init__(self, nc, name, nbuf):
        self.nbuf = nbuf
        self.sems = [nc.alloc_semaphore(f"st_{name}{i}") for i in range(nbuf)]
        self.cnt = [0] * nbuf

    def issue(self, instr, idx):
        sl = idx % self.nbuf
        self.cnt[sl] += 16
        instr.then_inc(self.sems[sl], 16)
        return self.cnt[sl]

    def wait(self, eng, idx):
        sl = idx % self.nbuf
        eng.wait_ge(self.sems[sl], 16 * (idx // self.nbuf + 1))


class KB:
    def __init__(self, dbg=None):
        self.dbg = dbg
        self.done = False
        nc = bass.Bass("TRN2", target_bir_lowering=False, debug=False,
                       detect_race_conditions=False)
        self.nc = nc
        dt = nc.dram_tensor
        self.xT = dt("xT", [V, T], F32, kind="ExternalInput").ap()
        self.winb = dt("winb", [4, 128, 8 * 512], F32, kind="ExternalInput").ap()
        self.w1b = dt("w1b", [NL, 2, 128, 4 * 1024], F32, kind="ExternalInput").ap()
        self.w2b = dt("w2b", [NL, 2, 128, 8 * 512], F32, kind="ExternalInput").ap()
        self.woutb = dt("woutb", [4, 128, 4096], F32, kind="ExternalInput").ap()
        self.fwb = dt("fwb", [32, 128, 32 * 128], F32, kind="ExternalInput").ap()
        self.centb = dt("centb", [128, 32 * 64], F32, kind="ExternalInput").ap()
        self.chp = dt("chp", [128, NL * 9 * 4], F32, kind="ExternalInput").ap()
        self.binr = dt("binr", [128, 4], F32, kind="ExternalInput").ap()
        self.b1all = dt("b1all", [128, NL * 16], F32, kind="ExternalInput").ap()
        self.b2all = dt("b2all", [128, NL * 4], F32, kind="ExternalInput").ap()
        self.lnlwb = dt("lnlwb", [128, 8], F32, kind="ExternalInput").ap()
        self.boutr = dt("boutr", [128, 32], F32, kind="ExternalInput").ap()
        self.fbr = dt("fbr", [128, 32], F32, kind="ExternalInput").ap()
        self.cmask = dt("cmask", [128, 32], F32, kind="ExternalInput").ap()
        self.lcselr = dt("lcselr", [128, 4], F32, kind="ExternalInput").ap()
        self.iot64 = dt("iot64", [64, 1], F32, kind="ExternalInput").ap()
        self.assign = dt("assign", [1, V], I32, kind="ExternalInput").ap()
        self.outT = dt("outT", [V, T], F32, kind="ExternalOutput").ap()
        self.cc_in = dt("cc_in", [128, 8], F32)
        self.cc_out = dt("cc_out", [1024, 8], F32)
        self.cl_in = dt("cl_in", [64, 1], F32)
        self.cl_out = dt("cl_out", [64, 1], F32)
        self.dbgout = None
        if dbg is not None:
            shp = {'h0': [128, 4096], 'x1': [128, 32768]}.get(dbg)
            if shp is None and (dbg.startswith('hmix') or dbg.startswith('hffn')):
                shp = [128, 4096]
            if shp is None and dbg.startswith('y'):
                shp = [128, 4096]
            if shp is None and dbg in ('x1pre', 'x2pre'):
                shp = [128, 32768]
            if shp is None and dbg == 'hL':
                shp = [128, 4096]
            if shp is None and dbg == 'masks':
                shp = [128, 1064]
            if shp is None and dbg in ('m1', 'm2', 'm3b', 'm4'):
                shp = [128, 16]
            if shp is None and dbg == 'p2a':
                shp = [128, 2048]
            if shp is None and dbg == 'm3':
                shp = [64, 4096]
            if shp is None and dbg.startswith('tabs'):
                shp = [128, 872]
            if shp is None and dbg.startswith('wsc'):
                shp = [128, 1096]
            self.dbgout = dt("dbgout", shp, F32, kind="ExternalOutput").ap()

        al = nc.alloc_sbuf_tensor
        self.bigA = al("bigA", [128, 32768], BF16)
        self.bigB = al("bigB", [128, 32768], BF16)
        self.h = al("h", [128, 4 * T], F32)
        self.Yb = al("Yb", [128, T], F32)
        self.wsq = al("wsq", [128, T], BF16)
        self.wsA = al("wsA", [128, 4096], BF16)
        self.wsB = al("wsB", [128, 4096], BF16)
        self.xtc = al("xtc", [128, 2 * T], BF16)
        self.sc1 = al("sc1", [128, TH], F32)
        self.md = al("md", [128, 2 * TH], F32)
        self.qsb = al("qsb", [128, T], F32)
        self.mub = al("mub", [128, TH], F32)
        self.rstdb = al("rstdb", [128, TH], F32)
        self.wre = al("wre", [128, 4 * NCH], F32)
        self.wim = al("wim", [128, 4 * NCH], F32)
        self.prB = al("prB", [128, 4 * NCH], F32)
        self.piB = al("piB", [128, 4 * NCH], F32)
        self.prA = self.wre
        self.piA = self.wim
        self.Sre = al("Sre", [128, 4 * NCH], F32)
        self.Sim = al("Sim", [128, 4 * NCH], F32)
        self.qcre = al("qcre", [128, 4 * NCH], F32)
        self.qcim = al("qcim", [128, 4 * NCH], F32)
        self.powre = al("powre", [128, 4 * (C + 1)], F32)
        self.powim = al("powim", [128, 4 * (C + 1)], F32)
        self.qsre = al("qsre", [128, 4 * 8], F32)
        self.qsim = al("qsim", [128, 4 * 8], F32)
        self.psclre = al("psclre", [128, 4 * C], F32)
        self.psclim = al("psclim", [128, 4 * C], F32)
        self.ppre = al("ppre", [128, 4 * 4], F32)
        self.ppim = al("ppim", [128, 4 * 4], F32)
        self.sinre = al("sinre", [128, 4], F32)
        self.sinim = al("sinim", [128, 4], F32)
        self.gath = al("gath", [128, 64], F32)
        self.t4a = al("t4a", [128, 4], F32)
        self.t4b = al("t4b", [128, 4], F32)
        self.t4c = al("t4c", [128, 4], F32)
        self.t4d = al("t4d", [128, 4], F32)
        self.chps = al("chps", [128, NL * 9 * 4], F32)
        self.bins = al("bins", [128, 4], F32)
        self.b1s = al("b1s", [128, NL * 16], F32)
        self.b2s = al("b2s", [128, NL * 4], F32)
        self.lnls = al("lnls", [128, 8], F32)
        self.bouts = al("bouts", [128, 32], F32)
        self.fbs = al("fbs", [128, 32], F32)
        self.cms = al("cms", [128, 32], F32)
        self.lcsel = al("lcsel", [128, 4], F32)
        self.io64 = al("io64", [64, 1], F32)
        pass
        self.rowf = al("rowf", [1, 2 * TH], F32)
        self.rowb = al("rowb", [1, 2 * TH], BF16)
        self.rowb2 = al("rowb2", [1, 2 * TH], BF16)
        self.hcnt = al("hcnt", [64, 12], F32)
        self.clsel = al("clsel", [64, 1], BF16)
        self.rsall = al("rsall", [128, 32], F32)
        self.inv512 = al("inv512", [128, 1], BF16)
        self.inv512f = al("inv512f", [128, 1], F32)
        self.inv4096 = al("inv4096", [128, 1], BF16)
        self.ones1 = al("ones1", [1, 128], BF16)
        self.ones64 = al("ones64", [64, 1], BF16)
        self.cm05 = al("cm05", [128, 1], F32)
        self.cp05 = al("cp05", [128, 1], F32)

        yb_addr = nc.lookup_mloc(self.Yb).addr
        self.cents = nc.alloc_sbuf_tensor_at("cents", [128, 32 * 64], BF16,
                                             offset=yb_addr)
        self.pb = [nc.alloc_psum_tensor(f"pb{i}", [128, 512], F32) for i in range(8)]
        self.sems = {k: nc.alloc_semaphore("s_" + k)
                     for k in ("xg", "xs", "pe", "v", "a", "p", "o")}
        self.ccsems = [nc.alloc_semaphore(f"s_cc{i}") for i in range(5)]
        self.n = {k: 0 for k in self.sems}
        self.streams = {}
        for nm, nb in (("ew", 2), ("ex", 4), ("w1", 2), ("w2", 2), ("wow", 2),
                       ("wox", 2), ("de", 2), ("out", 2), ("out2", 2)):
            self.streams[nm] = Stream(nc, nm, nb)

    def inc(self, instr, key, amt=1):
        self.n[key] += amt
        instr.then_inc(self.sems[key], amt)
        return self.n[key]

    def w(self, eng, *keys):
        for k in keys:
            if self.n[k] > 0:
                eng.wait_ge(self.sems[k], self.n[k])

    def wv(self, eng, key, val):
        if val > 0:
            eng.wait_ge(self.sems[key], val)

    def hg(self, g):
        return self.h[:, g * T:(g + 1) * T]

    def hgt(self, g, th):
        return self.h[:, g * T + th * TH: g * T + (th + 1) * TH]

    @property
    def hb(self):
        return self.bigA[:, 24576:26624]

    @property
    def hbsq(self):
        return self.bigA[:, 26624:28672]

    def xt(self, b):
        return self.xtc[:, b * T:(b + 1) * T]

    def mdv(self, i):
        return self.md[:, i * TH:(i + 1) * TH]

    def chan(self, i, j, g=None):
        base = (i * 9 + j) * 4
        if g is None:
            return self.chps[:, base:base + 4]
        return self.chps[:, base + g:base + g + 1]

    def yv(self, g, th=None):
        b = self.bigB.ap().bitcast(F32)
        if th is None:
            return b[:, 8192 + g * T: 8192 + (g + 1) * T]
        return b[:, 8192 + g * T + th * TH: 8192 + g * T + (th + 1) * TH]

    def lv(self, g, th):
        return self.bigB[:, 24576 + g * T + th * TH: 24576 + g * T + (th + 1) * TH]

    def wtmp3(self):
        return self.bigA.ap().bitcast(F32)[:, 15360:16384].rearrange(
            "p (c i) -> p c i", c=NCH)

    def dump(self, name):
        if self.dbg != name or self.done:
            return self.dbg == name and self.done
        kb = self
        if name == 'h0' or name.startswith('hmix') or name.startswith('hffn'):
            def s(sync):
                kb.w(sync, 'v', 'a', 'p', 'pe')
                kb.inc(sync.dma_start(kb.dbgout, kb.h[:]), 'o', 16)
                sync.wait_ge(kb.sems['o'], kb.n['o'])
            self.blk.sync(s)
        elif name.startswith('y'):
            src = self.bigB.ap().bitcast(F32)[:, 8192:12288]

            def s(sync):
                kb.w(sync, 'v', 'a', 'p', 'pe')
                kb.inc(sync.dma_start(kb.dbgout, src), 'o', 16)
                sync.wait_ge(kb.sems['o'], kb.n['o'])
            self.blk.sync(s)
        elif name in ('x1pre', 'x1'):
            def g(gp):
                kb.w(gp, 'v', 'a', 'p', 'pe')
                kb.inc(gp.dma_start(kb.dbgout, kb.bigA[:]), 'xg', 16)
                gp.wait_ge(kb.sems['xg'], kb.n['xg'])
            self.blk.gpsimd(g)
        elif name == 'm3':
            def sm(gp):
                kb.w(gp, 'v', 'a', 'p', 'pe', 'xg')
                i = gp.dma_start(kb.dbgout, kb.bigB[0:64, 24576:24576 + V])
                kb.inc(i, 'xg', 16)
                gp.wait_ge(kb.sems['xg'], kb.n['xg'])
            self.blk.gpsimd(sm)
        elif name in ('m1', 'm2', 'm3b', 'm4'):
            def sm(sync):
                kb.w(sync, 'v', 'a', 'p', 'pe', 'xg')
                kb.inc(sync.dma_start(kb.dbgout[0:64, 0:12], kb.hcnt[:]), 'o', 16)
                sync.wait_ge(kb.sems['o'], kb.n['o'])
            self.blk.sync(sm)
        elif name == 'p2a':
            def sp(sync):
                kb.w(sync, 'v', 'a', 'p', 'pe')
                i = sync.dma_start(kb.dbgout, kb.bigA.ap().bitcast(F32)[:, 0:2048])
                kb.inc(i, 'o', 16)
                sync.wait_ge(kb.sems['o'], kb.n['o'])
            self.blk.sync(sp)
        elif name == 'masks':
            def sm(sync):
                kb.w(sync, 'v', 'a', 'p', 'pe', 'xg')
                kb.inc(sync.dma_start(kb.dbgout[:, 0:1024], kb.qsb[:]), 'o', 16)
                kb.inc(sync.dma_start(kb.dbgout[:, 1024:1056], kb.rsall[:]), 'o', 16)
                kb.inc(sync.dma_start(kb.dbgout[0:64, 1056:1064], kb.hcnt[:, 4:12]), 'o', 16)
                sync.wait_ge(kb.sems['o'], kb.n['o'])
            self.blk.sync(sm)
        elif name == 'hL':
            def g(gp):
                kb.w(gp, 'v', 'a', 'p', 'pe')
                kb.inc(gp.dma_start(kb.dbgout, kb.bigB[:, 24576:28672]), 'xg', 16)
                gp.wait_ge(kb.sems['xg'], kb.n['xg'])
            self.blk.gpsimd(g)
        elif name.startswith('tabs') or name.startswith('wsc'):
            if name.startswith('tabs'):
                parts = [(kb.powre, 68), (kb.powim, 68), (kb.qsre, 32),
                         (kb.qsim, 32), (kb.qcre, 256), (kb.qcim, 256),
                         (kb.psclre, 64), (kb.psclim, 64), (kb.ppre, 16),
                         (kb.ppim, 16)]
            else:
                parts = [(kb.wre, 256), (kb.wim, 256), (kb.Sre, 256),
                         (kb.Sim, 256), (kb.gath, 64), (kb.sinre, 2 * 0 + 4),
                         (kb.sinim, 4)]

            def s(sync):
                kb.w(sync, 'v', 'a', 'p', 'pe')
                off = 0
                i = None
                for tns, wdt in parts:
                    i = sync.dma_start(kb.dbgout[:, off:off + wdt], tns[:, 0:wdt])
                    kb.inc(i, 'o', 16)
                    off += wdt
                sync.wait_ge(kb.sems['o'], kb.n['o'])
            self.blk.sync(s)
        elif name == 'x2pre':
            def g(gp):
                kb.w(gp, 'v', 'a', 'p', 'pe')
                kb.inc(gp.dma_start(kb.dbgout, kb.bigB[:]), 'xg', 16)
                gp.wait_ge(kb.sems['xg'], kb.n['xg'])
            self.blk.gpsimd(g)
        self.done = True
        return True

    def build(self):
        nc = self.nc
        with nc.Block() as blk:
            self.blk = blk
            self.ph_pre()
            self.ph_embed()
            if self.dump('h0'):
                return nc
            for i in range(NL):
                if self.ph_layer(i):
                    return nc
            self.ph_lnl_wout()
            if self.done:
                return nc
            if self.dump('x1pre'):
                return nc
            self.ph_postln1()
            if self.dump('x1'):
                return nc
            self.ph_masks()
            if self.done:
                return nc
            if self.dump('masks'):
                return nc
            self.ph_dense()
            if self.dump('x2pre'):
                return nc
            self.ph_postln2_out()
            if self.done:
                return nc
            self.ph_finish()
        return nc

    def ph_pre(self):
        kb = self

        def s(sync):
            for dst, src in ((kb.chps, kb.chp), (kb.bins, kb.binr),
                             (kb.b1s, kb.b1all), (kb.b2s, kb.b2all),
                             (kb.lnls, kb.lnlwb), (kb.bouts, kb.boutr),
                             (kb.fbs, kb.fbr), (kb.cms, kb.cmask),
                             (kb.lcsel, kb.lcselr), (kb.io64, kb.iot64)):
                kb.inc(sync.dma_start(dst[:], src), 'xs', 16)
        self.blk.sync(s)

        def v(vec):
            vec.memset(kb.inv512[:], 1.0 / 512)
            vec.memset(kb.inv512f[:], 1.0 / 512)
            vec.memset(kb.inv4096[:], 1.0 / 4096)
            vec.memset(kb.ones1[:], 1.0)
            vec.memset(kb.ones64[:], 1.0)
            vec.memset(kb.cm05[:], -0.5)
            i = vec.memset(kb.cp05[:], 0.5)
            kb.inc(i, 'v')
        self.blk.vector(v)

    def ph_embed(self):
        kb = self
        seq = [(th, q, kc) for th in range(2) for q in range(4) for kc in range(8)]
        base_pe = self.n['pe']
        xpe = {idx: base_pe + idx + 1 for idx in range(64)}
        winpe = {tq: base_pe + (tq + 1) * 8 for tq in range(8)}
        ew, ex = self.streams['ew'], self.streams['ex']

        def gdma(gp):
            kb.w(gp, 'xs')

            def win_load(tq):
                kb.wv(gp, 'pe', winpe.get(tq - 2, 0))
                q = tq % 4
                i = gp.dma_start((kb.wsA if tq % 2 == 0 else kb.wsB)[:],
                                 kb.winb[q])
                ew.issue(i, tq)
            win_load(0)
            win_load(1)
            for idx, (th, q, kc) in enumerate(seq):
                k = q * 8 + kc
                kb.wv(gp, 'pe', xpe.get(idx - 4, 0))
                src = bass.AP(kb.xT.tensor, (k * 128) * T + th * TH,
                              [[T, 128], [1, TH]])
                i = gp.dma_start(kb.xtc[:, (idx % 4) * TH:(idx % 4 + 1) * TH], src)
                ex.issue(i, idx)
                if idx % 8 == 0 and idx >= 8 and idx // 8 + 1 <= 7:
                    win_load(idx // 8 + 1)
        self.blk.gpsimd(gdma)

        def te(t):
            kb.w(t, 'v')
            for idx, (th, q, kc) in enumerate(seq):
                tq = th * 4 + q
                ws = kb.wsA if tq % 2 == 0 else kb.wsB
                ew.wait(t, tq)
                ex.wait(t, idx)
                first = (q == 0 and kc == 0)
                last = (q == 3 and kc == 7)
                i = None
                for m in range(4):
                    i = t.matmul(kb.pb[th * 4 + m][:, 0:TH],
                                 ws[:, kc * 512 + m * 128: kc * 512 + (m + 1) * 128],
                                 kb.xtc[:, (idx % 4) * TH:(idx % 4 + 1) * TH],
                                 start=first, stop=last)
                assert kb.inc(i, 'pe') == xpe[idx]
        self.blk.tensor(te)

        def ve(v):
            for th in range(2):
                kb.wv(v, 'pe', xpe[th * 32 + 31])
                i = None
                for m in range(4):
                    i = v.tensor_scalar(kb.hgt(m, th), kb.pb[th * 4 + m][:, 0:TH],
                                        kb.bins[:, m:m + 1], None, ALU.add)
                kb.inc(i, 'v')
        self.blk.vector(ve)

    def cast_chunks(self, srcf, nk):
        kb = self

        def a(sc):
            kb.w(sc, 'v', 'p', 'pe')
            i = None
            for k in range(nk):
                sc.activation(kb.hb[:, k * TH:(k + 1) * TH], srcf(k), AF.Identity)
                i = sc.activation(kb.hbsq[:, k * TH:(k + 1) * TH], srcf(k), AF.Square)
            kb.inc(i, 'a')
        self.blk.scalar(a)

    def stats_rows(self, nk, inv, pb_sum, pb_sq):
        kb = self

        def te(t):
            kb.w(t, 'a', 'v')
            i = None
            for k in range(nk):
                i = t.matmul(pb_sum[0:1, 0:TH], inv,
                             kb.hb[:, k * TH:(k + 1) * TH],
                             start=(k == 0), stop=(k == nk - 1))
            for k in range(nk):
                i = t.matmul(pb_sq[0:1, 0:TH], inv,
                             kb.hbsq[:, k * TH:(k + 1) * TH],
                             start=(k == 0), stop=(k == nk - 1))
            kb.inc(i, 'pe')
        self.blk.tensor(te)

    def rows_and_bcast(self, pb_sum, pb_sq, pb_mub, pb_rstdb):
        kb = self
        mu_bf = self.rowb[0:1, 0:TH]
        rstd_bf = self.rowb[0:1, TH:2 * TH]
        musq = self.rowf[0:1, 0:TH]
        varep = self.rowf[0:1, TH:2 * TH]

        def a(sc):
            kb.w(sc, 'pe')
            sc.activation(musq, pb_sum[0:1, 0:TH], AF.Square)
            i = sc.activation(mu_bf, pb_sum[0:1, 0:TH], AF.Identity)
            kb.inc(i, 'a')
        self.blk.scalar(a)

        def v(vec):
            kb.w(vec, 'a')
            vec.scalar_tensor_tensor(varep, musq, -1.0, pb_sq[0:1, 0:TH],
                                     ALU.mult, ALU.add)
            i = vec.tensor_scalar(varep, varep, EPS, None, ALU.add)
            kb.inc(i, 'v')
        self.blk.vector(v)

        def a2(sc):
            kb.w(sc, 'v')
            sc.activation(varep, varep, AF.Ln)
            i = sc.activation(rstd_bf, varep, AF.Exp, scale=kb.cm05[0:1, 0:1])
            kb.inc(i, 'a')
        self.blk.scalar(a2)

        def te(t):
            kb.w(t, 'a', 'v', 'p')
            t.matmul(pb_mub[:, 0:TH], kb.ones1[:], mu_bf, start=True, stop=True)
            i = t.matmul(pb_rstdb[:, 0:TH], kb.ones1[:], rstd_bf, start=True, stop=True)
            kb.inc(i, 'pe')
        self.blk.tensor(te)

    def colbc(self, tns, col, row, cnt, gstep=None, gcnt=None):
        t = tns.ap().tensor if hasattr(tns, 'ap') else tns.tensor
        if gstep is None:
            return bass.AP(t, col, [[row, 128], [0, cnt], [1, 1]])
        return bass.AP(t, col, [[row, 128], [gstep, gcnt], [0, cnt], [1, 1]])

    def ph_layer(self, li):
        self.ph_w1_prefetch(li)
        self.ph_psetup_pre(li)
        for th in range(2):
            self.ln_d512_stats(li, th)
        self.ph_psetup_post(li)
        if self.dump(f'tabs{li}'):
            return True
        self.ln_d512_apply(li, 'ln1')
        if self.dump(f'y{li}'):
            return True
        self.ph_sconv(li)
        if self.done:
            return True
        if self.dump(f'hmix{li}'):
            return True
        for th in range(2):
            self.ln_d512_stats(li, th, kind='ln2')
        self.ln_d512_apply(li, 'ln2')
        self.ph_w1(li)
        self.ph_w2(li)
        if self.dump(f'hffn{li}'):
            return True
        return False

    def _muS(self, th):
        return self.mdv(th * 2 - th) if False else (
            self.mdv(0) if th == 0 else self.qsb[:, 0:TH])

    def _rstdS(self, th):
        return self.mdv(1) if th == 0 else self.qsb[:, TH:2 * TH]

    def _sqbuf(self, th):
        return (self.bigA[:, 26624:28672] if th == 0
                else self.bigA[:, 22528:24576])

    def _rowf2(self, th):
        if th == 0:
            return self.rowf
        return self.bigA.ap().bitcast(F32)[0:1, 0:1024]

    def _rowb2t(self, th):
        if th == 0:
            return self.rowb
        return self.bigA[0:1, 28672:29696]

    def _hcast(self, th):
        return (self.bigA[:, 16384:18432] if th == 0
                else self.bigA[:, 18432:20480])

    def ln_d512_stats(self, li, th, kind='ln1'):
        kb = self
        b0 = th * 4
        sqb = self._sqbuf(th)
        hc = self._hcast(th)
        p0 = self.n['p']

        def psq(gp):
            kb.w(gp, 'v', 'pe')
            for g in range(4):
                gp.tensor_copy(hc[:, g * TH:(g + 1) * TH], kb.hgt(g, th))
                i = gp.tensor_tensor(sqb[:, g * TH:(g + 1) * TH],
                                     kb.hgt(g, th), kb.hgt(g, th), ALU.mult)
                kb.inc(i, 'p')
        self.blk.gpsimd(psq)

        def te(t):
            kb.w(t, 'a', 'v')
            i = None
            for g in range(4):
                kb.wv(t, 'p', p0 + g + 1)
                i = t.matmul(kb.pb[b0][0:1, 0:TH], kb.inv512[:],
                             hc[:, g * TH:(g + 1) * TH],
                             start=(g == 0), stop=(g == 3))
            for g in range(4):
                i = t.matmul(kb.pb[b0 + 1][0:1, 0:TH], kb.inv512[:],
                             sqb[:, g * TH:(g + 1) * TH],
                             start=(g == 0), stop=(g == 3))
            kb.inc(i, 'pe')
        self.blk.tensor(te)
        pe_stats = self.n['pe']
        rf = self._rowf2(th)
        rb = self._rowb2t(th)
        mu_bf = rb[0:1, 0:TH] if th == 0 else rb[0:1, 0:TH]
        rstd_bf = rb[0:1, TH:2 * TH] if th == 0 else None
        if th != 0:
            rstd_bf = self.bigA[0:1, 29184:29184 + TH]
        musq = rf[0:1, 0:TH]
        varep = rf[0:1, TH:2 * TH]

        def a(sc):
            kb.wv(sc, 'pe', pe_stats)
            sc.activation(musq, kb.pb[b0][0:1, 0:TH], AF.Square)
            i = sc.activation(mu_bf, kb.pb[b0][0:1, 0:TH], AF.Identity)
            kb.inc(i, 'a')
        self.blk.scalar(a)
        av = self.n['a']

        def v(vec):
            kb.wv(vec, 'a', av)
            vec.scalar_tensor_tensor(varep, musq, -1.0, kb.pb[b0 + 1][0:1, 0:TH],
                                     ALU.mult, ALU.add)
            i = vec.tensor_scalar(varep, varep, EPS, None, ALU.add)
            kb.inc(i, 'v')
        self.blk.vector(v)
        vv = self.n['v']

        def a2(sc):
            kb.wv(sc, 'v', vv)
            sc.activation(varep, varep, AF.Ln)
            i = sc.activation(rstd_bf, varep, AF.Exp, scale=kb.cm05[0:1, 0:1])
            kb.inc(i, 'a')
        self.blk.scalar(a2)
        av2 = self.n['a']

        def te2(t):
            kb.wv(t, 'a', av2)
            kb.w(t, 'v', 'p')
            t.matmul(kb.pb[b0 + 2][:, 0:TH], kb.ones1[:], mu_bf,
                     start=True, stop=True)
            i = t.matmul(kb.pb[b0 + 3][:, 0:TH], kb.ones1[:], rstd_bf,
                         start=True, stop=True)
            kb.inc(i, 'pe')
        self.blk.tensor(te2)
        pe_b = self.n['pe']
        if not hasattr(self, '_lnrows'):
            self._lnrows = {}
            self._lnbc = {}
        self._lnbc[th] = pe_b
        if th == 1:
            def vc(vec):
                kb.wv(vec, 'pe', pe_b)
                vec.tensor_copy(kb._muS(th), kb.pb[b0 + 2][:, 0:TH])
                i = vec.tensor_copy(kb._rstdS(th), kb.pb[b0 + 3][:, 0:TH])
                kb.inc(i, 'v')
            self.blk.vector(vc)
        self._lnrows[th] = self.n['v']

    def ln_d512_apply(self, li, kind):
        kb = self
        jg = 0 if kind == 'ln1' else 7
        jb = 1 if kind == 'ln1' else 8
        p0 = self.n['p']
        v0 = self.n['v']
        self._apply_p0 = p0
        self._apply_v0 = v0

        def po(gp):
            for g in range(4):
                kb.wv(gp, 'v', kb._lnrows[1])
                d = kb.yv(g, 1)
                gam = kb.colbc(kb.chps, (li * 9 + jg) * 4 + g, NL * 9 * 4, TH)
                gp.tensor_tensor(d, kb.hgt(g, 1), kb._muS(1), ALU.subtract)
                gp.tensor_tensor(d, d, kb._rstdS(1), ALU.mult)
                i = gp.tensor_tensor(
                    d.rearrange("p (c o) -> p c o", o=1),
                    d.rearrange("p (c o) -> p c o", o=1), gam, ALU.mult)
                kb.inc(i, 'p')
        self.blk.gpsimd(po)

        def vo(vec):
            for g in range(4):
                kb.wv(vec, 'pe', kb._lnbc[0])
                d = kb.yv(g, 0)
                vec.tensor_tensor(kb.sc1[:], kb.hgt(g, 0), kb.pb[2][:, 0:TH],
                                  ALU.subtract)
                i = vec.scalar_tensor_tensor(d, kb.sc1[:], kb.chan(li, jg, g),
                                             kb.pb[3][:, 0:TH], ALU.mult, ALU.mult)
                kb.inc(i, 'v')
        self.blk.vector(vo)

        self._silu_a0 = self.n['a']

        def a(sc):
            for g in range(4):
                for th in range(2):
                    if th == 0:
                        kb.wv(sc, 'v', v0 + g + 1)
                        src = kb.yv(g, 0)
                    else:
                        kb.wv(sc, 'p', p0 + g + 1)
                        src = kb.yv(g, 1)
                    if kind == 'ln1':
                        i = sc.activation(src, src, AF.Silu, bias=kb.chan(li, jb, g))
                    else:
                        i = sc.activation(kb.lv(g, th), src, AF.Identity,
                                          bias=kb.chan(li, jb, g))
                    kb.inc(i, 'a')
        self.blk.scalar(a)

    def ph_psetup_pre(self, li):
        kb = self
        pre, pim = self.t4a, self.t4b
        r2, tmp = self.t4c, self.t4d
        rr = self.sinre
        rsave = self.sinim

        def p1(gp):
            kb.w(gp, 'xs', 'a', 'p')
            gp.tensor_tensor(r2[:], kb.chan(li, 2), kb.chan(li, 2), ALU.mult)
            gp.tensor_tensor(tmp[:], kb.chan(li, 3), kb.chan(li, 3), ALU.mult)
            gp.tensor_tensor(r2[:], r2[:], tmp[:], ALU.add)
            i = gp.tensor_scalar(r2[:], r2[:], 1e-30, None, ALU.add)
            kb.inc(i, 'p')
        self.blk.gpsimd(p1)
        pv = self.n['p']

        def a1(sc):
            kb.wv(sc, 'p', pv)
            sc.activation(tmp[:], r2[:], AF.Ln)
            i = sc.activation(rr[:, 0:4], tmp[:], AF.Exp, scale=kb.cp05[:, 0:1])
            kb.inc(i, 'a')
        self.blk.scalar(a1)
        av = self.n['a']

        def v1(vec):
            kb.wv(vec, 'a', av)
            i = vec.reciprocal(tmp[:], rr[:, 0:4])
            kb.inc(i, 'v')
        self.blk.vector(v1)
        vv = self.n['v']

        def p2(gp):
            kb.wv(gp, 'v', vv)
            gp.tensor_tensor(rsave[:, 0:4], r2[:], tmp[:], ALU.mult)
            gp.tensor_tensor(rsave[:, 0:4], rsave[:, 0:4], rr[:, 0:4], ALU.add)
            i = gp.tensor_scalar(r2[:], rsave[:, 0:4], 0.5, None, ALU.mult)
            kb.inc(i, 'p')
        self.blk.gpsimd(p2)
        pv2 = self.n['p']

        def v2(vec):
            kb.wv(vec, 'p', pv2)
            i = vec.reciprocal(tmp[:], r2[:])
            kb.inc(i, 'v')
        self.blk.vector(v2)
        self._ps_v = self.n['v']

    def ph_psetup_post(self, li):
        kb = self
        pre, pim = self.t4a, self.t4b
        r2, tmp = self.t4c, self.t4d

        def o1(x):
            return x.rearrange("p (c o) -> p c o", o=1)

        def a2(sc):
            kb.wv(sc, 'v', kb._ps_v)
            i = sc.activation(r2[:], r2[:], AF.Tanh)
            kb.inc(i, 'a')
        self.blk.scalar(a2)
        av = self.n['a']

        def p4(gp):
            kb.wv(gp, 'a', av)
            gp.tensor_tensor(tmp[:], r2[:], tmp[:], ALU.mult)
            gp.tensor_tensor(pre[:], kb.chan(li, 2), tmp[:], ALU.mult)
            gp.tensor_tensor(pim[:], kb.chan(li, 3), tmp[:], ALU.mult)
            powre3 = kb.powre[:, :].rearrange("p (g k) -> p g k", g=4)
            powim3 = kb.powim[:, :].rearrange("p (g k) -> p g k", g=4)
            for g in range(4):
                gp.memset(powre3[:, g, 0:1], 1.0)
                gp.memset(powim3[:, g, 0:1], 0.0)
            gp.tensor_copy(powre3[:, :, 1:2], o1(pre[:]))
            gp.tensor_copy(powim3[:, :, 1:2], o1(pim[:]))
            sA = kb.prB[:, :].rearrange("p (g c) -> p g c", g=4)
            sB = kb.piB[:, :].rearrange("p (g c) -> p g c", g=4)
            for m in (1, 2, 4, 8):
                qr = kb.colbc(kb.powre, m, 4 * (C + 1), m, gstep=C + 1, gcnt=4)
                qi = kb.colbc(kb.powim, m, 4 * (C + 1), m, gstep=C + 1, gcnt=4)
                src_r = powre3[:, :, 1:m + 1].rearrange("p g (c o) -> p g c o", o=1)
                src_i = powim3[:, :, 1:m + 1].rearrange("p g (c o) -> p g c o", o=1)
                d_r = powre3[:, :, m + 1:2 * m + 1].rearrange("p g (c o) -> p g c o", o=1)
                d_i = powim3[:, :, m + 1:2 * m + 1].rearrange("p g (c o) -> p g c o", o=1)
                a_ = sA[:, :, 0:m].rearrange("p g (c o) -> p g c o", o=1)
                b_ = sB[:, :, 0:m].rearrange("p g (c o) -> p g c o", o=1)
                gp.tensor_tensor(a_, src_r, qr, ALU.mult)
                gp.tensor_tensor(b_, src_i, qi, ALU.mult)
                gp.tensor_tensor(d_r, a_, b_, ALU.subtract)
                gp.tensor_tensor(a_, src_r, qi, ALU.mult)
                gp.tensor_tensor(b_, src_i, qr, ALU.mult)
                gp.tensor_tensor(d_i, a_, b_, ALU.add)
            qre = kb.qsre[:, :].rearrange("p (g s) -> p g s", g=4)
            qim = kb.qsim[:, :].rearrange("p (g s) -> p g s", g=4)
            gp.tensor_copy(qre[:, :, 0:1], powre3[:, :, C:C + 1])
            gp.tensor_copy(qim[:, :, 0:1], powim3[:, :, C:C + 1])
            for st in range(6):
                a_ = qre[:, :, st:st + 1]
                b_ = qim[:, :, st:st + 1]
                gp.tensor_tensor(o1(kb.sinre[:]), a_, a_, ALU.mult)
                gp.tensor_tensor(o1(kb.sinim[:]), b_, b_, ALU.mult)
                gp.tensor_tensor(qre[:, :, st + 1:st + 2], o1(kb.sinre[:]), o1(kb.sinim[:]), ALU.subtract)
                gp.tensor_tensor(o1(kb.sinre[:]), a_, b_, ALU.mult)
                gp.tensor_scalar(qim[:, :, st + 1:st + 2], o1(kb.sinre[:]), 2.0, None, ALU.mult)
            qcre3 = kb.qcre[:, :].rearrange("p (g c) -> p g c", g=4)
            qcim3 = kb.qcim[:, :].rearrange("p (g c) -> p g c", g=4)
            for g in range(4):
                gp.memset(qcre3[:, g, 0:1], 1.0)
                gp.memset(qcim3[:, g, 0:1], 0.0)
            for st in range(6):
                m = 1 << st
                qr = kb.colbc(kb.qsre, st, 32, m, gstep=8, gcnt=4)
                qi = kb.colbc(kb.qsim, st, 32, m, gstep=8, gcnt=4)
                src_r = qcre3[:, :, 0:m].rearrange("p g (c o) -> p g c o", o=1)
                src_i = qcim3[:, :, 0:m].rearrange("p g (c o) -> p g c o", o=1)
                d_r = qcre3[:, :, m:2 * m].rearrange("p g (c o) -> p g c o", o=1)
                d_i = qcim3[:, :, m:2 * m].rearrange("p g (c o) -> p g c o", o=1)
                a_ = sA[:, :, 0:m].rearrange("p g (c o) -> p g c o", o=1)
                b_ = sB[:, :, 0:m].rearrange("p g (c o) -> p g c o", o=1)
                gp.tensor_tensor(a_, src_r, qr, ALU.mult)
                gp.tensor_tensor(b_, src_i, qi, ALU.mult)
                gp.tensor_tensor(d_r, a_, b_, ALU.subtract)
                gp.tensor_tensor(a_, src_r, qi, ALU.mult)
                gp.tensor_tensor(b_, src_i, qr, ALU.mult)
                gp.tensor_tensor(d_i, a_, b_, ALU.add)
            for g in range(4):
                scl_b = kb.colbc(kb.chps, (li * 9 + 6) * 4 + g, NL * 9 * 4, C)
                gp.tensor_tensor(o1(kb.psclre[:, g * C:(g + 1) * C]),
                                 o1(kb.powre[:, g * (C + 1) + 1:g * (C + 1) + C + 1]),
                                 scl_b, ALU.mult)
                gp.tensor_tensor(o1(kb.psclim[:, g * C:(g + 1) * C]),
                                 o1(kb.powim[:, g * (C + 1) + 1:g * (C + 1) + C + 1]),
                                 scl_b, ALU.mult)
                gp.tensor_scalar(kb.psclim[:, g * C:(g + 1) * C],
                                 kb.psclim[:, g * C:(g + 1) * C], -1.0, None, ALU.mult)
            pp_re = kb.ppre[:, :].rearrange("p (g e) -> p g e", g=4)
            pp_im = kb.ppim[:, :].rearrange("p (g e) -> p g e", g=4)
            Pr = qre[:, :, 6:7]
            Pi = qim[:, :, 6:7]
            for g in range(4):
                gp.memset(pp_re[:, g, 0:1], 1.0)
                gp.memset(pp_im[:, g, 0:1], 0.0)
            gp.tensor_copy(pp_re[:, :, 1:2], Pr)
            gp.tensor_copy(pp_im[:, :, 1:2], Pi)
            i = None
            for e in (2, 3):
                ar = pp_re[:, :, e - 1:e]
                ai = pp_im[:, :, e - 1:e]
                gp.tensor_tensor(o1(kb.sinre[:]), ar, Pr, ALU.mult)
                gp.tensor_tensor(o1(kb.sinim[:]), ai, Pi, ALU.mult)
                gp.tensor_tensor(pp_re[:, :, e:e + 1], o1(kb.sinre[:]), o1(kb.sinim[:]), ALU.subtract)
                gp.tensor_tensor(o1(kb.sinre[:]), ar, Pi, ALU.mult)
                gp.tensor_tensor(o1(kb.sinim[:]), ai, Pr, ALU.mult)
                i = gp.tensor_tensor(pp_im[:, :, e:e + 1], o1(kb.sinre[:]), o1(kb.sinim[:]), ALU.add)
            kb.inc(i, 'p')
        self.blk.gpsimd(p4)
        self._p4p = self.n['p']

    def ph_sconv(self, li):
        kb = self
        wre3 = self.wre[:, :].rearrange("p (g c) -> p g c", g=4)
        wim3 = self.wim[:, :].rearrange("p (g c) -> p g c", g=4)
        Sre3 = self.Sre[:, :].rearrange("p (g c) -> p g c", g=4)
        Sim3 = self.Sim[:, :].rearrange("p (g c) -> p g c", g=4)
        prow = 4 * (C + 1)

        def o1(x):
            return x.rearrange("p (c o) -> p c o", o=1)

        def v(vec):
            kb.wv(vec, 'p', kb._p4p)
            for g in range(4):
                kb.wv(vec, 'a', kb._silu_a0 + 2 * g + 2)
                yg = kb.yv(g)
                y3 = yg.rearrange("p (c i) -> p c i", c=NCH)
                rev_re = bass.AP(kb.powre.ap().tensor, g * (C + 1) + (C - 1),
                                 [[prow, 128], [0, NCH], [-1, C]])
                rev_im = bass.AP(kb.powim.ap().tensor, g * (C + 1) + (C - 1),
                                 [[prow, 128], [0, NCH], [-1, C]])
                vec.tensor_tensor(kb.wtmp3(), y3, rev_re, ALU.mult)
                vec.reduce_sum(wre3[:, g, :], kb.wtmp3(), AX.X)
                vec.tensor_tensor(kb.wtmp3(), y3, rev_im, ALU.mult)
                vec.reduce_sum(wim3[:, g, :], kb.wtmp3(), AX.X)
            N4 = 4 * NCH
            f32v = kb.bigA.ap().bitcast(F32)
            tmp1 = f32v[:, 15360:15360 + N4]
            tmp2 = f32v[:, 15616:15616 + N4]
            cur_r, cur_i, nxt_r, nxt_i = kb.prA, kb.piA, kb.prB, kb.piB

            def g3v(t, lo, hi):
                return t[:].rearrange("p (g c) -> p g c", g=4)[:, :, lo:hi] \
                    .rearrange("p g (c o) -> p g c o", o=1)
            for st in range(6):
                m = 1 << st
                cnt = NCH - m
                qr_b = kb.colbc(kb.qsre, st, 32, cnt, gstep=8, gcnt=4)
                qi_b = kb.colbc(kb.qsim, st, 32, cnt, gstep=8, gcnt=4)
                vec.tensor_copy(g3v(nxt_r, 0, m), g3v(cur_r, 0, m))
                vec.tensor_copy(g3v(nxt_i, 0, m), g3v(cur_i, 0, m))
                tm1 = tmp1.rearrange("p (g c) -> p g c", g=4)[:, :, 0:cnt] \
                    .rearrange("p g (c o) -> p g c o", o=1)
                tm2 = tmp2.rearrange("p (g c) -> p g c", g=4)[:, :, 0:cnt] \
                    .rearrange("p g (c o) -> p g c o", o=1)
                vec.tensor_tensor(tm1, g3v(cur_r, 0, cnt), qr_b, ALU.mult)
                vec.tensor_tensor(tm2, g3v(cur_i, 0, cnt), qi_b, ALU.mult)
                vec.tensor_tensor(g3v(nxt_r, m, NCH), g3v(cur_r, m, NCH), tm1, ALU.add)
                vec.tensor_tensor(g3v(nxt_r, m, NCH), g3v(nxt_r, m, NCH), tm2, ALU.subtract)
                vec.tensor_tensor(tm1, g3v(cur_i, 0, cnt), qr_b, ALU.mult)
                vec.tensor_tensor(tm2, g3v(cur_r, 0, cnt), qi_b, ALU.mult)
                vec.tensor_tensor(g3v(nxt_i, m, NCH), g3v(cur_i, m, NCH), tm1, ALU.add)
                vec.tensor_tensor(g3v(nxt_i, m, NCH), g3v(nxt_i, m, NCH), tm2, ALU.add)
                cur_r, nxt_r = nxt_r, cur_r
                cur_i, nxt_i = nxt_i, cur_i
            cr3 = cur_r[:].rearrange("p (g c) -> p g c", g=4)
            ci3 = cur_i[:].rearrange("p (g c) -> p g c", g=4)
            for g in range(4):
                vec.memset(Sre3[:, g, 0:1], 0.0)
                vec.memset(Sim3[:, g, 0:1], 0.0)
            vec.tensor_copy(Sre3[:, :, 1:NCH], cr3[:, :, 0:NCH - 1])
            vec.tensor_copy(Sim3[:, :, 1:NCH], ci3[:, :, 0:NCH - 1])
            vec.tensor_copy(
                kb.gath[:, 56:64].rearrange("p (g x) -> p g x", g=4)[:, :, 0:1],
                cr3[:, :, NCH - 1:NCH])
            i = vec.tensor_copy(
                kb.gath[:, 56:64].rearrange("p (g x) -> p g x", g=4)[:, :, 1:2],
                ci3[:, :, NCH - 1:NCH])
            kb.inc(i, 'v')
        self.blk.vector(v)

        def g1(gp):
            kb.w(gp, 'v')
            kb.inc(gp.dma_start(kb.cc_in[:, :], kb.gath[:, 56:64]), 'xg', 16)
            kb.w(gp, 'xg')
            i = gp.collective_compute(
                "AllGather", ALU.bypass, replica_groups=[list(range(8))],
                ins=[kb.cc_in.ap().opt()], outs=[kb.cc_out.ap().opt()])
            i.then_inc(kb.ccsems[li])
            gp.wait_ge(kb.ccsems[li], 1)
            src = bass.AP(kb.cc_out.ap().tensor, 0, [[8, 128], [1024, 8], [1, 8]])
            i = gp.dma_start(kb.gath[:].rearrange("p (r c) -> p r c", r=8), src)
            kb.inc(i, 'xg', 16)
        self.blk.gpsimd(g1)

        def vsh(vec):
            for g in range(4):
                kb.wv(vec, 'p', kb._apply_p0 + g + 1)
                yg = kb.yv(g)
                y3 = yg.rearrange("p (c i) -> p c i", c=NCH)
                h3 = kb.hg(g).rearrange("p (c i) -> p c i", c=NCH)
                vec.scalar_tensor_tensor(kb.hg(g), yg, kb.chan(li, 6, g),
                                         kb.hg(g), ALU.mult, ALU.add)
                for k in range(1, C):
                    i = vec.scalar_tensor_tensor(
                        h3[:, :, k:C], y3[:, :, 0:C - k],
                        kb.psclre[:, g * C + k - 1:g * C + k],
                        h3[:, :, k:C], ALU.mult, ALU.add)
                kb.inc(i, 'v')
        self.blk.vector(vsh)

        def pc(gp):
            kb.w(gp, 'xg')
            wjr = kb.t4a
            WJ = kb.prB[:, :].rearrange("p (g c) -> p g c", g=4)
            WJr = WJ[:, :, 0:9]
            WJi = WJ[:, :, 9:18]
            PR = kb.prA[:, :].rearrange("p (g c) -> p g c", g=4)
            prod = PR[:, :, 0:9]
            gp.memset(WJr, 0.0)
            gp.memset(WJi, 0.0)
            for e in range(4):
                cmsb = bass.AP(kb.cms.ap().tensor, e,
                               [[32, 128], [0, 4], [4, 8], [1, 1]])
                ppb_r = bass.AP(kb.ppre.ap().tensor, e,
                                [[16, 128], [4, 4], [0, 8], [1, 1]])
                ppb_i = bass.AP(kb.ppim.ap().tensor, e,
                                [[16, 128], [4, 4], [0, 8], [1, 1]])
                o4 = prod[:, :, 0:8].rearrange("p g (c o) -> p g c o", o=1)
                gp.tensor_tensor(o4, cmsb, ppb_r, ALU.mult)
                gp.tensor_tensor(WJr[:, :, 0:8], WJr[:, :, 0:8], prod[:, :, 0:8], ALU.add)
                gp.tensor_tensor(o4, cmsb, ppb_i, ALU.mult)
                gp.tensor_tensor(WJi[:, :, 0:8], WJi[:, :, 0:8], prod[:, :, 0:8], ALU.add)
                lcb = bass.AP(kb.lcsel.ap().tensor, e,
                              [[4, 128], [0, 4], [0, 1], [1, 1]])
                o41 = prod[:, :, 8:9].rearrange("p g (c o) -> p g c o", o=1)
                gp.tensor_tensor(o41, lcb, ppb_r[:, :, 0:1], ALU.mult)
                gp.tensor_tensor(WJr[:, :, 8:9], WJr[:, :, 8:9], prod[:, :, 8:9], ALU.add)
                gp.tensor_tensor(o41, lcb, ppb_i[:, :, 0:1], ALU.mult)
                gp.tensor_tensor(WJi[:, :, 8:9], WJi[:, :, 8:9], prod[:, :, 8:9], ALU.add)
            SJr = PR[:, :, 16:25]
            SJi = PR[:, :, 25:34]
            gathr = bass.AP(kb.gath.ap().tensor, 0, [[64, 128], [2, 4], [8, 8], [1, 1]])
            gathi = bass.AP(kb.gath.ap().tensor, 1, [[64, 128], [2, 4], [8, 8], [1, 1]])
            gp.tensor_copy(SJr[:, :, 0:8].rearrange("p g (c o) -> p g c o", o=1), gathr)
            gp.tensor_copy(SJi[:, :, 0:8].rearrange("p g (c o) -> p g c o", o=1), gathi)
            gp.tensor_copy(SJr[:, :, 8:9], kb.chan(li, 4).rearrange("p (g o) -> p g o", o=1))
            gp.tensor_copy(SJi[:, :, 8:9], kb.chan(li, 5).rearrange("p (g o) -> p g o", o=1))
            P1 = PR[:, :, 34:43]
            P2 = WJ[:, :, 18:27]
            P3 = PR[:, :, 43:52]
            P4 = WJ[:, :, 27:36]
            gp.tensor_tensor(P1, WJr, SJr, ALU.mult)
            gp.tensor_tensor(P2, WJi, SJi, ALU.mult)
            gp.tensor_tensor(P3, WJr, SJi, ALU.mult)
            i = gp.tensor_tensor(P4, WJi, SJr, ALU.mult)
            kb.inc(i, 'p')
        self.blk.gpsimd(pc)
        pcp = self.n['p']

        def v2(vec):
            kb.wv(vec, 'p', pcp)
            PR = kb.prA[:, :].rearrange("p (g c) -> p g c", g=4)
            WJ = kb.prB[:, :].rearrange("p (g c) -> p g c", g=4)
            vec.reduce_sum(o1(kb.sinre[:]), PR[:, :, 34:43], AX.X)
            vec.reduce_sum(o1(kb.t4c[:]), WJ[:, :, 18:27], AX.X)
            vec.reduce_sum(o1(kb.sinim[:]), PR[:, :, 43:52], AX.X)
            i = vec.reduce_sum(o1(kb.t4d[:]), WJ[:, :, 27:36], AX.X)
            fv = kb.inc(i, 'v')
            vec.wait_ge(kb.sems['v'], fv)
            vec.tensor_tensor(kb.sinre[:], kb.sinre[:], kb.t4c[:], ALU.subtract)
            i = vec.tensor_tensor(kb.sinim[:], kb.sinim[:], kb.t4d[:], ALU.add)
            fv = kb.inc(i, 'v')
            vec.wait_ge(kb.sems['v'], fv)
            qcre3 = kb.qcre[:, :].rearrange("p (g c) -> p g c", g=4)
            qcim3 = kb.qcim[:, :].rearrange("p (g c) -> p g c", g=4)
            i = None
            for g in range(4):
                sr = kb.sinre[:, g:g + 1]
                si = kb.sinim[:, g:g + 1]
                vec.scalar_tensor_tensor(Sre3[:, g, :], qcre3[:, g, :], sr,
                                         Sre3[:, g, :], ALU.mult, ALU.add)
                vec.tensor_scalar(kb.prA[:, 0:NCH], qcim3[:, g, :], si, None, ALU.mult)
                vec.tensor_tensor(Sre3[:, g, :], Sre3[:, g, :], kb.prA[:, 0:NCH], ALU.subtract)
                vec.scalar_tensor_tensor(Sim3[:, g, :], qcim3[:, g, :], sr,
                                         Sim3[:, g, :], ALU.mult, ALU.add)
                vec.tensor_scalar(kb.prA[:, 0:NCH], qcre3[:, g, :], si, None, ALU.mult)
                vec.tensor_tensor(Sim3[:, g, :], Sim3[:, g, :], kb.prA[:, 0:NCH], ALU.add)
                h3 = kb.hg(g).rearrange("p (c i) -> p c i", c=NCH)
                for ii in range(C):
                    vec.scalar_tensor_tensor(
                        h3[:, :, ii:ii + 1],
                        Sre3[:, g, :].rearrange("p (c o) -> p c o", o=1),
                        kb.psclre[:, g * C + ii:g * C + ii + 1],
                        h3[:, :, ii:ii + 1], ALU.mult, ALU.add)
                    i = vec.scalar_tensor_tensor(
                        h3[:, :, ii:ii + 1],
                        Sim3[:, g, :].rearrange("p (c o) -> p c o", o=1),
                        kb.psclim[:, g * C + ii:g * C + ii + 1],
                        h3[:, :, ii:ii + 1], ALU.mult, ALU.add)
            kb.inc(i, 'v')
        self.blk.vector(v2)

    def ph_w1_prefetch(self, li):
        kb = self
        w1s = self.streams['w1']
        w1i = 2 * li

        def g(gp):
            kb.w(gp, 'pe')
            i = gp.dma_start(kb.wsA[:], kb.w1b[li, 0])
            w1s.issue(i, w1i)
            i = gp.dma_start(kb.wsB[:], kb.w1b[li, 1])
            w1s.issue(i, w1i + 1)
        self.blk.gpsimd(g)

    def ph_w1(self, li):
        kb = self
        w1s = self.streams['w1']
        w1i = 2 * li
        a0 = self.n['a']
        pe0 = self.n['pe']
        self._w1_pe0 = pe0

        def te(t):
            kb.w(t, 'a')
            cnt = 0
            for hf in range(2):
                w1s.wait(t, w1i + hf)
                ws = kb.wsA if hf == 0 else kb.wsB
                for mm in range(8):
                    for th in range(2):
                        pbk = kb.pb[4 + (cnt % 2)]
                        if cnt >= 2:
                            kb.wv(t, 'a', a0 + cnt - 1)
                        i = None
                        for kc in range(4):
                            i = t.matmul(
                                pbk[:, 0:TH],
                                ws[:, kc * 1024 + mm * 128: kc * 1024 + (mm + 1) * 128],
                                kb.lv(kc, th), start=(kc == 0), stop=(kc == 3))
                        kb.inc(i, 'pe')
                        cnt += 1
        self.blk.tensor(te)

        def a(sc):
            cnt = 0
            for hf in range(2):
                for mm in range(8):
                    m = hf * 8 + mm
                    for th in range(2):
                        pbk = kb.pb[4 + (cnt % 2)]
                        kb.wv(sc, 'pe', pe0 + cnt + 1)
                        dst = kb.bigB[:, m * T + th * TH: m * T + (th + 1) * TH]
                        i = sc.activation(dst, pbk[:, 0:TH], AF.Silu,
                                          bias=kb.b1s[:, li * 16 + m:li * 16 + m + 1])
                        got = kb.inc(i, 'a')
                        assert got == a0 + cnt + 1
                        cnt += 1
        self.blk.scalar(a)

    def ph_w2(self, li):
        kb = self
        w2s = self.streams['w2']
        w2i = 2 * li
        w1_done = self.n['pe']

        def g(gp):
            kb.wv(gp, 'pe', kb._w1_pe0 + 16)
            i = gp.dma_start(kb.wsA[:], kb.w2b[li, 0])
            w2s.issue(i, w2i)
            kb.wv(gp, 'pe', w1_done)
            i = gp.dma_start(kb.wsB[:], kb.w2b[li, 1])
            w2s.issue(i, w2i + 1)
        self.blk.gpsimd(g)

        v0 = self.n['v']
        pe0 = self.n['pe']

        def te(t):
            kb.w(t, 'a')
            w2s.wait(t, w2i)
            cnt = 0
            for m in range(4):
                for th in range(2):
                    pbk = kb.pb[6 + (cnt % 2)]
                    if cnt >= 2:
                        kb.wv(t, 'v', v0 + cnt - 1)
                    i = None
                    for kc in range(16):
                        if kc == 8 and cnt == 0:
                            w2s.wait(t, w2i + 1)
                        hf = kc // 8
                        kch = kc % 8
                        ws = kb.wsA if hf == 0 else kb.wsB
                        i = t.matmul(
                            pbk[:, 0:TH],
                            ws[:, kch * 512 + m * 128: kch * 512 + (m + 1) * 128],
                            kb.bigB[:, kc * T + th * TH: kc * T + (th + 1) * TH],
                            start=(kc == 0), stop=(kc == 15))
                    kb.inc(i, 'pe')
                    cnt += 1
        self.blk.tensor(te)

        def v(vec):
            cnt = 0
            for m in range(4):
                for th in range(2):
                    pbk = kb.pb[6 + (cnt % 2)]
                    kb.wv(vec, 'pe', pe0 + cnt + 1)
                    i = vec.scalar_tensor_tensor(kb.hgt(m, th), pbk[:, 0:TH],
                                                 kb.b2s[:, li * 4 + m:li * 4 + m + 1],
                                                 kb.hgt(m, th), ALU.add, ALU.add)
                    got = kb.inc(i, 'v')
                    assert got == v0 + cnt + 1
                    cnt += 1
        self.blk.vector(v)

    def ph_lnl_wout(self):
        kb = self
        for th in range(2):
            self.cast_chunks(lambda k: kb.hgt(k, th), 4)
            self.stats_rows(4, kb.inv512[:], self.pb[0], self.pb[1])
            self.rows_and_bcast(self.pb[0], self.pb[1], self.pb[2], self.pb[3])
            v0 = self.n['v']

            def v(vec, th=th):
                kb.w(vec, 'pe')
                for g in range(4):
                    vec.tensor_tensor(kb.sc1[:], kb.hgt(g, th), kb.pb[2][:, 0:TH],
                                      ALU.subtract)
                    i = vec.scalar_tensor_tensor(kb.yv(g, th), kb.sc1[:],
                                                 kb.lnls[:, g:g + 1],
                                                 kb.pb[3][:, 0:TH], ALU.mult, ALU.mult)
                    kb.inc(i, 'v')
            self.blk.vector(v)

            def a(sc, th=th, v0=v0):
                for g in range(4):
                    kb.wv(sc, 'v', v0 + g + 1)
                    i = sc.activation(kb.lv(g, th), kb.yv(g, th), AF.Identity,
                                      bias=kb.lnls[:, 4 + g:5 + g])
                    kb.inc(i, 'a')
            self.blk.scalar(a)

        if self.dump('hL'):
            return
        wow, wox = self.streams['wow'], self.streams['wox']
        base_pe = self.n['pe']
        def grp_pe(cnt):
            return base_pe + (cnt + 1) + max(0, cnt - 2)
        def stats_pe(cnt):
            if cnt <= 61:
                return grp_pe(cnt + 2) + 1
            return base_pe + 126 + (cnt - 61)
        base_v = self.n['v']
        base_a = self.n['a']

        def g(gp):
            kb.w(gp, 'a')

            def wout_load(q):
                if q >= 2:
                    kb.wv(gp, 'pe', grp_pe((q - 1) * 16 - 1))
                i = gp.dma_start((kb.wsA if q % 2 == 0 else kb.wsB)[:], kb.woutb[q])
                wow.issue(i, q)
            wout_load(0)
            wout_load(1)
            for m in range(32):
                if m >= 2:
                    kb.wv(gp, 'v', base_v + (m - 2) * 2 + 2)
                src = bass.AP(kb.xT.tensor, m * 128 * T, [[T, 128], [1, T]])
                i = gp.dma_start(kb.xt(m % 2), src)
                wox.issue(i, m)
                if m == 7:
                    wout_load(2)
                elif m == 15:
                    wout_load(3)
        self.blk.gpsimd(g)

        def te(t):
            kb.w(t, 'v', 'a')
            cnt = 0
            for q in range(4):
                wow.wait(t, q)
                ws = kb.wsA if q % 2 == 0 else kb.wsB
                for mq in range(8):
                    m = q * 8 + mq
                    wox.wait(t, m)
                    for th in range(2):
                        pbk = kb.pb[cnt % 2]
                        if cnt >= 2:
                            kb.wv(t, 'v', base_v + cnt - 1)
                        i = None
                        for kc in range(4):
                            i = t.matmul(
                                pbk[:, 0:TH],
                                ws[:, kc * 1024 + mq * 128: kc * 1024 + (mq + 1) * 128],
                                kb.lv(kc, th), start=(kc == 0), stop=(kc == 3))
                        got = kb.inc(i, 'pe')
                        assert got == grp_pe(cnt)
                        if cnt >= 2:
                            kb._wo_stats_emit(t, cnt - 2, base_a, stats_pe)
                        cnt += 1
            for j in (62, 63):
                kb._wo_stats_emit(t, j, base_a, stats_pe)
        self.blk.tensor(te)

        def v(vec):
            cnt = 0
            for m in range(32):
                for th in range(2):
                    pbk = kb.pb[cnt % 2]
                    kb.wv(vec, 'pe', grp_pe(cnt))
                    wox.wait(vec, m)
                    dst = kb.bigA[:, m * T + th * TH: m * T + (th + 1) * TH]
                    i = vec.scalar_tensor_tensor(
                        dst, pbk[:, 0:TH], kb.bouts[:, m:m + 1],
                        kb.xt(m % 2)[:, th * TH:(th + 1) * TH], ALU.add, ALU.add)
                    got = kb.inc(i, 'v')
                    assert got == base_v + cnt + 1
                    cnt += 1
        self.blk.vector(v)

        def a(sc):
            cnt = 0
            for m in range(32):
                for th in range(2):
                    kb.wv(sc, 'v', base_v + cnt + 1)
                    if cnt >= 2:
                        kb.wv(sc, 'pe', stats_pe(cnt - 2))
                    src = kb.bigA[:, m * T + th * TH: m * T + (th + 1) * TH]
                    i = sc.activation(kb.wsq[:, th * TH:(th + 1) * TH], src, AF.Square)
                    got = kb.inc(i, 'a')
                    assert got == base_a + cnt + 1
                    cnt += 1
        self.blk.scalar(a)

    def _wo_stats_emit(self, t, cnt, base_a, stats_pe):
        kb = self
        m, th = cnt // 2, cnt % 2
        kb.wv(t, 'a', base_a + cnt + 1)
        src = kb.bigA[:, m * T + th * TH: m * T + (th + 1) * TH]
        t.matmul(kb.pb[2 + th][0:1, 0:TH], kb.inv4096[:], src,
                 start=(m == 0), stop=(m == 31))
        i = t.matmul(kb.pb[4 + th][0:1, 0:TH], kb.inv4096[:],
                     kb.wsq[:, th * TH:(th + 1) * TH],
                     start=(m == 0), stop=(m == 31))
        got = kb.inc(i, 'pe')
        assert got == stats_pe(cnt)

    def postln_apply(self, big, to_out):
        kb = self
        for th in range(2):
            self.rows_and_bcast(self.pb[2 + th], self.pb[4 + th],
                                self.pb[6], self.pb[7])

            def ve(vec, th=th):
                kb.w(vec, 'pe', 'p')
                vec.tensor_copy(kb.mub[:], kb.pb[6][:, 0:TH])
                i = vec.tensor_copy(kb.rstdb[:], kb.pb[7][:, 0:TH])
                kb.inc(i, 'v')
            self.blk.vector(ve)

            if not to_out:
                p0 = self.n['p']
                v0 = self.n['v']

                def po(gp, th=th, v0=v0):
                    kb.wv(gp, 'v', v0)
                    i = None
                    for m in range(16, 32):
                        src = big[:, m * T + th * TH: m * T + (th + 1) * TH]
                        gp.tensor_tensor(src, src, kb.mub[:], ALU.subtract)
                        i = gp.tensor_tensor(src, src, kb.rstdb[:], ALU.mult)
                    kb.inc(i, 'p')
                self.blk.gpsimd(po)

                def vo(vec, th=th):
                    i = None
                    for m in range(16):
                        src = big[:, m * T + th * TH: m * T + (th + 1) * TH]
                        vec.tensor_tensor(src, src, kb.mub[:], ALU.subtract)
                        i = vec.tensor_tensor(src, src, kb.rstdb[:], ALU.mult)
                    kb.inc(i, 'v')
                self.blk.vector(vo)
            else:
                outs = self.streams['out']
                outs2 = self.streams['out2']
                p0 = self.n['p']
                v0 = self.n['v']

                def po(gp, th=th, p0=p0, v0=v0):
                    kb.wv(gp, 'v', v0)
                    for j in range(16):
                        jg = th * 16 + j
                        m = 16 + j
                        if jg >= 2:
                            outs2.wait(gp, jg - 2)
                        src = big[:, m * T + th * TH: m * T + (th + 1) * TH]
                        d = kb._oslot(1, jg % 2)
                        gp.tensor_tensor(kb.mdv(1), src, kb.mub[:], ALU.subtract)
                        i = gp.tensor_tensor(d, kb.mdv(1), kb.rstdb[:], ALU.mult)
                        kb.inc(i, 'p')
                self.blk.gpsimd(po)

                def vo(vec, th=th, v0=v0):
                    for j in range(16):
                        jg = th * 16 + j
                        m = j
                        if jg >= 2:
                            outs.wait(vec, jg - 2)
                        src = big[:, m * T + th * TH: m * T + (th + 1) * TH]
                        d = kb._oslot(0, jg % 2)
                        vec.tensor_tensor(kb.mdv(0), src, kb.mub[:], ALU.subtract)
                        i = vec.tensor_tensor(d, kb.mdv(0), kb.rstdb[:], ALU.mult)
                        kb.inc(i, 'v')
                self.blk.vector(vo)

                def so(sync, th=th, p0=p0, v0=v0):
                    for j in range(16):
                        jg = th * 16 + j
                        mv, mp = j, 16 + j
                        kb.wv(sync, 'v', v0 + j + 1)
                        dst = bass.AP(kb.outT.tensor, mv * 128 * T + th * TH,
                                      [[T, 128], [1, TH]])
                        i = sync.dma_start(dst, kb._oslot(0, jg % 2))
                        outs.issue(i, jg)
                        kb.wv(sync, 'p', p0 + j + 1)
                        dst = bass.AP(kb.outT.tensor, mp * 128 * T + th * TH,
                                      [[T, 128], [1, TH]])
                        i = sync.dma_start(dst, kb._oslot(1, jg % 2))
                        outs2.issue(i, jg)
                self.blk.sync(so)

    def ph_postln1(self):
        self.postln_apply(self.bigA, False)

    def _oslot(self, grp, par):
        outv = self.bigA.ap().bitcast(F32)
        slot = grp * 2 + par
        return outv[:, slot * TH:(slot + 1) * TH]

    def ph_masks(self):
        kb = self

        def g(gp):
            kb.w(gp, 'p')
            kb.inc(gp.dma_start(kb.cents[:], kb.centb), 'xg', 16)
            asg_i32 = kb.bigB.ap().bitcast(I32)[0:1, 0:V]
            kb.inc(gp.dma_start(asg_i32, kb.assign), 'xg', 16)
        self.blk.gpsimd(g)

        def v0(vec):
            kb.w(vec, 'xg')
            asg_i32 = kb.bigB.ap().bitcast(I32)[0:1, 0:V]
            asg_f32 = kb.bigB.ap().bitcast(F32)[0:1, 4096:4096 + V]
            asg_bf = kb.bigB[0:1, 16384:16384 + V]
            vec.tensor_copy(asg_f32, asg_i32)
            kb.inc(vec.tensor_copy(asg_bf, asg_f32), 'v')
        self.blk.vector(v0)

        for th in range(2):
            def te(t, th=th):
                kb.w(t, 'xg', 'p', 'v')
                i = None
                for kc in range(32):
                    i = t.matmul(kb.pb[th][0:64, 0:TH],
                                 kb.cents[:, kc * 64:(kc + 1) * 64],
                                 kb.bigA[:, kc * T + th * TH: kc * T + (th + 1) * TH],
                                 start=(kc == 0), stop=(kc == 31))
                kb.inc(i, 'pe')
            self.blk.tensor(te)

            def a(sc, th=th):
                kb.w(sc, 'pe')
                ebf = kb.bigB[0:64, 20480 + th * TH:20480 + (th + 1) * TH]
                kb.inc(sc.activation(ebf, kb.pb[th][0:64, 0:TH], AF.Exp), 'a')
            self.blk.scalar(a)

            def te2(t, th=th):
                kb.w(t, 'a', 'v')
                ebf = kb.bigB[0:64, 20480 + th * TH:20480 + (th + 1) * TH]
                i = t.matmul(kb.pb[2][0:1, 0:TH], kb.ones64[:], ebf,
                             start=True, stop=True)
                kb.inc(i, 'pe')
            self.blk.tensor(te2)

            def v1(vec, th=th):
                kb.w(vec, 'pe')
                i = vec.tensor_scalar(kb.rowb[0:1, 0:TH], kb.pb[2][0:1, 0:TH],
                                      0.01, None, ALU.mult)
                kb.inc(i, 'v')
            self.blk.vector(v1)

            def te3(t, th=th):
                kb.w(t, 'v')
                i = t.matmul(kb.pb[3][0:64, 0:TH], kb.ones1[0:1, 0:64],
                             kb.rowb[0:1, 0:TH], start=True, stop=True)
                kb.inc(i, 'pe')
            self.blk.tensor(te3)

            def v2(vec, th=th):
                kb.w(vec, 'pe')
                ebf = kb.bigB[0:64, 20480 + th * TH:20480 + (th + 1) * TH]
                hits = kb.bigB[0:64, 22528 + th * TH:22528 + (th + 1) * TH]
                vec.tensor_tensor(hits, ebf, kb.pb[3][0:64, 0:TH], ALU.is_gt)
                i = vec.reduce_sum(kb.hcnt[:, th:th + 1], hits, AX.X)
                kb.inc(i, 'v')
            self.blk.vector(v2)

            def te4(t, th=th):
                kb.w(t, 'v')
                hits = kb.bigB[0:64, 22528 + th * TH:22528 + (th + 1) * TH]
                i = t.matmul(kb.pb[4][0:1, 0:TH], kb.ones64[:], hits,
                             start=True, stop=True)
                kb.inc(i, 'pe')
            self.blk.tensor(te4)

            def v3(vec, th=th):
                kb.w(vec, 'pe')
                i = vec.tensor_scalar(kb.rowb2[0:1, th * TH:(th + 1) * TH],
                                      kb.pb[4][0:1, 0:TH], 0.5, None, ALU.is_gt)
                kb.inc(i, 'v')
            self.blk.vector(v3)

        if self.dump('m1'):
            return
        def v4(vec):
            kb.w(vec, 'v')
            i = vec.tensor_tensor(kb.hcnt[:, 0:1], kb.hcnt[:, 0:1],
                                  kb.hcnt[:, 1:2], ALU.add)
            kb.inc(i, 'v')
        self.blk.vector(v4)

        def g2(gp):
            kb.w(gp, 'v')
            kb.inc(gp.dma_start(kb.cl_in[:, :], kb.hcnt[:, 0:1]), 'xg', 16)
            kb.w(gp, 'xg')
            i = gp.collective_compute(
                "AllReduce", ALU.add, replica_groups=[list(range(8))],
                ins=[kb.cl_in.ap().opt()], outs=[kb.cl_out.ap().opt()])
            i.then_inc(kb.ccsems[4])
            gp.wait_ge(kb.ccsems[4], 1)
            kb.inc(gp.dma_start(kb.hcnt[0:64, 10:11], kb.cl_out.ap()[0:64, :]),
                   'xg', 16)
        self.blk.gpsimd(g2)

        def v5(vec):
            kb.w(vec, 'xg')
            i = vec.tensor_scalar(kb.clsel[:, 0:1], kb.hcnt[:, 10:11], 0.5,
                                  None, ALU.is_gt)
            kb.inc(i, 'v')
        self.blk.vector(v5)

        if self.dump('m2'):
            return
        pe0 = self.n['pe']
        kb._oh_v0 = self.n['v']

        def te5(t):
            kb.w(t, 'v')
            for oc in range(8):
                asg_bf = kb.bigB[0:1, 16384 + oc * TH:16384 + (oc + 1) * TH]
                if oc >= 1:
                    kb.wv(t, 'v', kb._oh_v0 + oc)
                i = t.matmul(kb.pb[5][0:64, 0:TH], kb.ones1[0:1, 0:64], asg_bf,
                             start=True, stop=True)
                kb.inc(i, 'pe')
        self.blk.tensor(te5)

        def v5b(vec):
            for oc in range(8):
                kb.wv(vec, 'pe', pe0 + oc + 1)
                oh = kb.bigB[0:64, 24576 + oc * TH:24576 + (oc + 1) * TH]
                i = vec.tensor_scalar(oh, kb.pb[5][0:64, 0:TH], kb.io64[:, 0:1],
                                      None, ALU.is_equal)
                got = kb.inc(i, 'v')
                assert got == kb._oh_v0 + oc + 1
        self.blk.vector(v5b)

        if self.dump('m3'):
            return
        if self.dump('m3b'):
            return

        def te6(t):
            kb.w(t, 'v')
            oh = kb.bigB[0:64, 24576:24576 + V]
            i = None
            for m in range(32):
                i = t.matmul(kb.pb[6][:, m:m + 1], oh[:, m * 128:(m + 1) * 128],
                             kb.clsel[:, 0:1], start=True, stop=True)
            kb.inc(i, 'pe')
        self.blk.tensor(te6)

        def v6(vec):
            kb.w(vec, 'pe')
            kb.inc(vec.tensor_copy(kb.rsall[:], kb.pb[6][:, 0:32]), 'v')
        self.blk.vector(v6)
        if self.dump('m4'):
            return

        def te7(t):
            kb.w(t, 'v')
            t.matmul(kb.pb[6][:, 0:TH], kb.ones1[:], kb.rowb2[0:1, 0:TH],
                     start=True, stop=True)
            i = t.matmul(kb.pb[7][:, 0:TH], kb.ones1[:], kb.rowb2[0:1, TH:2 * TH],
                         start=True, stop=True)
            kb.inc(i, 'pe')
        self.blk.tensor(te7)

        def v7(vec):
            kb.w(vec, 'pe')
            vec.tensor_copy(kb.qsb[:, 0:TH], kb.pb[6][:, 0:TH])
            kb.inc(vec.tensor_copy(kb.qsb[:, TH:T], kb.pb[7][:, 0:TH]), 'v')
        self.blk.vector(v7)

    def ph_dense(self):
        kb = self
        des = self.streams['de']
        base_pe = self.n['pe']

        def grp_pe(cnt):
            return base_pe + (cnt + 1) + max(0, cnt - 2)

        def stats_pe(cnt):
            if cnt <= 61:
                return grp_pe(cnt + 2) + 1
            return base_pe + 126 + (cnt - 61)
        base_v = self.n['v']
        base_a = self.n['a']
        base_p = self.n['p']

        def te(t):
            kb.w(t, 'v', 'a', 'p')
            cnt = 0
            for m in range(32):
                des.wait(t, m)
                ws = kb.wsA if m % 2 == 0 else kb.wsB
                for th in range(2):
                    pbk = kb.pb[cnt % 2]
                    if cnt >= 2:
                        kb.wv(t, 'v', base_v + cnt - 1)
                    i = None
                    for kc in range(32):
                        i = t.matmul(
                            pbk[:, 0:TH], ws[:, kc * 128:(kc + 1) * 128],
                            kb.bigA[:, kc * T + th * TH: kc * T + (th + 1) * TH],
                            start=(kc == 0), stop=(kc == 31))
                    got = kb.inc(i, 'pe')
                    assert got == grp_pe(cnt)
                    if cnt >= 2:
                        kb._de_stats_emit(t, cnt - 2, base_a, stats_pe)
                    cnt += 1
            for j in (62, 63):
                kb._de_stats_emit(t, j, base_a, stats_pe)
        self.blk.tensor(te)

        def v(vec):
            cnt = 0
            for m in range(32):
                for th in range(2):
                    pbk = kb.pb[cnt % 2]
                    kb.wv(vec, 'pe', grp_pe(cnt))
                    i = vec.tensor_scalar(kb.mdv(cnt % 2), pbk[:, 0:TH],
                                          kb.fbs[:, m:m + 1], kb.rsall[:, m:m + 1],
                                          ALU.add, ALU.mult)
                    got = kb.inc(i, 'v')
                    assert got == base_v + cnt + 1
                    cnt += 1
        self.blk.vector(v)

        def p(gp):
            kb.wv(gp, 'v', base_v)
            kb.wv(gp, 'p', base_p)
            kb.wv(gp, 'pe', base_pe)
            for m0 in range(2):
                i = gp.dma_start((kb.wsA if m0 % 2 == 0 else kb.wsB)[:], kb.fwb[m0])
                des.issue(i, m0)
            cnt = 0
            for m in range(32):
                for th in range(2):
                    kb.wv(gp, 'v', base_v + cnt + 1)
                    dst = kb.bigB[:, m * T + th * TH: m * T + (th + 1) * TH]
                    gp.tensor_tensor(kb.mdv(cnt % 2), kb.mdv(cnt % 2),
                                     kb.qsb[:, th * TH:(th + 1) * TH], ALU.mult)
                    i = gp.tensor_tensor(
                        dst, kb.mdv(cnt % 2),
                        kb.bigA[:, m * T + th * TH: m * T + (th + 1) * TH], ALU.add)
                    got = kb.inc(i, 'p')
                    assert got == base_p + cnt + 1
                    if th == 1 and m + 2 < 32:
                        i = gp.dma_start((kb.wsA if m % 2 == 0 else kb.wsB)[:],
                                         kb.fwb[m + 2])
                        des.issue(i, m + 2)
                    cnt += 1
        self.blk.gpsimd(p)

        def a(sc):
            cnt = 0
            for m in range(32):
                for th in range(2):
                    kb.wv(sc, 'p', base_p + cnt + 1)
                    if cnt >= 2:
                        kb.wv(sc, 'pe', stats_pe(cnt - 2))
                    src = kb.bigB[:, m * T + th * TH: m * T + (th + 1) * TH]
                    i = sc.activation(kb.wsq[:, th * TH:(th + 1) * TH], src, AF.Square)
                    got = kb.inc(i, 'a')
                    assert got == base_a + cnt + 1
                    cnt += 1
        self.blk.scalar(a)

    def _de_stats_emit(self, t, cnt, base_a, stats_pe):
        kb = self
        m, th = cnt // 2, cnt % 2
        kb.wv(t, 'a', base_a + cnt + 1)
        src = kb.bigB[:, m * T + th * TH: m * T + (th + 1) * TH]
        t.matmul(kb.pb[2 + th][0:1, 0:TH], kb.inv4096[:], src,
                 start=(m == 0), stop=(m == 31))
        i = t.matmul(kb.pb[4 + th][0:1, 0:TH], kb.inv4096[:],
                     kb.wsq[:, th * TH:(th + 1) * TH],
                     start=(m == 0), stop=(m == 31))
        got = kb.inc(i, 'pe')
        assert got == stats_pe(cnt)

    def ph_postln2_out(self):
        self.postln_apply(self.bigB, True)
        if self.done:
            return

    def ph_finish(self):
        kb = self

        def s(sync):
            for nm in ('out', 'out2'):
                st = kb.streams[nm]
                for sl in range(st.nbuf):
                    sync.wait_ge(st.sems[sl], st.cnt[sl])
            sync.wait_ge(kb.sems['o'], kb.n['o'])
        self.blk.sync(s)


def _prep_shared(inputs):
    g = {}
    Win = inputs['L0_attn_Win']
    g['winb'] = np.ascontiguousarray(
        Win.T.reshape(4, 8, 128, 512).transpose(0, 2, 1, 3)).reshape(4, 128, 8 * 512)
    W1 = inputs['L0_attn_W1']
    g['w1b'] = np.ascontiguousarray(
        np.stack([W1[i].T.reshape(4, 128, 2, 8, 128).transpose(2, 1, 0, 3, 4)
                  for i in range(NL)])).reshape(NL, 2, 128, 4 * 1024)
    W2 = inputs['L0_attn_W2']
    g['w2b'] = np.ascontiguousarray(
        np.stack([W2[i].T.reshape(2, 8, 128, 512).transpose(0, 2, 1, 3)
                  for i in range(NL)])).reshape(NL, 2, 128, 8 * 512)
    Wout = inputs['L0_attn_Wout']
    g['woutb'] = np.ascontiguousarray(
        Wout.T.reshape(4, 128, 4, 8, 128).transpose(2, 1, 0, 3, 4)).reshape(4, 128, 4096)
    Wf = inputs['L0_ffn_weight']
    g['fwb'] = np.ascontiguousarray(
        Wf.T.reshape(32, 128, 32, 128).transpose(2, 1, 0, 3)).reshape(32, 128, 32 * 128)
    cent = inputs['L0_ffn_centroids']
    g['centb'] = np.ascontiguousarray(
        cent.T.reshape(32, 128, 64).transpose(1, 0, 2)).reshape(128, 32 * 64)
    chp = np.stack([inputs['L0_attn_' + n] for n in
                    ('ln1_w', 'ln1_b', 'phz_re', 'phz_im', 'lc_re', 'lc_im',
                     'sc_lin', 'ln2_w', 'ln2_b')], axis=1)
    g['chp'] = np.ascontiguousarray(
        chp.reshape(NL, 9, 4, 128).transpose(3, 0, 1, 2)).reshape(128, NL * 9 * 4)
    g['binr'] = np.ascontiguousarray(inputs['L0_attn_bin'].reshape(4, 128).T)
    g['b1all'] = np.ascontiguousarray(
        inputs['L0_attn_b1'].reshape(NL, 16, 128).transpose(2, 0, 1)).reshape(128, NL * 16)
    g['b2all'] = np.ascontiguousarray(
        inputs['L0_attn_b2'].reshape(NL, 4, 128).transpose(2, 0, 1)).reshape(128, NL * 4)
    lnl = np.stack([inputs['L0_attn_lnl_w'], inputs['L0_attn_lnl_b']])
    g['lnlwb'] = np.ascontiguousarray(
        lnl.reshape(2, 4, 128).transpose(2, 0, 1)).reshape(128, 8)
    g['boutr'] = np.ascontiguousarray(inputs['L0_attn_bout'].reshape(32, 128).T)
    g['fbr'] = np.ascontiguousarray(inputs['L0_ffn_bias'].reshape(32, 128).T)
    g['iot64'] = np.arange(64, dtype=np.float32).reshape(64, 1)
    g['assign'] = np.ascontiguousarray(
        inputs['L0_ffn_assignments'].reshape(1, V)).astype(np.int32)
    return g


def make_in_maps(inputs):
    x = inputs['L0_attn_x']
    g = _prep_shared(inputs)
    maps = []
    for core in range(8):
        b, pi = core // 4, core % 4
        d = dict(g)
        d['xT'] = np.ascontiguousarray(x[b, pi * T:(pi + 1) * T, :].T)
        cmask = np.zeros((128, 32), dtype=np.float32)
        for j in range(pi):
            e = pi - 1 - j
            cmask[:, (b * 4 + j) * 4 + e] = 1.0
        d['cmask'] = cmask
        ls = np.zeros((128, 4), dtype=np.float32)
        ls[:, pi] = 1.0
        d['lcselr'] = ls
        maps.append(d)
    return maps


def kernel(**inputs):
    kb = KB()
    nc = kb.build()
    maps = make_in_maps(inputs)
    res = run_bass_kernel_spmd(nc, maps, core_ids=list(range(8)))
    out = np.empty((2, 4096, V), dtype=np.float32)
    for core in range(8):
        b, pi = core // 4, core % 4
        out[b, pi * T:(pi + 1) * T, :] = res.results[core]['outT'].T
    return out


# version 11
# speedup vs baseline: 1.0059x; 1.0059x over previous
import numpy as np
import ml_dtypes
import concourse.bass as bass
import concourse.mybir as mybir
import concourse.bass_utils as _bu
from concourse.bass_utils import run_bass_kernel_spmd

_LDWOPT_DISABLED = True

F32 = mybir.dt.float32
BF16 = mybir.dt.bfloat16
I32 = mybir.dt.int32
AF = mybir.ActivationFunctionType
ALU = mybir.AluOpType
AX = mybir.AxisListType

T = 1024
TH = 512
C = 16
NCH = T // C
NL = 4
D = 512
V = 4096
EPS = 1e-5


class Stream:

    def __init__(self, nc, name, nbuf):
        self.nbuf = nbuf
        self.sems = [nc.alloc_semaphore(f"st_{name}{i}") for i in range(nbuf)]
        self.cnt = [0] * nbuf

    def issue(self, instr, idx):
        sl = idx % self.nbuf
        self.cnt[sl] += 16
        instr.then_inc(self.sems[sl], 16)
        return self.cnt[sl]

    def wait(self, eng, idx):
        sl = idx % self.nbuf
        eng.wait_ge(self.sems[sl], 16 * (idx // self.nbuf + 1))


class KB:
    def __init__(self, dbg=None):
        self.dbg = dbg
        self.done = False
        nc = bass.Bass("TRN2", target_bir_lowering=False, debug=False,
                       detect_race_conditions=False)
        self.nc = nc
        dt = nc.dram_tensor
        self.xT = dt("xT", [V, T], F32, kind="ExternalInput").ap()
        self.winb = dt("winb", [4, 128, 8 * 512], F32, kind="ExternalInput").ap()
        self.w1b = dt("w1b", [NL, 2, 128, 4 * 1024], F32, kind="ExternalInput").ap()
        self.w2b = dt("w2b", [NL, 2, 128, 8 * 512], F32, kind="ExternalInput").ap()
        self.woutb = dt("woutb", [4, 128, 4096], F32, kind="ExternalInput").ap()
        self.fwb = dt("fwb", [32, 128, 32 * 128], F32, kind="ExternalInput").ap()
        self.centb = dt("centb", [128, 32 * 64], F32, kind="ExternalInput").ap()
        self.chp = dt("chp", [128, NL * 9 * 4], F32, kind="ExternalInput").ap()
        self.binr = dt("binr", [128, 4], F32, kind="ExternalInput").ap()
        self.b1all = dt("b1all", [128, NL * 16], F32, kind="ExternalInput").ap()
        self.b2all = dt("b2all", [128, NL * 4], F32, kind="ExternalInput").ap()
        self.lnlwb = dt("lnlwb", [128, 8], F32, kind="ExternalInput").ap()
        self.boutr = dt("boutr", [128, 32], F32, kind="ExternalInput").ap()
        self.fbr = dt("fbr", [128, 32], F32, kind="ExternalInput").ap()
        self.cmask = dt("cmask", [128, 32], F32, kind="ExternalInput").ap()
        self.lcselr = dt("lcselr", [128, 4], F32, kind="ExternalInput").ap()
        self.iot64 = dt("iot64", [64, 1], F32, kind="ExternalInput").ap()
        self.assign = dt("assign", [1, V], I32, kind="ExternalInput").ap()
        self.outT = dt("outT", [V, T], F32, kind="ExternalOutput").ap()
        self.cc_in = dt("cc_in", [128, 8], F32)
        self.cc_out = dt("cc_out", [1024, 8], F32)
        self.cl_in = dt("cl_in", [64, 1], F32)
        self.cl_out = dt("cl_out", [64, 1], F32)
        self.dbgout = None
        if dbg is not None:
            shp = {'h0': [128, 4096], 'x1': [128, 32768]}.get(dbg)
            if shp is None and (dbg.startswith('hmix') or dbg.startswith('hffn')):
                shp = [128, 4096]
            if shp is None and dbg.startswith('y'):
                shp = [128, 4096]
            if shp is None and dbg in ('x1pre', 'x2pre'):
                shp = [128, 32768]
            if shp is None and dbg == 'hL':
                shp = [128, 4096]
            if shp is None and dbg == 'masks':
                shp = [128, 1064]
            if shp is None and dbg in ('m1', 'm2', 'm3b', 'm4'):
                shp = [128, 16]
            if shp is None and dbg == 'p2a':
                shp = [128, 2048]
            if shp is None and dbg == 'm3':
                shp = [64, 4096]
            if shp is None and dbg.startswith('tabs'):
                shp = [128, 872]
            if shp is None and dbg.startswith('wsc'):
                shp = [128, 1096]
            self.dbgout = dt("dbgout", shp, F32, kind="ExternalOutput").ap()

        al = nc.alloc_sbuf_tensor
        self.bigA = al("bigA", [128, 32768], BF16)
        self.bigB = al("bigB", [128, 32768], BF16)
        self.h = al("h", [128, 4 * T], F32)
        self.Yb = al("Yb", [128, T], F32)
        self.wsq = al("wsq", [128, T], BF16)
        self.wsA = al("wsA", [128, 4096], BF16)
        self.wsB = al("wsB", [128, 4096], BF16)
        self.xtc = al("xtc", [128, 2 * T], BF16)
        self.sc1 = al("sc1", [128, TH], F32)
        self.md = al("md", [128, 2 * TH], F32)
        self.qsb = al("qsb", [128, T], F32)
        self.mub = al("mub", [128, TH], F32)
        self.rstdb = al("rstdb", [128, TH], F32)
        self.wre = al("wre", [128, 4 * NCH], F32)
        self.wim = al("wim", [128, 4 * NCH], F32)
        self.prB = al("prB", [128, 4 * NCH], F32)
        self.piB = al("piB", [128, 4 * NCH], F32)
        self.prA = self.wre
        self.piA = self.wim
        self.Sre = al("Sre", [128, 4 * NCH], F32)
        self.Sim = al("Sim", [128, 4 * NCH], F32)
        self.qcre = al("qcre", [128, 4 * NCH], F32)
        self.qcim = al("qcim", [128, 4 * NCH], F32)
        self.powre = al("powre", [128, 4 * (C + 1)], F32)
        self.powim = al("powim", [128, 4 * (C + 1)], F32)
        self.qsre = al("qsre", [128, 4 * 8], F32)
        self.qsim = al("qsim", [128, 4 * 8], F32)
        self.psclre = al("psclre", [128, 4 * C], F32)
        self.psclim = al("psclim", [128, 4 * C], F32)
        self.ppre = al("ppre", [128, 4 * 4], F32)
        self.ppim = al("ppim", [128, 4 * 4], F32)
        self.sinre = al("sinre", [128, 4], F32)
        self.sinim = al("sinim", [128, 4], F32)
        self.gath = al("gath", [128, 64], F32)
        self.t4a = al("t4a", [128, 4], F32)
        self.t4b = al("t4b", [128, 4], F32)
        self.t4c = al("t4c", [128, 4], F32)
        self.t4d = al("t4d", [128, 4], F32)
        self.chps = al("chps", [128, NL * 9 * 4], F32)
        self.bins = al("bins", [128, 4], F32)
        self.b1s = al("b1s", [128, NL * 16], F32)
        self.b2s = al("b2s", [128, NL * 4], F32)
        self.lnls = al("lnls", [128, 8], F32)
        self.bouts = al("bouts", [128, 32], F32)
        self.fbs = al("fbs", [128, 32], F32)
        self.cms = al("cms", [128, 32], F32)
        self.lcsel = al("lcsel", [128, 4], F32)
        self.io64 = al("io64", [64, 1], F32)
        pass
        self.rowf = al("rowf", [1, 2 * TH], F32)
        self.rowb = al("rowb", [1, 2 * TH], BF16)
        self.rowb2 = al("rowb2", [1, 2 * TH], BF16)
        self.hcnt = al("hcnt", [64, 12], F32)
        self.clsel = al("clsel", [64, 1], BF16)
        self.rsall = al("rsall", [128, 32], F32)
        self.inv512 = al("inv512", [128, 1], BF16)
        self.inv512f = al("inv512f", [128, 1], F32)
        self.inv4096 = al("inv4096", [128, 1], BF16)
        self.ones1 = al("ones1", [1, 128], BF16)
        self.ones64 = al("ones64", [64, 1], BF16)
        self.cm05 = al("cm05", [128, 1], F32)
        self.cp05 = al("cp05", [128, 1], F32)

        yb_addr = nc.lookup_mloc(self.Yb).addr
        self.cents = nc.alloc_sbuf_tensor_at("cents", [128, 32 * 64], BF16,
                                             offset=yb_addr)
        self.pb = [nc.alloc_psum_tensor(f"pb{i}", [128, 512], F32) for i in range(8)]
        self.sems = {k: nc.alloc_semaphore("s_" + k)
                     for k in ("xg", "xs", "pe", "v", "a", "p", "o")}
        self.ccsems = [nc.alloc_semaphore(f"s_cc{i}") for i in range(5)]
        self.n = {k: 0 for k in self.sems}
        self.streams = {}
        for nm, nb in (("ew", 2), ("ex", 4), ("w1", 2), ("w2", 2), ("wow", 2),
                       ("wox", 2), ("de", 2), ("out", 2), ("out2", 2)):
            self.streams[nm] = Stream(nc, nm, nb)

    def inc(self, instr, key, amt=1):
        self.n[key] += amt
        instr.then_inc(self.sems[key], amt)
        return self.n[key]

    def w(self, eng, *keys):
        for k in keys:
            if self.n[k] > 0:
                eng.wait_ge(self.sems[k], self.n[k])

    def wv(self, eng, key, val):
        if val > 0:
            eng.wait_ge(self.sems[key], val)

    def hg(self, g):
        return self.h[:, g * T:(g + 1) * T]

    def hgt(self, g, th):
        return self.h[:, g * T + th * TH: g * T + (th + 1) * TH]

    @property
    def hb(self):
        return self.bigA[:, 24576:26624]

    @property
    def hbsq(self):
        return self.bigA[:, 26624:28672]

    def xt(self, b):
        return self.xtc[:, b * T:(b + 1) * T]

    def mdv(self, i):
        return self.md[:, i * TH:(i + 1) * TH]

    def chan(self, i, j, g=None):
        base = (i * 9 + j) * 4
        if g is None:
            return self.chps[:, base:base + 4]
        return self.chps[:, base + g:base + g + 1]

    def yv(self, g, th=None):
        b = self.bigB.ap().bitcast(F32)
        if th is None:
            return b[:, 8192 + g * T: 8192 + (g + 1) * T]
        return b[:, 8192 + g * T + th * TH: 8192 + g * T + (th + 1) * TH]

    def lv(self, g, th):
        return self.bigB[:, 24576 + g * T + th * TH: 24576 + g * T + (th + 1) * TH]

    def wtmp3(self):
        return self.bigA.ap().bitcast(F32)[:, 15360:16384].rearrange(
            "p (c i) -> p c i", c=NCH)

    def dump(self, name):
        if self.dbg != name or self.done:
            return self.dbg == name and self.done
        kb = self
        if name == 'h0' or name.startswith('hmix') or name.startswith('hffn'):
            def s(sync):
                kb.w(sync, 'v', 'a', 'p', 'pe')
                kb.inc(sync.dma_start(kb.dbgout, kb.h[:]), 'o', 16)
                sync.wait_ge(kb.sems['o'], kb.n['o'])
            self.blk.sync(s)
        elif name.startswith('y'):
            src = self.bigB.ap().bitcast(F32)[:, 8192:12288]

            def s(sync):
                kb.w(sync, 'v', 'a', 'p', 'pe')
                kb.inc(sync.dma_start(kb.dbgout, src), 'o', 16)
                sync.wait_ge(kb.sems['o'], kb.n['o'])
            self.blk.sync(s)
        elif name in ('x1pre', 'x1'):
            def g(gp):
                kb.w(gp, 'v', 'a', 'p', 'pe')
                kb.inc(gp.dma_start(kb.dbgout, kb.bigA[:]), 'xg', 16)
                gp.wait_ge(kb.sems['xg'], kb.n['xg'])
            self.blk.gpsimd(g)
        elif name == 'm3':
            def sm(gp):
                kb.w(gp, 'v', 'a', 'p', 'pe', 'xg')
                i = gp.dma_start(kb.dbgout, kb.bigB[0:64, 24576:24576 + V])
                kb.inc(i, 'xg', 16)
                gp.wait_ge(kb.sems['xg'], kb.n['xg'])
            self.blk.gpsimd(sm)
        elif name in ('m1', 'm2', 'm3b', 'm4'):
            def sm(sync):
                kb.w(sync, 'v', 'a', 'p', 'pe', 'xg')
                kb.inc(sync.dma_start(kb.dbgout[0:64, 0:12], kb.hcnt[:]), 'o', 16)
                sync.wait_ge(kb.sems['o'], kb.n['o'])
            self.blk.sync(sm)
        elif name == 'p2a':
            def sp(sync):
                kb.w(sync, 'v', 'a', 'p', 'pe')
                i = sync.dma_start(kb.dbgout, kb.bigA.ap().bitcast(F32)[:, 0:2048])
                kb.inc(i, 'o', 16)
                sync.wait_ge(kb.sems['o'], kb.n['o'])
            self.blk.sync(sp)
        elif name == 'masks':
            def sm(sync):
                kb.w(sync, 'v', 'a', 'p', 'pe', 'xg')
                kb.inc(sync.dma_start(kb.dbgout[:, 0:1024], kb.qsb[:]), 'o', 16)
                kb.inc(sync.dma_start(kb.dbgout[:, 1024:1056], kb.rsall[:]), 'o', 16)
                kb.inc(sync.dma_start(kb.dbgout[0:64, 1056:1064], kb.hcnt[:, 4:12]), 'o', 16)
                sync.wait_ge(kb.sems['o'], kb.n['o'])
            self.blk.sync(sm)
        elif name == 'hL':
            def g(gp):
                kb.w(gp, 'v', 'a', 'p', 'pe')
                kb.inc(gp.dma_start(kb.dbgout, kb.bigB[:, 24576:28672]), 'xg', 16)
                gp.wait_ge(kb.sems['xg'], kb.n['xg'])
            self.blk.gpsimd(g)
        elif name.startswith('tabs') or name.startswith('wsc'):
            if name.startswith('tabs'):
                parts = [(kb.powre, 68), (kb.powim, 68), (kb.qsre, 32),
                         (kb.qsim, 32), (kb.qcre, 256), (kb.qcim, 256),
                         (kb.psclre, 64), (kb.psclim, 64), (kb.ppre, 16),
                         (kb.ppim, 16)]
            else:
                parts = [(kb.wre, 256), (kb.wim, 256), (kb.Sre, 256),
                         (kb.Sim, 256), (kb.gath, 64), (kb.sinre, 2 * 0 + 4),
                         (kb.sinim, 4)]

            def s(sync):
                kb.w(sync, 'v', 'a', 'p', 'pe')
                off = 0
                i = None
                for tns, wdt in parts:
                    i = sync.dma_start(kb.dbgout[:, off:off + wdt], tns[:, 0:wdt])
                    kb.inc(i, 'o', 16)
                    off += wdt
                sync.wait_ge(kb.sems['o'], kb.n['o'])
            self.blk.sync(s)
        elif name == 'x2pre':
            def g(gp):
                kb.w(gp, 'v', 'a', 'p', 'pe')
                kb.inc(gp.dma_start(kb.dbgout, kb.bigB[:]), 'xg', 16)
                gp.wait_ge(kb.sems['xg'], kb.n['xg'])
            self.blk.gpsimd(g)
        self.done = True
        return True

    def build(self):
        nc = self.nc
        with nc.Block() as blk:
            self.blk = blk
            self.ph_pre()
            self.ph_embed()
            if self.dump('h0'):
                return nc
            for i in range(NL):
                if self.ph_layer(i):
                    return nc
            self.ph_lnl_wout()
            if self.done:
                return nc
            if self.dump('x1pre'):
                return nc
            self.ph_postln1()
            if self.dump('x1'):
                return nc
            self.ph_masks()
            if self.done:
                return nc
            if self.dump('masks'):
                return nc
            self.ph_dense()
            if self.dump('x2pre'):
                return nc
            self.ph_postln2_out()
            if self.done:
                return nc
            self.ph_finish()
        return nc

    def ph_pre(self):
        kb = self

        def s(sync):
            for dst, src in ((kb.chps, kb.chp), (kb.bins, kb.binr),
                             (kb.b1s, kb.b1all), (kb.b2s, kb.b2all),
                             (kb.lnls, kb.lnlwb), (kb.bouts, kb.boutr),
                             (kb.fbs, kb.fbr), (kb.cms, kb.cmask),
                             (kb.lcsel, kb.lcselr), (kb.io64, kb.iot64)):
                kb.inc(sync.dma_start(dst[:], src), 'xs', 16)
        self.blk.sync(s)

        def v(vec):
            vec.memset(kb.inv512[:], 1.0 / 512)
            vec.memset(kb.inv512f[:], 1.0 / 512)
            vec.memset(kb.inv4096[:], 1.0 / 4096)
            vec.memset(kb.ones1[:], 1.0)
            vec.memset(kb.ones64[:], 1.0)
            vec.memset(kb.cm05[:], -0.5)
            i = vec.memset(kb.cp05[:], 0.5)
            kb.inc(i, 'v')
        self.blk.vector(v)

    def ph_embed(self):
        kb = self
        seq = [(th, q, kc) for th in range(2) for q in range(4) for kc in range(8)]
        base_pe = self.n['pe']
        xpe = {idx: base_pe + idx + 1 for idx in range(64)}
        winpe = {tq: base_pe + (tq + 1) * 8 for tq in range(8)}
        ew, ex = self.streams['ew'], self.streams['ex']

        def gdma(gp):
            kb.w(gp, 'xs')

            def win_load(tq):
                kb.wv(gp, 'pe', winpe.get(tq - 2, 0))
                q = tq % 4
                i = gp.dma_start((kb.wsA if tq % 2 == 0 else kb.wsB)[:],
                                 kb.winb[q])
                ew.issue(i, tq)
            win_load(0)
            win_load(1)
            for idx, (th, q, kc) in enumerate(seq):
                k = q * 8 + kc
                kb.wv(gp, 'pe', xpe.get(idx - 4, 0))
                src = bass.AP(kb.xT.tensor, (k * 128) * T + th * TH,
                              [[T, 128], [1, TH]])
                i = gp.dma_start(kb.xtc[:, (idx % 4) * TH:(idx % 4 + 1) * TH], src)
                ex.issue(i, idx)
                if idx % 8 == 0 and idx >= 8 and idx // 8 + 1 <= 7:
                    win_load(idx // 8 + 1)
        self.blk.gpsimd(gdma)

        def te(t):
            kb.w(t, 'v')
            for idx, (th, q, kc) in enumerate(seq):
                tq = th * 4 + q
                ws = kb.wsA if tq % 2 == 0 else kb.wsB
                ew.wait(t, tq)
                ex.wait(t, idx)
                first = (q == 0 and kc == 0)
                last = (q == 3 and kc == 7)
                i = None
                for m in range(4):
                    i = t.matmul(kb.pb[th * 4 + m][:, 0:TH],
                                 ws[:, kc * 512 + m * 128: kc * 512 + (m + 1) * 128],
                                 kb.xtc[:, (idx % 4) * TH:(idx % 4 + 1) * TH],
                                 start=first, stop=last)
                assert kb.inc(i, 'pe') == xpe[idx]
        self.blk.tensor(te)

        def ve(v):
            for th in range(2):
                kb.wv(v, 'pe', xpe[th * 32 + 31])
                i = None
                for m in range(4):
                    i = v.tensor_scalar(kb.hgt(m, th), kb.pb[th * 4 + m][:, 0:TH],
                                        kb.bins[:, m:m + 1], None, ALU.add)
                kb.inc(i, 'v')
        self.blk.vector(ve)

    def cast_chunks(self, srcf, nk):
        kb = self

        def a(sc):
            kb.w(sc, 'v', 'p', 'pe')
            i = None
            for k in range(nk):
                sc.activation(kb.hb[:, k * TH:(k + 1) * TH], srcf(k), AF.Identity)
                i = sc.activation(kb.hbsq[:, k * TH:(k + 1) * TH], srcf(k), AF.Square)
            kb.inc(i, 'a')
        self.blk.scalar(a)

    def stats_rows(self, nk, inv, pb_sum, pb_sq):
        kb = self

        def te(t):
            kb.w(t, 'a', 'v')
            i = None
            for k in range(nk):
                i = t.matmul(pb_sum[0:1, 0:TH], inv,
                             kb.hb[:, k * TH:(k + 1) * TH],
                             start=(k == 0), stop=(k == nk - 1))
            for k in range(nk):
                i = t.matmul(pb_sq[0:1, 0:TH], inv,
                             kb.hbsq[:, k * TH:(k + 1) * TH],
                             start=(k == 0), stop=(k == nk - 1))
            kb.inc(i, 'pe')
        self.blk.tensor(te)

    def rows_and_bcast(self, pb_sum, pb_sq, pb_mub, pb_rstdb):
        kb = self
        mu_bf = self.rowb[0:1, 0:TH]
        rstd_bf = self.rowb[0:1, TH:2 * TH]
        musq = self.rowf[0:1, 0:TH]
        varep = self.rowf[0:1, TH:2 * TH]

        def a(sc):
            kb.w(sc, 'pe')
            sc.activation(musq, pb_sum[0:1, 0:TH], AF.Square)
            i = sc.activation(mu_bf, pb_sum[0:1, 0:TH], AF.Identity)
            kb.inc(i, 'a')
        self.blk.scalar(a)

        def v(vec):
            kb.w(vec, 'a')
            vec.scalar_tensor_tensor(varep, musq, -1.0, pb_sq[0:1, 0:TH],
                                     ALU.mult, ALU.add)
            i = vec.tensor_scalar(varep, varep, EPS, None, ALU.add)
            kb.inc(i, 'v')
        self.blk.vector(v)

        def a2(sc):
            kb.w(sc, 'v')
            sc.activation(varep, varep, AF.Ln)
            i = sc.activation(rstd_bf, varep, AF.Exp, scale=kb.cm05[0:1, 0:1])
            kb.inc(i, 'a')
        self.blk.scalar(a2)

        def te(t):
            kb.w(t, 'a', 'v', 'p')
            t.matmul(pb_mub[:, 0:TH], kb.ones1[:], mu_bf, start=True, stop=True)
            i = t.matmul(pb_rstdb[:, 0:TH], kb.ones1[:], rstd_bf, start=True, stop=True)
            kb.inc(i, 'pe')
        self.blk.tensor(te)

    def colbc(self, tns, col, row, cnt, gstep=None, gcnt=None):
        t = tns.ap().tensor if hasattr(tns, 'ap') else tns.tensor
        if gstep is None:
            return bass.AP(t, col, [[row, 128], [0, cnt], [1, 1]])
        return bass.AP(t, col, [[row, 128], [gstep, gcnt], [0, cnt], [1, 1]])

    def ph_layer(self, li):
        self.ph_w1_prefetch(li)
        self.ph_psetup_pre(li)
        for th in range(2):
            self.ln_d512_stats(li, th)
        self.ph_psetup_post(li)
        if self.dump(f'tabs{li}'):
            return True
        self.ln_d512_apply(li, 'ln1')
        if self.dump(f'y{li}'):
            return True
        self.ph_sconv(li)
        if self.done:
            return True
        if self.dump(f'hmix{li}'):
            return True
        for th in range(2):
            self.ln_d512_stats(li, th, kind='ln2')
        self.ln_d512_apply(li, 'ln2')
        self.ph_w1(li)
        self.ph_w2(li)
        if self.dump(f'hffn{li}'):
            return True
        return False

    def _muS(self, th):
        return self.mdv(th * 2 - th) if False else (
            self.mdv(0) if th == 0 else self.qsb[:, 0:TH])

    def _rstdS(self, th):
        return self.mdv(1) if th == 0 else self.qsb[:, TH:2 * TH]

    def _sqbuf(self, th):
        return (self.bigA[:, 26624:28672] if th == 0
                else self.bigA[:, 22528:24576])

    def _rowf2(self, th):
        if th == 0:
            return self.rowf
        return self.bigA.ap().bitcast(F32)[0:1, 0:1024]

    def _rowb2t(self, th):
        if th == 0:
            return self.rowb
        return self.bigA[0:1, 28672:29696]

    def ln_d512_stats(self, li, th, kind='ln1'):
        kb = self
        b0 = th * 4
        sqb = self._sqbuf(th)

        def vsq(vec):
            i = None
            for g in range(4):
                i = vec.tensor_tensor(sqb[:, g * TH:(g + 1) * TH],
                                      kb.hgt(g, th), kb.hgt(g, th), ALU.mult)
            kb.inc(i, 'v')
        self.blk.vector(vsq)
        sqv = self.n['v']

        def te(t):
            kb.w(t, 'a', 'v')
            i = None
            for g in range(4):
                i = t.matmul(kb.pb[b0][0:1, 0:TH], kb.inv512f[:], kb.hgt(g, th),
                             start=(g == 0), stop=(g == 3))
            kb.wv(t, 'v', sqv)
            for g in range(4):
                i = t.matmul(kb.pb[b0 + 1][0:1, 0:TH], kb.inv512[:],
                             sqb[:, g * TH:(g + 1) * TH],
                             start=(g == 0), stop=(g == 3))
            kb.inc(i, 'pe')
        self.blk.tensor(te)
        pe_stats = self.n['pe']
        rf = self._rowf2(th)
        rb = self._rowb2t(th)
        mu_bf = rb[0:1, 0:TH] if th == 0 else rb[0:1, 0:TH]
        rstd_bf = rb[0:1, TH:2 * TH] if th == 0 else None
        if th != 0:
            rstd_bf = self.bigA[0:1, 29184:29184 + TH]
        musq = rf[0:1, 0:TH]
        varep = rf[0:1, TH:2 * TH]

        def a(sc):
            kb.wv(sc, 'pe', pe_stats)
            sc.activation(musq, kb.pb[b0][0:1, 0:TH], AF.Square)
            i = sc.activation(mu_bf, kb.pb[b0][0:1, 0:TH], AF.Identity)
            kb.inc(i, 'a')
        self.blk.scalar(a)
        av = self.n['a']

        def v(vec):
            kb.wv(vec, 'a', av)
            vec.scalar_tensor_tensor(varep, musq, -1.0, kb.pb[b0 + 1][0:1, 0:TH],
                                     ALU.mult, ALU.add)
            i = vec.tensor_scalar(varep, varep, EPS, None, ALU.add)
            kb.inc(i, 'v')
        self.blk.vector(v)
        vv = self.n['v']

        def a2(sc):
            kb.wv(sc, 'v', vv)
            sc.activation(varep, varep, AF.Ln)
            i = sc.activation(rstd_bf, varep, AF.Exp, scale=kb.cm05[0:1, 0:1])
            kb.inc(i, 'a')
        self.blk.scalar(a2)
        av2 = self.n['a']

        def te2(t):
            kb.wv(t, 'a', av2)
            kb.w(t, 'v', 'p')
            t.matmul(kb.pb[b0 + 2][:, 0:TH], kb.ones1[:], mu_bf,
                     start=True, stop=True)
            i = t.matmul(kb.pb[b0 + 3][:, 0:TH], kb.ones1[:], rstd_bf,
                         start=True, stop=True)
            kb.inc(i, 'pe')
        self.blk.tensor(te2)
        pe_b = self.n['pe']
        if not hasattr(self, '_lnrows'):
            self._lnrows = {}
            self._lnbc = {}
        self._lnbc[th] = pe_b
        if th == 1:
            def vc(vec):
                kb.wv(vec, 'pe', pe_b)
                vec.tensor_copy(kb._muS(th), kb.pb[b0 + 2][:, 0:TH])
                i = vec.tensor_copy(kb._rstdS(th), kb.pb[b0 + 3][:, 0:TH])
                kb.inc(i, 'v')
            self.blk.vector(vc)
        self._lnrows[th] = self.n['v']

    def ln_d512_apply(self, li, kind):
        kb = self
        jg = 0 if kind == 'ln1' else 7
        jb = 1 if kind == 'ln1' else 8
        p0 = self.n['p']
        v0 = self.n['v']
        self._apply_p0 = p0
        self._apply_v0 = v0

        def po(gp):
            for g in range(4):
                kb.wv(gp, 'v', kb._lnrows[1])
                d = kb.yv(g, 1)
                gam = kb.colbc(kb.chps, (li * 9 + jg) * 4 + g, NL * 9 * 4, TH)
                gp.tensor_tensor(d, kb.hgt(g, 1), kb._muS(1), ALU.subtract)
                gp.tensor_tensor(d, d, kb._rstdS(1), ALU.mult)
                i = gp.tensor_tensor(
                    d.rearrange("p (c o) -> p c o", o=1),
                    d.rearrange("p (c o) -> p c o", o=1), gam, ALU.mult)
                kb.inc(i, 'p')
        self.blk.gpsimd(po)

        def vo(vec):
            for g in range(4):
                kb.wv(vec, 'pe', kb._lnbc[0])
                d = kb.yv(g, 0)
                vec.tensor_tensor(kb.sc1[:], kb.hgt(g, 0), kb.pb[2][:, 0:TH],
                                  ALU.subtract)
                i = vec.scalar_tensor_tensor(d, kb.sc1[:], kb.chan(li, jg, g),
                                             kb.pb[3][:, 0:TH], ALU.mult, ALU.mult)
                kb.inc(i, 'v')
        self.blk.vector(vo)

        self._silu_a0 = self.n['a']

        def a(sc):
            for g in range(4):
                for th in range(2):
                    if th == 0:
                        kb.wv(sc, 'v', v0 + g + 1)
                        src = kb.yv(g, 0)
                    else:
                        kb.wv(sc, 'p', p0 + g + 1)
                        src = kb.yv(g, 1)
                    if kind == 'ln1':
                        i = sc.activation(src, src, AF.Silu, bias=kb.chan(li, jb, g))
                    else:
                        i = sc.activation(kb.lv(g, th), src, AF.Identity,
                                          bias=kb.chan(li, jb, g))
                    kb.inc(i, 'a')
        self.blk.scalar(a)

    def ph_psetup_pre(self, li):
        kb = self
        pre, pim = self.t4a, self.t4b
        r2, tmp = self.t4c, self.t4d
        rr = self.sinre
        rsave = self.sinim

        def p1(gp):
            kb.w(gp, 'xs', 'a', 'p')
            gp.tensor_tensor(r2[:], kb.chan(li, 2), kb.chan(li, 2), ALU.mult)
            gp.tensor_tensor(tmp[:], kb.chan(li, 3), kb.chan(li, 3), ALU.mult)
            gp.tensor_tensor(r2[:], r2[:], tmp[:], ALU.add)
            i = gp.tensor_scalar(r2[:], r2[:], 1e-30, None, ALU.add)
            kb.inc(i, 'p')
        self.blk.gpsimd(p1)
        pv = self.n['p']

        def a1(sc):
            kb.wv(sc, 'p', pv)
            sc.activation(tmp[:], r2[:], AF.Ln)
            i = sc.activation(rr[:, 0:4], tmp[:], AF.Exp, scale=kb.cp05[:, 0:1])
            kb.inc(i, 'a')
        self.blk.scalar(a1)
        av = self.n['a']

        def v1(vec):
            kb.wv(vec, 'a', av)
            i = vec.reciprocal(tmp[:], rr[:, 0:4])
            kb.inc(i, 'v')
        self.blk.vector(v1)
        vv = self.n['v']

        def p2(gp):
            kb.wv(gp, 'v', vv)
            gp.tensor_tensor(rsave[:, 0:4], r2[:], tmp[:], ALU.mult)
            gp.tensor_tensor(rsave[:, 0:4], rsave[:, 0:4], rr[:, 0:4], ALU.add)
            i = gp.tensor_scalar(r2[:], rsave[:, 0:4], 0.5, None, ALU.mult)
            kb.inc(i, 'p')
        self.blk.gpsimd(p2)
        pv2 = self.n['p']

        def v2(vec):
            kb.wv(vec, 'p', pv2)
            i = vec.reciprocal(tmp[:], r2[:])
            kb.inc(i, 'v')
        self.blk.vector(v2)
        self._ps_v = self.n['v']

    def ph_psetup_post(self, li):
        kb = self
        pre, pim = self.t4a, self.t4b
        r2, tmp = self.t4c, self.t4d

        def o1(x):
            return x.rearrange("p (c o) -> p c o", o=1)

        def a2(sc):
            kb.wv(sc, 'v', kb._ps_v)
            i = sc.activation(r2[:], r2[:], AF.Tanh)
            kb.inc(i, 'a')
        self.blk.scalar(a2)
        av = self.n['a']

        def p4(gp):
            kb.wv(gp, 'a', av)
            gp.tensor_tensor(tmp[:], r2[:], tmp[:], ALU.mult)
            gp.tensor_tensor(pre[:], kb.chan(li, 2), tmp[:], ALU.mult)
            gp.tensor_tensor(pim[:], kb.chan(li, 3), tmp[:], ALU.mult)
            powre3 = kb.powre[:, :].rearrange("p (g k) -> p g k", g=4)
            powim3 = kb.powim[:, :].rearrange("p (g k) -> p g k", g=4)
            for g in range(4):
                gp.memset(powre3[:, g, 0:1], 1.0)
                gp.memset(powim3[:, g, 0:1], 0.0)
            gp.tensor_copy(powre3[:, :, 1:2], o1(pre[:]))
            gp.tensor_copy(powim3[:, :, 1:2], o1(pim[:]))
            sA = kb.prB[:, :].rearrange("p (g c) -> p g c", g=4)
            sB = kb.piB[:, :].rearrange("p (g c) -> p g c", g=4)
            for m in (1, 2, 4, 8):
                qr = kb.colbc(kb.powre, m, 4 * (C + 1), m, gstep=C + 1, gcnt=4)
                qi = kb.colbc(kb.powim, m, 4 * (C + 1), m, gstep=C + 1, gcnt=4)
                src_r = powre3[:, :, 1:m + 1].rearrange("p g (c o) -> p g c o", o=1)
                src_i = powim3[:, :, 1:m + 1].rearrange("p g (c o) -> p g c o", o=1)
                d_r = powre3[:, :, m + 1:2 * m + 1].rearrange("p g (c o) -> p g c o", o=1)
                d_i = powim3[:, :, m + 1:2 * m + 1].rearrange("p g (c o) -> p g c o", o=1)
                a_ = sA[:, :, 0:m].rearrange("p g (c o) -> p g c o", o=1)
                b_ = sB[:, :, 0:m].rearrange("p g (c o) -> p g c o", o=1)
                gp.tensor_tensor(a_, src_r, qr, ALU.mult)
                gp.tensor_tensor(b_, src_i, qi, ALU.mult)
                gp.tensor_tensor(d_r, a_, b_, ALU.subtract)
                gp.tensor_tensor(a_, src_r, qi, ALU.mult)
                gp.tensor_tensor(b_, src_i, qr, ALU.mult)
                gp.tensor_tensor(d_i, a_, b_, ALU.add)
            qre = kb.qsre[:, :].rearrange("p (g s) -> p g s", g=4)
            qim = kb.qsim[:, :].rearrange("p (g s) -> p g s", g=4)
            gp.tensor_copy(qre[:, :, 0:1], powre3[:, :, C:C + 1])
            gp.tensor_copy(qim[:, :, 0:1], powim3[:, :, C:C + 1])
            for st in range(6):
                a_ = qre[:, :, st:st + 1]
                b_ = qim[:, :, st:st + 1]
                gp.tensor_tensor(o1(kb.sinre[:]), a_, a_, ALU.mult)
                gp.tensor_tensor(o1(kb.sinim[:]), b_, b_, ALU.mult)
                gp.tensor_tensor(qre[:, :, st + 1:st + 2], o1(kb.sinre[:]), o1(kb.sinim[:]), ALU.subtract)
                gp.tensor_tensor(o1(kb.sinre[:]), a_, b_, ALU.mult)
                gp.tensor_scalar(qim[:, :, st + 1:st + 2], o1(kb.sinre[:]), 2.0, None, ALU.mult)
            qcre3 = kb.qcre[:, :].rearrange("p (g c) -> p g c", g=4)
            qcim3 = kb.qcim[:, :].rearrange("p (g c) -> p g c", g=4)
            for g in range(4):
                gp.memset(qcre3[:, g, 0:1], 1.0)
                gp.memset(qcim3[:, g, 0:1], 0.0)
            for st in range(6):
                m = 1 << st
                qr = kb.colbc(kb.qsre, st, 32, m, gstep=8, gcnt=4)
                qi = kb.colbc(kb.qsim, st, 32, m, gstep=8, gcnt=4)
                src_r = qcre3[:, :, 0:m].rearrange("p g (c o) -> p g c o", o=1)
                src_i = qcim3[:, :, 0:m].rearrange("p g (c o) -> p g c o", o=1)
                d_r = qcre3[:, :, m:2 * m].rearrange("p g (c o) -> p g c o", o=1)
                d_i = qcim3[:, :, m:2 * m].rearrange("p g (c o) -> p g c o", o=1)
                a_ = sA[:, :, 0:m].rearrange("p g (c o) -> p g c o", o=1)
                b_ = sB[:, :, 0:m].rearrange("p g (c o) -> p g c o", o=1)
                gp.tensor_tensor(a_, src_r, qr, ALU.mult)
                gp.tensor_tensor(b_, src_i, qi, ALU.mult)
                gp.tensor_tensor(d_r, a_, b_, ALU.subtract)
                gp.tensor_tensor(a_, src_r, qi, ALU.mult)
                gp.tensor_tensor(b_, src_i, qr, ALU.mult)
                gp.tensor_tensor(d_i, a_, b_, ALU.add)
            for g in range(4):
                scl_b = kb.colbc(kb.chps, (li * 9 + 6) * 4 + g, NL * 9 * 4, C)
                gp.tensor_tensor(o1(kb.psclre[:, g * C:(g + 1) * C]),
                                 o1(kb.powre[:, g * (C + 1) + 1:g * (C + 1) + C + 1]),
                                 scl_b, ALU.mult)
                gp.tensor_tensor(o1(kb.psclim[:, g * C:(g + 1) * C]),
                                 o1(kb.powim[:, g * (C + 1) + 1:g * (C + 1) + C + 1]),
                                 scl_b, ALU.mult)
                gp.tensor_scalar(kb.psclim[:, g * C:(g + 1) * C],
                                 kb.psclim[:, g * C:(g + 1) * C], -1.0, None, ALU.mult)
            pp_re = kb.ppre[:, :].rearrange("p (g e) -> p g e", g=4)
            pp_im = kb.ppim[:, :].rearrange("p (g e) -> p g e", g=4)
            Pr = qre[:, :, 6:7]
            Pi = qim[:, :, 6:7]
            for g in range(4):
                gp.memset(pp_re[:, g, 0:1], 1.0)
                gp.memset(pp_im[:, g, 0:1], 0.0)
            gp.tensor_copy(pp_re[:, :, 1:2], Pr)
            gp.tensor_copy(pp_im[:, :, 1:2], Pi)
            i = None
            for e in (2, 3):
                ar = pp_re[:, :, e - 1:e]
                ai = pp_im[:, :, e - 1:e]
                gp.tensor_tensor(o1(kb.sinre[:]), ar, Pr, ALU.mult)
                gp.tensor_tensor(o1(kb.sinim[:]), ai, Pi, ALU.mult)
                gp.tensor_tensor(pp_re[:, :, e:e + 1], o1(kb.sinre[:]), o1(kb.sinim[:]), ALU.subtract)
                gp.tensor_tensor(o1(kb.sinre[:]), ar, Pi, ALU.mult)
                gp.tensor_tensor(o1(kb.sinim[:]), ai, Pr, ALU.mult)
                i = gp.tensor_tensor(pp_im[:, :, e:e + 1], o1(kb.sinre[:]), o1(kb.sinim[:]), ALU.add)
            kb.inc(i, 'p')
        self.blk.gpsimd(p4)
        self._p4p = self.n['p']

    def ph_sconv(self, li):
        kb = self
        wre3 = self.wre[:, :].rearrange("p (g c) -> p g c", g=4)
        wim3 = self.wim[:, :].rearrange("p (g c) -> p g c", g=4)
        Sre3 = self.Sre[:, :].rearrange("p (g c) -> p g c", g=4)
        Sim3 = self.Sim[:, :].rearrange("p (g c) -> p g c", g=4)
        prow = 4 * (C + 1)

        def o1(x):
            return x.rearrange("p (c o) -> p c o", o=1)

        def v(vec):
            kb.wv(vec, 'p', kb._p4p)
            for g in range(4):
                kb.wv(vec, 'a', kb._silu_a0 + 2 * g + 2)
                yg = kb.yv(g)
                y3 = yg.rearrange("p (c i) -> p c i", c=NCH)
                rev_re = bass.AP(kb.powre.ap().tensor, g * (C + 1) + (C - 1),
                                 [[prow, 128], [0, NCH], [-1, C]])
                rev_im = bass.AP(kb.powim.ap().tensor, g * (C + 1) + (C - 1),
                                 [[prow, 128], [0, NCH], [-1, C]])
                vec.tensor_tensor(kb.wtmp3(), y3, rev_re, ALU.mult)
                vec.reduce_sum(wre3[:, g, :], kb.wtmp3(), AX.X)
                vec.tensor_tensor(kb.wtmp3(), y3, rev_im, ALU.mult)
                vec.reduce_sum(wim3[:, g, :], kb.wtmp3(), AX.X)
            N4 = 4 * NCH
            f32v = kb.bigA.ap().bitcast(F32)
            tmp1 = f32v[:, 15360:15360 + N4]
            tmp2 = f32v[:, 15616:15616 + N4]
            cur_r, cur_i, nxt_r, nxt_i = kb.prA, kb.piA, kb.prB, kb.piB

            def g3v(t, lo, hi):
                return t[:].rearrange("p (g c) -> p g c", g=4)[:, :, lo:hi] \
                    .rearrange("p g (c o) -> p g c o", o=1)
            for st in range(6):
                m = 1 << st
                cnt = NCH - m
                qr_b = kb.colbc(kb.qsre, st, 32, cnt, gstep=8, gcnt=4)
                qi_b = kb.colbc(kb.qsim, st, 32, cnt, gstep=8, gcnt=4)
                vec.tensor_copy(g3v(nxt_r, 0, m), g3v(cur_r, 0, m))
                vec.tensor_copy(g3v(nxt_i, 0, m), g3v(cur_i, 0, m))
                tm1 = tmp1.rearrange("p (g c) -> p g c", g=4)[:, :, 0:cnt] \
                    .rearrange("p g (c o) -> p g c o", o=1)
                tm2 = tmp2.rearrange("p (g c) -> p g c", g=4)[:, :, 0:cnt] \
                    .rearrange("p g (c o) -> p g c o", o=1)
                vec.tensor_tensor(tm1, g3v(cur_r, 0, cnt), qr_b, ALU.mult)
                vec.tensor_tensor(tm2, g3v(cur_i, 0, cnt), qi_b, ALU.mult)
                vec.tensor_tensor(g3v(nxt_r, m, NCH), g3v(cur_r, m, NCH), tm1, ALU.add)
                vec.tensor_tensor(g3v(nxt_r, m, NCH), g3v(nxt_r, m, NCH), tm2, ALU.subtract)
                vec.tensor_tensor(tm1, g3v(cur_i, 0, cnt), qr_b, ALU.mult)
                vec.tensor_tensor(tm2, g3v(cur_r, 0, cnt), qi_b, ALU.mult)
                vec.tensor_tensor(g3v(nxt_i, m, NCH), g3v(cur_i, m, NCH), tm1, ALU.add)
                vec.tensor_tensor(g3v(nxt_i, m, NCH), g3v(nxt_i, m, NCH), tm2, ALU.add)
                cur_r, nxt_r = nxt_r, cur_r
                cur_i, nxt_i = nxt_i, cur_i
            cr3 = cur_r[:].rearrange("p (g c) -> p g c", g=4)
            ci3 = cur_i[:].rearrange("p (g c) -> p g c", g=4)
            for g in range(4):
                vec.memset(Sre3[:, g, 0:1], 0.0)
                vec.memset(Sim3[:, g, 0:1], 0.0)
            vec.tensor_copy(Sre3[:, :, 1:NCH], cr3[:, :, 0:NCH - 1])
            vec.tensor_copy(Sim3[:, :, 1:NCH], ci3[:, :, 0:NCH - 1])
            vec.tensor_copy(
                kb.gath[:, 56:64].rearrange("p (g x) -> p g x", g=4)[:, :, 0:1],
                cr3[:, :, NCH - 1:NCH])
            i = vec.tensor_copy(
                kb.gath[:, 56:64].rearrange("p (g x) -> p g x", g=4)[:, :, 1:2],
                ci3[:, :, NCH - 1:NCH])
            kb.inc(i, 'v')
        self.blk.vector(v)

        def g1(gp):
            kb.w(gp, 'v')
            kb.inc(gp.dma_start(kb.cc_in[:, :], kb.gath[:, 56:64]), 'xg', 16)
            kb.w(gp, 'xg')
            i = gp.collective_compute(
                "AllGather", ALU.bypass, replica_groups=[list(range(8))],
                ins=[kb.cc_in.ap().opt()], outs=[kb.cc_out.ap().opt()])
            i.then_inc(kb.ccsems[li])
            gp.wait_ge(kb.ccsems[li], 1)
            src = bass.AP(kb.cc_out.ap().tensor, 0, [[8, 128], [1024, 8], [1, 8]])
            i = gp.dma_start(kb.gath[:].rearrange("p (r c) -> p r c", r=8), src)
            kb.inc(i, 'xg', 16)
        self.blk.gpsimd(g1)

        def vsh(vec):
            for g in range(4):
                kb.wv(vec, 'p', kb._apply_p0 + g + 1)
                yg = kb.yv(g)
                y3 = yg.rearrange("p (c i) -> p c i", c=NCH)
                h3 = kb.hg(g).rearrange("p (c i) -> p c i", c=NCH)
                vec.scalar_tensor_tensor(kb.hg(g), yg, kb.chan(li, 6, g),
                                         kb.hg(g), ALU.mult, ALU.add)
                for k in range(1, C):
                    i = vec.scalar_tensor_tensor(
                        h3[:, :, k:C], y3[:, :, 0:C - k],
                        kb.psclre[:, g * C + k - 1:g * C + k],
                        h3[:, :, k:C], ALU.mult, ALU.add)
                kb.inc(i, 'v')
        self.blk.vector(vsh)

        def pc(gp):
            kb.w(gp, 'xg')
            wjr = kb.t4a
            WJ = kb.prB[:, :].rearrange("p (g c) -> p g c", g=4)
            WJr = WJ[:, :, 0:9]
            WJi = WJ[:, :, 9:18]
            PR = kb.prA[:, :].rearrange("p (g c) -> p g c", g=4)
            prod = PR[:, :, 0:9]
            gp.memset(WJr, 0.0)
            gp.memset(WJi, 0.0)
            for e in range(4):
                cmsb = bass.AP(kb.cms.ap().tensor, e,
                               [[32, 128], [0, 4], [4, 8], [1, 1]])
                ppb_r = bass.AP(kb.ppre.ap().tensor, e,
                                [[16, 128], [4, 4], [0, 8], [1, 1]])
                ppb_i = bass.AP(kb.ppim.ap().tensor, e,
                                [[16, 128], [4, 4], [0, 8], [1, 1]])
                o4 = prod[:, :, 0:8].rearrange("p g (c o) -> p g c o", o=1)
                gp.tensor_tensor(o4, cmsb, ppb_r, ALU.mult)
                gp.tensor_tensor(WJr[:, :, 0:8], WJr[:, :, 0:8], prod[:, :, 0:8], ALU.add)
                gp.tensor_tensor(o4, cmsb, ppb_i, ALU.mult)
                gp.tensor_tensor(WJi[:, :, 0:8], WJi[:, :, 0:8], prod[:, :, 0:8], ALU.add)
                lcb = bass.AP(kb.lcsel.ap().tensor, e,
                              [[4, 128], [0, 4], [0, 1], [1, 1]])
                o41 = prod[:, :, 8:9].rearrange("p g (c o) -> p g c o", o=1)
                gp.tensor_tensor(o41, lcb, ppb_r[:, :, 0:1], ALU.mult)
                gp.tensor_tensor(WJr[:, :, 8:9], WJr[:, :, 8:9], prod[:, :, 8:9], ALU.add)
                gp.tensor_tensor(o41, lcb, ppb_i[:, :, 0:1], ALU.mult)
                gp.tensor_tensor(WJi[:, :, 8:9], WJi[:, :, 8:9], prod[:, :, 8:9], ALU.add)
            SJr = PR[:, :, 16:25]
            SJi = PR[:, :, 25:34]
            gathr = bass.AP(kb.gath.ap().tensor, 0, [[64, 128], [2, 4], [8, 8], [1, 1]])
            gathi = bass.AP(kb.gath.ap().tensor, 1, [[64, 128], [2, 4], [8, 8], [1, 1]])
            gp.tensor_copy(SJr[:, :, 0:8].rearrange("p g (c o) -> p g c o", o=1), gathr)
            gp.tensor_copy(SJi[:, :, 0:8].rearrange("p g (c o) -> p g c o", o=1), gathi)
            gp.tensor_copy(SJr[:, :, 8:9], kb.chan(li, 4).rearrange("p (g o) -> p g o", o=1))
            gp.tensor_copy(SJi[:, :, 8:9], kb.chan(li, 5).rearrange("p (g o) -> p g o", o=1))
            P1 = PR[:, :, 34:43]
            P2 = WJ[:, :, 18:27]
            P3 = PR[:, :, 43:52]
            P4 = WJ[:, :, 27:36]
            gp.tensor_tensor(P1, WJr, SJr, ALU.mult)
            gp.tensor_tensor(P2, WJi, SJi, ALU.mult)
            gp.tensor_tensor(P3, WJr, SJi, ALU.mult)
            i = gp.tensor_tensor(P4, WJi, SJr, ALU.mult)
            kb.inc(i, 'p')
        self.blk.gpsimd(pc)
        pcp = self.n['p']

        def v2(vec):
            kb.wv(vec, 'p', pcp)
            PR = kb.prA[:, :].rearrange("p (g c) -> p g c", g=4)
            WJ = kb.prB[:, :].rearrange("p (g c) -> p g c", g=4)
            vec.reduce_sum(o1(kb.sinre[:]), PR[:, :, 34:43], AX.X)
            vec.reduce_sum(o1(kb.t4c[:]), WJ[:, :, 18:27], AX.X)
            vec.reduce_sum(o1(kb.sinim[:]), PR[:, :, 43:52], AX.X)
            i = vec.reduce_sum(o1(kb.t4d[:]), WJ[:, :, 27:36], AX.X)
            fv = kb.inc(i, 'v')
            vec.wait_ge(kb.sems['v'], fv)
            vec.tensor_tensor(kb.sinre[:], kb.sinre[:], kb.t4c[:], ALU.subtract)
            i = vec.tensor_tensor(kb.sinim[:], kb.sinim[:], kb.t4d[:], ALU.add)
            fv = kb.inc(i, 'v')
            vec.wait_ge(kb.sems['v'], fv)
            qcre3 = kb.qcre[:, :].rearrange("p (g c) -> p g c", g=4)
            qcim3 = kb.qcim[:, :].rearrange("p (g c) -> p g c", g=4)
            i = None
            for g in range(4):
                sr = kb.sinre[:, g:g + 1]
                si = kb.sinim[:, g:g + 1]
                vec.scalar_tensor_tensor(Sre3[:, g, :], qcre3[:, g, :], sr,
                                         Sre3[:, g, :], ALU.mult, ALU.add)
                vec.tensor_scalar(kb.prA[:, 0:NCH], qcim3[:, g, :], si, None, ALU.mult)
                vec.tensor_tensor(Sre3[:, g, :], Sre3[:, g, :], kb.prA[:, 0:NCH], ALU.subtract)
                vec.scalar_tensor_tensor(Sim3[:, g, :], qcim3[:, g, :], sr,
                                         Sim3[:, g, :], ALU.mult, ALU.add)
                vec.tensor_scalar(kb.prA[:, 0:NCH], qcre3[:, g, :], si, None, ALU.mult)
                vec.tensor_tensor(Sim3[:, g, :], Sim3[:, g, :], kb.prA[:, 0:NCH], ALU.add)
                h3 = kb.hg(g).rearrange("p (c i) -> p c i", c=NCH)
                for ii in range(C):
                    vec.scalar_tensor_tensor(
                        h3[:, :, ii:ii + 1],
                        Sre3[:, g, :].rearrange("p (c o) -> p c o", o=1),
                        kb.psclre[:, g * C + ii:g * C + ii + 1],
                        h3[:, :, ii:ii + 1], ALU.mult, ALU.add)
                    i = vec.scalar_tensor_tensor(
                        h3[:, :, ii:ii + 1],
                        Sim3[:, g, :].rearrange("p (c o) -> p c o", o=1),
                        kb.psclim[:, g * C + ii:g * C + ii + 1],
                        h3[:, :, ii:ii + 1], ALU.mult, ALU.add)
            kb.inc(i, 'v')
        self.blk.vector(v2)

    def ph_w1_prefetch(self, li):
        kb = self
        w1s = self.streams['w1']
        w1i = 2 * li

        def g(gp):
            kb.w(gp, 'pe')
            i = gp.dma_start(kb.wsA[:], kb.w1b[li, 0])
            w1s.issue(i, w1i)
            i = gp.dma_start(kb.wsB[:], kb.w1b[li, 1])
            w1s.issue(i, w1i + 1)
        self.blk.gpsimd(g)

    def ph_w1(self, li):
        kb = self
        w1s = self.streams['w1']
        w1i = 2 * li
        a0 = self.n['a']
        pe0 = self.n['pe']
        self._w1_pe0 = pe0

        def te(t):
            kb.w(t, 'a')
            cnt = 0
            for hf in range(2):
                w1s.wait(t, w1i + hf)
                ws = kb.wsA if hf == 0 else kb.wsB
                for mm in range(8):
                    for th in range(2):
                        pbk = kb.pb[4 + (cnt % 2)]
                        if cnt >= 2:
                            kb.wv(t, 'a', a0 + cnt - 1)
                        i = None
                        for kc in range(4):
                            i = t.matmul(
                                pbk[:, 0:TH],
                                ws[:, kc * 1024 + mm * 128: kc * 1024 + (mm + 1) * 128],
                                kb.lv(kc, th), start=(kc == 0), stop=(kc == 3))
                        kb.inc(i, 'pe')
                        cnt += 1
        self.blk.tensor(te)

        def a(sc):
            cnt = 0
            for hf in range(2):
                for mm in range(8):
                    m = hf * 8 + mm
                    for th in range(2):
                        pbk = kb.pb[4 + (cnt % 2)]
                        kb.wv(sc, 'pe', pe0 + cnt + 1)
                        dst = kb.bigB[:, m * T + th * TH: m * T + (th + 1) * TH]
                        i = sc.activation(dst, pbk[:, 0:TH], AF.Silu,
                                          bias=kb.b1s[:, li * 16 + m:li * 16 + m + 1])
                        got = kb.inc(i, 'a')
                        assert got == a0 + cnt + 1
                        cnt += 1
        self.blk.scalar(a)

    def ph_w2(self, li):
        kb = self
        w2s = self.streams['w2']
        w2i = 2 * li
        w1_done = self.n['pe']

        def g(gp):
            kb.wv(gp, 'pe', kb._w1_pe0 + 16)
            i = gp.dma_start(kb.wsA[:], kb.w2b[li, 0])
            w2s.issue(i, w2i)
            kb.wv(gp, 'pe', w1_done)
            i = gp.dma_start(kb.wsB[:], kb.w2b[li, 1])
            w2s.issue(i, w2i + 1)
        self.blk.gpsimd(g)

        v0 = self.n['v']
        pe0 = self.n['pe']

        def te(t):
            kb.w(t, 'a')
            w2s.wait(t, w2i)
            cnt = 0
            for m in range(4):
                for th in range(2):
                    pbk = kb.pb[6 + (cnt % 2)]
                    if cnt >= 2:
                        kb.wv(t, 'v', v0 + cnt - 1)
                    i = None
                    for kc in range(16):
                        if kc == 8 and cnt == 0:
                            w2s.wait(t, w2i + 1)
                        hf = kc // 8
                        kch = kc % 8
                        ws = kb.wsA if hf == 0 else kb.wsB
                        i = t.matmul(
                            pbk[:, 0:TH],
                            ws[:, kch * 512 + m * 128: kch * 512 + (m + 1) * 128],
                            kb.bigB[:, kc * T + th * TH: kc * T + (th + 1) * TH],
                            start=(kc == 0), stop=(kc == 15))
                    kb.inc(i, 'pe')
                    cnt += 1
        self.blk.tensor(te)

        def v(vec):
            cnt = 0
            for m in range(4):
                for th in range(2):
                    pbk = kb.pb[6 + (cnt % 2)]
                    kb.wv(vec, 'pe', pe0 + cnt + 1)
                    i = vec.scalar_tensor_tensor(kb.hgt(m, th), pbk[:, 0:TH],
                                                 kb.b2s[:, li * 4 + m:li * 4 + m + 1],
                                                 kb.hgt(m, th), ALU.add, ALU.add)
                    got = kb.inc(i, 'v')
                    assert got == v0 + cnt + 1
                    cnt += 1
        self.blk.vector(v)

    def ph_lnl_wout(self):
        kb = self
        for th in range(2):
            self.cast_chunks(lambda k: kb.hgt(k, th), 4)
            self.stats_rows(4, kb.inv512[:], self.pb[0], self.pb[1])
            self.rows_and_bcast(self.pb[0], self.pb[1], self.pb[2], self.pb[3])
            v0 = self.n['v']

            def v(vec, th=th):
                kb.w(vec, 'pe')
                for g in range(4):
                    vec.tensor_tensor(kb.sc1[:], kb.hgt(g, th), kb.pb[2][:, 0:TH],
                                      ALU.subtract)
                    i = vec.scalar_tensor_tensor(kb.yv(g, th), kb.sc1[:],
                                                 kb.lnls[:, g:g + 1],
                                                 kb.pb[3][:, 0:TH], ALU.mult, ALU.mult)
                    kb.inc(i, 'v')
            self.blk.vector(v)

            def a(sc, th=th, v0=v0):
                for g in range(4):
                    kb.wv(sc, 'v', v0 + g + 1)
                    i = sc.activation(kb.lv(g, th), kb.yv(g, th), AF.Identity,
                                      bias=kb.lnls[:, 4 + g:5 + g])
                    kb.inc(i, 'a')
            self.blk.scalar(a)

        if self.dump('hL'):
            return
        wow, wox = self.streams['wow'], self.streams['wox']
        base_pe = self.n['pe']
        def grp_pe(cnt):
            return base_pe + (cnt + 1) + max(0, cnt - 2)
        def stats_pe(cnt):
            if cnt <= 61:
                return grp_pe(cnt + 2) + 1
            return base_pe + 126 + (cnt - 61)
        base_v = self.n['v']
        base_a = self.n['a']

        def g(gp):
            kb.w(gp, 'a')

            def wout_load(q):
                if q >= 2:
                    kb.wv(gp, 'pe', grp_pe((q - 1) * 16 - 1))
                i = gp.dma_start((kb.wsA if q % 2 == 0 else kb.wsB)[:], kb.woutb[q])
                wow.issue(i, q)
            wout_load(0)
            wout_load(1)
            for m in range(32):
                if m >= 2:
                    kb.wv(gp, 'v', base_v + (m - 2) * 2 + 2)
                src = bass.AP(kb.xT.tensor, m * 128 * T, [[T, 128], [1, T]])
                i = gp.dma_start(kb.xt(m % 2), src)
                wox.issue(i, m)
                if m == 7:
                    wout_load(2)
                elif m == 15:
                    wout_load(3)
        self.blk.gpsimd(g)

        def te(t):
            kb.w(t, 'v', 'a')
            cnt = 0
            for q in range(4):
                wow.wait(t, q)
                ws = kb.wsA if q % 2 == 0 else kb.wsB
                for mq in range(8):
                    m = q * 8 + mq
                    wox.wait(t, m)
                    for th in range(2):
                        pbk = kb.pb[cnt % 2]
                        if cnt >= 2:
                            kb.wv(t, 'v', base_v + cnt - 1)
                        i = None
                        for kc in range(4):
                            i = t.matmul(
                                pbk[:, 0:TH],
                                ws[:, kc * 1024 + mq * 128: kc * 1024 + (mq + 1) * 128],
                                kb.lv(kc, th), start=(kc == 0), stop=(kc == 3))
                        got = kb.inc(i, 'pe')
                        assert got == grp_pe(cnt)
                        if cnt >= 2:
                            kb._wo_stats_emit(t, cnt - 2, base_a, stats_pe)
                        cnt += 1
            for j in (62, 63):
                kb._wo_stats_emit(t, j, base_a, stats_pe)
        self.blk.tensor(te)

        def v(vec):
            cnt = 0
            for m in range(32):
                for th in range(2):
                    pbk = kb.pb[cnt % 2]
                    kb.wv(vec, 'pe', grp_pe(cnt))
                    wox.wait(vec, m)
                    dst = kb.bigA[:, m * T + th * TH: m * T + (th + 1) * TH]
                    i = vec.scalar_tensor_tensor(
                        dst, pbk[:, 0:TH], kb.bouts[:, m:m + 1],
                        kb.xt(m % 2)[:, th * TH:(th + 1) * TH], ALU.add, ALU.add)
                    got = kb.inc(i, 'v')
                    assert got == base_v + cnt + 1
                    cnt += 1
        self.blk.vector(v)

        def a(sc):
            cnt = 0
            for m in range(32):
                for th in range(2):
                    kb.wv(sc, 'v', base_v + cnt + 1)
                    if cnt >= 2:
                        kb.wv(sc, 'pe', stats_pe(cnt - 2))
                    src = kb.bigA[:, m * T + th * TH: m * T + (th + 1) * TH]
                    i = sc.activation(kb.wsq[:, th * TH:(th + 1) * TH], src, AF.Square)
                    got = kb.inc(i, 'a')
                    assert got == base_a + cnt + 1
                    cnt += 1
        self.blk.scalar(a)

    def _wo_stats_emit(self, t, cnt, base_a, stats_pe):
        kb = self
        m, th = cnt // 2, cnt % 2
        kb.wv(t, 'a', base_a + cnt + 1)
        src = kb.bigA[:, m * T + th * TH: m * T + (th + 1) * TH]
        t.matmul(kb.pb[2 + th][0:1, 0:TH], kb.inv4096[:], src,
                 start=(m == 0), stop=(m == 31))
        i = t.matmul(kb.pb[4 + th][0:1, 0:TH], kb.inv4096[:],
                     kb.wsq[:, th * TH:(th + 1) * TH],
                     start=(m == 0), stop=(m == 31))
        got = kb.inc(i, 'pe')
        assert got == stats_pe(cnt)

    def postln_apply(self, big, to_out):
        kb = self
        for th in range(2):
            self.rows_and_bcast(self.pb[2 + th], self.pb[4 + th],
                                self.pb[6], self.pb[7])

            def ve(vec, th=th):
                kb.w(vec, 'pe', 'p')
                vec.tensor_copy(kb.mub[:], kb.pb[6][:, 0:TH])
                i = vec.tensor_copy(kb.rstdb[:], kb.pb[7][:, 0:TH])
                kb.inc(i, 'v')
            self.blk.vector(ve)

            if not to_out:
                p0 = self.n['p']
                v0 = self.n['v']
                if not hasattr(self, '_pl1'):
                    self._pl1 = {}
                self._pl1[th] = (v0, p0)

                def po(gp, th=th, v0=v0):
                    kb.wv(gp, 'v', v0)
                    for m in range(16, 32):
                        src = big[:, m * T + th * TH: m * T + (th + 1) * TH]
                        gp.tensor_tensor(src, src, kb.mub[:], ALU.subtract)
                        i = gp.tensor_tensor(src, src, kb.rstdb[:], ALU.mult)
                        kb.inc(i, 'p')
                self.blk.gpsimd(po)

                def vo(vec, th=th):
                    for m in range(16):
                        src = big[:, m * T + th * TH: m * T + (th + 1) * TH]
                        vec.tensor_tensor(src, src, kb.mub[:], ALU.subtract)
                        i = vec.tensor_tensor(src, src, kb.rstdb[:], ALU.mult)
                        kb.inc(i, 'v')
                self.blk.vector(vo)
            else:
                outs = self.streams['out']
                outs2 = self.streams['out2']
                p0 = self.n['p']
                v0 = self.n['v']

                def po(gp, th=th, p0=p0, v0=v0):
                    kb.wv(gp, 'v', v0)
                    for j in range(16):
                        jg = th * 16 + j
                        m = 16 + j
                        if jg >= 2:
                            outs2.wait(gp, jg - 2)
                        src = big[:, m * T + th * TH: m * T + (th + 1) * TH]
                        d = kb._oslot(1, jg % 2)
                        gp.tensor_tensor(kb.mdv(1), src, kb.mub[:], ALU.subtract)
                        i = gp.tensor_tensor(d, kb.mdv(1), kb.rstdb[:], ALU.mult)
                        kb.inc(i, 'p')
                self.blk.gpsimd(po)

                def vo(vec, th=th, v0=v0):
                    for j in range(16):
                        jg = th * 16 + j
                        m = j
                        if jg >= 2:
                            outs.wait(vec, jg - 2)
                        src = big[:, m * T + th * TH: m * T + (th + 1) * TH]
                        d = kb._oslot(0, jg % 2)
                        vec.tensor_tensor(kb.mdv(0), src, kb.mub[:], ALU.subtract)
                        i = vec.tensor_tensor(d, kb.mdv(0), kb.rstdb[:], ALU.mult)
                        kb.inc(i, 'v')
                self.blk.vector(vo)

                def so(sync, th=th, p0=p0, v0=v0):
                    for j in range(16):
                        jg = th * 16 + j
                        mv, mp = j, 16 + j
                        kb.wv(sync, 'v', v0 + j + 1)
                        dst = bass.AP(kb.outT.tensor, mv * 128 * T + th * TH,
                                      [[T, 128], [1, TH]])
                        i = sync.dma_start(dst, kb._oslot(0, jg % 2))
                        outs.issue(i, jg)
                        kb.wv(sync, 'p', p0 + j + 1)
                        dst = bass.AP(kb.outT.tensor, mp * 128 * T + th * TH,
                                      [[T, 128], [1, TH]])
                        i = sync.dma_start(dst, kb._oslot(1, jg % 2))
                        outs2.issue(i, jg)
                self.blk.sync(so)

    def ph_postln1(self):
        kb = self
        self._wout_pe_end = self.n['pe']
        des = self.streams['de']

        def g(gp):
            kb.wv(gp, 'pe', kb._wout_pe_end)
            for m0 in range(2):
                i = gp.dma_start((kb.wsA if m0 % 2 == 0 else kb.wsB)[:], kb.fwb[m0])
                des.issue(i, m0)
        self.blk.gpsimd(g)
        self.postln_apply(self.bigA, False)

    def _oslot(self, grp, par):
        outv = self.bigA.ap().bitcast(F32)
        slot = grp * 2 + par
        return outv[:, slot * TH:(slot + 1) * TH]

    def ph_masks(self):
        kb = self

        def g(gp):
            kb.wv(gp, 'pe', kb._wout_pe_end)
            kb.inc(gp.dma_start(kb.cents[:], kb.centb), 'xg', 16)
            asg_i32 = kb.bigB.ap().bitcast(I32)[0:1, 0:V]
            kb.inc(gp.dma_start(asg_i32, kb.assign), 'xg', 16)
        self.blk.gpsimd(g)

        def v0(vec):
            kb.w(vec, 'xg')
            asg_i32 = kb.bigB.ap().bitcast(I32)[0:1, 0:V]
            asg_f32 = kb.bigB.ap().bitcast(F32)[0:1, 4096:4096 + V]
            asg_bf = kb.bigB[0:1, 16384:16384 + V]
            vec.tensor_copy(asg_f32, asg_i32)
            kb.inc(vec.tensor_copy(asg_bf, asg_f32), 'v')
        self.blk.vector(v0)

        for th in range(2):
            def te(t, th=th):
                kb.w(t, 'xg')
                v0a, p0a = kb._pl1[th]
                i = None
                for kc in range(32):
                    if kc < 16:
                        kb.wv(t, 'v', v0a + kc + 1)
                    else:
                        kb.wv(t, 'p', p0a + (kc - 16) + 1)
                    i = t.matmul(kb.pb[th][0:64, 0:TH],
                                 kb.cents[:, kc * 64:(kc + 1) * 64],
                                 kb.bigA[:, kc * T + th * TH: kc * T + (th + 1) * TH],
                                 start=(kc == 0), stop=(kc == 31))
                kb.inc(i, 'pe')
            self.blk.tensor(te)

            def a(sc, th=th):
                kb.w(sc, 'pe')
                ebf = kb.bigB[0:64, 20480 + th * TH:20480 + (th + 1) * TH]
                kb.inc(sc.activation(ebf, kb.pb[th][0:64, 0:TH], AF.Exp), 'a')
            self.blk.scalar(a)

            def te2(t, th=th):
                kb.w(t, 'a', 'v')
                ebf = kb.bigB[0:64, 20480 + th * TH:20480 + (th + 1) * TH]
                i = t.matmul(kb.pb[2][0:1, 0:TH], kb.ones64[:], ebf,
                             start=True, stop=True)
                kb.inc(i, 'pe')
            self.blk.tensor(te2)

            def v1(vec, th=th):
                kb.w(vec, 'pe')
                i = vec.tensor_scalar(kb.rowb[0:1, 0:TH], kb.pb[2][0:1, 0:TH],
                                      0.01, None, ALU.mult)
                kb.inc(i, 'v')
            self.blk.vector(v1)

            def te3(t, th=th):
                kb.w(t, 'v')
                i = t.matmul(kb.pb[3][0:64, 0:TH], kb.ones1[0:1, 0:64],
                             kb.rowb[0:1, 0:TH], start=True, stop=True)
                kb.inc(i, 'pe')
            self.blk.tensor(te3)

            def v2(vec, th=th):
                kb.w(vec, 'pe')
                ebf = kb.bigB[0:64, 20480 + th * TH:20480 + (th + 1) * TH]
                hits = kb.bigB[0:64, 22528 + th * TH:22528 + (th + 1) * TH]
                vec.tensor_tensor(hits, ebf, kb.pb[3][0:64, 0:TH], ALU.is_gt)
                i = vec.reduce_sum(kb.hcnt[:, th:th + 1], hits, AX.X)
                kb.inc(i, 'v')
            self.blk.vector(v2)

            def te4(t, th=th):
                kb.w(t, 'v')
                hits = kb.bigB[0:64, 22528 + th * TH:22528 + (th + 1) * TH]
                i = t.matmul(kb.pb[4][0:1, 0:TH], kb.ones64[:], hits,
                             start=True, stop=True)
                kb.inc(i, 'pe')
            self.blk.tensor(te4)

            def v3(vec, th=th):
                kb.w(vec, 'pe')
                i = vec.tensor_scalar(kb.rowb2[0:1, th * TH:(th + 1) * TH],
                                      kb.pb[4][0:1, 0:TH], 0.5, None, ALU.is_gt)
                kb.inc(i, 'v')
            self.blk.vector(v3)

        if self.dump('m1'):
            return
        def v4(vec):
            kb.w(vec, 'v')
            i = vec.tensor_tensor(kb.hcnt[:, 0:1], kb.hcnt[:, 0:1],
                                  kb.hcnt[:, 1:2], ALU.add)
            kb.inc(i, 'v')
        self.blk.vector(v4)

        def g2(gp):
            kb.w(gp, 'v')
            kb.inc(gp.dma_start(kb.cl_in[:, :], kb.hcnt[:, 0:1]), 'xg', 16)
            kb.w(gp, 'xg')
            i = gp.collective_compute(
                "AllReduce", ALU.add, replica_groups=[list(range(8))],
                ins=[kb.cl_in.ap().opt()], outs=[kb.cl_out.ap().opt()])
            i.then_inc(kb.ccsems[4])
            gp.wait_ge(kb.ccsems[4], 1)
            kb.inc(gp.dma_start(kb.hcnt[0:64, 10:11], kb.cl_out.ap()[0:64, :]),
                   'xg', 16)
        self.blk.gpsimd(g2)

        def v5(vec):
            kb.w(vec, 'xg')
            i = vec.tensor_scalar(kb.clsel[:, 0:1], kb.hcnt[:, 10:11], 0.5,
                                  None, ALU.is_gt)
            kb.inc(i, 'v')
        self.blk.vector(v5)

        if self.dump('m2'):
            return
        pe0 = self.n['pe']
        kb._oh_v0 = self.n['v']

        def te5(t):
            kb.w(t, 'v')
            for oc in range(8):
                asg_bf = kb.bigB[0:1, 16384 + oc * TH:16384 + (oc + 1) * TH]
                if oc >= 1:
                    kb.wv(t, 'v', kb._oh_v0 + oc)
                i = t.matmul(kb.pb[5][0:64, 0:TH], kb.ones1[0:1, 0:64], asg_bf,
                             start=True, stop=True)
                kb.inc(i, 'pe')
        self.blk.tensor(te5)

        def v5b(vec):
            for oc in range(8):
                kb.wv(vec, 'pe', pe0 + oc + 1)
                oh = kb.bigB[0:64, 24576 + oc * TH:24576 + (oc + 1) * TH]
                i = vec.tensor_scalar(oh, kb.pb[5][0:64, 0:TH], kb.io64[:, 0:1],
                                      None, ALU.is_equal)
                got = kb.inc(i, 'v')
                assert got == kb._oh_v0 + oc + 1
        self.blk.vector(v5b)

        if self.dump('m3'):
            return
        if self.dump('m3b'):
            return

        def te6(t):
            kb.w(t, 'v')
            oh = kb.bigB[0:64, 24576:24576 + V]
            i = None
            for m in range(32):
                i = t.matmul(kb.pb[6][:, m:m + 1], oh[:, m * 128:(m + 1) * 128],
                             kb.clsel[:, 0:1], start=True, stop=True)
            kb.inc(i, 'pe')
        self.blk.tensor(te6)

        def v6(vec):
            kb.w(vec, 'pe')
            kb.inc(vec.tensor_copy(kb.rsall[:], kb.pb[6][:, 0:32]), 'v')
        self.blk.vector(v6)
        if self.dump('m4'):
            return

        def te7(t):
            kb.w(t, 'v')
            t.matmul(kb.pb[6][:, 0:TH], kb.ones1[:], kb.rowb2[0:1, 0:TH],
                     start=True, stop=True)
            i = t.matmul(kb.pb[7][:, 0:TH], kb.ones1[:], kb.rowb2[0:1, TH:2 * TH],
                         start=True, stop=True)
            kb.inc(i, 'pe')
        self.blk.tensor(te7)

        def v7(vec):
            kb.w(vec, 'pe')
            vec.tensor_copy(kb.qsb[:, 0:TH], kb.pb[6][:, 0:TH])
            kb.inc(vec.tensor_copy(kb.qsb[:, TH:T], kb.pb[7][:, 0:TH]), 'v')
        self.blk.vector(v7)

    def ph_dense(self):
        kb = self
        des = self.streams['de']
        base_pe = self.n['pe']

        def grp_pe(cnt):
            return base_pe + (cnt + 1) + max(0, cnt - 2)

        def stats_pe(cnt):
            if cnt <= 61:
                return grp_pe(cnt + 2) + 1
            return base_pe + 126 + (cnt - 61)
        base_v = self.n['v']
        base_a = self.n['a']
        base_p = self.n['p']

        def te(t):
            kb.w(t, 'v', 'a', 'p')
            cnt = 0
            for m in range(32):
                des.wait(t, m)
                ws = kb.wsA if m % 2 == 0 else kb.wsB
                for th in range(2):
                    pbk = kb.pb[cnt % 2]
                    if cnt >= 2:
                        kb.wv(t, 'v', base_v + cnt - 1)
                    i = None
                    for kc in range(32):
                        i = t.matmul(
                            pbk[:, 0:TH], ws[:, kc * 128:(kc + 1) * 128],
                            kb.bigA[:, kc * T + th * TH: kc * T + (th + 1) * TH],
                            start=(kc == 0), stop=(kc == 31))
                    got = kb.inc(i, 'pe')
                    assert got == grp_pe(cnt)
                    if cnt >= 2:
                        kb._de_stats_emit(t, cnt - 2, base_a, stats_pe)
                    cnt += 1
            for j in (62, 63):
                kb._de_stats_emit(t, j, base_a, stats_pe)
        self.blk.tensor(te)

        def v(vec):
            cnt = 0
            for m in range(32):
                for th in range(2):
                    pbk = kb.pb[cnt % 2]
                    kb.wv(vec, 'pe', grp_pe(cnt))
                    i = vec.tensor_scalar(kb.mdv(cnt % 2), pbk[:, 0:TH],
                                          kb.fbs[:, m:m + 1], kb.rsall[:, m:m + 1],
                                          ALU.add, ALU.mult)
                    got = kb.inc(i, 'v')
                    assert got == base_v + cnt + 1
                    cnt += 1
        self.blk.vector(v)

        def p(gp):
            kb.wv(gp, 'v', base_v)
            kb.wv(gp, 'p', base_p)
            kb.wv(gp, 'pe', base_pe)
            cnt = 0
            for m in range(32):
                for th in range(2):
                    kb.wv(gp, 'v', base_v + cnt + 1)
                    dst = kb.bigB[:, m * T + th * TH: m * T + (th + 1) * TH]
                    gp.tensor_tensor(kb.mdv(cnt % 2), kb.mdv(cnt % 2),
                                     kb.qsb[:, th * TH:(th + 1) * TH], ALU.mult)
                    i = gp.tensor_tensor(
                        dst, kb.mdv(cnt % 2),
                        kb.bigA[:, m * T + th * TH: m * T + (th + 1) * TH], ALU.add)
                    got = kb.inc(i, 'p')
                    assert got == base_p + cnt + 1
                    if th == 1 and m + 2 < 32:
                        i = gp.dma_start((kb.wsA if m % 2 == 0 else kb.wsB)[:],
                                         kb.fwb[m + 2])
                        des.issue(i, m + 2)
                    cnt += 1
        self.blk.gpsimd(p)

        def a(sc):
            cnt = 0
            for m in range(32):
                for th in range(2):
                    kb.wv(sc, 'p', base_p + cnt + 1)
                    if cnt >= 2:
                        kb.wv(sc, 'pe', stats_pe(cnt - 2))
                    src = kb.bigB[:, m * T + th * TH: m * T + (th + 1) * TH]
                    i = sc.activation(kb.wsq[:, th * TH:(th + 1) * TH], src, AF.Square)
                    got = kb.inc(i, 'a')
                    assert got == base_a + cnt + 1
                    cnt += 1
        self.blk.scalar(a)

    def _de_stats_emit(self, t, cnt, base_a, stats_pe):
        kb = self
        m, th = cnt // 2, cnt % 2
        kb.wv(t, 'a', base_a + cnt + 1)
        src = kb.bigB[:, m * T + th * TH: m * T + (th + 1) * TH]
        t.matmul(kb.pb[2 + th][0:1, 0:TH], kb.inv4096[:], src,
                 start=(m == 0), stop=(m == 31))
        i = t.matmul(kb.pb[4 + th][0:1, 0:TH], kb.inv4096[:],
                     kb.wsq[:, th * TH:(th + 1) * TH],
                     start=(m == 0), stop=(m == 31))
        got = kb.inc(i, 'pe')
        assert got == stats_pe(cnt)

    def ph_postln2_out(self):
        self.postln_apply(self.bigB, True)
        if self.done:
            return

    def ph_finish(self):
        kb = self

        def s(sync):
            for nm in ('out', 'out2'):
                st = kb.streams[nm]
                for sl in range(st.nbuf):
                    sync.wait_ge(st.sems[sl], st.cnt[sl])
            sync.wait_ge(kb.sems['o'], kb.n['o'])
        self.blk.sync(s)


def _prep_shared(inputs):
    g = {}
    Win = inputs['L0_attn_Win']
    g['winb'] = np.ascontiguousarray(
        Win.T.reshape(4, 8, 128, 512).transpose(0, 2, 1, 3)).reshape(4, 128, 8 * 512)
    W1 = inputs['L0_attn_W1']
    g['w1b'] = np.ascontiguousarray(
        np.stack([W1[i].T.reshape(4, 128, 2, 8, 128).transpose(2, 1, 0, 3, 4)
                  for i in range(NL)])).reshape(NL, 2, 128, 4 * 1024)
    W2 = inputs['L0_attn_W2']
    g['w2b'] = np.ascontiguousarray(
        np.stack([W2[i].T.reshape(2, 8, 128, 512).transpose(0, 2, 1, 3)
                  for i in range(NL)])).reshape(NL, 2, 128, 8 * 512)
    Wout = inputs['L0_attn_Wout']
    g['woutb'] = np.ascontiguousarray(
        Wout.T.reshape(4, 128, 4, 8, 128).transpose(2, 1, 0, 3, 4)).reshape(4, 128, 4096)
    Wf = inputs['L0_ffn_weight']
    g['fwb'] = np.ascontiguousarray(
        Wf.T.reshape(32, 128, 32, 128).transpose(2, 1, 0, 3)).reshape(32, 128, 32 * 128)
    cent = inputs['L0_ffn_centroids']
    g['centb'] = np.ascontiguousarray(
        cent.T.reshape(32, 128, 64).transpose(1, 0, 2)).reshape(128, 32 * 64)
    chp = np.stack([inputs['L0_attn_' + n] for n in
                    ('ln1_w', 'ln1_b', 'phz_re', 'phz_im', 'lc_re', 'lc_im',
                     'sc_lin', 'ln2_w', 'ln2_b')], axis=1)
    g['chp'] = np.ascontiguousarray(
        chp.reshape(NL, 9, 4, 128).transpose(3, 0, 1, 2)).reshape(128, NL * 9 * 4)
    g['binr'] = np.ascontiguousarray(inputs['L0_attn_bin'].reshape(4, 128).T)
    g['b1all'] = np.ascontiguousarray(
        inputs['L0_attn_b1'].reshape(NL, 16, 128).transpose(2, 0, 1)).reshape(128, NL * 16)
    g['b2all'] = np.ascontiguousarray(
        inputs['L0_attn_b2'].reshape(NL, 4, 128).transpose(2, 0, 1)).reshape(128, NL * 4)
    lnl = np.stack([inputs['L0_attn_lnl_w'], inputs['L0_attn_lnl_b']])
    g['lnlwb'] = np.ascontiguousarray(
        lnl.reshape(2, 4, 128).transpose(2, 0, 1)).reshape(128, 8)
    g['boutr'] = np.ascontiguousarray(inputs['L0_attn_bout'].reshape(32, 128).T)
    g['fbr'] = np.ascontiguousarray(inputs['L0_ffn_bias'].reshape(32, 128).T)
    g['iot64'] = np.arange(64, dtype=np.float32).reshape(64, 1)
    g['assign'] = np.ascontiguousarray(
        inputs['L0_ffn_assignments'].reshape(1, V)).astype(np.int32)
    return g


def make_in_maps(inputs):
    x = inputs['L0_attn_x']
    g = _prep_shared(inputs)
    maps = []
    for core in range(8):
        b, pi = core // 4, core % 4
        d = dict(g)
        d['xT'] = np.ascontiguousarray(x[b, pi * T:(pi + 1) * T, :].T)
        cmask = np.zeros((128, 32), dtype=np.float32)
        for j in range(pi):
            e = pi - 1 - j
            cmask[:, (b * 4 + j) * 4 + e] = 1.0
        d['cmask'] = cmask
        ls = np.zeros((128, 4), dtype=np.float32)
        ls[:, pi] = 1.0
        d['lcselr'] = ls
        maps.append(d)
    return maps


def kernel(**inputs):
    kb = KB()
    nc = kb.build()
    maps = make_in_maps(inputs)
    res = run_bass_kernel_spmd(nc, maps, core_ids=list(range(8)))
    out = np.empty((2, 4096, V), dtype=np.float32)
    for core in range(8):
        b, pi = core // 4, core % 4
        out[b, pi * T:(pi + 1) * T, :] = res.results[core]['outT'].T
    return out


# version 12
# speedup vs baseline: 1.0137x; 1.0078x over previous
import numpy as np
import ml_dtypes
import concourse.bass as bass
import concourse.mybir as mybir
import concourse.bass_utils as _bu
from concourse.bass_utils import run_bass_kernel_spmd

_LDWOPT_DISABLED = True

F32 = mybir.dt.float32
BF16 = mybir.dt.bfloat16
I32 = mybir.dt.int32
AF = mybir.ActivationFunctionType
ALU = mybir.AluOpType
AX = mybir.AxisListType

T = 1024
TH = 512
C = 16
NCH = T // C
NL = 4
D = 512
V = 4096
EPS = 1e-5


class Stream:

    def __init__(self, nc, name, nbuf):
        self.nbuf = nbuf
        self.sems = [nc.alloc_semaphore(f"st_{name}{i}") for i in range(nbuf)]
        self.cnt = [0] * nbuf

    def issue(self, instr, idx):
        sl = idx % self.nbuf
        self.cnt[sl] += 16
        instr.then_inc(self.sems[sl], 16)
        return self.cnt[sl]

    def wait(self, eng, idx):
        sl = idx % self.nbuf
        eng.wait_ge(self.sems[sl], 16 * (idx // self.nbuf + 1))


class KB:
    def __init__(self, dbg=None):
        self.dbg = dbg
        self.done = False
        nc = bass.Bass("TRN2", target_bir_lowering=False, debug=False,
                       detect_race_conditions=False)
        self.nc = nc
        dt = nc.dram_tensor
        self.xT = dt("xT", [V, T], F32, kind="ExternalInput").ap()
        self.winb = dt("winb", [4, 128, 8 * 512], F32, kind="ExternalInput").ap()
        self.w1b = dt("w1b", [NL, 2, 128, 4 * 1024], F32, kind="ExternalInput").ap()
        self.w2b = dt("w2b", [NL, 2, 128, 8 * 512], F32, kind="ExternalInput").ap()
        self.woutb = dt("woutb", [4, 128, 4096], F32, kind="ExternalInput").ap()
        self.fwb = dt("fwb", [32, 128, 32 * 128], F32, kind="ExternalInput").ap()
        self.centb = dt("centb", [128, 32 * 64], F32, kind="ExternalInput").ap()
        self.chp = dt("chp", [128, NL * 9 * 4], F32, kind="ExternalInput").ap()
        self.binr = dt("binr", [128, 4], F32, kind="ExternalInput").ap()
        self.b1all = dt("b1all", [128, NL * 16], F32, kind="ExternalInput").ap()
        self.b2all = dt("b2all", [128, NL * 4], F32, kind="ExternalInput").ap()
        self.lnlwb = dt("lnlwb", [128, 8], F32, kind="ExternalInput").ap()
        self.boutr = dt("boutr", [128, 32], F32, kind="ExternalInput").ap()
        self.fbr = dt("fbr", [128, 32], F32, kind="ExternalInput").ap()
        self.cmask = dt("cmask", [128, 32], F32, kind="ExternalInput").ap()
        self.lcselr = dt("lcselr", [128, 4], F32, kind="ExternalInput").ap()
        self.iot64 = dt("iot64", [64, 1], F32, kind="ExternalInput").ap()
        self.assign = dt("assign", [1, V], I32, kind="ExternalInput").ap()
        self.outT = dt("outT", [V, T], F32, kind="ExternalOutput").ap()
        self.cc_in = dt("cc_in", [128, 8], F32)
        self.cc_out = dt("cc_out", [1024, 8], F32)
        self.cl_in = dt("cl_in", [64, 1], F32)
        self.cl_out = dt("cl_out", [64, 1], F32)
        self.dbgout = None
        if dbg is not None:
            shp = {'h0': [128, 4096], 'x1': [128, 32768]}.get(dbg)
            if shp is None and (dbg.startswith('hmix') or dbg.startswith('hffn')):
                shp = [128, 4096]
            if shp is None and dbg.startswith('y'):
                shp = [128, 4096]
            if shp is None and dbg in ('x1pre', 'x2pre'):
                shp = [128, 32768]
            if shp is None and dbg == 'hL':
                shp = [128, 4096]
            if shp is None and dbg == 'masks':
                shp = [128, 1064]
            if shp is None and dbg in ('m1', 'm2', 'm3b', 'm4'):
                shp = [128, 16]
            if shp is None and dbg == 'p2a':
                shp = [128, 2048]
            if shp is None and dbg == 'm3':
                shp = [64, 4096]
            if shp is None and dbg.startswith('tabs'):
                shp = [128, 872]
            if shp is None and dbg.startswith('wsc'):
                shp = [128, 1096]
            self.dbgout = dt("dbgout", shp, F32, kind="ExternalOutput").ap()

        al = nc.alloc_sbuf_tensor
        self.bigA = al("bigA", [128, 32768], BF16)
        self.bigB = al("bigB", [128, 32768], BF16)
        self.h = al("h", [128, 4 * T], F32)
        self.Yb = al("Yb", [128, T], F32)
        self.wsq = al("wsq", [128, T], BF16)
        self.wsA = al("wsA", [128, 4096], BF16)
        self.wsB = al("wsB", [128, 4096], BF16)
        self.xtc = al("xtc", [128, 2 * T], BF16)
        self.sc1 = al("sc1", [128, TH], F32)
        self.md = al("md", [128, 2 * TH], F32)
        self.qsb = al("qsb", [128, T], F32)
        self.mub = al("mub", [128, TH], F32)
        self.rstdb = al("rstdb", [128, TH], F32)
        self.wre = al("wre", [128, 4 * NCH], F32)
        self.wim = al("wim", [128, 4 * NCH], F32)
        self.prB = al("prB", [128, 4 * NCH], F32)
        self.piB = al("piB", [128, 4 * NCH], F32)
        self.prA = self.wre
        self.piA = self.wim
        self.Sre = al("Sre", [128, 4 * NCH], F32)
        self.Sim = al("Sim", [128, 4 * NCH], F32)
        self.qcre = al("qcre", [128, 4 * NCH], F32)
        self.qcim = al("qcim", [128, 4 * NCH], F32)
        self.powre = al("powre", [128, 4 * (C + 1)], F32)
        self.powim = al("powim", [128, 4 * (C + 1)], F32)
        self.qsre = al("qsre", [128, 4 * 8], F32)
        self.qsim = al("qsim", [128, 4 * 8], F32)
        self.psclre = al("psclre", [128, 4 * C], F32)
        self.psclim = al("psclim", [128, 4 * C], F32)
        self.ppre = al("ppre", [128, 4 * 4], F32)
        self.ppim = al("ppim", [128, 4 * 4], F32)
        self.sinre = al("sinre", [128, 4], F32)
        self.sinim = al("sinim", [128, 4], F32)
        self.gath = al("gath", [128, 64], F32)
        self.t4a = al("t4a", [128, 4], F32)
        self.t4b = al("t4b", [128, 4], F32)
        self.t4c = al("t4c", [128, 4], F32)
        self.t4d = al("t4d", [128, 4], F32)
        self.chps = al("chps", [128, NL * 9 * 4], F32)
        self.bins = al("bins", [128, 4], F32)
        self.b1s = al("b1s", [128, NL * 16], F32)
        self.b2s = al("b2s", [128, NL * 4], F32)
        self.lnls = al("lnls", [128, 8], F32)
        self.bouts = al("bouts", [128, 32], F32)
        self.fbs = al("fbs", [128, 32], F32)
        self.cms = al("cms", [128, 32], F32)
        self.lcsel = al("lcsel", [128, 4], F32)
        self.io64 = al("io64", [64, 1], F32)
        pass
        self.rowf = al("rowf", [1, 2 * TH], F32)
        self.rowb = al("rowb", [1, 2 * TH], BF16)
        self.rowb2 = al("rowb2", [1, 2 * TH], BF16)
        self.hcnt = al("hcnt", [64, 12], F32)
        self.clsel = al("clsel", [64, 1], BF16)
        self.rsall = al("rsall", [128, 32], F32)
        self.inv512 = al("inv512", [128, 1], BF16)
        self.inv512f = al("inv512f", [128, 1], F32)
        self.inv4096 = al("inv4096", [128, 1], BF16)
        self.ones1 = al("ones1", [1, 128], BF16)
        self.ones64 = al("ones64", [64, 1], BF16)
        self.cm05 = al("cm05", [128, 1], F32)
        self.cp05 = al("cp05", [128, 1], F32)

        yb_addr = nc.lookup_mloc(self.Yb).addr
        self.cents = nc.alloc_sbuf_tensor_at("cents", [128, 32 * 64], BF16,
                                             offset=yb_addr)
        self.pb = [nc.alloc_psum_tensor(f"pb{i}", [128, 512], F32) for i in range(8)]
        self.sems = {k: nc.alloc_semaphore("s_" + k)
                     for k in ("xg", "xs", "pe", "v", "a", "p", "o")}
        self.ccsems = [nc.alloc_semaphore(f"s_cc{i}") for i in range(5)]
        self.n = {k: 0 for k in self.sems}
        self.streams = {}
        for nm, nb in (("ew", 2), ("ex", 4), ("w1", 2), ("w2", 2), ("wow", 2),
                       ("wox", 2), ("de", 2), ("out", 2), ("out2", 2)):
            self.streams[nm] = Stream(nc, nm, nb)

    def inc(self, instr, key, amt=1):
        self.n[key] += amt
        instr.then_inc(self.sems[key], amt)
        return self.n[key]

    def w(self, eng, *keys):
        for k in keys:
            if self.n[k] > 0:
                eng.wait_ge(self.sems[k], self.n[k])

    def wv(self, eng, key, val):
        if val > 0:
            eng.wait_ge(self.sems[key], val)

    def hg(self, g):
        return self.h[:, g * T:(g + 1) * T]

    def hgt(self, g, th):
        return self.h[:, g * T + th * TH: g * T + (th + 1) * TH]

    @property
    def hb(self):
        return self.bigA[:, 24576:26624]

    @property
    def hbsq(self):
        return self.bigA[:, 26624:28672]

    def xt(self, b):
        return self.xtc[:, b * T:(b + 1) * T]

    def mdv(self, i):
        return self.md[:, i * TH:(i + 1) * TH]

    def chan(self, i, j, g=None):
        base = (i * 9 + j) * 4
        if g is None:
            return self.chps[:, base:base + 4]
        return self.chps[:, base + g:base + g + 1]

    def yv(self, g, th=None):
        b = self.bigB.ap().bitcast(F32)
        if th is None:
            return b[:, 8192 + g * T: 8192 + (g + 1) * T]
        return b[:, 8192 + g * T + th * TH: 8192 + g * T + (th + 1) * TH]

    def lv(self, g, th):
        return self.bigB[:, 24576 + g * T + th * TH: 24576 + g * T + (th + 1) * TH]

    def wtmp3(self):
        return self.bigA.ap().bitcast(F32)[:, 15360:16384].rearrange(
            "p (c i) -> p c i", c=NCH)

    def dump(self, name):
        if self.dbg != name or self.done:
            return self.dbg == name and self.done
        kb = self
        if name == 'h0' or name.startswith('hmix') or name.startswith('hffn'):
            def s(sync):
                kb.w(sync, 'v', 'a', 'p', 'pe')
                kb.inc(sync.dma_start(kb.dbgout, kb.h[:]), 'o', 16)
                sync.wait_ge(kb.sems['o'], kb.n['o'])
            self.blk.sync(s)
        elif name.startswith('y'):
            src = self.bigB.ap().bitcast(F32)[:, 8192:12288]

            def s(sync):
                kb.w(sync, 'v', 'a', 'p', 'pe')
                kb.inc(sync.dma_start(kb.dbgout, src), 'o', 16)
                sync.wait_ge(kb.sems['o'], kb.n['o'])
            self.blk.sync(s)
        elif name in ('x1pre', 'x1'):
            def g(gp):
                kb.w(gp, 'v', 'a', 'p', 'pe')
                kb.inc(gp.dma_start(kb.dbgout, kb.bigA[:]), 'xg', 16)
                gp.wait_ge(kb.sems['xg'], kb.n['xg'])
            self.blk.gpsimd(g)
        elif name == 'm3':
            def sm(gp):
                kb.w(gp, 'v', 'a', 'p', 'pe', 'xg')
                i = gp.dma_start(kb.dbgout, kb.bigB[0:64, 24576:24576 + V])
                kb.inc(i, 'xg', 16)
                gp.wait_ge(kb.sems['xg'], kb.n['xg'])
            self.blk.gpsimd(sm)
        elif name in ('m1', 'm2', 'm3b', 'm4'):
            def sm(sync):
                kb.w(sync, 'v', 'a', 'p', 'pe', 'xg')
                kb.inc(sync.dma_start(kb.dbgout[0:64, 0:12], kb.hcnt[:]), 'o', 16)
                sync.wait_ge(kb.sems['o'], kb.n['o'])
            self.blk.sync(sm)
        elif name == 'p2a':
            def sp(sync):
                kb.w(sync, 'v', 'a', 'p', 'pe')
                i = sync.dma_start(kb.dbgout, kb.bigA.ap().bitcast(F32)[:, 0:2048])
                kb.inc(i, 'o', 16)
                sync.wait_ge(kb.sems['o'], kb.n['o'])
            self.blk.sync(sp)
        elif name == 'masks':
            def sm(sync):
                kb.w(sync, 'v', 'a', 'p', 'pe', 'xg')
                kb.inc(sync.dma_start(kb.dbgout[:, 0:1024], kb.qsb[:]), 'o', 16)
                kb.inc(sync.dma_start(kb.dbgout[:, 1024:1056], kb.rsall[:]), 'o', 16)
                kb.inc(sync.dma_start(kb.dbgout[0:64, 1056:1064], kb.hcnt[:, 4:12]), 'o', 16)
                sync.wait_ge(kb.sems['o'], kb.n['o'])
            self.blk.sync(sm)
        elif name == 'hL':
            def g(gp):
                kb.w(gp, 'v', 'a', 'p', 'pe')
                kb.inc(gp.dma_start(kb.dbgout, kb.bigB[:, 24576:28672]), 'xg', 16)
                gp.wait_ge(kb.sems['xg'], kb.n['xg'])
            self.blk.gpsimd(g)
        elif name.startswith('tabs') or name.startswith('wsc'):
            if name.startswith('tabs'):
                parts = [(kb.powre, 68), (kb.powim, 68), (kb.qsre, 32),
                         (kb.qsim, 32), (kb.qcre, 256), (kb.qcim, 256),
                         (kb.psclre, 64), (kb.psclim, 64), (kb.ppre, 16),
                         (kb.ppim, 16)]
            else:
                parts = [(kb.wre, 256), (kb.wim, 256), (kb.Sre, 256),
                         (kb.Sim, 256), (kb.gath, 64), (kb.sinre, 2 * 0 + 4),
                         (kb.sinim, 4)]

            def s(sync):
                kb.w(sync, 'v', 'a', 'p', 'pe')
                off = 0
                i = None
                for tns, wdt in parts:
                    i = sync.dma_start(kb.dbgout[:, off:off + wdt], tns[:, 0:wdt])
                    kb.inc(i, 'o', 16)
                    off += wdt
                sync.wait_ge(kb.sems['o'], kb.n['o'])
            self.blk.sync(s)
        elif name == 'x2pre':
            def g(gp):
                kb.w(gp, 'v', 'a', 'p', 'pe')
                kb.inc(gp.dma_start(kb.dbgout, kb.bigB[:]), 'xg', 16)
                gp.wait_ge(kb.sems['xg'], kb.n['xg'])
            self.blk.gpsimd(g)
        self.done = True
        return True

    def build(self):
        nc = self.nc
        with nc.Block() as blk:
            self.blk = blk
            self.ph_pre()
            self.ph_embed()
            if self.dump('h0'):
                return nc
            for i in range(NL):
                if self.ph_layer(i):
                    return nc
            self.ph_lnl_wout()
            if self.done:
                return nc
            if self.dump('x1pre'):
                return nc
            self.ph_postln1()
            if self.dump('x1'):
                return nc
            self.ph_masks()
            if self.done:
                return nc
            if self.dump('masks'):
                return nc
            self.ph_dense()
            if self.dump('x2pre'):
                return nc
            self.ph_postln2_out()
            if self.done:
                return nc
            self.ph_finish()
        return nc

    def ph_pre(self):
        kb = self

        def s(sync):
            for dst, src in ((kb.chps, kb.chp), (kb.bins, kb.binr),
                             (kb.b1s, kb.b1all), (kb.b2s, kb.b2all),
                             (kb.lnls, kb.lnlwb), (kb.bouts, kb.boutr),
                             (kb.fbs, kb.fbr), (kb.cms, kb.cmask),
                             (kb.lcsel, kb.lcselr), (kb.io64, kb.iot64)):
                kb.inc(sync.dma_start(dst[:], src), 'xs', 16)
        self.blk.sync(s)

        def v(vec):
            vec.memset(kb.inv512[:], 1.0 / 512)
            vec.memset(kb.inv512f[:], 1.0 / 512)
            vec.memset(kb.inv4096[:], 1.0 / 4096)
            vec.memset(kb.ones1[:], 1.0)
            vec.memset(kb.ones64[:], 1.0)
            vec.memset(kb.cm05[:], -0.5)
            i = vec.memset(kb.cp05[:], 0.5)
            kb.inc(i, 'v')
        self.blk.vector(v)

    def ph_embed(self):
        kb = self
        seq = [(th, q, kc) for th in range(2) for q in range(4) for kc in range(8)]
        base_pe = self.n['pe']
        xpe = {idx: base_pe + idx + 1 for idx in range(64)}
        winpe = {tq: base_pe + (tq + 1) * 8 for tq in range(8)}
        ew, ex = self.streams['ew'], self.streams['ex']

        def gdma(gp):
            kb.w(gp, 'xs')

            def win_load(tq):
                kb.wv(gp, 'pe', winpe.get(tq - 2, 0))
                q = tq % 4
                i = gp.dma_start((kb.wsA if tq % 2 == 0 else kb.wsB)[:],
                                 kb.winb[q])
                ew.issue(i, tq)
            win_load(0)
            win_load(1)
            for idx, (th, q, kc) in enumerate(seq):
                k = q * 8 + kc
                kb.wv(gp, 'pe', xpe.get(idx - 4, 0))
                src = bass.AP(kb.xT.tensor, (k * 128) * T + th * TH,
                              [[T, 128], [1, TH]])
                i = gp.dma_start(kb.xtc[:, (idx % 4) * TH:(idx % 4 + 1) * TH], src)
                ex.issue(i, idx)
                if idx % 8 == 0 and idx >= 8 and idx // 8 + 1 <= 7:
                    win_load(idx // 8 + 1)
        self.blk.gpsimd(gdma)

        def te(t):
            kb.w(t, 'v')
            for idx, (th, q, kc) in enumerate(seq):
                tq = th * 4 + q
                ws = kb.wsA if tq % 2 == 0 else kb.wsB
                ew.wait(t, tq)
                ex.wait(t, idx)
                first = (q == 0 and kc == 0)
                last = (q == 3 and kc == 7)
                i = None
                for m in range(4):
                    i = t.matmul(kb.pb[th * 4 + m][:, 0:TH],
                                 ws[:, kc * 512 + m * 128: kc * 512 + (m + 1) * 128],
                                 kb.xtc[:, (idx % 4) * TH:(idx % 4 + 1) * TH],
                                 start=first, stop=last)
                assert kb.inc(i, 'pe') == xpe[idx]
        self.blk.tensor(te)

        def ve(v):
            for th in range(2):
                kb.wv(v, 'pe', xpe[th * 32 + 31])
                i = None
                for m in range(4):
                    i = v.tensor_scalar(kb.hgt(m, th), kb.pb[th * 4 + m][:, 0:TH],
                                        kb.bins[:, m:m + 1], None, ALU.add)
                kb.inc(i, 'v')
        self.blk.vector(ve)

    def cast_chunks(self, srcf, nk):
        kb = self

        def a(sc):
            kb.w(sc, 'v', 'p', 'pe')
            i = None
            for k in range(nk):
                sc.activation(kb.hb[:, k * TH:(k + 1) * TH], srcf(k), AF.Identity)
                i = sc.activation(kb.hbsq[:, k * TH:(k + 1) * TH], srcf(k), AF.Square)
            kb.inc(i, 'a')
        self.blk.scalar(a)

    def stats_rows(self, nk, inv, pb_sum, pb_sq):
        kb = self

        def te(t):
            kb.w(t, 'a', 'v')
            i = None
            for k in range(nk):
                i = t.matmul(pb_sum[0:1, 0:TH], inv,
                             kb.hb[:, k * TH:(k + 1) * TH],
                             start=(k == 0), stop=(k == nk - 1))
            for k in range(nk):
                i = t.matmul(pb_sq[0:1, 0:TH], inv,
                             kb.hbsq[:, k * TH:(k + 1) * TH],
                             start=(k == 0), stop=(k == nk - 1))
            kb.inc(i, 'pe')
        self.blk.tensor(te)

    def rows_and_bcast(self, pb_sum, pb_sq, pb_mub, pb_rstdb):
        kb = self
        mu_bf = self.rowb[0:1, 0:TH]
        rstd_bf = self.rowb[0:1, TH:2 * TH]
        musq = self.rowf[0:1, 0:TH]
        varep = self.rowf[0:1, TH:2 * TH]

        def a(sc):
            kb.w(sc, 'pe')
            sc.activation(musq, pb_sum[0:1, 0:TH], AF.Square)
            i = sc.activation(mu_bf, pb_sum[0:1, 0:TH], AF.Identity)
            kb.inc(i, 'a')
        self.blk.scalar(a)

        def v(vec):
            kb.w(vec, 'a')
            vec.scalar_tensor_tensor(varep, musq, -1.0, pb_sq[0:1, 0:TH],
                                     ALU.mult, ALU.add)
            i = vec.tensor_scalar(varep, varep, EPS, None, ALU.add)
            kb.inc(i, 'v')
        self.blk.vector(v)

        def a2(sc):
            kb.w(sc, 'v')
            sc.activation(varep, varep, AF.Ln)
            i = sc.activation(rstd_bf, varep, AF.Exp, scale=kb.cm05[0:1, 0:1])
            kb.inc(i, 'a')
        self.blk.scalar(a2)

        def te(t):
            kb.w(t, 'a', 'v', 'p')
            t.matmul(pb_mub[:, 0:TH], kb.ones1[:], mu_bf, start=True, stop=True)
            i = t.matmul(pb_rstdb[:, 0:TH], kb.ones1[:], rstd_bf, start=True, stop=True)
            kb.inc(i, 'pe')
        self.blk.tensor(te)

    def colbc(self, tns, col, row, cnt, gstep=None, gcnt=None):
        t = tns.ap().tensor if hasattr(tns, 'ap') else tns.tensor
        if gstep is None:
            return bass.AP(t, col, [[row, 128], [0, cnt], [1, 1]])
        return bass.AP(t, col, [[row, 128], [gstep, gcnt], [0, cnt], [1, 1]])

    def ph_layer(self, li):
        self.ph_w1_prefetch(li)
        self.ph_psetup_pre(li)
        for th in range(2):
            self.ln_d512_stats(li, th)
        self.ph_psetup_post(li)
        if self.dump(f'tabs{li}'):
            return True
        self.ln_d512_apply(li, 'ln1')
        if self.dump(f'y{li}'):
            return True
        self.ph_sconv(li)
        if self.done:
            return True
        if self.dump(f'hmix{li}'):
            return True
        for th in range(2):
            self.ln_d512_stats(li, th, kind='ln2')
        self.ln_d512_apply(li, 'ln2')
        self.ph_w1(li)
        self.ph_w2(li)
        if self.dump(f'hffn{li}'):
            return True
        return False

    def _muS(self, th):
        return self.mdv(th * 2 - th) if False else (
            self.mdv(0) if th == 0 else self.qsb[:, 0:TH])

    def _rstdS(self, th):
        return self.mdv(1) if th == 0 else self.qsb[:, TH:2 * TH]

    def _sqbuf(self, th):
        return (self.bigA[:, 26624:28672] if th == 0
                else self.bigA[:, 22528:24576])

    def _rowf2(self, th):
        if th == 0:
            return self.rowf
        return self.bigA.ap().bitcast(F32)[0:1, 0:1024]

    def _rowb2t(self, th):
        if th == 0:
            return self.rowb
        return self.bigA[0:1, 28672:29696]

    def ln_d512_stats(self, li, th, kind='ln1'):
        kb = self
        b0 = th * 4
        sqb = self._sqbuf(th)

        def vsq(vec):
            i = None
            for g in range(4):
                i = vec.tensor_tensor(sqb[:, g * TH:(g + 1) * TH],
                                      kb.hgt(g, th), kb.hgt(g, th), ALU.mult)
            kb.inc(i, 'v')
        self.blk.vector(vsq)
        sqv = self.n['v']

        def te(t):
            kb.w(t, 'a', 'v')
            i = None
            for g in range(4):
                i = t.matmul(kb.pb[b0][0:1, 0:TH], kb.inv512f[:], kb.hgt(g, th),
                             start=(g == 0), stop=(g == 3))
            kb.wv(t, 'v', sqv)
            for g in range(4):
                i = t.matmul(kb.pb[b0 + 1][0:1, 0:TH], kb.inv512[:],
                             sqb[:, g * TH:(g + 1) * TH],
                             start=(g == 0), stop=(g == 3))
            kb.inc(i, 'pe')
        self.blk.tensor(te)
        pe_stats = self.n['pe']
        rf = self._rowf2(th)
        rb = self._rowb2t(th)
        mu_bf = rb[0:1, 0:TH] if th == 0 else rb[0:1, 0:TH]
        rstd_bf = rb[0:1, TH:2 * TH] if th == 0 else None
        if th != 0:
            rstd_bf = self.bigA[0:1, 29184:29184 + TH]
        musq = rf[0:1, 0:TH]
        varep = rf[0:1, TH:2 * TH]

        def a(sc):
            kb.wv(sc, 'pe', pe_stats)
            sc.activation(musq, kb.pb[b0][0:1, 0:TH], AF.Square)
            i = sc.activation(mu_bf, kb.pb[b0][0:1, 0:TH], AF.Identity)
            kb.inc(i, 'a')
        self.blk.scalar(a)
        av = self.n['a']

        def v(vec):
            kb.wv(vec, 'a', av)
            vec.scalar_tensor_tensor(varep, musq, -1.0, kb.pb[b0 + 1][0:1, 0:TH],
                                     ALU.mult, ALU.add)
            i = vec.tensor_scalar(varep, varep, EPS, None, ALU.add)
            kb.inc(i, 'v')
        self.blk.vector(v)
        vv = self.n['v']

        def a2(sc):
            kb.wv(sc, 'v', vv)
            sc.activation(varep, varep, AF.Ln)
            i = sc.activation(rstd_bf, varep, AF.Exp, scale=kb.cm05[0:1, 0:1])
            kb.inc(i, 'a')
        self.blk.scalar(a2)
        av2 = self.n['a']

        def te2(t):
            kb.wv(t, 'a', av2)
            kb.w(t, 'v', 'p')
            t.matmul(kb.pb[b0 + 2][:, 0:TH], kb.ones1[:], mu_bf,
                     start=True, stop=True)
            i = t.matmul(kb.pb[b0 + 3][:, 0:TH], kb.ones1[:], rstd_bf,
                         start=True, stop=True)
            kb.inc(i, 'pe')
        self.blk.tensor(te2)
        pe_b = self.n['pe']
        if not hasattr(self, '_lnrows'):
            self._lnrows = {}
            self._lnbc = {}
        self._lnbc[th] = pe_b
        if th == 1:
            def vc(vec):
                kb.wv(vec, 'pe', pe_b)
                vec.tensor_copy(kb._muS(th), kb.pb[b0 + 2][:, 0:TH])
                i = vec.tensor_copy(kb._rstdS(th), kb.pb[b0 + 3][:, 0:TH])
                kb.inc(i, 'v')
            self.blk.vector(vc)
        self._lnrows[th] = self.n['v']

    def ln_d512_apply(self, li, kind):
        kb = self
        jg = 0 if kind == 'ln1' else 7
        jb = 1 if kind == 'ln1' else 8
        p0 = self.n['p']
        v0 = self.n['v']
        self._apply_p0 = p0
        self._apply_v0 = v0

        def po(gp):
            for g in range(4):
                kb.wv(gp, 'v', kb._lnrows[1])
                d = kb.yv(g, 1)
                gam = kb.colbc(kb.chps, (li * 9 + jg) * 4 + g, NL * 9 * 4, TH)
                gp.tensor_tensor(d, kb.hgt(g, 1), kb._muS(1), ALU.subtract)
                gp.tensor_tensor(d, d, kb._rstdS(1), ALU.mult)
                i = gp.tensor_tensor(
                    d.rearrange("p (c o) -> p c o", o=1),
                    d.rearrange("p (c o) -> p c o", o=1), gam, ALU.mult)
                kb.inc(i, 'p')
        self.blk.gpsimd(po)

        def vo(vec):
            for g in range(4):
                kb.wv(vec, 'pe', kb._lnbc[0])
                d = kb.yv(g, 0)
                vec.tensor_tensor(kb.sc1[:], kb.hgt(g, 0), kb.pb[2][:, 0:TH],
                                  ALU.subtract)
                i = vec.scalar_tensor_tensor(d, kb.sc1[:], kb.chan(li, jg, g),
                                             kb.pb[3][:, 0:TH], ALU.mult, ALU.mult)
                kb.inc(i, 'v')
        self.blk.vector(vo)

        self._silu_a0 = self.n['a']

        def a(sc):
            for g in range(4):
                for th in range(2):
                    if th == 0:
                        kb.wv(sc, 'v', v0 + g + 1)
                        src = kb.yv(g, 0)
                    else:
                        kb.wv(sc, 'p', p0 + g + 1)
                        src = kb.yv(g, 1)
                    if kind == 'ln1':
                        i = sc.activation(src, src, AF.Silu, bias=kb.chan(li, jb, g))
                    else:
                        i = sc.activation(kb.lv(g, th), src, AF.Identity,
                                          bias=kb.chan(li, jb, g))
                    kb.inc(i, 'a')
        self.blk.scalar(a)

    def ph_psetup_pre(self, li):
        kb = self
        pre, pim = self.t4a, self.t4b
        r2, tmp = self.t4c, self.t4d
        rr = self.sinre
        rsave = self.sinim

        def p1(gp):
            kb.w(gp, 'xs', 'a', 'p')
            gp.tensor_tensor(r2[:], kb.chan(li, 2), kb.chan(li, 2), ALU.mult)
            gp.tensor_tensor(tmp[:], kb.chan(li, 3), kb.chan(li, 3), ALU.mult)
            gp.tensor_tensor(r2[:], r2[:], tmp[:], ALU.add)
            i = gp.tensor_scalar(r2[:], r2[:], 1e-30, None, ALU.add)
            kb.inc(i, 'p')
        self.blk.gpsimd(p1)
        pv = self.n['p']

        def a1(sc):
            kb.wv(sc, 'p', pv)
            sc.activation(tmp[:], r2[:], AF.Ln)
            i = sc.activation(rr[:, 0:4], tmp[:], AF.Exp, scale=kb.cp05[:, 0:1])
            kb.inc(i, 'a')
        self.blk.scalar(a1)
        av = self.n['a']

        def v1(vec):
            kb.wv(vec, 'a', av)
            i = vec.reciprocal(tmp[:], rr[:, 0:4])
            kb.inc(i, 'v')
        self.blk.vector(v1)
        vv = self.n['v']

        def p2(gp):
            kb.wv(gp, 'v', vv)
            gp.tensor_tensor(rsave[:, 0:4], r2[:], tmp[:], ALU.mult)
            gp.tensor_tensor(rsave[:, 0:4], rsave[:, 0:4], rr[:, 0:4], ALU.add)
            i = gp.tensor_scalar(r2[:], rsave[:, 0:4], 0.5, None, ALU.mult)
            kb.inc(i, 'p')
        self.blk.gpsimd(p2)
        pv2 = self.n['p']

        def v2(vec):
            kb.wv(vec, 'p', pv2)
            i = vec.reciprocal(tmp[:], r2[:])
            kb.inc(i, 'v')
        self.blk.vector(v2)
        self._ps_v = self.n['v']

    def ph_psetup_post(self, li):
        kb = self
        pre, pim = self.t4a, self.t4b
        r2, tmp = self.t4c, self.t4d

        def o1(x):
            return x.rearrange("p (c o) -> p c o", o=1)

        def a2(sc):
            kb.wv(sc, 'v', kb._ps_v)
            i = sc.activation(r2[:], r2[:], AF.Tanh)
            kb.inc(i, 'a')
        self.blk.scalar(a2)
        av = self.n['a']

        def p4(gp):
            kb.wv(gp, 'a', av)
            gp.tensor_tensor(tmp[:], r2[:], tmp[:], ALU.mult)
            gp.tensor_tensor(pre[:], kb.chan(li, 2), tmp[:], ALU.mult)
            gp.tensor_tensor(pim[:], kb.chan(li, 3), tmp[:], ALU.mult)
            powre3 = kb.powre[:, :].rearrange("p (g k) -> p g k", g=4)
            powim3 = kb.powim[:, :].rearrange("p (g k) -> p g k", g=4)
            for g in range(4):
                gp.memset(powre3[:, g, 0:1], 1.0)
                gp.memset(powim3[:, g, 0:1], 0.0)
            gp.tensor_copy(powre3[:, :, 1:2], o1(pre[:]))
            gp.tensor_copy(powim3[:, :, 1:2], o1(pim[:]))
            sA = kb.prB[:, :].rearrange("p (g c) -> p g c", g=4)
            sB = kb.piB[:, :].rearrange("p (g c) -> p g c", g=4)
            for m in (1, 2, 4, 8):
                qr = kb.colbc(kb.powre, m, 4 * (C + 1), m, gstep=C + 1, gcnt=4)
                qi = kb.colbc(kb.powim, m, 4 * (C + 1), m, gstep=C + 1, gcnt=4)
                src_r = powre3[:, :, 1:m + 1].rearrange("p g (c o) -> p g c o", o=1)
                src_i = powim3[:, :, 1:m + 1].rearrange("p g (c o) -> p g c o", o=1)
                d_r = powre3[:, :, m + 1:2 * m + 1].rearrange("p g (c o) -> p g c o", o=1)
                d_i = powim3[:, :, m + 1:2 * m + 1].rearrange("p g (c o) -> p g c o", o=1)
                a_ = sA[:, :, 0:m].rearrange("p g (c o) -> p g c o", o=1)
                b_ = sB[:, :, 0:m].rearrange("p g (c o) -> p g c o", o=1)
                gp.tensor_tensor(a_, src_r, qr, ALU.mult)
                gp.tensor_tensor(b_, src_i, qi, ALU.mult)
                gp.tensor_tensor(d_r, a_, b_, ALU.subtract)
                gp.tensor_tensor(a_, src_r, qi, ALU.mult)
                gp.tensor_tensor(b_, src_i, qr, ALU.mult)
                gp.tensor_tensor(d_i, a_, b_, ALU.add)
            qre = kb.qsre[:, :].rearrange("p (g s) -> p g s", g=4)
            qim = kb.qsim[:, :].rearrange("p (g s) -> p g s", g=4)
            gp.tensor_copy(qre[:, :, 0:1], powre3[:, :, C:C + 1])
            gp.tensor_copy(qim[:, :, 0:1], powim3[:, :, C:C + 1])
            for st in range(6):
                a_ = qre[:, :, st:st + 1]
                b_ = qim[:, :, st:st + 1]
                gp.tensor_tensor(o1(kb.sinre[:]), a_, a_, ALU.mult)
                gp.tensor_tensor(o1(kb.sinim[:]), b_, b_, ALU.mult)
                gp.tensor_tensor(qre[:, :, st + 1:st + 2], o1(kb.sinre[:]), o1(kb.sinim[:]), ALU.subtract)
                gp.tensor_tensor(o1(kb.sinre[:]), a_, b_, ALU.mult)
                gp.tensor_scalar(qim[:, :, st + 1:st + 2], o1(kb.sinre[:]), 2.0, None, ALU.mult)
            qcre3 = kb.qcre[:, :].rearrange("p (g c) -> p g c", g=4)
            qcim3 = kb.qcim[:, :].rearrange("p (g c) -> p g c", g=4)
            for g in range(4):
                gp.memset(qcre3[:, g, 0:1], 1.0)
                gp.memset(qcim3[:, g, 0:1], 0.0)
            for st in range(6):
                m = 1 << st
                qr = kb.colbc(kb.qsre, st, 32, m, gstep=8, gcnt=4)
                qi = kb.colbc(kb.qsim, st, 32, m, gstep=8, gcnt=4)
                src_r = qcre3[:, :, 0:m].rearrange("p g (c o) -> p g c o", o=1)
                src_i = qcim3[:, :, 0:m].rearrange("p g (c o) -> p g c o", o=1)
                d_r = qcre3[:, :, m:2 * m].rearrange("p g (c o) -> p g c o", o=1)
                d_i = qcim3[:, :, m:2 * m].rearrange("p g (c o) -> p g c o", o=1)
                a_ = sA[:, :, 0:m].rearrange("p g (c o) -> p g c o", o=1)
                b_ = sB[:, :, 0:m].rearrange("p g (c o) -> p g c o", o=1)
                gp.tensor_tensor(a_, src_r, qr, ALU.mult)
                gp.tensor_tensor(b_, src_i, qi, ALU.mult)
                gp.tensor_tensor(d_r, a_, b_, ALU.subtract)
                gp.tensor_tensor(a_, src_r, qi, ALU.mult)
                gp.tensor_tensor(b_, src_i, qr, ALU.mult)
                gp.tensor_tensor(d_i, a_, b_, ALU.add)
            for g in range(4):
                scl_b = kb.colbc(kb.chps, (li * 9 + 6) * 4 + g, NL * 9 * 4, C)
                gp.tensor_tensor(o1(kb.psclre[:, g * C:(g + 1) * C]),
                                 o1(kb.powre[:, g * (C + 1) + 1:g * (C + 1) + C + 1]),
                                 scl_b, ALU.mult)
                gp.tensor_tensor(o1(kb.psclim[:, g * C:(g + 1) * C]),
                                 o1(kb.powim[:, g * (C + 1) + 1:g * (C + 1) + C + 1]),
                                 scl_b, ALU.mult)
                gp.tensor_scalar(kb.psclim[:, g * C:(g + 1) * C],
                                 kb.psclim[:, g * C:(g + 1) * C], -1.0, None, ALU.mult)
            pp_re = kb.ppre[:, :].rearrange("p (g e) -> p g e", g=4)
            pp_im = kb.ppim[:, :].rearrange("p (g e) -> p g e", g=4)
            Pr = qre[:, :, 6:7]
            Pi = qim[:, :, 6:7]
            for g in range(4):
                gp.memset(pp_re[:, g, 0:1], 1.0)
                gp.memset(pp_im[:, g, 0:1], 0.0)
            gp.tensor_copy(pp_re[:, :, 1:2], Pr)
            gp.tensor_copy(pp_im[:, :, 1:2], Pi)
            i = None
            for e in (2, 3):
                ar = pp_re[:, :, e - 1:e]
                ai = pp_im[:, :, e - 1:e]
                gp.tensor_tensor(o1(kb.sinre[:]), ar, Pr, ALU.mult)
                gp.tensor_tensor(o1(kb.sinim[:]), ai, Pi, ALU.mult)
                gp.tensor_tensor(pp_re[:, :, e:e + 1], o1(kb.sinre[:]), o1(kb.sinim[:]), ALU.subtract)
                gp.tensor_tensor(o1(kb.sinre[:]), ar, Pi, ALU.mult)
                gp.tensor_tensor(o1(kb.sinim[:]), ai, Pr, ALU.mult)
                i = gp.tensor_tensor(pp_im[:, :, e:e + 1], o1(kb.sinre[:]), o1(kb.sinim[:]), ALU.add)
            kb.inc(i, 'p')
        self.blk.gpsimd(p4)
        self._p4p = self.n['p']

    def ph_sconv(self, li):
        kb = self
        wre3 = self.wre[:, :].rearrange("p (g c) -> p g c", g=4)
        wim3 = self.wim[:, :].rearrange("p (g c) -> p g c", g=4)
        Sre3 = self.Sre[:, :].rearrange("p (g c) -> p g c", g=4)
        Sim3 = self.Sim[:, :].rearrange("p (g c) -> p g c", g=4)
        prow = 4 * (C + 1)

        def o1(x):
            return x.rearrange("p (c o) -> p c o", o=1)

        def v(vec):
            kb.wv(vec, 'p', kb._p4p)
            for g in range(4):
                kb.wv(vec, 'a', kb._silu_a0 + 2 * g + 2)
                yg = kb.yv(g)
                y3 = yg.rearrange("p (c i) -> p c i", c=NCH)
                rev_re = bass.AP(kb.powre.ap().tensor, g * (C + 1) + (C - 1),
                                 [[prow, 128], [0, NCH], [-1, C]])
                rev_im = bass.AP(kb.powim.ap().tensor, g * (C + 1) + (C - 1),
                                 [[prow, 128], [0, NCH], [-1, C]])
                vec.tensor_tensor(kb.wtmp3(), y3, rev_re, ALU.mult)
                vec.reduce_sum(wre3[:, g, :], kb.wtmp3(), AX.X)
                vec.tensor_tensor(kb.wtmp3(), y3, rev_im, ALU.mult)
                vec.reduce_sum(wim3[:, g, :], kb.wtmp3(), AX.X)
            N4 = 4 * NCH
            f32v = kb.bigA.ap().bitcast(F32)
            tmp1 = f32v[:, 15360:15360 + N4]
            tmp2 = f32v[:, 15616:15616 + N4]
            cur_r, cur_i, nxt_r, nxt_i = kb.prA, kb.piA, kb.prB, kb.piB

            def g3v(t, lo, hi):
                return t[:].rearrange("p (g c) -> p g c", g=4)[:, :, lo:hi] \
                    .rearrange("p g (c o) -> p g c o", o=1)
            for st in range(6):
                m = 1 << st
                cnt = NCH - m
                qr_b = kb.colbc(kb.qsre, st, 32, cnt, gstep=8, gcnt=4)
                qi_b = kb.colbc(kb.qsim, st, 32, cnt, gstep=8, gcnt=4)
                vec.tensor_copy(g3v(nxt_r, 0, m), g3v(cur_r, 0, m))
                vec.tensor_copy(g3v(nxt_i, 0, m), g3v(cur_i, 0, m))
                tm1 = tmp1.rearrange("p (g c) -> p g c", g=4)[:, :, 0:cnt] \
                    .rearrange("p g (c o) -> p g c o", o=1)
                tm2 = tmp2.rearrange("p (g c) -> p g c", g=4)[:, :, 0:cnt] \
                    .rearrange("p g (c o) -> p g c o", o=1)
                vec.tensor_tensor(tm1, g3v(cur_r, 0, cnt), qr_b, ALU.mult)
                vec.tensor_tensor(tm2, g3v(cur_i, 0, cnt), qi_b, ALU.mult)
                vec.tensor_tensor(g3v(nxt_r, m, NCH), g3v(cur_r, m, NCH), tm1, ALU.add)
                vec.tensor_tensor(g3v(nxt_r, m, NCH), g3v(nxt_r, m, NCH), tm2, ALU.subtract)
                vec.tensor_tensor(tm1, g3v(cur_i, 0, cnt), qr_b, ALU.mult)
                vec.tensor_tensor(tm2, g3v(cur_r, 0, cnt), qi_b, ALU.mult)
                vec.tensor_tensor(g3v(nxt_i, m, NCH), g3v(cur_i, m, NCH), tm1, ALU.add)
                vec.tensor_tensor(g3v(nxt_i, m, NCH), g3v(nxt_i, m, NCH), tm2, ALU.add)
                cur_r, nxt_r = nxt_r, cur_r
                cur_i, nxt_i = nxt_i, cur_i
            cr3 = cur_r[:].rearrange("p (g c) -> p g c", g=4)
            ci3 = cur_i[:].rearrange("p (g c) -> p g c", g=4)
            for g in range(4):
                vec.memset(Sre3[:, g, 0:1], 0.0)
                vec.memset(Sim3[:, g, 0:1], 0.0)
            vec.tensor_copy(Sre3[:, :, 1:NCH], cr3[:, :, 0:NCH - 1])
            vec.tensor_copy(Sim3[:, :, 1:NCH], ci3[:, :, 0:NCH - 1])
            vec.tensor_copy(
                kb.gath[:, 56:64].rearrange("p (g x) -> p g x", g=4)[:, :, 0:1],
                cr3[:, :, NCH - 1:NCH])
            i = vec.tensor_copy(
                kb.gath[:, 56:64].rearrange("p (g x) -> p g x", g=4)[:, :, 1:2],
                ci3[:, :, NCH - 1:NCH])
            kb.inc(i, 'v')
        self.blk.vector(v)

        def g1(gp):
            kb.w(gp, 'v')
            kb.inc(gp.dma_start(kb.cc_in[:, :], kb.gath[:, 56:64]), 'xg', 16)
            kb.w(gp, 'xg')
            i = gp.collective_compute(
                "AllGather", ALU.bypass, replica_groups=[list(range(8))],
                ins=[kb.cc_in.ap().opt()], outs=[kb.cc_out.ap().opt()])
            i.then_inc(kb.ccsems[li])
            gp.wait_ge(kb.ccsems[li], 1)
            src = bass.AP(kb.cc_out.ap().tensor, 0, [[8, 128], [1024, 8], [1, 8]])
            i = gp.dma_start(kb.gath[:].rearrange("p (r c) -> p r c", r=8), src)
            kb.inc(i, 'xg', 16)
        self.blk.gpsimd(g1)

        def vsh(vec):
            for g in range(4):
                kb.wv(vec, 'p', kb._apply_p0 + g + 1)
                yg = kb.yv(g)
                y3 = yg.rearrange("p (c i) -> p c i", c=NCH)
                h3 = kb.hg(g).rearrange("p (c i) -> p c i", c=NCH)
                vec.scalar_tensor_tensor(kb.hg(g), yg, kb.chan(li, 6, g),
                                         kb.hg(g), ALU.mult, ALU.add)
                for k in range(1, C):
                    i = vec.scalar_tensor_tensor(
                        h3[:, :, k:C], y3[:, :, 0:C - k],
                        kb.psclre[:, g * C + k - 1:g * C + k],
                        h3[:, :, k:C], ALU.mult, ALU.add)
                kb.inc(i, 'v')
        self.blk.vector(vsh)

        def pc(gp):
            kb.w(gp, 'xg')
            wjr = kb.t4a
            WJ = kb.prB[:, :].rearrange("p (g c) -> p g c", g=4)
            WJr = WJ[:, :, 0:9]
            WJi = WJ[:, :, 9:18]
            PR = kb.prA[:, :].rearrange("p (g c) -> p g c", g=4)
            prod = PR[:, :, 0:9]
            gp.memset(WJr, 0.0)
            gp.memset(WJi, 0.0)
            for e in range(4):
                cmsb = bass.AP(kb.cms.ap().tensor, e,
                               [[32, 128], [0, 4], [4, 8], [1, 1]])
                ppb_r = bass.AP(kb.ppre.ap().tensor, e,
                                [[16, 128], [4, 4], [0, 8], [1, 1]])
                ppb_i = bass.AP(kb.ppim.ap().tensor, e,
                                [[16, 128], [4, 4], [0, 8], [1, 1]])
                o4 = prod[:, :, 0:8].rearrange("p g (c o) -> p g c o", o=1)
                gp.tensor_tensor(o4, cmsb, ppb_r, ALU.mult)
                gp.tensor_tensor(WJr[:, :, 0:8], WJr[:, :, 0:8], prod[:, :, 0:8], ALU.add)
                gp.tensor_tensor(o4, cmsb, ppb_i, ALU.mult)
                gp.tensor_tensor(WJi[:, :, 0:8], WJi[:, :, 0:8], prod[:, :, 0:8], ALU.add)
                lcb = bass.AP(kb.lcsel.ap().tensor, e,
                              [[4, 128], [0, 4], [0, 1], [1, 1]])
                o41 = prod[:, :, 8:9].rearrange("p g (c o) -> p g c o", o=1)
                gp.tensor_tensor(o41, lcb, ppb_r[:, :, 0:1], ALU.mult)
                gp.tensor_tensor(WJr[:, :, 8:9], WJr[:, :, 8:9], prod[:, :, 8:9], ALU.add)
                gp.tensor_tensor(o41, lcb, ppb_i[:, :, 0:1], ALU.mult)
                gp.tensor_tensor(WJi[:, :, 8:9], WJi[:, :, 8:9], prod[:, :, 8:9], ALU.add)
            SJr = PR[:, :, 16:25]
            SJi = PR[:, :, 25:34]
            gathr = bass.AP(kb.gath.ap().tensor, 0, [[64, 128], [2, 4], [8, 8], [1, 1]])
            gathi = bass.AP(kb.gath.ap().tensor, 1, [[64, 128], [2, 4], [8, 8], [1, 1]])
            gp.tensor_copy(SJr[:, :, 0:8].rearrange("p g (c o) -> p g c o", o=1), gathr)
            gp.tensor_copy(SJi[:, :, 0:8].rearrange("p g (c o) -> p g c o", o=1), gathi)
            gp.tensor_copy(SJr[:, :, 8:9], kb.chan(li, 4).rearrange("p (g o) -> p g o", o=1))
            gp.tensor_copy(SJi[:, :, 8:9], kb.chan(li, 5).rearrange("p (g o) -> p g o", o=1))
            P1 = PR[:, :, 34:43]
            P2 = WJ[:, :, 18:27]
            P3 = PR[:, :, 43:52]
            P4 = WJ[:, :, 27:36]
            gp.tensor_tensor(P1, WJr, SJr, ALU.mult)
            gp.tensor_tensor(P2, WJi, SJi, ALU.mult)
            gp.tensor_tensor(P3, WJr, SJi, ALU.mult)
            i = gp.tensor_tensor(P4, WJi, SJr, ALU.mult)
            kb.inc(i, 'p')
        self.blk.gpsimd(pc)
        pcp = self.n['p']

        def v2(vec):
            kb.wv(vec, 'p', pcp)
            PR = kb.prA[:, :].rearrange("p (g c) -> p g c", g=4)
            WJ = kb.prB[:, :].rearrange("p (g c) -> p g c", g=4)
            vec.reduce_sum(o1(kb.sinre[:]), PR[:, :, 34:43], AX.X)
            vec.reduce_sum(o1(kb.t4c[:]), WJ[:, :, 18:27], AX.X)
            vec.reduce_sum(o1(kb.sinim[:]), PR[:, :, 43:52], AX.X)
            i = vec.reduce_sum(o1(kb.t4d[:]), WJ[:, :, 27:36], AX.X)
            fv = kb.inc(i, 'v')
            vec.wait_ge(kb.sems['v'], fv)
            vec.tensor_tensor(kb.sinre[:], kb.sinre[:], kb.t4c[:], ALU.subtract)
            i = vec.tensor_tensor(kb.sinim[:], kb.sinim[:], kb.t4d[:], ALU.add)
            fv = kb.inc(i, 'v')
            vec.wait_ge(kb.sems['v'], fv)
            qcre3 = kb.qcre[:, :].rearrange("p (g c) -> p g c", g=4)
            qcim3 = kb.qcim[:, :].rearrange("p (g c) -> p g c", g=4)
            i = None
            for g in range(4):
                sr = kb.sinre[:, g:g + 1]
                si = kb.sinim[:, g:g + 1]
                vec.scalar_tensor_tensor(Sre3[:, g, :], qcre3[:, g, :], sr,
                                         Sre3[:, g, :], ALU.mult, ALU.add)
                vec.tensor_scalar(kb.prA[:, 0:NCH], qcim3[:, g, :], si, None, ALU.mult)
                vec.tensor_tensor(Sre3[:, g, :], Sre3[:, g, :], kb.prA[:, 0:NCH], ALU.subtract)
                vec.scalar_tensor_tensor(Sim3[:, g, :], qcim3[:, g, :], sr,
                                         Sim3[:, g, :], ALU.mult, ALU.add)
                vec.tensor_scalar(kb.prA[:, 0:NCH], qcre3[:, g, :], si, None, ALU.mult)
                vec.tensor_tensor(Sim3[:, g, :], Sim3[:, g, :], kb.prA[:, 0:NCH], ALU.add)
                h3 = kb.hg(g).rearrange("p (c i) -> p c i", c=NCH)
                for ii in range(C):
                    vec.scalar_tensor_tensor(
                        h3[:, :, ii:ii + 1],
                        Sre3[:, g, :].rearrange("p (c o) -> p c o", o=1),
                        kb.psclre[:, g * C + ii:g * C + ii + 1],
                        h3[:, :, ii:ii + 1], ALU.mult, ALU.add)
                    i = vec.scalar_tensor_tensor(
                        h3[:, :, ii:ii + 1],
                        Sim3[:, g, :].rearrange("p (c o) -> p c o", o=1),
                        kb.psclim[:, g * C + ii:g * C + ii + 1],
                        h3[:, :, ii:ii + 1], ALU.mult, ALU.add)
            kb.inc(i, 'v')
        self.blk.vector(v2)

    def ph_w1_prefetch(self, li):
        kb = self
        w1s = self.streams['w1']
        w1i = 2 * li

        def g(gp):
            kb.w(gp, 'pe')
            i = gp.dma_start(kb.wsA[:], kb.w1b[li, 0])
            w1s.issue(i, w1i)
            i = gp.dma_start(kb.wsB[:], kb.w1b[li, 1])
            w1s.issue(i, w1i + 1)
        self.blk.gpsimd(g)

    def ph_w1(self, li):
        kb = self
        w1s = self.streams['w1']
        w1i = 2 * li
        a0 = self.n['a']
        pe0 = self.n['pe']
        self._w1_pe0 = pe0

        def te(t):
            kb.w(t, 'a')
            cnt = 0
            for hf in range(2):
                w1s.wait(t, w1i + hf)
                ws = kb.wsA if hf == 0 else kb.wsB
                for mm in range(8):
                    for th in range(2):
                        pbk = kb.pb[4 + (cnt % 2)]
                        if cnt >= 2:
                            kb.wv(t, 'a', a0 + cnt - 1)
                        i = None
                        for kc in range(4):
                            i = t.matmul(
                                pbk[:, 0:TH],
                                ws[:, kc * 1024 + mm * 128: kc * 1024 + (mm + 1) * 128],
                                kb.lv(kc, th), start=(kc == 0), stop=(kc == 3))
                        kb.inc(i, 'pe')
                        cnt += 1
        self.blk.tensor(te)

        def a(sc):
            cnt = 0
            for hf in range(2):
                for mm in range(8):
                    m = hf * 8 + mm
                    for th in range(2):
                        pbk = kb.pb[4 + (cnt % 2)]
                        kb.wv(sc, 'pe', pe0 + cnt + 1)
                        dst = kb.bigB[:, m * T + th * TH: m * T + (th + 1) * TH]
                        i = sc.activation(dst, pbk[:, 0:TH], AF.Silu,
                                          bias=kb.b1s[:, li * 16 + m:li * 16 + m + 1])
                        got = kb.inc(i, 'a')
                        assert got == a0 + cnt + 1
                        cnt += 1
        self.blk.scalar(a)

    def ph_w2(self, li):
        kb = self
        w2s = self.streams['w2']
        w2i = 2 * li
        w1_done = self.n['pe']

        def g(gp):
            kb.wv(gp, 'pe', kb._w1_pe0 + 16)
            i = gp.dma_start(kb.wsA[:], kb.w2b[li, 0])
            w2s.issue(i, w2i)
            kb.wv(gp, 'pe', w1_done)
            i = gp.dma_start(kb.wsB[:], kb.w2b[li, 1])
            w2s.issue(i, w2i + 1)
        self.blk.gpsimd(g)

        v0 = self.n['v']
        pe0 = self.n['pe']

        def te(t):
            kb.w(t, 'a')
            w2s.wait(t, w2i)
            cnt = 0
            for m in range(4):
                for th in range(2):
                    pbk = kb.pb[6 + (cnt % 2)]
                    if cnt >= 2:
                        kb.wv(t, 'v', v0 + cnt - 1)
                    i = None
                    for kc in range(16):
                        if kc == 8 and cnt == 0:
                            w2s.wait(t, w2i + 1)
                        hf = kc // 8
                        kch = kc % 8
                        ws = kb.wsA if hf == 0 else kb.wsB
                        i = t.matmul(
                            pbk[:, 0:TH],
                            ws[:, kch * 512 + m * 128: kch * 512 + (m + 1) * 128],
                            kb.bigB[:, kc * T + th * TH: kc * T + (th + 1) * TH],
                            start=(kc == 0), stop=(kc == 15))
                    kb.inc(i, 'pe')
                    cnt += 1
        self.blk.tensor(te)

        def v(vec):
            cnt = 0
            for m in range(4):
                for th in range(2):
                    pbk = kb.pb[6 + (cnt % 2)]
                    kb.wv(vec, 'pe', pe0 + cnt + 1)
                    i = vec.scalar_tensor_tensor(kb.hgt(m, th), pbk[:, 0:TH],
                                                 kb.b2s[:, li * 4 + m:li * 4 + m + 1],
                                                 kb.hgt(m, th), ALU.add, ALU.add)
                    got = kb.inc(i, 'v')
                    assert got == v0 + cnt + 1
                    cnt += 1
        self.blk.vector(v)

    def ph_lnl_wout(self):
        kb = self
        for th in range(2):
            self.cast_chunks(lambda k: kb.hgt(k, th), 4)
            self.stats_rows(4, kb.inv512[:], self.pb[0], self.pb[1])
            self.rows_and_bcast(self.pb[0], self.pb[1], self.pb[2], self.pb[3])
            v0 = self.n['v']

            def v(vec, th=th):
                kb.w(vec, 'pe')
                for g in range(4):
                    vec.tensor_tensor(kb.sc1[:], kb.hgt(g, th), kb.pb[2][:, 0:TH],
                                      ALU.subtract)
                    i = vec.scalar_tensor_tensor(kb.yv(g, th), kb.sc1[:],
                                                 kb.lnls[:, g:g + 1],
                                                 kb.pb[3][:, 0:TH], ALU.mult, ALU.mult)
                    kb.inc(i, 'v')
            self.blk.vector(v)

            def a(sc, th=th, v0=v0):
                for g in range(4):
                    kb.wv(sc, 'v', v0 + g + 1)
                    i = sc.activation(kb.lv(g, th), kb.yv(g, th), AF.Identity,
                                      bias=kb.lnls[:, 4 + g:5 + g])
                    kb.inc(i, 'a')
            self.blk.scalar(a)

        if self.dump('hL'):
            return
        wow, wox = self.streams['wow'], self.streams['wox']
        base_pe = self.n['pe']
        def grp_pe(cnt):
            return base_pe + (cnt + 1) + max(0, cnt - 2)
        def stats_pe(cnt):
            if cnt <= 61:
                return grp_pe(cnt + 2) + 1
            return base_pe + 126 + (cnt - 61)
        base_v = self.n['v']
        base_a = self.n['a']

        def g(gp):
            kb.w(gp, 'a')

            def wout_load(q):
                if q >= 2:
                    kb.wv(gp, 'pe', grp_pe((q - 1) * 16 - 1))
                i = gp.dma_start((kb.wsA if q % 2 == 0 else kb.wsB)[:], kb.woutb[q])
                wow.issue(i, q)
            wout_load(0)
            wout_load(1)
            for m in range(32):
                if m >= 2:
                    kb.wv(gp, 'v', base_v + (m - 2) * 2 + 2)
                src = bass.AP(kb.xT.tensor, m * 128 * T, [[T, 128], [1, T]])
                i = gp.dma_start(kb.xt(m % 2), src)
                wox.issue(i, m)
                if m == 7:
                    wout_load(2)
                elif m == 15:
                    wout_load(3)
        self.blk.gpsimd(g)

        def te(t):
            kb.w(t, 'v', 'a')
            cnt = 0
            for q in range(4):
                wow.wait(t, q)
                ws = kb.wsA if q % 2 == 0 else kb.wsB
                for mq in range(8):
                    m = q * 8 + mq
                    wox.wait(t, m)
                    for th in range(2):
                        pbk = kb.pb[cnt % 2]
                        if cnt >= 2:
                            kb.wv(t, 'v', base_v + cnt - 1)
                        i = None
                        for kc in range(4):
                            i = t.matmul(
                                pbk[:, 0:TH],
                                ws[:, kc * 1024 + mq * 128: kc * 1024 + (mq + 1) * 128],
                                kb.lv(kc, th), start=(kc == 0), stop=(kc == 3))
                        got = kb.inc(i, 'pe')
                        assert got == grp_pe(cnt)
                        if cnt >= 2:
                            kb._wo_stats_emit(t, cnt - 2, base_a, stats_pe)
                        cnt += 1
            for j in (62, 63):
                kb._wo_stats_emit(t, j, base_a, stats_pe)
        self.blk.tensor(te)

        def v(vec):
            cnt = 0
            for m in range(32):
                for th in range(2):
                    pbk = kb.pb[cnt % 2]
                    kb.wv(vec, 'pe', grp_pe(cnt))
                    wox.wait(vec, m)
                    dst = kb.bigA[:, m * T + th * TH: m * T + (th + 1) * TH]
                    i = vec.scalar_tensor_tensor(
                        dst, pbk[:, 0:TH], kb.bouts[:, m:m + 1],
                        kb.xt(m % 2)[:, th * TH:(th + 1) * TH], ALU.add, ALU.add)
                    got = kb.inc(i, 'v')
                    assert got == base_v + cnt + 1
                    cnt += 1
        self.blk.vector(v)

        def a(sc):
            cnt = 0
            for m in range(32):
                for th in range(2):
                    kb.wv(sc, 'v', base_v + cnt + 1)
                    if cnt >= 2:
                        kb.wv(sc, 'pe', stats_pe(cnt - 2))
                    src = kb.bigA[:, m * T + th * TH: m * T + (th + 1) * TH]
                    i = sc.activation(kb.wsq[:, th * TH:(th + 1) * TH], src, AF.Square)
                    got = kb.inc(i, 'a')
                    assert got == base_a + cnt + 1
                    cnt += 1
        self.blk.scalar(a)

    def _wo_stats_emit(self, t, cnt, base_a, stats_pe):
        kb = self
        m, th = cnt // 2, cnt % 2
        kb.wv(t, 'a', base_a + cnt + 1)
        src = kb.bigA[:, m * T + th * TH: m * T + (th + 1) * TH]
        t.matmul(kb.pb[2 + th][0:1, 0:TH], kb.inv4096[:], src,
                 start=(m == 0), stop=(m == 31))
        i = t.matmul(kb.pb[4 + th][0:1, 0:TH], kb.inv4096[:],
                     kb.wsq[:, th * TH:(th + 1) * TH],
                     start=(m == 0), stop=(m == 31))
        got = kb.inc(i, 'pe')
        assert got == stats_pe(cnt)

    def postln_apply(self, big, to_out):
        kb = self
        for th in range(2):
            self.rows_and_bcast(self.pb[2 + th], self.pb[4 + th],
                                self.pb[6], self.pb[7])

            def ve(vec, th=th):
                kb.w(vec, 'pe', 'p')
                vec.tensor_copy(kb.mub[:], kb.pb[6][:, 0:TH])
                i = vec.tensor_copy(kb.rstdb[:], kb.pb[7][:, 0:TH])
                kb.inc(i, 'v')
            self.blk.vector(ve)

            if not to_out:
                p0 = self.n['p']
                v0 = self.n['v']
                if not hasattr(self, '_pl1'):
                    self._pl1 = {}
                self._pl1[th] = (v0, p0)

                def po(gp, th=th, v0=v0):
                    kb.wv(gp, 'v', v0)
                    for m in range(16, 32):
                        src = big[:, m * T + th * TH: m * T + (th + 1) * TH]
                        gp.tensor_tensor(src, src, kb.mub[:], ALU.subtract)
                        i = gp.tensor_tensor(src, src, kb.rstdb[:], ALU.mult)
                        kb.inc(i, 'p')
                self.blk.gpsimd(po)

                def vo(vec, th=th):
                    for m in range(16):
                        src = big[:, m * T + th * TH: m * T + (th + 1) * TH]
                        vec.tensor_tensor(src, src, kb.mub[:], ALU.subtract)
                        i = vec.tensor_tensor(src, src, kb.rstdb[:], ALU.mult)
                        kb.inc(i, 'v')
                self.blk.vector(vo)
            else:
                outs = self.streams['out']
                outs2 = self.streams['out2']
                p0 = self.n['p']
                v0 = self.n['v']

                def po(gp, th=th, p0=p0, v0=v0):
                    kb.wv(gp, 'v', v0)
                    for j in range(16):
                        jg = th * 16 + j
                        m = 16 + j
                        if jg >= 2:
                            outs2.wait(gp, jg - 2)
                        src = big[:, m * T + th * TH: m * T + (th + 1) * TH]
                        d = kb._oslot(1, jg % 2)
                        gp.tensor_tensor(kb.mdv(1), src, kb.mub[:], ALU.subtract)
                        i = gp.tensor_tensor(d, kb.mdv(1), kb.rstdb[:], ALU.mult)
                        kb.inc(i, 'p')
                self.blk.gpsimd(po)

                def vo(vec, th=th, v0=v0):
                    for j in range(16):
                        jg = th * 16 + j
                        m = j
                        if jg >= 2:
                            outs.wait(vec, jg - 2)
                        src = big[:, m * T + th * TH: m * T + (th + 1) * TH]
                        d = kb._oslot(0, jg % 2)
                        vec.tensor_tensor(kb.mdv(0), src, kb.mub[:], ALU.subtract)
                        i = vec.tensor_tensor(d, kb.mdv(0), kb.rstdb[:], ALU.mult)
                        kb.inc(i, 'v')
                self.blk.vector(vo)

                def so(sync, th=th, p0=p0, v0=v0):
                    for j in range(16):
                        jg = th * 16 + j
                        mv, mp = j, 16 + j
                        kb.wv(sync, 'v', v0 + j + 1)
                        dst = bass.AP(kb.outT.tensor, mv * 128 * T + th * TH,
                                      [[T, 128], [1, TH]])
                        i = sync.dma_start(dst, kb._oslot(0, jg % 2))
                        outs.issue(i, jg)
                        kb.wv(sync, 'p', p0 + j + 1)
                        dst = bass.AP(kb.outT.tensor, mp * 128 * T + th * TH,
                                      [[T, 128], [1, TH]])
                        i = sync.dma_start(dst, kb._oslot(1, jg % 2))
                        outs2.issue(i, jg)
                self.blk.sync(so)

    def ph_postln1(self):
        kb = self
        self._wout_pe_end = self.n['pe']
        des = self.streams['de']

        def g(gp):
            kb.wv(gp, 'pe', kb._wout_pe_end)
            kb.inc(gp.dma_start(kb.cents[:], kb.centb), 'xg', 16)
            asg_i32 = kb.bigB.ap().bitcast(I32)[0:1, 0:V]
            kb.inc(gp.dma_start(asg_i32, kb.assign), 'xg', 16)
            for m0 in range(2):
                i = gp.dma_start((kb.wsA if m0 % 2 == 0 else kb.wsB)[:], kb.fwb[m0])
                des.issue(i, m0)
        self.blk.gpsimd(g)
        self.postln_apply(self.bigA, False)

    def _oslot(self, grp, par):
        outv = self.bigA.ap().bitcast(F32)
        slot = grp * 2 + par
        return outv[:, slot * TH:(slot + 1) * TH]

    def ph_masks(self):
        kb = self

        def v0(vec):
            kb.w(vec, 'xg')
            asg_i32 = kb.bigB.ap().bitcast(I32)[0:1, 0:V]
            asg_f32 = kb.bigB.ap().bitcast(F32)[0:1, 4096:4096 + V]
            asg_bf = kb.bigB[0:1, 16384:16384 + V]
            vec.tensor_copy(asg_f32, asg_i32)
            kb.inc(vec.tensor_copy(asg_bf, asg_f32), 'v')
        self.blk.vector(v0)

        for th in range(2):
            def te(t, th=th):
                kb.w(t, 'xg')
                v0a, p0a = kb._pl1[th]
                i = None
                for kc in range(32):
                    if kc < 16:
                        kb.wv(t, 'v', v0a + kc + 1)
                    else:
                        kb.wv(t, 'p', p0a + (kc - 16) + 1)
                    i = t.matmul(kb.pb[th][0:64, 0:TH],
                                 kb.cents[:, kc * 64:(kc + 1) * 64],
                                 kb.bigA[:, kc * T + th * TH: kc * T + (th + 1) * TH],
                                 start=(kc == 0), stop=(kc == 31))
                kb.inc(i, 'pe')
            self.blk.tensor(te)

            def a(sc, th=th):
                kb.w(sc, 'pe')
                ebf = kb.bigB[0:64, 20480 + th * TH:20480 + (th + 1) * TH]
                kb.inc(sc.activation(ebf, kb.pb[th][0:64, 0:TH], AF.Exp), 'a')
            self.blk.scalar(a)
            if not hasattr(self, '_mask_a_th'):
                self._mask_a_th = {}
            self._mask_a_th[th] = self.n['a']

            def te2(t, th=th):
                kb.w(t, 'a', 'v')
                ebf = kb.bigB[0:64, 20480 + th * TH:20480 + (th + 1) * TH]
                i = t.matmul(kb.pb[2][0:1, 0:TH], kb.ones64[:], ebf,
                             start=True, stop=True)
                kb.inc(i, 'pe')
            self.blk.tensor(te2)

            def v1(vec, th=th):
                kb.w(vec, 'pe')
                i = vec.tensor_scalar(kb.rowb[0:1, 0:TH], kb.pb[2][0:1, 0:TH],
                                      0.01, None, ALU.mult)
                kb.inc(i, 'v')
            self.blk.vector(v1)

            def te3(t, th=th):
                kb.w(t, 'v')
                i = t.matmul(kb.pb[3][0:64, 0:TH], kb.ones1[0:1, 0:64],
                             kb.rowb[0:1, 0:TH], start=True, stop=True)
                kb.inc(i, 'pe')
            self.blk.tensor(te3)

            def v2(vec, th=th):
                kb.w(vec, 'pe')
                ebf = kb.bigB[0:64, 20480 + th * TH:20480 + (th + 1) * TH]
                hits = kb.bigB[0:64, 22528 + th * TH:22528 + (th + 1) * TH]
                vec.tensor_tensor(hits, ebf, kb.pb[3][0:64, 0:TH], ALU.is_gt)
                i = vec.reduce_sum(kb.hcnt[:, th:th + 1], hits, AX.X)
                kb.inc(i, 'v')
            self.blk.vector(v2)

            def te4(t, th=th):
                kb.w(t, 'v')
                hits = kb.bigB[0:64, 22528 + th * TH:22528 + (th + 1) * TH]
                i = t.matmul(kb.pb[4][0:1, 0:TH], kb.ones64[:], hits,
                             start=True, stop=True)
                kb.inc(i, 'pe')
            self.blk.tensor(te4)

            def v3(vec, th=th):
                kb.w(vec, 'pe')
                i = vec.tensor_scalar(kb.rowb2[0:1, th * TH:(th + 1) * TH],
                                      kb.pb[4][0:1, 0:TH], 0.5, None, ALU.is_gt)
                kb.inc(i, 'v')
            self.blk.vector(v3)

        if self.dump('m1'):
            return
        def v4(vec):
            kb.w(vec, 'v')
            i = vec.tensor_tensor(kb.hcnt[:, 0:1], kb.hcnt[:, 0:1],
                                  kb.hcnt[:, 1:2], ALU.add)
            kb.inc(i, 'v')
        self.blk.vector(v4)

        def g2(gp):
            kb.w(gp, 'v')
            kb.inc(gp.dma_start(kb.cl_in[:, :], kb.hcnt[:, 0:1]), 'xg', 16)
            kb.w(gp, 'xg')
            i = gp.collective_compute(
                "AllReduce", ALU.add, replica_groups=[list(range(8))],
                ins=[kb.cl_in.ap().opt()], outs=[kb.cl_out.ap().opt()])
            i.then_inc(kb.ccsems[4])
            gp.wait_ge(kb.ccsems[4], 1)
            kb.inc(gp.dma_start(kb.hcnt[0:64, 10:11], kb.cl_out.ap()[0:64, :]),
                   'xg', 16)
        self.blk.gpsimd(g2)

        def v5(vec):
            kb.w(vec, 'xg')
            i = vec.tensor_scalar(kb.clsel[:, 0:1], kb.hcnt[:, 10:11], 0.5,
                                  None, ALU.is_gt)
            kb.inc(i, 'v')
        self.blk.vector(v5)

        if self.dump('m2'):
            return
        pe0 = self.n['pe']
        kb._oh_v0 = self.n['v']

        def te5(t):
            kb.w(t, 'v')
            for oc in range(8):
                asg_bf = kb.bigB[0:1, 16384 + oc * TH:16384 + (oc + 1) * TH]
                if oc >= 1:
                    kb.wv(t, 'v', kb._oh_v0 + oc)
                i = t.matmul(kb.pb[5][0:64, 0:TH], kb.ones1[0:1, 0:64], asg_bf,
                             start=True, stop=True)
                kb.inc(i, 'pe')
        self.blk.tensor(te5)

        def v5b(vec):
            for oc in range(8):
                kb.wv(vec, 'pe', pe0 + oc + 1)
                oh = kb.bigB[0:64, 24576 + oc * TH:24576 + (oc + 1) * TH]
                i = vec.tensor_scalar(oh, kb.pb[5][0:64, 0:TH], kb.io64[:, 0:1],
                                      None, ALU.is_equal)
                got = kb.inc(i, 'v')
                assert got == kb._oh_v0 + oc + 1
        self.blk.vector(v5b)

        if self.dump('m3'):
            return
        if self.dump('m3b'):
            return

        def te6(t):
            kb.w(t, 'v')
            oh = kb.bigB[0:64, 24576:24576 + V]
            i = None
            for m in range(32):
                i = t.matmul(kb.pb[5][:, m:m + 1], oh[:, m * 128:(m + 1) * 128],
                             kb.clsel[:, 0:1], start=True, stop=True)
            kb.inc(i, 'pe')
        self.blk.tensor(te6)

        def v6(vec):
            kb.w(vec, 'pe')
            kb.inc(vec.tensor_copy(kb.rsall[:], kb.pb[5][:, 0:32]), 'v')
        self.blk.vector(v6)
        if self.dump('m4'):
            return

        def te7(t):
            kb.w(t, 'a', 'v')
            t.matmul(kb.pb[2][:, 0:TH], kb.ones1[:], kb.rowb2[0:1, 0:TH],
                     start=True, stop=True)
            i = t.matmul(kb.pb[3][:, 0:TH], kb.ones1[:], kb.rowb2[0:1, TH:2 * TH],
                         start=True, stop=True)
            kb.inc(i, 'pe')
        self.blk.tensor(te7)

        def v7(vec):
            kb.w(vec, 'pe')
            vec.tensor_copy(kb.qsb[:, 0:TH], kb.pb[2][:, 0:TH])
            kb.inc(vec.tensor_copy(kb.qsb[:, TH:T], kb.pb[3][:, 0:TH]), 'v')
        self.blk.vector(v7)

    def ph_dense(self):
        kb = self
        des = self.streams['de']
        base_pe = self.n['pe']

        def grp_pe(cnt):
            return base_pe + (cnt + 1) + max(0, cnt - 2)

        def stats_pe(cnt):
            if cnt <= 61:
                return grp_pe(cnt + 2) + 1
            return base_pe + 126 + (cnt - 61)
        base_v = self.n['v']
        base_a = self.n['a']
        base_p = self.n['p']

        BNK = (0, 1, 6, 7)

        def te(t):
            cnt = 0
            for m in range(32):
                des.wait(t, m)
                ws = kb.wsA if m % 2 == 0 else kb.wsB
                for th in range(2):
                    pbk = kb.pb[BNK[cnt % 4]]
                    if cnt < 2:
                        kb.wv(t, 'v', kb._pl1[th][0] + 16)
                        kb.wv(t, 'p', kb._pl1[th][1] + 16)
                        kb.wv(t, 'a', kb._mask_a_th[th])
                    if cnt >= 4:
                        kb.wv(t, 'v', base_v + cnt - 3)
                    i = None
                    for kc in range(32):
                        i = t.matmul(
                            pbk[:, 0:TH], ws[:, kc * 128:(kc + 1) * 128],
                            kb.bigA[:, kc * T + th * TH: kc * T + (th + 1) * TH],
                            start=(kc == 0), stop=(kc == 31))
                    got = kb.inc(i, 'pe')
                    assert got == grp_pe(cnt)
                    if cnt >= 2:
                        kb._de_stats_emit(t, cnt - 2, base_a, stats_pe)
                    cnt += 1
            for j in (62, 63):
                kb._de_stats_emit(t, j, base_a, stats_pe)
        self.blk.tensor(te)

        def v(vec):
            cnt = 0
            for m in range(32):
                for th in range(2):
                    pbk = kb.pb[BNK[cnt % 4]]
                    kb.wv(vec, 'pe', grp_pe(cnt))
                    if cnt >= 2:
                        kb.wv(vec, 'p', base_p + cnt - 1)
                    i = vec.tensor_scalar(kb.mdv(cnt % 2), pbk[:, 0:TH],
                                          kb.fbs[:, m:m + 1], kb.rsall[:, m:m + 1],
                                          ALU.add, ALU.mult)
                    got = kb.inc(i, 'v')
                    assert got == base_v + cnt + 1
                    cnt += 1
        self.blk.vector(v)

        def p(gp):
            kb.wv(gp, 'v', base_v)
            kb.wv(gp, 'p', base_p)
            kb.wv(gp, 'pe', base_pe)
            cnt = 0
            for m in range(32):
                for th in range(2):
                    kb.wv(gp, 'v', base_v + cnt + 1)
                    dst = kb.bigB[:, m * T + th * TH: m * T + (th + 1) * TH]
                    gp.tensor_tensor(kb.mdv(cnt % 2), kb.mdv(cnt % 2),
                                     kb.qsb[:, th * TH:(th + 1) * TH], ALU.mult)
                    i = gp.tensor_tensor(
                        dst, kb.mdv(cnt % 2),
                        kb.bigA[:, m * T + th * TH: m * T + (th + 1) * TH], ALU.add)
                    got = kb.inc(i, 'p')
                    assert got == base_p + cnt + 1
                    if th == 1 and m + 2 < 32:
                        i = gp.dma_start((kb.wsA if m % 2 == 0 else kb.wsB)[:],
                                         kb.fwb[m + 2])
                        des.issue(i, m + 2)
                    cnt += 1
        self.blk.gpsimd(p)

        def a(sc):
            cnt = 0
            for m in range(32):
                for th in range(2):
                    kb.wv(sc, 'p', base_p + cnt + 1)
                    if cnt >= 2:
                        kb.wv(sc, 'pe', stats_pe(cnt - 2))
                    src = kb.bigB[:, m * T + th * TH: m * T + (th + 1) * TH]
                    i = sc.activation(kb.wsq[:, th * TH:(th + 1) * TH], src, AF.Square)
                    got = kb.inc(i, 'a')
                    assert got == base_a + cnt + 1
                    cnt += 1
        self.blk.scalar(a)

    def _de_stats_emit(self, t, cnt, base_a, stats_pe):
        kb = self
        m, th = cnt // 2, cnt % 2
        kb.wv(t, 'a', base_a + cnt + 1)
        src = kb.bigB[:, m * T + th * TH: m * T + (th + 1) * TH]
        t.matmul(kb.pb[2 + th][0:1, 0:TH], kb.inv4096[:], src,
                 start=(m == 0), stop=(m == 31))
        i = t.matmul(kb.pb[4 + th][0:1, 0:TH], kb.inv4096[:],
                     kb.wsq[:, th * TH:(th + 1) * TH],
                     start=(m == 0), stop=(m == 31))
        got = kb.inc(i, 'pe')
        assert got == stats_pe(cnt)

    def ph_postln2_out(self):
        self.postln_apply(self.bigB, True)
        if self.done:
            return

    def ph_finish(self):
        kb = self

        def s(sync):
            for nm in ('out', 'out2'):
                st = kb.streams[nm]
                for sl in range(st.nbuf):
                    sync.wait_ge(st.sems[sl], st.cnt[sl])
            sync.wait_ge(kb.sems['o'], kb.n['o'])
        self.blk.sync(s)


def _prep_shared(inputs):
    g = {}
    Win = inputs['L0_attn_Win']
    g['winb'] = np.ascontiguousarray(
        Win.T.reshape(4, 8, 128, 512).transpose(0, 2, 1, 3)).reshape(4, 128, 8 * 512)
    W1 = inputs['L0_attn_W1']
    g['w1b'] = np.ascontiguousarray(
        np.stack([W1[i].T.reshape(4, 128, 2, 8, 128).transpose(2, 1, 0, 3, 4)
                  for i in range(NL)])).reshape(NL, 2, 128, 4 * 1024)
    W2 = inputs['L0_attn_W2']
    g['w2b'] = np.ascontiguousarray(
        np.stack([W2[i].T.reshape(2, 8, 128, 512).transpose(0, 2, 1, 3)
                  for i in range(NL)])).reshape(NL, 2, 128, 8 * 512)
    Wout = inputs['L0_attn_Wout']
    g['woutb'] = np.ascontiguousarray(
        Wout.T.reshape(4, 128, 4, 8, 128).transpose(2, 1, 0, 3, 4)).reshape(4, 128, 4096)
    Wf = inputs['L0_ffn_weight']
    g['fwb'] = np.ascontiguousarray(
        Wf.T.reshape(32, 128, 32, 128).transpose(2, 1, 0, 3)).reshape(32, 128, 32 * 128)
    cent = inputs['L0_ffn_centroids']
    g['centb'] = np.ascontiguousarray(
        cent.T.reshape(32, 128, 64).transpose(1, 0, 2)).reshape(128, 32 * 64)
    chp = np.stack([inputs['L0_attn_' + n] for n in
                    ('ln1_w', 'ln1_b', 'phz_re', 'phz_im', 'lc_re', 'lc_im',
                     'sc_lin', 'ln2_w', 'ln2_b')], axis=1)
    g['chp'] = np.ascontiguousarray(
        chp.reshape(NL, 9, 4, 128).transpose(3, 0, 1, 2)).reshape(128, NL * 9 * 4)
    g['binr'] = np.ascontiguousarray(inputs['L0_attn_bin'].reshape(4, 128).T)
    g['b1all'] = np.ascontiguousarray(
        inputs['L0_attn_b1'].reshape(NL, 16, 128).transpose(2, 0, 1)).reshape(128, NL * 16)
    g['b2all'] = np.ascontiguousarray(
        inputs['L0_attn_b2'].reshape(NL, 4, 128).transpose(2, 0, 1)).reshape(128, NL * 4)
    lnl = np.stack([inputs['L0_attn_lnl_w'], inputs['L0_attn_lnl_b']])
    g['lnlwb'] = np.ascontiguousarray(
        lnl.reshape(2, 4, 128).transpose(2, 0, 1)).reshape(128, 8)
    g['boutr'] = np.ascontiguousarray(inputs['L0_attn_bout'].reshape(32, 128).T)
    g['fbr'] = np.ascontiguousarray(inputs['L0_ffn_bias'].reshape(32, 128).T)
    g['iot64'] = np.arange(64, dtype=np.float32).reshape(64, 1)
    g['assign'] = np.ascontiguousarray(
        inputs['L0_ffn_assignments'].reshape(1, V)).astype(np.int32)
    return g


def make_in_maps(inputs):
    x = inputs['L0_attn_x']
    g = _prep_shared(inputs)
    maps = []
    for core in range(8):
        b, pi = core // 4, core % 4
        d = dict(g)
        d['xT'] = np.ascontiguousarray(x[b, pi * T:(pi + 1) * T, :].T)
        cmask = np.zeros((128, 32), dtype=np.float32)
        for j in range(pi):
            e = pi - 1 - j
            cmask[:, (b * 4 + j) * 4 + e] = 1.0
        d['cmask'] = cmask
        ls = np.zeros((128, 4), dtype=np.float32)
        ls[:, pi] = 1.0
        d['lcselr'] = ls
        maps.append(d)
    return maps


def kernel(**inputs):
    kb = KB()
    nc = kb.build()
    maps = make_in_maps(inputs)
    res = run_bass_kernel_spmd(nc, maps, core_ids=list(range(8)))
    out = np.empty((2, 4096, V), dtype=np.float32)
    for core in range(8):
        b, pi = core // 4, core % 4
        out[b, pi * T:(pi + 1) * T, :] = res.results[core]['outT'].T
    return out


# version 13
# speedup vs baseline: 1.0211x; 1.0073x over previous
import numpy as np
import ml_dtypes
import concourse.bass as bass
import concourse.mybir as mybir
import concourse.bass_utils as _bu
from concourse.bass_utils import run_bass_kernel_spmd

_LDWOPT_DISABLED = True

F32 = mybir.dt.float32
BF16 = mybir.dt.bfloat16
I32 = mybir.dt.int32
AF = mybir.ActivationFunctionType
ALU = mybir.AluOpType
AX = mybir.AxisListType

T = 1024
TH = 512
C = 16
NCH = T // C
NL = 4
D = 512
V = 4096
EPS = 1e-5


class Stream:

    def __init__(self, nc, name, nbuf):
        self.nbuf = nbuf
        self.sems = [nc.alloc_semaphore(f"st_{name}{i}") for i in range(nbuf)]
        self.cnt = [0] * nbuf

    def issue(self, instr, idx):
        sl = idx % self.nbuf
        self.cnt[sl] += 16
        instr.then_inc(self.sems[sl], 16)
        return self.cnt[sl]

    def wait(self, eng, idx):
        sl = idx % self.nbuf
        eng.wait_ge(self.sems[sl], 16 * (idx // self.nbuf + 1))


class KB:
    def __init__(self, dbg=None):
        self.dbg = dbg
        self.done = False
        nc = bass.Bass("TRN2", target_bir_lowering=False, debug=False,
                       detect_race_conditions=False)
        self.nc = nc
        dt = nc.dram_tensor
        self.xT = dt("xT", [V, T], F32, kind="ExternalInput").ap()
        self.winb = dt("winb", [4, 128, 8 * 512], F32, kind="ExternalInput").ap()
        self.w1b = dt("w1b", [NL, 2, 128, 4 * 1024], F32, kind="ExternalInput").ap()
        self.w2b = dt("w2b", [NL, 2, 128, 8 * 512], F32, kind="ExternalInput").ap()
        self.woutb = dt("woutb", [4, 128, 4096], F32, kind="ExternalInput").ap()
        self.fwb = dt("fwb", [32, 128, 32 * 128], F32, kind="ExternalInput").ap()
        self.centb = dt("centb", [128, 32 * 64], F32, kind="ExternalInput").ap()
        self.chp = dt("chp", [128, NL * 9 * 4], F32, kind="ExternalInput").ap()
        self.binr = dt("binr", [128, 4], F32, kind="ExternalInput").ap()
        self.b1all = dt("b1all", [128, NL * 16], F32, kind="ExternalInput").ap()
        self.b2all = dt("b2all", [128, NL * 4], F32, kind="ExternalInput").ap()
        self.lnlwb = dt("lnlwb", [128, 8], F32, kind="ExternalInput").ap()
        self.boutr = dt("boutr", [128, 32], F32, kind="ExternalInput").ap()
        self.fbr = dt("fbr", [128, 32], F32, kind="ExternalInput").ap()
        self.cmask = dt("cmask", [128, 32], F32, kind="ExternalInput").ap()
        self.lcselr = dt("lcselr", [128, 4], F32, kind="ExternalInput").ap()
        self.iot64 = dt("iot64", [64, 1], F32, kind="ExternalInput").ap()
        self.assign = dt("assign", [1, V], I32, kind="ExternalInput").ap()
        self.outT = dt("outT", [V, T], F32, kind="ExternalOutput").ap()
        self.cc_in = dt("cc_in", [128, 8], F32)
        self.cc_out = dt("cc_out", [1024, 8], F32)
        self.cl_in = dt("cl_in", [64, 1], F32)
        self.cl_out = dt("cl_out", [64, 1], F32)
        self.dbgout = None
        if dbg is not None:
            shp = {'h0': [128, 4096], 'x1': [128, 32768]}.get(dbg)
            if shp is None and (dbg.startswith('hmix') or dbg.startswith('hffn')):
                shp = [128, 4096]
            if shp is None and dbg.startswith('y'):
                shp = [128, 4096]
            if shp is None and dbg in ('x1pre', 'x2pre'):
                shp = [128, 32768]
            if shp is None and dbg == 'hL':
                shp = [128, 4096]
            if shp is None and dbg == 'masks':
                shp = [128, 1064]
            if shp is None and dbg in ('m1', 'm2', 'm3b', 'm4'):
                shp = [128, 16]
            if shp is None and dbg == 'p2a':
                shp = [128, 2048]
            if shp is None and dbg == 'm3':
                shp = [64, 4096]
            if shp is None and dbg.startswith('tabs'):
                shp = [128, 872]
            if shp is None and dbg.startswith('wsc'):
                shp = [128, 1096]
            self.dbgout = dt("dbgout", shp, F32, kind="ExternalOutput").ap()

        al = nc.alloc_sbuf_tensor
        self.bigA = al("bigA", [128, 32768], BF16)
        self.bigB = al("bigB", [128, 32768], BF16)
        self.h = al("h", [128, 4 * T], F32)
        self.Yb = al("Yb", [128, T], F32)
        self.wsq = al("wsq", [128, T], BF16)
        self.wsA = al("wsA", [128, 4096], BF16)
        self.wsB = al("wsB", [128, 4096], BF16)
        self.xtc = al("xtc", [128, 2 * T], BF16)
        self.sc1 = al("sc1", [128, TH], F32)
        self.md = al("md", [128, 2 * TH], F32)
        self.qsb = al("qsb", [128, T], F32)
        self.mub = al("mub", [128, TH], F32)
        self.rstdb = al("rstdb", [128, TH], F32)
        self.wre = al("wre", [128, 4 * NCH], F32)
        self.wim = al("wim", [128, 4 * NCH], F32)
        self.prB = al("prB", [128, 4 * NCH], F32)
        self.piB = al("piB", [128, 4 * NCH], F32)
        self.prA = self.wre
        self.piA = self.wim
        self.Sre = al("Sre", [128, 4 * NCH], F32)
        self.Sim = al("Sim", [128, 4 * NCH], F32)
        self.qcre = al("qcre", [128, 4 * NCH], F32)
        self.qcim = al("qcim", [128, 4 * NCH], F32)
        self.powre = al("powre", [128, 4 * (C + 1)], F32)
        self.powim = al("powim", [128, 4 * (C + 1)], F32)
        self.qsre = al("qsre", [128, 4 * 8], F32)
        self.qsim = al("qsim", [128, 4 * 8], F32)
        self.psclre = al("psclre", [128, 4 * C], F32)
        self.psclim = al("psclim", [128, 4 * C], F32)
        self.ppre = al("ppre", [128, 4 * 4], F32)
        self.ppim = al("ppim", [128, 4 * 4], F32)
        self.sinre = al("sinre", [128, 4], F32)
        self.sinim = al("sinim", [128, 4], F32)
        self.gath = al("gath", [128, 64], F32)
        self.t4a = al("t4a", [128, 4], F32)
        self.t4b = al("t4b", [128, 4], F32)
        self.t4c = al("t4c", [128, 4], F32)
        self.t4d = al("t4d", [128, 4], F32)
        self.chps = al("chps", [128, NL * 9 * 4], F32)
        self.bins = al("bins", [128, 4], F32)
        self.b1s = al("b1s", [128, NL * 16], F32)
        self.b2s = al("b2s", [128, NL * 4], F32)
        self.lnls = al("lnls", [128, 8], F32)
        self.bouts = al("bouts", [128, 32], F32)
        self.fbs = al("fbs", [128, 32], F32)
        self.cms = al("cms", [128, 32], F32)
        self.lcsel = al("lcsel", [128, 4], F32)
        self.io64 = al("io64", [64, 1], F32)
        pass
        self.rowf = al("rowf", [1, 2 * TH], F32)
        self.rowb = al("rowb", [1, 2 * TH], BF16)
        self.rowb2 = al("rowb2", [1, 2 * TH], BF16)
        self.hcnt = al("hcnt", [64, 12], F32)
        self.clsel = al("clsel", [64, 1], BF16)
        self.rsall = al("rsall", [128, 32], F32)
        self.inv512 = al("inv512", [128, 1], BF16)
        self.inv512f = al("inv512f", [128, 1], F32)
        self.inv4096 = al("inv4096", [128, 1], BF16)
        self.ones1 = al("ones1", [1, 128], BF16)
        self.ones64 = al("ones64", [64, 1], BF16)
        self.cm05 = al("cm05", [128, 1], F32)
        self.cp05 = al("cp05", [128, 1], F32)

        yb_addr = nc.lookup_mloc(self.Yb).addr
        self.cents = nc.alloc_sbuf_tensor_at("cents", [128, 32 * 64], BF16,
                                             offset=yb_addr)
        self.pb = [nc.alloc_psum_tensor(f"pb{i}", [128, 512], F32) for i in range(8)]
        self.sems = {k: nc.alloc_semaphore("s_" + k)
                     for k in ("xg", "xs", "pe", "v", "a", "p", "o")}
        self.ccsems = [nc.alloc_semaphore(f"s_cc{i}") for i in range(5)]
        self.n = {k: 0 for k in self.sems}
        self.streams = {}
        for nm, nb in (("ew", 2), ("ex", 4), ("w1", 2), ("w2", 2), ("wow", 2),
                       ("wox", 2), ("de", 2), ("out", 2), ("out2", 2)):
            self.streams[nm] = Stream(nc, nm, nb)

    def inc(self, instr, key, amt=1):
        self.n[key] += amt
        instr.then_inc(self.sems[key], amt)
        return self.n[key]

    def w(self, eng, *keys):
        for k in keys:
            if self.n[k] > 0:
                eng.wait_ge(self.sems[k], self.n[k])

    def wv(self, eng, key, val):
        if val > 0:
            eng.wait_ge(self.sems[key], val)

    def hg(self, g):
        return self.h[:, g * T:(g + 1) * T]

    def hgt(self, g, th):
        return self.h[:, g * T + th * TH: g * T + (th + 1) * TH]

    @property
    def hb(self):
        return self.bigA[:, 24576:26624]

    @property
    def hbsq(self):
        return self.bigA[:, 26624:28672]

    def xt(self, b):
        return self.xtc[:, b * T:(b + 1) * T]

    def mdv(self, i):
        return self.md[:, i * TH:(i + 1) * TH]

    def chan(self, i, j, g=None):
        base = (i * 9 + j) * 4
        if g is None:
            return self.chps[:, base:base + 4]
        return self.chps[:, base + g:base + g + 1]

    def yv(self, g, th=None):
        b = self.bigB.ap().bitcast(F32)
        if th is None:
            return b[:, 8192 + g * T: 8192 + (g + 1) * T]
        return b[:, 8192 + g * T + th * TH: 8192 + g * T + (th + 1) * TH]

    def lv(self, g, th):
        return self.bigB[:, 24576 + g * T + th * TH: 24576 + g * T + (th + 1) * TH]

    def wtmp3(self):
        return self.bigA.ap().bitcast(F32)[:, 15360:16384].rearrange(
            "p (c i) -> p c i", c=NCH)

    def dump(self, name):
        if self.dbg != name or self.done:
            return self.dbg == name and self.done
        kb = self
        if name == 'h0' or name.startswith('hmix') or name.startswith('hffn'):
            def s(sync):
                kb.w(sync, 'v', 'a', 'p', 'pe')
                kb.inc(sync.dma_start(kb.dbgout, kb.h[:]), 'o', 16)
                sync.wait_ge(kb.sems['o'], kb.n['o'])
            self.blk.sync(s)
        elif name.startswith('y'):
            src = self.bigB.ap().bitcast(F32)[:, 8192:12288]

            def s(sync):
                kb.w(sync, 'v', 'a', 'p', 'pe')
                kb.inc(sync.dma_start(kb.dbgout, src), 'o', 16)
                sync.wait_ge(kb.sems['o'], kb.n['o'])
            self.blk.sync(s)
        elif name in ('x1pre', 'x1'):
            def g(gp):
                kb.w(gp, 'v', 'a', 'p', 'pe')
                kb.inc(gp.dma_start(kb.dbgout, kb.bigA[:]), 'xg', 16)
                gp.wait_ge(kb.sems['xg'], kb.n['xg'])
            self.blk.gpsimd(g)
        elif name == 'm3':
            def sm(gp):
                kb.w(gp, 'v', 'a', 'p', 'pe', 'xg')
                i = gp.dma_start(kb.dbgout, kb.bigB[0:64, 24576:24576 + V])
                kb.inc(i, 'xg', 16)
                gp.wait_ge(kb.sems['xg'], kb.n['xg'])
            self.blk.gpsimd(sm)
        elif name in ('m1', 'm2', 'm3b', 'm4'):
            def sm(sync):
                kb.w(sync, 'v', 'a', 'p', 'pe', 'xg')
                kb.inc(sync.dma_start(kb.dbgout[0:64, 0:12], kb.hcnt[:]), 'o', 16)
                sync.wait_ge(kb.sems['o'], kb.n['o'])
            self.blk.sync(sm)
        elif name == 'p2a':
            def sp(sync):
                kb.w(sync, 'v', 'a', 'p', 'pe')
                i = sync.dma_start(kb.dbgout, kb.bigA.ap().bitcast(F32)[:, 0:2048])
                kb.inc(i, 'o', 16)
                sync.wait_ge(kb.sems['o'], kb.n['o'])
            self.blk.sync(sp)
        elif name == 'masks':
            def sm(sync):
                kb.w(sync, 'v', 'a', 'p', 'pe', 'xg')
                kb.inc(sync.dma_start(kb.dbgout[:, 0:1024], kb.qsb[:]), 'o', 16)
                kb.inc(sync.dma_start(kb.dbgout[:, 1024:1056], kb.rsall[:]), 'o', 16)
                kb.inc(sync.dma_start(kb.dbgout[0:64, 1056:1064], kb.hcnt[:, 4:12]), 'o', 16)
                sync.wait_ge(kb.sems['o'], kb.n['o'])
            self.blk.sync(sm)
        elif name == 'hL':
            def g(gp):
                kb.w(gp, 'v', 'a', 'p', 'pe')
                kb.inc(gp.dma_start(kb.dbgout, kb.bigB[:, 24576:28672]), 'xg', 16)
                gp.wait_ge(kb.sems['xg'], kb.n['xg'])
            self.blk.gpsimd(g)
        elif name.startswith('tabs') or name.startswith('wsc'):
            if name.startswith('tabs'):
                parts = [(kb.powre, 68), (kb.powim, 68), (kb.qsre, 32),
                         (kb.qsim, 32), (kb.qcre, 256), (kb.qcim, 256),
                         (kb.psclre, 64), (kb.psclim, 64), (kb.ppre, 16),
                         (kb.ppim, 16)]
            else:
                parts = [(kb.wre, 256), (kb.wim, 256), (kb.Sre, 256),
                         (kb.Sim, 256), (kb.gath, 64), (kb.sinre, 2 * 0 + 4),
                         (kb.sinim, 4)]

            def s(sync):
                kb.w(sync, 'v', 'a', 'p', 'pe')
                off = 0
                i = None
                for tns, wdt in parts:
                    i = sync.dma_start(kb.dbgout[:, off:off + wdt], tns[:, 0:wdt])
                    kb.inc(i, 'o', 16)
                    off += wdt
                sync.wait_ge(kb.sems['o'], kb.n['o'])
            self.blk.sync(s)
        elif name == 'x2pre':
            def g(gp):
                kb.w(gp, 'v', 'a', 'p', 'pe')
                kb.inc(gp.dma_start(kb.dbgout, kb.bigB[:]), 'xg', 16)
                gp.wait_ge(kb.sems['xg'], kb.n['xg'])
            self.blk.gpsimd(g)
        self.done = True
        return True

    def build(self):
        nc = self.nc
        with nc.Block() as blk:
            self.blk = blk
            self.ph_pre()
            self.ph_embed()
            if self.dump('h0'):
                return nc
            for i in range(NL):
                if self.ph_layer(i):
                    return nc
            self.ph_lnl_wout()
            if self.done:
                return nc
            if self.dump('x1pre'):
                return nc
            self.ph_postln1()
            if self.dump('x1'):
                return nc
            self.ph_masks()
            if self.done:
                return nc
            if self.dump('masks'):
                return nc
            self.ph_dense()
            if self.dump('x2pre'):
                return nc
            self.ph_postln2_out()
            if self.done:
                return nc
            self.ph_finish()
        return nc

    def ph_pre(self):
        kb = self

        def s(sync):
            for dst, src in ((kb.chps, kb.chp), (kb.bins, kb.binr),
                             (kb.b1s, kb.b1all), (kb.b2s, kb.b2all),
                             (kb.lnls, kb.lnlwb), (kb.bouts, kb.boutr),
                             (kb.fbs, kb.fbr), (kb.cms, kb.cmask),
                             (kb.lcsel, kb.lcselr), (kb.io64, kb.iot64)):
                kb.inc(sync.dma_start(dst[:], src), 'xs', 16)
        self.blk.sync(s)

        def v(vec):
            vec.memset(kb.inv512[:], 1.0 / 512)
            vec.memset(kb.inv512f[:], 1.0 / 512)
            vec.memset(kb.inv4096[:], 1.0 / 4096)
            vec.memset(kb.ones1[:], 1.0)
            vec.memset(kb.ones64[:], 1.0)
            vec.memset(kb.cm05[:], -0.5)
            i = vec.memset(kb.cp05[:], 0.5)
            kb.inc(i, 'v')
        self.blk.vector(v)

    def ph_embed(self):
        kb = self
        seq = [(th, q, kc) for th in range(2) for q in range(4) for kc in range(8)]
        base_pe = self.n['pe']
        xpe = {idx: base_pe + idx + 1 for idx in range(64)}
        winpe = {tq: base_pe + (tq + 1) * 8 for tq in range(8)}
        ew, ex = self.streams['ew'], self.streams['ex']

        def gdma(gp):
            kb.w(gp, 'xs')

            def win_load(tq):
                kb.wv(gp, 'pe', winpe.get(tq - 2, 0))
                q = tq % 4
                i = gp.dma_start((kb.wsA if tq % 2 == 0 else kb.wsB)[:],
                                 kb.winb[q])
                ew.issue(i, tq)
            win_load(0)
            win_load(1)
            for idx, (th, q, kc) in enumerate(seq):
                k = q * 8 + kc
                kb.wv(gp, 'pe', xpe.get(idx - 4, 0))
                src = bass.AP(kb.xT.tensor, (k * 128) * T + th * TH,
                              [[T, 128], [1, TH]])
                i = gp.dma_start(kb.xtc[:, (idx % 4) * TH:(idx % 4 + 1) * TH], src)
                ex.issue(i, idx)
                if idx % 8 == 0 and idx >= 8 and idx // 8 + 1 <= 7:
                    win_load(idx // 8 + 1)
        self.blk.gpsimd(gdma)

        def te(t):
            kb.w(t, 'v')
            for idx, (th, q, kc) in enumerate(seq):
                tq = th * 4 + q
                ws = kb.wsA if tq % 2 == 0 else kb.wsB
                ew.wait(t, tq)
                ex.wait(t, idx)
                first = (q == 0 and kc == 0)
                last = (q == 3 and kc == 7)
                i = None
                for m in range(4):
                    i = t.matmul(kb.pb[th * 4 + m][:, 0:TH],
                                 ws[:, kc * 512 + m * 128: kc * 512 + (m + 1) * 128],
                                 kb.xtc[:, (idx % 4) * TH:(idx % 4 + 1) * TH],
                                 start=first, stop=last)
                assert kb.inc(i, 'pe') == xpe[idx]
        self.blk.tensor(te)

        def ve(v):
            for th in range(2):
                kb.wv(v, 'pe', xpe[th * 32 + 31])
                i = None
                for m in range(4):
                    i = v.tensor_scalar(kb.hgt(m, th), kb.pb[th * 4 + m][:, 0:TH],
                                        kb.bins[:, m:m + 1], None, ALU.add)
                kb.inc(i, 'v')
        self.blk.vector(ve)

    def cast_chunks(self, srcf, nk):
        kb = self

        def a(sc):
            kb.w(sc, 'v', 'p', 'pe')
            i = None
            for k in range(nk):
                sc.activation(kb.hb[:, k * TH:(k + 1) * TH], srcf(k), AF.Identity)
                i = sc.activation(kb.hbsq[:, k * TH:(k + 1) * TH], srcf(k), AF.Square)
            kb.inc(i, 'a')
        self.blk.scalar(a)

    def stats_rows(self, nk, inv, pb_sum, pb_sq):
        kb = self

        def te(t):
            kb.w(t, 'a', 'v')
            i = None
            for k in range(nk):
                i = t.matmul(pb_sum[0:1, 0:TH], inv,
                             kb.hb[:, k * TH:(k + 1) * TH],
                             start=(k == 0), stop=(k == nk - 1))
            for k in range(nk):
                i = t.matmul(pb_sq[0:1, 0:TH], inv,
                             kb.hbsq[:, k * TH:(k + 1) * TH],
                             start=(k == 0), stop=(k == nk - 1))
            kb.inc(i, 'pe')
        self.blk.tensor(te)

    def rows_and_bcast(self, pb_sum, pb_sq, pb_mub, pb_rstdb):
        kb = self
        mu_bf = self.rowb[0:1, 0:TH]
        rstd_bf = self.rowb[0:1, TH:2 * TH]
        musq = self.rowf[0:1, 0:TH]
        varep = self.rowf[0:1, TH:2 * TH]

        def a(sc):
            kb.w(sc, 'pe')
            sc.activation(musq, pb_sum[0:1, 0:TH], AF.Square)
            i = sc.activation(mu_bf, pb_sum[0:1, 0:TH], AF.Identity)
            kb.inc(i, 'a')
        self.blk.scalar(a)

        def v(vec):
            kb.w(vec, 'a')
            vec.scalar_tensor_tensor(varep, musq, -1.0, pb_sq[0:1, 0:TH],
                                     ALU.mult, ALU.add)
            i = vec.tensor_scalar(varep, varep, EPS, None, ALU.add)
            kb.inc(i, 'v')
        self.blk.vector(v)

        def a2(sc):
            kb.w(sc, 'v')
            sc.activation(varep, varep, AF.Ln)
            i = sc.activation(rstd_bf, varep, AF.Exp, scale=kb.cm05[0:1, 0:1])
            kb.inc(i, 'a')
        self.blk.scalar(a2)

        def te(t):
            kb.w(t, 'a', 'v', 'p')
            t.matmul(pb_mub[:, 0:TH], kb.ones1[:], mu_bf, start=True, stop=True)
            i = t.matmul(pb_rstdb[:, 0:TH], kb.ones1[:], rstd_bf, start=True, stop=True)
            kb.inc(i, 'pe')
        self.blk.tensor(te)

    def colbc(self, tns, col, row, cnt, gstep=None, gcnt=None):
        t = tns.ap().tensor if hasattr(tns, 'ap') else tns.tensor
        if gstep is None:
            return bass.AP(t, col, [[row, 128], [0, cnt], [1, 1]])
        return bass.AP(t, col, [[row, 128], [gstep, gcnt], [0, cnt], [1, 1]])

    def ph_layer(self, li):
        self.ph_w1_prefetch(li)
        self.ph_psetup_pre(li)
        for th in range(2):
            self.ln_d512_stats(li, th)
        self.ph_psetup_post(li)
        if self.dump(f'tabs{li}'):
            return True
        self.ln_d512_apply(li, 'ln1')
        if self.dump(f'y{li}'):
            return True
        self.ph_sconv(li)
        if self.done:
            return True
        if self.dump(f'hmix{li}'):
            return True
        for th in range(2):
            self.ln_d512_stats(li, th, kind='ln2')
        self.ln_d512_apply(li, 'ln2')
        self.ph_w1(li)
        self.ph_w2(li)
        if self.dump(f'hffn{li}'):
            return True
        return False

    def _muS(self, th):
        return self.mdv(th * 2 - th) if False else (
            self.mdv(0) if th == 0 else self.qsb[:, 0:TH])

    def _rstdS(self, th):
        return self.mdv(1) if th == 0 else self.qsb[:, TH:2 * TH]

    def _sqbuf(self, th):
        return (self.bigA[:, 26624:28672] if th == 0
                else self.bigA[:, 22528:24576])

    def _rowf2(self, th):
        if th == 0:
            return self.rowf
        return self.bigA.ap().bitcast(F32)[0:1, 0:1024]

    def _rowb2t(self, th):
        if th == 0:
            return self.rowb
        return self.bigA[0:1, 28672:29696]

    def ln_d512_stats(self, li, th, kind='ln1'):
        kb = self
        b0 = th * 4
        sqb = self._sqbuf(th)

        def vsq(vec):
            i = None
            for g in range(4):
                i = vec.tensor_tensor(sqb[:, g * TH:(g + 1) * TH],
                                      kb.hgt(g, th), kb.hgt(g, th), ALU.mult)
            kb.inc(i, 'v')
        self.blk.vector(vsq)
        sqv = self.n['v']

        def te(t):
            kb.w(t, 'a', 'v')
            i = None
            for g in range(4):
                i = t.matmul(kb.pb[b0][0:1, 0:TH], kb.inv512f[:], kb.hgt(g, th),
                             start=(g == 0), stop=(g == 3))
            kb.wv(t, 'v', sqv)
            for g in range(4):
                i = t.matmul(kb.pb[b0 + 1][0:1, 0:TH], kb.inv512[:],
                             sqb[:, g * TH:(g + 1) * TH],
                             start=(g == 0), stop=(g == 3))
            kb.inc(i, 'pe')
        self.blk.tensor(te)
        pe_stats = self.n['pe']
        rf = self._rowf2(th)
        rb = self._rowb2t(th)
        mu_bf = rb[0:1, 0:TH] if th == 0 else rb[0:1, 0:TH]
        rstd_bf = rb[0:1, TH:2 * TH] if th == 0 else None
        if th != 0:
            rstd_bf = self.bigA[0:1, 29184:29184 + TH]
        musq = rf[0:1, 0:TH]
        varep = rf[0:1, TH:2 * TH]

        def a(sc):
            kb.wv(sc, 'pe', pe_stats)
            sc.activation(musq, kb.pb[b0][0:1, 0:TH], AF.Square)
            i = sc.activation(mu_bf, kb.pb[b0][0:1, 0:TH], AF.Identity)
            kb.inc(i, 'a')
        self.blk.scalar(a)
        av = self.n['a']

        def v(vec):
            kb.wv(vec, 'a', av)
            vec.scalar_tensor_tensor(varep, musq, -1.0, kb.pb[b0 + 1][0:1, 0:TH],
                                     ALU.mult, ALU.add)
            i = vec.tensor_scalar(varep, varep, EPS, None, ALU.add)
            kb.inc(i, 'v')
        self.blk.vector(v)
        vv = self.n['v']

        def a2(sc):
            kb.wv(sc, 'v', vv)
            sc.activation(varep, varep, AF.Ln)
            i = sc.activation(rstd_bf, varep, AF.Exp, scale=kb.cm05[0:1, 0:1])
            kb.inc(i, 'a')
        self.blk.scalar(a2)
        av2 = self.n['a']

        def te2(t):
            kb.wv(t, 'a', av2)
            kb.w(t, 'v', 'p')
            t.matmul(kb.pb[b0 + 2][:, 0:TH], kb.ones1[:], mu_bf,
                     start=True, stop=True)
            i = t.matmul(kb.pb[b0 + 3][:, 0:TH], kb.ones1[:], rstd_bf,
                         start=True, stop=True)
            kb.inc(i, 'pe')
        self.blk.tensor(te2)
        pe_b = self.n['pe']
        if not hasattr(self, '_lnrows'):
            self._lnrows = {}
            self._lnbc = {}
        self._lnbc[th] = pe_b
        if th == 1:
            def vc(vec):
                kb.wv(vec, 'pe', pe_b)
                vec.tensor_copy(kb._muS(th), kb.pb[b0 + 2][:, 0:TH])
                i = vec.tensor_copy(kb._rstdS(th), kb.pb[b0 + 3][:, 0:TH])
                kb.inc(i, 'v')
            self.blk.vector(vc)
        self._lnrows[th] = self.n['v']

    def ln_d512_apply(self, li, kind):
        kb = self
        jg = 0 if kind == 'ln1' else 7
        jb = 1 if kind == 'ln1' else 8
        p0 = self.n['p']
        v0 = self.n['v']
        self._apply_p0 = p0
        self._apply_v0 = v0

        def po(gp):
            for g in range(4):
                kb.wv(gp, 'v', kb._lnrows[1])
                d = kb.yv(g, 1)
                gam = kb.colbc(kb.chps, (li * 9 + jg) * 4 + g, NL * 9 * 4, TH)
                gp.tensor_tensor(d, kb.hgt(g, 1), kb._muS(1), ALU.subtract)
                gp.tensor_tensor(d, d, kb._rstdS(1), ALU.mult)
                i = gp.tensor_tensor(
                    d.rearrange("p (c o) -> p c o", o=1),
                    d.rearrange("p (c o) -> p c o", o=1), gam, ALU.mult)
                kb.inc(i, 'p')
        self.blk.gpsimd(po)

        def vo(vec):
            for g in range(4):
                kb.wv(vec, 'pe', kb._lnbc[0])
                d = kb.yv(g, 0)
                vec.tensor_tensor(kb.sc1[:], kb.hgt(g, 0), kb.pb[2][:, 0:TH],
                                  ALU.subtract)
                i = vec.scalar_tensor_tensor(d, kb.sc1[:], kb.chan(li, jg, g),
                                             kb.pb[3][:, 0:TH], ALU.mult, ALU.mult)
                kb.inc(i, 'v')
        self.blk.vector(vo)

        self._silu_a0 = self.n['a']

        def a(sc):
            for g in range(4):
                for th in range(2):
                    if th == 0:
                        kb.wv(sc, 'v', v0 + g + 1)
                        src = kb.yv(g, 0)
                    else:
                        kb.wv(sc, 'p', p0 + g + 1)
                        src = kb.yv(g, 1)
                    if kind == 'ln1':
                        i = sc.activation(src, src, AF.Silu, bias=kb.chan(li, jb, g))
                    else:
                        i = sc.activation(kb.lv(g, th), src, AF.Identity,
                                          bias=kb.chan(li, jb, g))
                    kb.inc(i, 'a')
        self.blk.scalar(a)

    def ph_psetup_pre(self, li):
        kb = self
        pre, pim = self.t4a, self.t4b
        r2, tmp = self.t4c, self.t4d
        rr = self.sinre
        rsave = self.sinim

        def p1(gp):
            kb.w(gp, 'xs', 'a', 'p')
            gp.tensor_tensor(r2[:], kb.chan(li, 2), kb.chan(li, 2), ALU.mult)
            gp.tensor_tensor(tmp[:], kb.chan(li, 3), kb.chan(li, 3), ALU.mult)
            gp.tensor_tensor(r2[:], r2[:], tmp[:], ALU.add)
            i = gp.tensor_scalar(r2[:], r2[:], 1e-30, None, ALU.add)
            kb.inc(i, 'p')
        self.blk.gpsimd(p1)
        pv = self.n['p']

        def a1(sc):
            kb.wv(sc, 'p', pv)
            sc.activation(tmp[:], r2[:], AF.Ln)
            i = sc.activation(rr[:, 0:4], tmp[:], AF.Exp, scale=kb.cp05[:, 0:1])
            kb.inc(i, 'a')
        self.blk.scalar(a1)
        av = self.n['a']

        def v1(vec):
            kb.wv(vec, 'a', av)
            i = vec.reciprocal(tmp[:], rr[:, 0:4])
            kb.inc(i, 'v')
        self.blk.vector(v1)
        vv = self.n['v']

        def p2(gp):
            kb.wv(gp, 'v', vv)
            gp.tensor_tensor(rsave[:, 0:4], r2[:], tmp[:], ALU.mult)
            gp.tensor_tensor(rsave[:, 0:4], rsave[:, 0:4], rr[:, 0:4], ALU.add)
            i = gp.tensor_scalar(r2[:], rsave[:, 0:4], 0.5, None, ALU.mult)
            kb.inc(i, 'p')
        self.blk.gpsimd(p2)
        pv2 = self.n['p']

        def v2(vec):
            kb.wv(vec, 'p', pv2)
            i = vec.reciprocal(tmp[:], r2[:])
            kb.inc(i, 'v')
        self.blk.vector(v2)
        self._ps_v = self.n['v']

    def ph_psetup_post(self, li):
        kb = self
        pre, pim = self.t4a, self.t4b
        r2, tmp = self.t4c, self.t4d

        def o1(x):
            return x.rearrange("p (c o) -> p c o", o=1)

        def a2(sc):
            kb.wv(sc, 'v', kb._ps_v)
            i = sc.activation(r2[:], r2[:], AF.Tanh)
            kb.inc(i, 'a')
        self.blk.scalar(a2)
        av = self.n['a']

        def p4(gp):
            kb.wv(gp, 'a', av)
            gp.tensor_tensor(tmp[:], r2[:], tmp[:], ALU.mult)
            gp.tensor_tensor(pre[:], kb.chan(li, 2), tmp[:], ALU.mult)
            gp.tensor_tensor(pim[:], kb.chan(li, 3), tmp[:], ALU.mult)
            powre3 = kb.powre[:, :].rearrange("p (g k) -> p g k", g=4)
            powim3 = kb.powim[:, :].rearrange("p (g k) -> p g k", g=4)
            for g in range(4):
                gp.memset(powre3[:, g, 0:1], 1.0)
                gp.memset(powim3[:, g, 0:1], 0.0)
            gp.tensor_copy(powre3[:, :, 1:2], o1(pre[:]))
            gp.tensor_copy(powim3[:, :, 1:2], o1(pim[:]))
            sA = kb.prB[:, :].rearrange("p (g c) -> p g c", g=4)
            sB = kb.piB[:, :].rearrange("p (g c) -> p g c", g=4)
            for m in (1, 2, 4, 8):
                qr = kb.colbc(kb.powre, m, 4 * (C + 1), m, gstep=C + 1, gcnt=4)
                qi = kb.colbc(kb.powim, m, 4 * (C + 1), m, gstep=C + 1, gcnt=4)
                src_r = powre3[:, :, 1:m + 1].rearrange("p g (c o) -> p g c o", o=1)
                src_i = powim3[:, :, 1:m + 1].rearrange("p g (c o) -> p g c o", o=1)
                d_r = powre3[:, :, m + 1:2 * m + 1].rearrange("p g (c o) -> p g c o", o=1)
                d_i = powim3[:, :, m + 1:2 * m + 1].rearrange("p g (c o) -> p g c o", o=1)
                a_ = sA[:, :, 0:m].rearrange("p g (c o) -> p g c o", o=1)
                b_ = sB[:, :, 0:m].rearrange("p g (c o) -> p g c o", o=1)
                gp.tensor_tensor(a_, src_r, qr, ALU.mult)
                gp.tensor_tensor(b_, src_i, qi, ALU.mult)
                gp.tensor_tensor(d_r, a_, b_, ALU.subtract)
                gp.tensor_tensor(a_, src_r, qi, ALU.mult)
                gp.tensor_tensor(b_, src_i, qr, ALU.mult)
                gp.tensor_tensor(d_i, a_, b_, ALU.add)
            qre = kb.qsre[:, :].rearrange("p (g s) -> p g s", g=4)
            qim = kb.qsim[:, :].rearrange("p (g s) -> p g s", g=4)
            gp.tensor_copy(qre[:, :, 0:1], powre3[:, :, C:C + 1])
            gp.tensor_copy(qim[:, :, 0:1], powim3[:, :, C:C + 1])
            for st in range(6):
                a_ = qre[:, :, st:st + 1]
                b_ = qim[:, :, st:st + 1]
                gp.tensor_tensor(o1(kb.sinre[:]), a_, a_, ALU.mult)
                gp.tensor_tensor(o1(kb.sinim[:]), b_, b_, ALU.mult)
                gp.tensor_tensor(qre[:, :, st + 1:st + 2], o1(kb.sinre[:]), o1(kb.sinim[:]), ALU.subtract)
                gp.tensor_tensor(o1(kb.sinre[:]), a_, b_, ALU.mult)
                gp.tensor_scalar(qim[:, :, st + 1:st + 2], o1(kb.sinre[:]), 2.0, None, ALU.mult)
            qcre3 = kb.qcre[:, :].rearrange("p (g c) -> p g c", g=4)
            qcim3 = kb.qcim[:, :].rearrange("p (g c) -> p g c", g=4)
            for g in range(4):
                gp.memset(qcre3[:, g, 0:1], 1.0)
                gp.memset(qcim3[:, g, 0:1], 0.0)
            for st in range(6):
                m = 1 << st
                qr = kb.colbc(kb.qsre, st, 32, m, gstep=8, gcnt=4)
                qi = kb.colbc(kb.qsim, st, 32, m, gstep=8, gcnt=4)
                src_r = qcre3[:, :, 0:m].rearrange("p g (c o) -> p g c o", o=1)
                src_i = qcim3[:, :, 0:m].rearrange("p g (c o) -> p g c o", o=1)
                d_r = qcre3[:, :, m:2 * m].rearrange("p g (c o) -> p g c o", o=1)
                d_i = qcim3[:, :, m:2 * m].rearrange("p g (c o) -> p g c o", o=1)
                a_ = sA[:, :, 0:m].rearrange("p g (c o) -> p g c o", o=1)
                b_ = sB[:, :, 0:m].rearrange("p g (c o) -> p g c o", o=1)
                gp.tensor_tensor(a_, src_r, qr, ALU.mult)
                gp.tensor_tensor(b_, src_i, qi, ALU.mult)
                gp.tensor_tensor(d_r, a_, b_, ALU.subtract)
                gp.tensor_tensor(a_, src_r, qi, ALU.mult)
                gp.tensor_tensor(b_, src_i, qr, ALU.mult)
                gp.tensor_tensor(d_i, a_, b_, ALU.add)
            for g in range(4):
                scl_b = kb.colbc(kb.chps, (li * 9 + 6) * 4 + g, NL * 9 * 4, C)
                gp.tensor_tensor(o1(kb.psclre[:, g * C:(g + 1) * C]),
                                 o1(kb.powre[:, g * (C + 1) + 1:g * (C + 1) + C + 1]),
                                 scl_b, ALU.mult)
                gp.tensor_tensor(o1(kb.psclim[:, g * C:(g + 1) * C]),
                                 o1(kb.powim[:, g * (C + 1) + 1:g * (C + 1) + C + 1]),
                                 scl_b, ALU.mult)
                gp.tensor_scalar(kb.psclim[:, g * C:(g + 1) * C],
                                 kb.psclim[:, g * C:(g + 1) * C], -1.0, None, ALU.mult)
            pp_re = kb.ppre[:, :].rearrange("p (g e) -> p g e", g=4)
            pp_im = kb.ppim[:, :].rearrange("p (g e) -> p g e", g=4)
            Pr = qre[:, :, 6:7]
            Pi = qim[:, :, 6:7]
            for g in range(4):
                gp.memset(pp_re[:, g, 0:1], 1.0)
                gp.memset(pp_im[:, g, 0:1], 0.0)
            gp.tensor_copy(pp_re[:, :, 1:2], Pr)
            gp.tensor_copy(pp_im[:, :, 1:2], Pi)
            i = None
            for e in (2, 3):
                ar = pp_re[:, :, e - 1:e]
                ai = pp_im[:, :, e - 1:e]
                gp.tensor_tensor(o1(kb.sinre[:]), ar, Pr, ALU.mult)
                gp.tensor_tensor(o1(kb.sinim[:]), ai, Pi, ALU.mult)
                gp.tensor_tensor(pp_re[:, :, e:e + 1], o1(kb.sinre[:]), o1(kb.sinim[:]), ALU.subtract)
                gp.tensor_tensor(o1(kb.sinre[:]), ar, Pi, ALU.mult)
                gp.tensor_tensor(o1(kb.sinim[:]), ai, Pr, ALU.mult)
                i = gp.tensor_tensor(pp_im[:, :, e:e + 1], o1(kb.sinre[:]), o1(kb.sinim[:]), ALU.add)
            kb.inc(i, 'p')
        self.blk.gpsimd(p4)
        self._p4p = self.n['p']

    def ph_sconv(self, li):
        kb = self
        wre3 = self.wre[:, :].rearrange("p (g c) -> p g c", g=4)
        wim3 = self.wim[:, :].rearrange("p (g c) -> p g c", g=4)
        Sre3 = self.Sre[:, :].rearrange("p (g c) -> p g c", g=4)
        Sim3 = self.Sim[:, :].rearrange("p (g c) -> p g c", g=4)
        prow = 4 * (C + 1)

        def o1(x):
            return x.rearrange("p (c o) -> p c o", o=1)

        def v(vec):
            kb.wv(vec, 'p', kb._p4p)
            for g in range(4):
                kb.wv(vec, 'a', kb._silu_a0 + 2 * g + 2)
                yg = kb.yv(g)
                y3 = yg.rearrange("p (c i) -> p c i", c=NCH)
                rev_re = bass.AP(kb.powre.ap().tensor, g * (C + 1) + (C - 1),
                                 [[prow, 128], [0, NCH], [-1, C]])
                rev_im = bass.AP(kb.powim.ap().tensor, g * (C + 1) + (C - 1),
                                 [[prow, 128], [0, NCH], [-1, C]])
                vec.tensor_tensor(kb.wtmp3(), y3, rev_re, ALU.mult)
                vec.reduce_sum(wre3[:, g, :], kb.wtmp3(), AX.X)
                vec.tensor_tensor(kb.wtmp3(), y3, rev_im, ALU.mult)
                vec.reduce_sum(wim3[:, g, :], kb.wtmp3(), AX.X)
            N4 = 4 * NCH
            f32v = kb.bigA.ap().bitcast(F32)
            tmp1 = f32v[:, 15360:15360 + N4]
            tmp2 = f32v[:, 15616:15616 + N4]
            cur_r, cur_i, nxt_r, nxt_i = kb.prA, kb.piA, kb.prB, kb.piB

            def g3v(t, lo, hi):
                return t[:].rearrange("p (g c) -> p g c", g=4)[:, :, lo:hi] \
                    .rearrange("p g (c o) -> p g c o", o=1)
            for st in range(6):
                m = 1 << st
                cnt = NCH - m
                qr_b = kb.colbc(kb.qsre, st, 32, cnt, gstep=8, gcnt=4)
                qi_b = kb.colbc(kb.qsim, st, 32, cnt, gstep=8, gcnt=4)
                vec.tensor_copy(g3v(nxt_r, 0, m), g3v(cur_r, 0, m))
                vec.tensor_copy(g3v(nxt_i, 0, m), g3v(cur_i, 0, m))
                tm1 = tmp1.rearrange("p (g c) -> p g c", g=4)[:, :, 0:cnt] \
                    .rearrange("p g (c o) -> p g c o", o=1)
                tm2 = tmp2.rearrange("p (g c) -> p g c", g=4)[:, :, 0:cnt] \
                    .rearrange("p g (c o) -> p g c o", o=1)
                vec.tensor_tensor(tm1, g3v(cur_r, 0, cnt), qr_b, ALU.mult)
                vec.tensor_tensor(tm2, g3v(cur_i, 0, cnt), qi_b, ALU.mult)
                vec.tensor_tensor(g3v(nxt_r, m, NCH), g3v(cur_r, m, NCH), tm1, ALU.add)
                vec.tensor_tensor(g3v(nxt_r, m, NCH), g3v(nxt_r, m, NCH), tm2, ALU.subtract)
                vec.tensor_tensor(tm1, g3v(cur_i, 0, cnt), qr_b, ALU.mult)
                vec.tensor_tensor(tm2, g3v(cur_r, 0, cnt), qi_b, ALU.mult)
                vec.tensor_tensor(g3v(nxt_i, m, NCH), g3v(cur_i, m, NCH), tm1, ALU.add)
                vec.tensor_tensor(g3v(nxt_i, m, NCH), g3v(nxt_i, m, NCH), tm2, ALU.add)
                cur_r, nxt_r = nxt_r, cur_r
                cur_i, nxt_i = nxt_i, cur_i
            cr3 = cur_r[:].rearrange("p (g c) -> p g c", g=4)
            ci3 = cur_i[:].rearrange("p (g c) -> p g c", g=4)
            for g in range(4):
                vec.memset(Sre3[:, g, 0:1], 0.0)
                vec.memset(Sim3[:, g, 0:1], 0.0)
            vec.tensor_copy(Sre3[:, :, 1:NCH], cr3[:, :, 0:NCH - 1])
            vec.tensor_copy(Sim3[:, :, 1:NCH], ci3[:, :, 0:NCH - 1])
            vec.tensor_copy(
                kb.gath[:, 56:64].rearrange("p (g x) -> p g x", g=4)[:, :, 0:1],
                cr3[:, :, NCH - 1:NCH])
            i = vec.tensor_copy(
                kb.gath[:, 56:64].rearrange("p (g x) -> p g x", g=4)[:, :, 1:2],
                ci3[:, :, NCH - 1:NCH])
            kb.inc(i, 'v')
        self.blk.vector(v)

        def g1(gp):
            kb.w(gp, 'v')
            kb.inc(gp.dma_start(kb.cc_in[:, :], kb.gath[:, 56:64]), 'xg', 16)
            kb.w(gp, 'xg')
            i = gp.collective_compute(
                "AllGather", ALU.bypass, replica_groups=[list(range(8))],
                ins=[kb.cc_in.ap().opt()], outs=[kb.cc_out.ap().opt()])
            i.then_inc(kb.ccsems[li])
            gp.wait_ge(kb.ccsems[li], 1)
            src = bass.AP(kb.cc_out.ap().tensor, 0, [[8, 128], [1024, 8], [1, 8]])
            i = gp.dma_start(kb.gath[:].rearrange("p (r c) -> p r c", r=8), src)
            kb.inc(i, 'xg', 16)
        self.blk.gpsimd(g1)

        def vsh(vec):
            for g in range(4):
                kb.wv(vec, 'p', kb._apply_p0 + g + 1)
                yg = kb.yv(g)
                y3 = yg.rearrange("p (c i) -> p c i", c=NCH)
                h3 = kb.hg(g).rearrange("p (c i) -> p c i", c=NCH)
                vec.scalar_tensor_tensor(kb.hg(g), yg, kb.chan(li, 6, g),
                                         kb.hg(g), ALU.mult, ALU.add)
                for k in range(1, C):
                    i = vec.scalar_tensor_tensor(
                        h3[:, :, k:C], y3[:, :, 0:C - k],
                        kb.psclre[:, g * C + k - 1:g * C + k],
                        h3[:, :, k:C], ALU.mult, ALU.add)
                kb.inc(i, 'v')
        self.blk.vector(vsh)

        def pc(gp):
            kb.w(gp, 'xg')
            wjr = kb.t4a
            WJ = kb.prB[:, :].rearrange("p (g c) -> p g c", g=4)
            WJr = WJ[:, :, 0:9]
            WJi = WJ[:, :, 9:18]
            PR = kb.prA[:, :].rearrange("p (g c) -> p g c", g=4)
            prod = PR[:, :, 0:9]
            gp.memset(WJr, 0.0)
            gp.memset(WJi, 0.0)
            for e in range(4):
                cmsb = bass.AP(kb.cms.ap().tensor, e,
                               [[32, 128], [0, 4], [4, 8], [1, 1]])
                ppb_r = bass.AP(kb.ppre.ap().tensor, e,
                                [[16, 128], [4, 4], [0, 8], [1, 1]])
                ppb_i = bass.AP(kb.ppim.ap().tensor, e,
                                [[16, 128], [4, 4], [0, 8], [1, 1]])
                o4 = prod[:, :, 0:8].rearrange("p g (c o) -> p g c o", o=1)
                gp.tensor_tensor(o4, cmsb, ppb_r, ALU.mult)
                gp.tensor_tensor(WJr[:, :, 0:8], WJr[:, :, 0:8], prod[:, :, 0:8], ALU.add)
                gp.tensor_tensor(o4, cmsb, ppb_i, ALU.mult)
                gp.tensor_tensor(WJi[:, :, 0:8], WJi[:, :, 0:8], prod[:, :, 0:8], ALU.add)
                lcb = bass.AP(kb.lcsel.ap().tensor, e,
                              [[4, 128], [0, 4], [0, 1], [1, 1]])
                o41 = prod[:, :, 8:9].rearrange("p g (c o) -> p g c o", o=1)
                gp.tensor_tensor(o41, lcb, ppb_r[:, :, 0:1], ALU.mult)
                gp.tensor_tensor(WJr[:, :, 8:9], WJr[:, :, 8:9], prod[:, :, 8:9], ALU.add)
                gp.tensor_tensor(o41, lcb, ppb_i[:, :, 0:1], ALU.mult)
                gp.tensor_tensor(WJi[:, :, 8:9], WJi[:, :, 8:9], prod[:, :, 8:9], ALU.add)
            SJr = PR[:, :, 16:25]
            SJi = PR[:, :, 25:34]
            gathr = bass.AP(kb.gath.ap().tensor, 0, [[64, 128], [2, 4], [8, 8], [1, 1]])
            gathi = bass.AP(kb.gath.ap().tensor, 1, [[64, 128], [2, 4], [8, 8], [1, 1]])
            gp.tensor_copy(SJr[:, :, 0:8].rearrange("p g (c o) -> p g c o", o=1), gathr)
            gp.tensor_copy(SJi[:, :, 0:8].rearrange("p g (c o) -> p g c o", o=1), gathi)
            gp.tensor_copy(SJr[:, :, 8:9], kb.chan(li, 4).rearrange("p (g o) -> p g o", o=1))
            gp.tensor_copy(SJi[:, :, 8:9], kb.chan(li, 5).rearrange("p (g o) -> p g o", o=1))
            P1 = PR[:, :, 34:43]
            P2 = WJ[:, :, 18:27]
            P3 = PR[:, :, 43:52]
            P4 = WJ[:, :, 27:36]
            gp.tensor_tensor(P1, WJr, SJr, ALU.mult)
            gp.tensor_tensor(P2, WJi, SJi, ALU.mult)
            gp.tensor_tensor(P3, WJr, SJi, ALU.mult)
            i = gp.tensor_tensor(P4, WJi, SJr, ALU.mult)
            kb.inc(i, 'p')
        self.blk.gpsimd(pc)
        pcp = self.n['p']

        def v2(vec):
            kb.wv(vec, 'p', pcp)
            PR = kb.prA[:, :].rearrange("p (g c) -> p g c", g=4)
            WJ = kb.prB[:, :].rearrange("p (g c) -> p g c", g=4)
            vec.reduce_sum(o1(kb.sinre[:]), PR[:, :, 34:43], AX.X)
            vec.reduce_sum(o1(kb.t4c[:]), WJ[:, :, 18:27], AX.X)
            vec.reduce_sum(o1(kb.sinim[:]), PR[:, :, 43:52], AX.X)
            i = vec.reduce_sum(o1(kb.t4d[:]), WJ[:, :, 27:36], AX.X)
            fv = kb.inc(i, 'v')
            vec.wait_ge(kb.sems['v'], fv)
            vec.tensor_tensor(kb.sinre[:], kb.sinre[:], kb.t4c[:], ALU.subtract)
            i = vec.tensor_tensor(kb.sinim[:], kb.sinim[:], kb.t4d[:], ALU.add)
            fv = kb.inc(i, 'v')
            vec.wait_ge(kb.sems['v'], fv)
            qcre3 = kb.qcre[:, :].rearrange("p (g c) -> p g c", g=4)
            qcim3 = kb.qcim[:, :].rearrange("p (g c) -> p g c", g=4)
            i = None
            for g in range(4):
                sr = kb.sinre[:, g:g + 1]
                si = kb.sinim[:, g:g + 1]
                vec.scalar_tensor_tensor(Sre3[:, g, :], qcre3[:, g, :], sr,
                                         Sre3[:, g, :], ALU.mult, ALU.add)
                vec.tensor_scalar(kb.prA[:, 0:NCH], qcim3[:, g, :], si, None, ALU.mult)
                vec.tensor_tensor(Sre3[:, g, :], Sre3[:, g, :], kb.prA[:, 0:NCH], ALU.subtract)
                vec.scalar_tensor_tensor(Sim3[:, g, :], qcim3[:, g, :], sr,
                                         Sim3[:, g, :], ALU.mult, ALU.add)
                vec.tensor_scalar(kb.prA[:, 0:NCH], qcre3[:, g, :], si, None, ALU.mult)
                vec.tensor_tensor(Sim3[:, g, :], Sim3[:, g, :], kb.prA[:, 0:NCH], ALU.add)
                h3 = kb.hg(g).rearrange("p (c i) -> p c i", c=NCH)
                for ii in range(C):
                    vec.scalar_tensor_tensor(
                        h3[:, :, ii:ii + 1],
                        Sre3[:, g, :].rearrange("p (c o) -> p c o", o=1),
                        kb.psclre[:, g * C + ii:g * C + ii + 1],
                        h3[:, :, ii:ii + 1], ALU.mult, ALU.add)
                    i = vec.scalar_tensor_tensor(
                        h3[:, :, ii:ii + 1],
                        Sim3[:, g, :].rearrange("p (c o) -> p c o", o=1),
                        kb.psclim[:, g * C + ii:g * C + ii + 1],
                        h3[:, :, ii:ii + 1], ALU.mult, ALU.add)
            kb.inc(i, 'v')
        self.blk.vector(v2)

    def ph_w1_prefetch(self, li):
        kb = self
        w1s = self.streams['w1']
        w1i = 2 * li

        def g(gp):
            kb.w(gp, 'pe')
            i = gp.dma_start(kb.wsA[:], kb.w1b[li, 0])
            w1s.issue(i, w1i)
            i = gp.dma_start(kb.wsB[:], kb.w1b[li, 1])
            w1s.issue(i, w1i + 1)
        self.blk.gpsimd(g)

    def ph_w1(self, li):
        kb = self
        w1s = self.streams['w1']
        w1i = 2 * li
        a0 = self.n['a']
        pe0 = self.n['pe']
        self._w1_pe0 = pe0

        def te(t):
            kb.w(t, 'a')
            cnt = 0
            for hf in range(2):
                w1s.wait(t, w1i + hf)
                ws = kb.wsA if hf == 0 else kb.wsB
                for mm in range(8):
                    for th in range(2):
                        pbk = kb.pb[4 + (cnt % 2)]
                        if cnt >= 2:
                            kb.wv(t, 'a', a0 + cnt - 1)
                        i = None
                        for kc in range(4):
                            i = t.matmul(
                                pbk[:, 0:TH],
                                ws[:, kc * 1024 + mm * 128: kc * 1024 + (mm + 1) * 128],
                                kb.lv(kc, th), start=(kc == 0), stop=(kc == 3))
                        kb.inc(i, 'pe')
                        cnt += 1
        self.blk.tensor(te)

        def a(sc):
            cnt = 0
            for hf in range(2):
                for mm in range(8):
                    m = hf * 8 + mm
                    for th in range(2):
                        pbk = kb.pb[4 + (cnt % 2)]
                        kb.wv(sc, 'pe', pe0 + cnt + 1)
                        dst = kb.bigB[:, m * T + th * TH: m * T + (th + 1) * TH]
                        i = sc.activation(dst, pbk[:, 0:TH], AF.Silu,
                                          bias=kb.b1s[:, li * 16 + m:li * 16 + m + 1])
                        got = kb.inc(i, 'a')
                        assert got == a0 + cnt + 1
                        cnt += 1
        self.blk.scalar(a)

    def ph_w2(self, li):
        kb = self
        w2s = self.streams['w2']
        w2i = 2 * li
        w1_done = self.n['pe']

        def g(gp):
            kb.wv(gp, 'pe', kb._w1_pe0 + 16)
            i = gp.dma_start(kb.wsA[:], kb.w2b[li, 0])
            w2s.issue(i, w2i)
            kb.wv(gp, 'pe', w1_done)
            i = gp.dma_start(kb.wsB[:], kb.w2b[li, 1])
            w2s.issue(i, w2i + 1)
        self.blk.gpsimd(g)

        v0 = self.n['v']
        pe0 = self.n['pe']

        def te(t):
            kb.w(t, 'a')
            w2s.wait(t, w2i)
            cnt = 0
            for m in range(4):
                for th in range(2):
                    pbk = kb.pb[6 + (cnt % 2)]
                    if cnt >= 2:
                        kb.wv(t, 'v', v0 + cnt - 1)
                    i = None
                    for kc in range(16):
                        if kc == 8 and cnt == 0:
                            w2s.wait(t, w2i + 1)
                        hf = kc // 8
                        kch = kc % 8
                        ws = kb.wsA if hf == 0 else kb.wsB
                        i = t.matmul(
                            pbk[:, 0:TH],
                            ws[:, kch * 512 + m * 128: kch * 512 + (m + 1) * 128],
                            kb.bigB[:, kc * T + th * TH: kc * T + (th + 1) * TH],
                            start=(kc == 0), stop=(kc == 15))
                    kb.inc(i, 'pe')
                    cnt += 1
        self.blk.tensor(te)

        def v(vec):
            cnt = 0
            for m in range(4):
                for th in range(2):
                    pbk = kb.pb[6 + (cnt % 2)]
                    kb.wv(vec, 'pe', pe0 + cnt + 1)
                    i = vec.scalar_tensor_tensor(kb.hgt(m, th), pbk[:, 0:TH],
                                                 kb.b2s[:, li * 4 + m:li * 4 + m + 1],
                                                 kb.hgt(m, th), ALU.add, ALU.add)
                    got = kb.inc(i, 'v')
                    assert got == v0 + cnt + 1
                    cnt += 1
        self.blk.vector(v)

    def ph_lnl_wout(self):
        kb = self
        for th in range(2):
            self.cast_chunks(lambda k: kb.hgt(k, th), 4)
            self.stats_rows(4, kb.inv512[:], self.pb[0], self.pb[1])
            self.rows_and_bcast(self.pb[0], self.pb[1], self.pb[2], self.pb[3])
            v0 = self.n['v']

            def v(vec, th=th):
                kb.w(vec, 'pe')
                for g in range(4):
                    vec.tensor_tensor(kb.sc1[:], kb.hgt(g, th), kb.pb[2][:, 0:TH],
                                      ALU.subtract)
                    i = vec.scalar_tensor_tensor(kb.yv(g, th), kb.sc1[:],
                                                 kb.lnls[:, g:g + 1],
                                                 kb.pb[3][:, 0:TH], ALU.mult, ALU.mult)
                    kb.inc(i, 'v')
            self.blk.vector(v)

            def a(sc, th=th, v0=v0):
                for g in range(4):
                    kb.wv(sc, 'v', v0 + g + 1)
                    i = sc.activation(kb.lv(g, th), kb.yv(g, th), AF.Identity,
                                      bias=kb.lnls[:, 4 + g:5 + g])
                    kb.inc(i, 'a')
            self.blk.scalar(a)

        if self.dump('hL'):
            return
        wow, wox = self.streams['wow'], self.streams['wox']
        base_pe = self.n['pe']
        def grp_pe(cnt):
            return base_pe + (cnt + 1) + max(0, cnt - 2)
        def stats_pe(cnt):
            if cnt <= 61:
                return grp_pe(cnt + 2) + 1
            return base_pe + 126 + (cnt - 61)
        base_v = self.n['v']
        base_a = self.n['a']

        def g(gp):
            kb.w(gp, 'a')

            def wout_load(q):
                if q >= 2:
                    kb.wv(gp, 'pe', grp_pe((q - 1) * 16 - 1))
                i = gp.dma_start((kb.wsA if q % 2 == 0 else kb.wsB)[:], kb.woutb[q])
                wow.issue(i, q)
            wout_load(0)
            wout_load(1)
            for m in range(32):
                if m >= 2:
                    kb.wv(gp, 'v', base_v + (m - 2) * 2 + 2)
                src = bass.AP(kb.xT.tensor, m * 128 * T, [[T, 128], [1, T]])
                i = gp.dma_start(kb.xt(m % 2), src)
                wox.issue(i, m)
                if m == 7:
                    wout_load(2)
                elif m == 15:
                    wout_load(3)
        self.blk.gpsimd(g)

        def te(t):
            kb.w(t, 'v', 'a')
            cnt = 0
            for q in range(4):
                wow.wait(t, q)
                ws = kb.wsA if q % 2 == 0 else kb.wsB
                for mq in range(8):
                    m = q * 8 + mq
                    wox.wait(t, m)
                    for th in range(2):
                        pbk = kb.pb[cnt % 2]
                        if cnt >= 2:
                            kb.wv(t, 'v', base_v + cnt - 1)
                        i = None
                        for kc in range(4):
                            i = t.matmul(
                                pbk[:, 0:TH],
                                ws[:, kc * 1024 + mq * 128: kc * 1024 + (mq + 1) * 128],
                                kb.lv(kc, th), start=(kc == 0), stop=(kc == 3))
                        got = kb.inc(i, 'pe')
                        assert got == grp_pe(cnt)
                        if cnt >= 2:
                            kb._wo_stats_emit(t, cnt - 2, base_a, stats_pe)
                        cnt += 1
            for j in (62, 63):
                kb._wo_stats_emit(t, j, base_a, stats_pe)
        self.blk.tensor(te)

        def v(vec):
            cnt = 0
            for m in range(32):
                for th in range(2):
                    pbk = kb.pb[cnt % 2]
                    kb.wv(vec, 'pe', grp_pe(cnt))
                    wox.wait(vec, m)
                    dst = kb.bigA[:, m * T + th * TH: m * T + (th + 1) * TH]
                    i = vec.scalar_tensor_tensor(
                        dst, pbk[:, 0:TH], kb.bouts[:, m:m + 1],
                        kb.xt(m % 2)[:, th * TH:(th + 1) * TH], ALU.add, ALU.add)
                    got = kb.inc(i, 'v')
                    assert got == base_v + cnt + 1
                    cnt += 1
        self.blk.vector(v)

        def a(sc):
            cnt = 0
            for m in range(32):
                for th in range(2):
                    kb.wv(sc, 'v', base_v + cnt + 1)
                    if cnt >= 2:
                        kb.wv(sc, 'pe', stats_pe(cnt - 2))
                    src = kb.bigA[:, m * T + th * TH: m * T + (th + 1) * TH]
                    i = sc.activation(kb.wsq[:, th * TH:(th + 1) * TH], src, AF.Square)
                    got = kb.inc(i, 'a')
                    assert got == base_a + cnt + 1
                    cnt += 1
        self.blk.scalar(a)

    def _wo_stats_emit(self, t, cnt, base_a, stats_pe):
        kb = self
        m, th = cnt // 2, cnt % 2
        kb.wv(t, 'a', base_a + cnt + 1)
        src = kb.bigA[:, m * T + th * TH: m * T + (th + 1) * TH]
        t.matmul(kb.pb[2 + th][0:1, 0:TH], kb.inv4096[:], src,
                 start=(m == 0), stop=(m == 31))
        i = t.matmul(kb.pb[4 + th][0:1, 0:TH], kb.inv4096[:],
                     kb.wsq[:, th * TH:(th + 1) * TH],
                     start=(m == 0), stop=(m == 31))
        got = kb.inc(i, 'pe')
        assert got == stats_pe(cnt)

    def postln_apply(self, big, to_out):
        kb = self
        for th in range(2):
            self.rows_and_bcast(self.pb[2 + th], self.pb[4 + th],
                                self.pb[6], self.pb[7])

            def ve(vec, th=th):
                kb.w(vec, 'pe', 'p')
                vec.tensor_copy(kb.mub[:], kb.pb[6][:, 0:TH])
                i = vec.tensor_copy(kb.rstdb[:], kb.pb[7][:, 0:TH])
                kb.inc(i, 'v')
            self.blk.vector(ve)

            if not to_out:
                p0 = self.n['p']
                v0 = self.n['v']
                if not hasattr(self, '_pl1'):
                    self._pl1 = {}
                self._pl1[th] = (v0, p0)

                def po(gp, th=th, v0=v0):
                    kb.wv(gp, 'v', v0)
                    for m in range(16, 32):
                        src = big[:, m * T + th * TH: m * T + (th + 1) * TH]
                        gp.tensor_tensor(src, src, kb.mub[:], ALU.subtract)
                        i = gp.tensor_tensor(src, src, kb.rstdb[:], ALU.mult)
                        kb.inc(i, 'p')
                self.blk.gpsimd(po)

                def vo(vec, th=th):
                    for m in range(16):
                        src = big[:, m * T + th * TH: m * T + (th + 1) * TH]
                        vec.tensor_tensor(src, src, kb.mub[:], ALU.subtract)
                        i = vec.tensor_tensor(src, src, kb.rstdb[:], ALU.mult)
                        kb.inc(i, 'v')
                self.blk.vector(vo)
            else:
                outs = self.streams['out']
                outs2 = self.streams['out2']
                p0 = self.n['p']
                v0 = self.n['v']

                def po(gp, th=th, p0=p0, v0=v0):
                    kb.wv(gp, 'v', v0)
                    for j in range(16):
                        jg = th * 16 + j
                        m = 16 + j
                        if jg >= 2:
                            outs2.wait(gp, jg - 2)
                        src = big[:, m * T + th * TH: m * T + (th + 1) * TH]
                        d = kb._oslot(1, jg % 2)
                        gp.tensor_tensor(kb.mdv(1), src, kb.mub[:], ALU.subtract)
                        i = gp.tensor_tensor(d, kb.mdv(1), kb.rstdb[:], ALU.mult)
                        kb.inc(i, 'p')
                self.blk.gpsimd(po)

                def vo(vec, th=th, v0=v0):
                    for j in range(16):
                        jg = th * 16 + j
                        m = j
                        if jg >= 2:
                            outs.wait(vec, jg - 2)
                        src = big[:, m * T + th * TH: m * T + (th + 1) * TH]
                        d = kb._oslot(0, jg % 2)
                        vec.tensor_tensor(kb.mdv(0), src, kb.mub[:], ALU.subtract)
                        i = vec.tensor_tensor(d, kb.mdv(0), kb.rstdb[:], ALU.mult)
                        kb.inc(i, 'v')
                self.blk.vector(vo)

                def so(sync, th=th, p0=p0, v0=v0):
                    for j in range(16):
                        jg = th * 16 + j
                        mv, mp = j, 16 + j
                        kb.wv(sync, 'v', v0 + j + 1)
                        dst = bass.AP(kb.outT.tensor, mv * 128 * T + th * TH,
                                      [[T, 128], [1, TH]])
                        i = sync.dma_start(dst, kb._oslot(0, jg % 2))
                        outs.issue(i, jg)
                        kb.wv(sync, 'p', p0 + j + 1)
                        dst = bass.AP(kb.outT.tensor, mp * 128 * T + th * TH,
                                      [[T, 128], [1, TH]])
                        i = sync.dma_start(dst, kb._oslot(1, jg % 2))
                        outs2.issue(i, jg)
                self.blk.sync(so)

    def ph_postln1(self):
        kb = self
        self._wout_pe_end = self.n['pe']
        des = self.streams['de']

        def g(gp):
            kb.wv(gp, 'pe', kb._wout_pe_end)
            kb.inc(gp.dma_start(kb.cents[:], kb.centb), 'xg', 16)
            asg_i32 = kb.bigB.ap().bitcast(I32)[0:1, 0:V]
            kb.inc(gp.dma_start(asg_i32, kb.assign), 'xg', 16)
            for m0 in range(2):
                i = gp.dma_start((kb.wsA if m0 % 2 == 0 else kb.wsB)[:], kb.fwb[m0])
                des.issue(i, m0)
        self.blk.gpsimd(g)
        self.postln_apply(self.bigA, False)

    def _oslot(self, grp, par):
        outv = self.bigA.ap().bitcast(F32)
        slot = grp * 2 + par
        return outv[:, slot * TH:(slot + 1) * TH]

    def ph_masks(self):
        kb = self

        def v0(vec):
            kb.w(vec, 'xg')
            asg_i32 = kb.bigB.ap().bitcast(I32)[0:1, 0:V]
            asg_f32 = kb.bigB.ap().bitcast(F32)[0:1, 4096:4096 + V]
            asg_bf = kb.bigB[0:1, 16384:16384 + V]
            vec.tensor_copy(asg_f32, asg_i32)
            kb.inc(vec.tensor_copy(asg_bf, asg_f32), 'v')
        self.blk.vector(v0)

        for th in range(2):
            def te(t, th=th):
                kb.w(t, 'xg')
                v0a, p0a = kb._pl1[th]
                i = None
                for kc in range(32):
                    if kc < 16:
                        kb.wv(t, 'v', v0a + kc + 1)
                    else:
                        kb.wv(t, 'p', p0a + (kc - 16) + 1)
                    i = t.matmul(kb.pb[th][0:64, 0:TH],
                                 kb.cents[:, kc * 64:(kc + 1) * 64],
                                 kb.bigA[:, kc * T + th * TH: kc * T + (th + 1) * TH],
                                 start=(kc == 0), stop=(kc == 31))
                kb.inc(i, 'pe')
            self.blk.tensor(te)

            def a(sc, th=th):
                kb.w(sc, 'pe')
                ebf = kb.bigB[0:64, 20480 + th * TH:20480 + (th + 1) * TH]
                kb.inc(sc.activation(ebf, kb.pb[th][0:64, 0:TH], AF.Exp), 'a')
            self.blk.scalar(a)
            if not hasattr(self, '_mask_a_th'):
                self._mask_a_th = {}
            self._mask_a_th[th] = self.n['a']

            def te2(t, th=th):
                kb.w(t, 'a', 'v')
                ebf = kb.bigB[0:64, 20480 + th * TH:20480 + (th + 1) * TH]
                i = t.matmul(kb.pb[2][0:1, 0:TH], kb.ones64[:], ebf,
                             start=True, stop=True)
                kb.inc(i, 'pe')
            self.blk.tensor(te2)

            def v1(vec, th=th):
                kb.w(vec, 'pe')
                i = vec.tensor_scalar(kb.rowb[0:1, 0:TH], kb.pb[2][0:1, 0:TH],
                                      0.01, None, ALU.mult)
                kb.inc(i, 'v')
            self.blk.vector(v1)

            def te3(t, th=th):
                kb.w(t, 'v')
                i = t.matmul(kb.pb[3][0:64, 0:TH], kb.ones1[0:1, 0:64],
                             kb.rowb[0:1, 0:TH], start=True, stop=True)
                kb.inc(i, 'pe')
            self.blk.tensor(te3)

            def v2(vec, th=th):
                kb.w(vec, 'pe')
                ebf = kb.bigB[0:64, 20480 + th * TH:20480 + (th + 1) * TH]
                hits = kb.bigB[0:64, 22528 + th * TH:22528 + (th + 1) * TH]
                vec.tensor_tensor(hits, ebf, kb.pb[3][0:64, 0:TH], ALU.is_gt)
                i = vec.reduce_sum(kb.hcnt[:, th:th + 1], hits, AX.X)
                kb.inc(i, 'v')
            self.blk.vector(v2)

            def te4(t, th=th):
                kb.w(t, 'v')
                hits = kb.bigB[0:64, 22528 + th * TH:22528 + (th + 1) * TH]
                i = t.matmul(kb.pb[4][0:1, 0:TH], kb.ones64[:], hits,
                             start=True, stop=True)
                kb.inc(i, 'pe')
            self.blk.tensor(te4)

            def v3(vec, th=th):
                kb.w(vec, 'pe')
                i = vec.tensor_scalar(kb.rowb2[0:1, th * TH:(th + 1) * TH],
                                      kb.pb[4][0:1, 0:TH], 0.5, None, ALU.is_gt)
                kb.inc(i, 'v')
            self.blk.vector(v3)

        if self.dump('m1'):
            return
        def v4(vec):
            kb.w(vec, 'v')
            i = vec.tensor_tensor(kb.hcnt[:, 0:1], kb.hcnt[:, 0:1],
                                  kb.hcnt[:, 1:2], ALU.add)
            kb.inc(i, 'v')
        self.blk.vector(v4)

        def g2(gp):
            kb.w(gp, 'v')
            kb.inc(gp.dma_start(kb.cl_in[:, :], kb.hcnt[:, 0:1]), 'xg', 16)
            kb.w(gp, 'xg')
            i = gp.collective_compute(
                "AllReduce", ALU.add, replica_groups=[list(range(8))],
                ins=[kb.cl_in.ap().opt()], outs=[kb.cl_out.ap().opt()])
            i.then_inc(kb.ccsems[4])
            gp.wait_ge(kb.ccsems[4], 1)
            kb.inc(gp.dma_start(kb.hcnt[0:64, 10:11], kb.cl_out.ap()[0:64, :]),
                   'xg', 16)
        self.blk.gpsimd(g2)

        if self.dump('m2'):
            return

    def ph_dense(self):
        kb = self
        des = self.streams['de']
        BNK = (0, 1, 6, 7)

        pe0 = self.n['pe']
        kb._oh_v0 = self.n['v']

        def te5(t):
            kb.w(t, 'v')
            for oc in range(8):
                asg_bf = kb.bigB[0:1, 16384 + oc * TH:16384 + (oc + 1) * TH]
                if oc >= 1:
                    kb.wv(t, 'v', kb._oh_v0 + oc)
                i = t.matmul(kb.pb[5][0:64, 0:TH], kb.ones1[0:1, 0:64], asg_bf,
                             start=True, stop=True)
                kb.inc(i, 'pe')
        self.blk.tensor(te5)

        def v5b(vec):
            for oc in range(8):
                kb.wv(vec, 'pe', pe0 + oc + 1)
                oh = kb.bigB[0:64, 24576 + oc * TH:24576 + (oc + 1) * TH]
                i = vec.tensor_scalar(oh, kb.pb[5][0:64, 0:TH], kb.io64[:, 0:1],
                                      None, ALU.is_equal)
                got = kb.inc(i, 'v')
                assert got == kb._oh_v0 + oc + 1
        self.blk.vector(v5b)

        if self.dump('m3'):
            return
        if self.dump('m3b'):
            return

        def te7(t):
            kb.w(t, 'a', 'v')
            t.matmul(kb.pb[2][:, 0:TH], kb.ones1[:], kb.rowb2[0:1, 0:TH],
                     start=True, stop=True)
            i = t.matmul(kb.pb[3][:, 0:TH], kb.ones1[:], kb.rowb2[0:1, TH:2 * TH],
                         start=True, stop=True)
            kb.inc(i, 'pe')
        self.blk.tensor(te7)

        def v7(vec):
            kb.w(vec, 'pe')
            vec.tensor_copy(kb.qsb[:, 0:TH], kb.pb[2][:, 0:TH])
            kb.inc(vec.tensor_copy(kb.qsb[:, TH:T], kb.pb[3][:, 0:TH]), 'v')
        self.blk.vector(v7)

        gmap = {}
        smap = {}

        def te_head(t):
            cnt = 0
            for m in range(2):
                des.wait(t, m)
                ws = kb.wsA if m % 2 == 0 else kb.wsB
                for th in range(2):
                    pbk = kb.pb[BNK[cnt % 4]]
                    if cnt < 2:
                        kb.wv(t, 'v', kb._pl1[th][0] + 16)
                        kb.wv(t, 'p', kb._pl1[th][1] + 16)
                        kb.wv(t, 'a', kb._mask_a_th[th])
                    i = None
                    for kc in range(32):
                        i = t.matmul(
                            pbk[:, 0:TH], ws[:, kc * 128:(kc + 1) * 128],
                            kb.bigA[:, kc * T + th * TH: kc * T + (th + 1) * TH],
                            start=(kc == 0), stop=(kc == 31))
                    gmap[cnt] = kb.inc(i, 'pe')
                    cnt += 1
        self.blk.tensor(te_head)

        def v5(vec):
            kb.w(vec, 'xg')
            i = vec.tensor_scalar(kb.clsel[:, 0:1], kb.hcnt[:, 10:11], 0.5,
                                  None, ALU.is_gt)
            kb.inc(i, 'v')
        self.blk.vector(v5)

        def te6(t):
            kb.w(t, 'v')
            oh = kb.bigB[0:64, 24576:24576 + V]
            i = None
            for m in range(32):
                i = t.matmul(kb.pb[5][:, m:m + 1], oh[:, m * 128:(m + 1) * 128],
                             kb.clsel[:, 0:1], start=True, stop=True)
            kb.inc(i, 'pe')
        self.blk.tensor(te6)

        def v6(vec):
            kb.w(vec, 'pe')
            kb.inc(vec.tensor_copy(kb.rsall[:], kb.pb[5][:, 0:32]), 'v')
        self.blk.vector(v6)
        if self.dump('m4'):
            return

        base_v = self.n['v']
        base_a = self.n['a']
        base_p = self.n['p']

        def te(t):
            for j in (0, 1):
                kb._de_stats_emit(t, j, base_a, smap)
            cnt = 4
            for m in range(2, 32):
                des.wait(t, m)
                ws = kb.wsA if m % 2 == 0 else kb.wsB
                for th in range(2):
                    pbk = kb.pb[BNK[cnt % 4]]
                    kb.wv(t, 'v', base_v + cnt - 3)
                    i = None
                    for kc in range(32):
                        i = t.matmul(
                            pbk[:, 0:TH], ws[:, kc * 128:(kc + 1) * 128],
                            kb.bigA[:, kc * T + th * TH: kc * T + (th + 1) * TH],
                            start=(kc == 0), stop=(kc == 31))
                    gmap[cnt] = kb.inc(i, 'pe')
                    kb._de_stats_emit(t, cnt - 2, base_a, smap)
                    cnt += 1
            for j in (62, 63):
                kb._de_stats_emit(t, j, base_a, smap)
        self.blk.tensor(te)

        def v(vec):
            cnt = 0
            for m in range(32):
                for th in range(2):
                    pbk = kb.pb[BNK[cnt % 4]]
                    kb.wv(vec, 'pe', gmap[cnt])
                    if cnt >= 2:
                        kb.wv(vec, 'p', base_p + cnt - 1)
                    i = vec.tensor_scalar(kb.mdv(cnt % 2), pbk[:, 0:TH],
                                          kb.fbs[:, m:m + 1], kb.rsall[:, m:m + 1],
                                          ALU.add, ALU.mult)
                    got = kb.inc(i, 'v')
                    assert got == base_v + cnt + 1
                    cnt += 1
        self.blk.vector(v)

        def p(gp):
            kb.wv(gp, 'v', base_v)
            kb.wv(gp, 'p', base_p)
            cnt = 0
            for m in range(32):
                for th in range(2):
                    kb.wv(gp, 'v', base_v + cnt + 1)
                    dst = kb.bigB[:, m * T + th * TH: m * T + (th + 1) * TH]
                    gp.tensor_tensor(kb.mdv(cnt % 2), kb.mdv(cnt % 2),
                                     kb.qsb[:, th * TH:(th + 1) * TH], ALU.mult)
                    i = gp.tensor_tensor(
                        dst, kb.mdv(cnt % 2),
                        kb.bigA[:, m * T + th * TH: m * T + (th + 1) * TH], ALU.add)
                    got = kb.inc(i, 'p')
                    assert got == base_p + cnt + 1
                    if th == 1 and m + 2 < 32:
                        i = gp.dma_start((kb.wsA if m % 2 == 0 else kb.wsB)[:],
                                         kb.fwb[m + 2])
                        des.issue(i, m + 2)
                    cnt += 1
        self.blk.gpsimd(p)

        def a(sc):
            cnt = 0
            for m in range(32):
                for th in range(2):
                    kb.wv(sc, 'p', base_p + cnt + 1)
                    if cnt >= 2:
                        kb.wv(sc, 'pe', smap[cnt - 2])
                    src = kb.bigB[:, m * T + th * TH: m * T + (th + 1) * TH]
                    i = sc.activation(kb.wsq[:, th * TH:(th + 1) * TH], src, AF.Square)
                    got = kb.inc(i, 'a')
                    assert got == base_a + cnt + 1
                    cnt += 1
        self.blk.scalar(a)

    def _de_stats_emit(self, t, cnt, base_a, smap):
        kb = self
        m, th = cnt // 2, cnt % 2
        kb.wv(t, 'a', base_a + cnt + 1)
        src = kb.bigB[:, m * T + th * TH: m * T + (th + 1) * TH]
        t.matmul(kb.pb[2 + th][0:1, 0:TH], kb.inv4096[:], src,
                 start=(m == 0), stop=(m == 31))
        i = t.matmul(kb.pb[4 + th][0:1, 0:TH], kb.inv4096[:],
                     kb.wsq[:, th * TH:(th + 1) * TH],
                     start=(m == 0), stop=(m == 31))
        smap[cnt] = kb.inc(i, 'pe')

    def ph_postln2_out(self):
        self.postln_apply(self.bigB, True)
        if self.done:
            return

    def ph_finish(self):
        kb = self

        def s(sync):
            for nm in ('out', 'out2'):
                st = kb.streams[nm]
                for sl in range(st.nbuf):
                    sync.wait_ge(st.sems[sl], st.cnt[sl])
            sync.wait_ge(kb.sems['o'], kb.n['o'])
        self.blk.sync(s)


def _prep_shared(inputs):
    g = {}
    Win = inputs['L0_attn_Win']
    g['winb'] = np.ascontiguousarray(
        Win.T.reshape(4, 8, 128, 512).transpose(0, 2, 1, 3)).reshape(4, 128, 8 * 512)
    W1 = inputs['L0_attn_W1']
    g['w1b'] = np.ascontiguousarray(
        np.stack([W1[i].T.reshape(4, 128, 2, 8, 128).transpose(2, 1, 0, 3, 4)
                  for i in range(NL)])).reshape(NL, 2, 128, 4 * 1024)
    W2 = inputs['L0_attn_W2']
    g['w2b'] = np.ascontiguousarray(
        np.stack([W2[i].T.reshape(2, 8, 128, 512).transpose(0, 2, 1, 3)
                  for i in range(NL)])).reshape(NL, 2, 128, 8 * 512)
    Wout = inputs['L0_attn_Wout']
    g['woutb'] = np.ascontiguousarray(
        Wout.T.reshape(4, 128, 4, 8, 128).transpose(2, 1, 0, 3, 4)).reshape(4, 128, 4096)
    Wf = inputs['L0_ffn_weight']
    g['fwb'] = np.ascontiguousarray(
        Wf.T.reshape(32, 128, 32, 128).transpose(2, 1, 0, 3)).reshape(32, 128, 32 * 128)
    cent = inputs['L0_ffn_centroids']
    g['centb'] = np.ascontiguousarray(
        cent.T.reshape(32, 128, 64).transpose(1, 0, 2)).reshape(128, 32 * 64)
    chp = np.stack([inputs['L0_attn_' + n] for n in
                    ('ln1_w', 'ln1_b', 'phz_re', 'phz_im', 'lc_re', 'lc_im',
                     'sc_lin', 'ln2_w', 'ln2_b')], axis=1)
    g['chp'] = np.ascontiguousarray(
        chp.reshape(NL, 9, 4, 128).transpose(3, 0, 1, 2)).reshape(128, NL * 9 * 4)
    g['binr'] = np.ascontiguousarray(inputs['L0_attn_bin'].reshape(4, 128).T)
    g['b1all'] = np.ascontiguousarray(
        inputs['L0_attn_b1'].reshape(NL, 16, 128).transpose(2, 0, 1)).reshape(128, NL * 16)
    g['b2all'] = np.ascontiguousarray(
        inputs['L0_attn_b2'].reshape(NL, 4, 128).transpose(2, 0, 1)).reshape(128, NL * 4)
    lnl = np.stack([inputs['L0_attn_lnl_w'], inputs['L0_attn_lnl_b']])
    g['lnlwb'] = np.ascontiguousarray(
        lnl.reshape(2, 4, 128).transpose(2, 0, 1)).reshape(128, 8)
    g['boutr'] = np.ascontiguousarray(inputs['L0_attn_bout'].reshape(32, 128).T)
    g['fbr'] = np.ascontiguousarray(inputs['L0_ffn_bias'].reshape(32, 128).T)
    g['iot64'] = np.arange(64, dtype=np.float32).reshape(64, 1)
    g['assign'] = np.ascontiguousarray(
        inputs['L0_ffn_assignments'].reshape(1, V)).astype(np.int32)
    return g


def make_in_maps(inputs):
    x = inputs['L0_attn_x']
    g = _prep_shared(inputs)
    maps = []
    for core in range(8):
        b, pi = core // 4, core % 4
        d = dict(g)
        d['xT'] = np.ascontiguousarray(x[b, pi * T:(pi + 1) * T, :].T)
        cmask = np.zeros((128, 32), dtype=np.float32)
        for j in range(pi):
            e = pi - 1 - j
            cmask[:, (b * 4 + j) * 4 + e] = 1.0
        d['cmask'] = cmask
        ls = np.zeros((128, 4), dtype=np.float32)
        ls[:, pi] = 1.0
        d['lcselr'] = ls
        maps.append(d)
    return maps


def kernel(**inputs):
    kb = KB()
    nc = kb.build()
    maps = make_in_maps(inputs)
    res = run_bass_kernel_spmd(nc, maps, core_ids=list(range(8)))
    out = np.empty((2, 4096, V), dtype=np.float32)
    for core in range(8):
        b, pi = core // 4, core % 4
        out[b, pi * T:(pi + 1) * T, :] = res.results[core]['outT'].T
    return out


# version 14
# speedup vs baseline: 1.0256x; 1.0043x over previous
import numpy as np
import ml_dtypes
import concourse.bass as bass
import concourse.mybir as mybir
import concourse.bass_utils as _bu
from concourse.bass_utils import run_bass_kernel_spmd

_LDWOPT_DISABLED = True

F32 = mybir.dt.float32
BF16 = mybir.dt.bfloat16
I32 = mybir.dt.int32
AF = mybir.ActivationFunctionType
ALU = mybir.AluOpType
AX = mybir.AxisListType

T = 1024
TH = 512
C = 16
NCH = T // C
NL = 4
D = 512
V = 4096
EPS = 1e-5


class Stream:

    def __init__(self, nc, name, nbuf):
        self.nbuf = nbuf
        self.sems = [nc.alloc_semaphore(f"st_{name}{i}") for i in range(nbuf)]
        self.cnt = [0] * nbuf

    def issue(self, instr, idx):
        sl = idx % self.nbuf
        self.cnt[sl] += 16
        instr.then_inc(self.sems[sl], 16)
        return self.cnt[sl]

    def wait(self, eng, idx):
        sl = idx % self.nbuf
        eng.wait_ge(self.sems[sl], 16 * (idx // self.nbuf + 1))


class KB:
    def __init__(self, dbg=None):
        self.dbg = dbg
        self.done = False
        nc = bass.Bass("TRN2", target_bir_lowering=False, debug=False,
                       detect_race_conditions=False)
        self.nc = nc
        dt = nc.dram_tensor
        self.xT = dt("xT", [V, T], F32, kind="ExternalInput").ap()
        self.winb = dt("winb", [4, 128, 8 * 512], F32, kind="ExternalInput").ap()
        self.w1b = dt("w1b", [NL, 2, 128, 4 * 1024], F32, kind="ExternalInput").ap()
        self.w2b = dt("w2b", [NL, 2, 128, 8 * 512], F32, kind="ExternalInput").ap()
        self.woutb = dt("woutb", [4, 128, 4096], F32, kind="ExternalInput").ap()
        self.fwb = dt("fwb", [32, 128, 32 * 128], F32, kind="ExternalInput").ap()
        self.centb = dt("centb", [128, 32 * 64], F32, kind="ExternalInput").ap()
        self.chp = dt("chp", [128, NL * 9 * 4], F32, kind="ExternalInput").ap()
        self.binr = dt("binr", [128, 4], F32, kind="ExternalInput").ap()
        self.b1all = dt("b1all", [128, NL * 16], F32, kind="ExternalInput").ap()
        self.b2all = dt("b2all", [128, NL * 4], F32, kind="ExternalInput").ap()
        self.lnlwb = dt("lnlwb", [128, 8], F32, kind="ExternalInput").ap()
        self.boutr = dt("boutr", [128, 32], F32, kind="ExternalInput").ap()
        self.fbr = dt("fbr", [128, 32], F32, kind="ExternalInput").ap()
        self.cmask = dt("cmask", [128, 32], F32, kind="ExternalInput").ap()
        self.lcselr = dt("lcselr", [128, 4], F32, kind="ExternalInput").ap()
        self.iot64 = dt("iot64", [64, 1], F32, kind="ExternalInput").ap()
        self.assign = dt("assign", [1, V], I32, kind="ExternalInput").ap()
        self.outT = dt("outT", [V, T], F32, kind="ExternalOutput").ap()
        self.cc_in = dt("cc_in", [128, 8], F32)
        self.cc_out = dt("cc_out", [1024, 8], F32)
        self.cl_in = dt("cl_in", [64, 1], F32)
        self.cl_out = dt("cl_out", [64, 1], F32)
        self.dbgout = None
        if dbg is not None:
            shp = {'h0': [128, 4096], 'x1': [128, 32768]}.get(dbg)
            if shp is None and (dbg.startswith('hmix') or dbg.startswith('hffn')):
                shp = [128, 4096]
            if shp is None and dbg.startswith('y'):
                shp = [128, 4096]
            if shp is None and dbg in ('x1pre', 'x2pre'):
                shp = [128, 32768]
            if shp is None and dbg == 'hL':
                shp = [128, 4096]
            if shp is None and dbg == 'masks':
                shp = [128, 1064]
            if shp is None and dbg in ('m1', 'm2', 'm3b', 'm4'):
                shp = [128, 16]
            if shp is None and dbg == 'p2a':
                shp = [128, 2048]
            if shp is None and dbg == 'm3':
                shp = [64, 4096]
            if shp is None and dbg.startswith('tabs'):
                shp = [128, 872]
            if shp is None and dbg.startswith('wsc'):
                shp = [128, 1096]
            self.dbgout = dt("dbgout", shp, F32, kind="ExternalOutput").ap()

        al = nc.alloc_sbuf_tensor
        self.bigA = al("bigA", [128, 32768], BF16)
        self.bigB = al("bigB", [128, 32768], BF16)
        self.h = al("h", [128, 4 * T], F32)
        self.Yb = al("Yb", [128, T], F32)
        self.wsq = al("wsq", [128, T], BF16)
        self.wsA = al("wsA", [128, 4096], BF16)
        self.wsB = al("wsB", [128, 4096], BF16)
        self.xtc = al("xtc", [128, 2 * T], BF16)
        self.sc1 = al("sc1", [128, TH], F32)
        self.md = al("md", [128, 2 * TH], F32)
        self.qsb = al("qsb", [128, T], F32)
        self.mub = al("mub", [128, TH], F32)
        self.rstdb = al("rstdb", [128, TH], F32)
        self.wre = al("wre", [128, 4 * NCH], F32)
        self.wim = al("wim", [128, 4 * NCH], F32)
        self.prB = al("prB", [128, 4 * NCH], F32)
        self.piB = al("piB", [128, 4 * NCH], F32)
        self.prA = self.wre
        self.piA = self.wim
        self.Sre = al("Sre", [128, 4 * NCH], F32)
        self.Sim = al("Sim", [128, 4 * NCH], F32)
        self.qcre = al("qcre", [128, 4 * NCH], F32)
        self.qcim = al("qcim", [128, 4 * NCH], F32)
        self.powre = al("powre", [128, 4 * (C + 1)], F32)
        self.powim = al("powim", [128, 4 * (C + 1)], F32)
        self.qsre = al("qsre", [128, 4 * 8], F32)
        self.qsim = al("qsim", [128, 4 * 8], F32)
        self.psclre = al("psclre", [128, 4 * C], F32)
        self.psclim = al("psclim", [128, 4 * C], F32)
        self.ppre = al("ppre", [128, 4 * 4], F32)
        self.ppim = al("ppim", [128, 4 * 4], F32)
        self.sinre = al("sinre", [128, 4], F32)
        self.sinim = al("sinim", [128, 4], F32)
        self.gath = al("gath", [128, 64], F32)
        self.t4a = al("t4a", [128, 4], F32)
        self.t4b = al("t4b", [128, 4], F32)
        self.t4c = al("t4c", [128, 4], F32)
        self.t4d = al("t4d", [128, 4], F32)
        self.chps = al("chps", [128, NL * 9 * 4], F32)
        self.bins = al("bins", [128, 4], F32)
        self.b1s = al("b1s", [128, NL * 16], F32)
        self.b2s = al("b2s", [128, NL * 4], F32)
        self.lnls = al("lnls", [128, 8], F32)
        self.bouts = al("bouts", [128, 32], F32)
        self.fbs = al("fbs", [128, 32], F32)
        self.cms = al("cms", [128, 32], F32)
        self.lcsel = al("lcsel", [128, 4], F32)
        self.io64 = al("io64", [64, 1], F32)
        pass
        self.rowf = al("rowf", [1, 2 * TH], F32)
        self.rowb = al("rowb", [1, 2 * TH], BF16)
        self.rowb2 = al("rowb2", [1, 2 * TH], BF16)
        self.hcnt = al("hcnt", [64, 12], F32)
        self.clsel = al("clsel", [64, 1], BF16)
        self.rsall = al("rsall", [128, 32], F32)
        self.fbrm = al("fbrm", [128, 32], F32)
        self.inv512 = al("inv512", [128, 1], BF16)
        self.inv512f = al("inv512f", [128, 1], F32)
        self.inv4096 = al("inv4096", [128, 1], BF16)
        self.ones1 = al("ones1", [1, 128], BF16)
        self.ones64 = al("ones64", [64, 1], BF16)
        self.cm05 = al("cm05", [128, 1], F32)
        self.cp05 = al("cp05", [128, 1], F32)

        yb_addr = nc.lookup_mloc(self.Yb).addr
        self.cents = nc.alloc_sbuf_tensor_at("cents", [128, 32 * 64], BF16,
                                             offset=yb_addr)
        self.pb = [nc.alloc_psum_tensor(f"pb{i}", [128, 512], F32) for i in range(8)]
        self.sems = {k: nc.alloc_semaphore("s_" + k)
                     for k in ("xg", "xs", "pe", "v", "a", "p", "o", "dc")}
        self.ccsems = [nc.alloc_semaphore(f"s_cc{i}") for i in range(5)]
        self.n = {k: 0 for k in self.sems}
        self.streams = {}
        for nm, nb in (("ew", 2), ("ex", 4), ("w1", 2), ("w2", 2), ("wow", 2),
                       ("wox", 2), ("de", 2), ("out", 2), ("out2", 2)):
            self.streams[nm] = Stream(nc, nm, nb)

    def inc(self, instr, key, amt=1):
        self.n[key] += amt
        instr.then_inc(self.sems[key], amt)
        return self.n[key]

    def w(self, eng, *keys):
        for k in keys:
            if self.n[k] > 0:
                eng.wait_ge(self.sems[k], self.n[k])

    def wv(self, eng, key, val):
        if val > 0:
            eng.wait_ge(self.sems[key], val)

    def hg(self, g):
        return self.h[:, g * T:(g + 1) * T]

    def hgt(self, g, th):
        return self.h[:, g * T + th * TH: g * T + (th + 1) * TH]

    @property
    def hb(self):
        return self.bigA[:, 24576:26624]

    @property
    def hbsq(self):
        return self.bigA[:, 26624:28672]

    def xt(self, b):
        return self.xtc[:, b * T:(b + 1) * T]

    def mdv(self, i):
        return self.md[:, i * TH:(i + 1) * TH]

    def chan(self, i, j, g=None):
        base = (i * 9 + j) * 4
        if g is None:
            return self.chps[:, base:base + 4]
        return self.chps[:, base + g:base + g + 1]

    def yv(self, g, th=None):
        b = self.bigB.ap().bitcast(F32)
        if th is None:
            return b[:, 8192 + g * T: 8192 + (g + 1) * T]
        return b[:, 8192 + g * T + th * TH: 8192 + g * T + (th + 1) * TH]

    def lv(self, g, th):
        return self.bigB[:, 24576 + g * T + th * TH: 24576 + g * T + (th + 1) * TH]

    def wtmp3(self):
        return self.bigA.ap().bitcast(F32)[:, 15360:16384].rearrange(
            "p (c i) -> p c i", c=NCH)

    def dump(self, name):
        if self.dbg != name or self.done:
            return self.dbg == name and self.done
        kb = self
        if name == 'h0' or name.startswith('hmix') or name.startswith('hffn'):
            def s(sync):
                kb.w(sync, 'v', 'a', 'p', 'pe')
                kb.inc(sync.dma_start(kb.dbgout, kb.h[:]), 'o', 16)
                sync.wait_ge(kb.sems['o'], kb.n['o'])
            self.blk.sync(s)
        elif name.startswith('y'):
            src = self.bigB.ap().bitcast(F32)[:, 8192:12288]

            def s(sync):
                kb.w(sync, 'v', 'a', 'p', 'pe')
                kb.inc(sync.dma_start(kb.dbgout, src), 'o', 16)
                sync.wait_ge(kb.sems['o'], kb.n['o'])
            self.blk.sync(s)
        elif name in ('x1pre', 'x1'):
            def g(gp):
                kb.w(gp, 'v', 'a', 'p', 'pe')
                kb.inc(gp.dma_start(kb.dbgout, kb.bigA[:]), 'xg', 16)
                gp.wait_ge(kb.sems['xg'], kb.n['xg'])
            self.blk.gpsimd(g)
        elif name == 'm3':
            def sm(gp):
                kb.w(gp, 'v', 'a', 'p', 'pe', 'xg')
                i = gp.dma_start(kb.dbgout, kb.bigB[0:64, 24576:24576 + V])
                kb.inc(i, 'xg', 16)
                gp.wait_ge(kb.sems['xg'], kb.n['xg'])
            self.blk.gpsimd(sm)
        elif name in ('m1', 'm2', 'm3b', 'm4'):
            def sm(sync):
                kb.w(sync, 'v', 'a', 'p', 'pe', 'xg')
                kb.inc(sync.dma_start(kb.dbgout[0:64, 0:12], kb.hcnt[:]), 'o', 16)
                sync.wait_ge(kb.sems['o'], kb.n['o'])
            self.blk.sync(sm)
        elif name == 'p2a':
            def sp(sync):
                kb.w(sync, 'v', 'a', 'p', 'pe')
                i = sync.dma_start(kb.dbgout, kb.bigA.ap().bitcast(F32)[:, 0:2048])
                kb.inc(i, 'o', 16)
                sync.wait_ge(kb.sems['o'], kb.n['o'])
            self.blk.sync(sp)
        elif name == 'masks':
            def sm(sync):
                kb.w(sync, 'v', 'a', 'p', 'pe', 'xg')
                kb.inc(sync.dma_start(kb.dbgout[:, 0:1024], kb.qsb[:]), 'o', 16)
                kb.inc(sync.dma_start(kb.dbgout[:, 1024:1056], kb.rsall[:]), 'o', 16)
                kb.inc(sync.dma_start(kb.dbgout[0:64, 1056:1064], kb.hcnt[:, 4:12]), 'o', 16)
                sync.wait_ge(kb.sems['o'], kb.n['o'])
            self.blk.sync(sm)
        elif name == 'hL':
            def g(gp):
                kb.w(gp, 'v', 'a', 'p', 'pe')
                kb.inc(gp.dma_start(kb.dbgout, kb.bigB[:, 24576:28672]), 'xg', 16)
                gp.wait_ge(kb.sems['xg'], kb.n['xg'])
            self.blk.gpsimd(g)
        elif name.startswith('tabs') or name.startswith('wsc'):
            if name.startswith('tabs'):
                parts = [(kb.powre, 68), (kb.powim, 68), (kb.qsre, 32),
                         (kb.qsim, 32), (kb.qcre, 256), (kb.qcim, 256),
                         (kb.psclre, 64), (kb.psclim, 64), (kb.ppre, 16),
                         (kb.ppim, 16)]
            else:
                parts = [(kb.wre, 256), (kb.wim, 256), (kb.Sre, 256),
                         (kb.Sim, 256), (kb.gath, 64), (kb.sinre, 2 * 0 + 4),
                         (kb.sinim, 4)]

            def s(sync):
                kb.w(sync, 'v', 'a', 'p', 'pe')
                off = 0
                i = None
                for tns, wdt in parts:
                    i = sync.dma_start(kb.dbgout[:, off:off + wdt], tns[:, 0:wdt])
                    kb.inc(i, 'o', 16)
                    off += wdt
                sync.wait_ge(kb.sems['o'], kb.n['o'])
            self.blk.sync(s)
        elif name == 'x2pre':
            def g(gp):
                kb.w(gp, 'v', 'a', 'p', 'pe')
                kb.inc(gp.dma_start(kb.dbgout, kb.bigB[:]), 'xg', 16)
                gp.wait_ge(kb.sems['xg'], kb.n['xg'])
            self.blk.gpsimd(g)
        self.done = True
        return True

    def build(self):
        nc = self.nc
        with nc.Block() as blk:
            self.blk = blk
            self.ph_pre()
            self.ph_embed()
            if self.dump('h0'):
                return nc
            for i in range(NL):
                if self.ph_layer(i):
                    return nc
            self.ph_lnl_wout()
            if self.done:
                return nc
            if self.dump('x1pre'):
                return nc
            self.ph_postln1()
            if self.dump('x1'):
                return nc
            self.ph_masks()
            if self.done:
                return nc
            if self.dump('masks'):
                return nc
            self.ph_dense()
            if self.dump('x2pre'):
                return nc
            self.ph_postln2_out()
            if self.done:
                return nc
            self.ph_finish()
        return nc

    def ph_pre(self):
        kb = self

        def s(sync):
            for dst, src in ((kb.chps, kb.chp), (kb.bins, kb.binr),
                             (kb.b1s, kb.b1all), (kb.b2s, kb.b2all),
                             (kb.lnls, kb.lnlwb), (kb.bouts, kb.boutr),
                             (kb.fbs, kb.fbr), (kb.cms, kb.cmask),
                             (kb.lcsel, kb.lcselr), (kb.io64, kb.iot64)):
                kb.inc(sync.dma_start(dst[:], src), 'xs', 16)
        self.blk.sync(s)

        def v(vec):
            vec.memset(kb.inv512[:], 1.0 / 512)
            vec.memset(kb.inv512f[:], 1.0 / 512)
            vec.memset(kb.inv4096[:], 1.0 / 4096)
            vec.memset(kb.ones1[:], 1.0)
            vec.memset(kb.ones64[:], 1.0)
            vec.memset(kb.cm05[:], -0.5)
            i = vec.memset(kb.cp05[:], 0.5)
            kb.inc(i, 'v')
        self.blk.vector(v)

    def ph_embed(self):
        kb = self
        seq = [(th, q, kc) for th in range(2) for q in range(4) for kc in range(8)]
        base_pe = self.n['pe']
        xpe = {idx: base_pe + idx + 1 for idx in range(64)}
        winpe = {tq: base_pe + (tq + 1) * 8 for tq in range(8)}
        ew, ex = self.streams['ew'], self.streams['ex']

        def gdma(gp):
            kb.w(gp, 'xs')

            def win_load(tq):
                kb.wv(gp, 'pe', winpe.get(tq - 2, 0))
                q = tq % 4
                i = gp.dma_start((kb.wsA if tq % 2 == 0 else kb.wsB)[:],
                                 kb.winb[q])
                ew.issue(i, tq)
            win_load(0)
            win_load(1)
            for idx, (th, q, kc) in enumerate(seq):
                k = q * 8 + kc
                kb.wv(gp, 'pe', xpe.get(idx - 4, 0))
                src = bass.AP(kb.xT.tensor, (k * 128) * T + th * TH,
                              [[T, 128], [1, TH]])
                i = gp.dma_start(kb.xtc[:, (idx % 4) * TH:(idx % 4 + 1) * TH], src)
                ex.issue(i, idx)
                if idx % 8 == 0 and idx >= 8 and idx // 8 + 1 <= 7:
                    win_load(idx // 8 + 1)
        self.blk.gpsimd(gdma)

        def te(t):
            kb.w(t, 'v')
            for idx, (th, q, kc) in enumerate(seq):
                tq = th * 4 + q
                ws = kb.wsA if tq % 2 == 0 else kb.wsB
                ew.wait(t, tq)
                ex.wait(t, idx)
                first = (q == 0 and kc == 0)
                last = (q == 3 and kc == 7)
                i = None
                for m in range(4):
                    i = t.matmul(kb.pb[th * 4 + m][:, 0:TH],
                                 ws[:, kc * 512 + m * 128: kc * 512 + (m + 1) * 128],
                                 kb.xtc[:, (idx % 4) * TH:(idx % 4 + 1) * TH],
                                 start=first, stop=last)
                assert kb.inc(i, 'pe') == xpe[idx]
        self.blk.tensor(te)

        def ve(v):
            for th in range(2):
                kb.wv(v, 'pe', xpe[th * 32 + 31])
                i = None
                for m in range(4):
                    i = v.tensor_scalar(kb.hgt(m, th), kb.pb[th * 4 + m][:, 0:TH],
                                        kb.bins[:, m:m + 1], None, ALU.add)
                kb.inc(i, 'v')
        self.blk.vector(ve)

    def cast_chunks(self, srcf, nk):
        kb = self

        def a(sc):
            kb.w(sc, 'v', 'p', 'pe')
            i = None
            for k in range(nk):
                sc.activation(kb.hb[:, k * TH:(k + 1) * TH], srcf(k), AF.Identity)
                i = sc.activation(kb.hbsq[:, k * TH:(k + 1) * TH], srcf(k), AF.Square)
            kb.inc(i, 'a')
        self.blk.scalar(a)

    def stats_rows(self, nk, inv, pb_sum, pb_sq):
        kb = self

        def te(t):
            kb.w(t, 'a', 'v')
            i = None
            for k in range(nk):
                i = t.matmul(pb_sum[0:1, 0:TH], inv,
                             kb.hb[:, k * TH:(k + 1) * TH],
                             start=(k == 0), stop=(k == nk - 1))
            for k in range(nk):
                i = t.matmul(pb_sq[0:1, 0:TH], inv,
                             kb.hbsq[:, k * TH:(k + 1) * TH],
                             start=(k == 0), stop=(k == nk - 1))
            kb.inc(i, 'pe')
        self.blk.tensor(te)

    def rows_and_bcast(self, pb_sum, pb_sq, pb_mub, pb_rstdb):
        kb = self
        mu_bf = self.rowb[0:1, 0:TH]
        rstd_bf = self.rowb[0:1, TH:2 * TH]
        musq = self.rowf[0:1, 0:TH]
        varep = self.rowf[0:1, TH:2 * TH]

        def a(sc):
            kb.w(sc, 'pe')
            sc.activation(musq, pb_sum[0:1, 0:TH], AF.Square)
            i = sc.activation(mu_bf, pb_sum[0:1, 0:TH], AF.Identity)
            kb.inc(i, 'a')
        self.blk.scalar(a)

        def v(vec):
            kb.w(vec, 'a')
            vec.scalar_tensor_tensor(varep, musq, -1.0, pb_sq[0:1, 0:TH],
                                     ALU.mult, ALU.add)
            i = vec.tensor_scalar(varep, varep, EPS, None, ALU.add)
            kb.inc(i, 'v')
        self.blk.vector(v)

        def a2(sc):
            kb.w(sc, 'v')
            sc.activation(varep, varep, AF.Ln)
            i = sc.activation(rstd_bf, varep, AF.Exp, scale=kb.cm05[0:1, 0:1])
            kb.inc(i, 'a')
        self.blk.scalar(a2)

        def te(t):
            kb.w(t, 'a', 'v', 'p')
            t.matmul(pb_mub[:, 0:TH], kb.ones1[:], mu_bf, start=True, stop=True)
            i = t.matmul(pb_rstdb[:, 0:TH], kb.ones1[:], rstd_bf, start=True, stop=True)
            kb.inc(i, 'pe')
        self.blk.tensor(te)

    def colbc(self, tns, col, row, cnt, gstep=None, gcnt=None):
        t = tns.ap().tensor if hasattr(tns, 'ap') else tns.tensor
        if gstep is None:
            return bass.AP(t, col, [[row, 128], [0, cnt], [1, 1]])
        return bass.AP(t, col, [[row, 128], [gstep, gcnt], [0, cnt], [1, 1]])

    def ph_layer(self, li):
        self.ph_w1_prefetch(li)
        self.ph_psetup_pre(li)
        for th in range(2):
            self.ln_d512_stats(li, th)
        self.ph_psetup_post(li)
        if self.dump(f'tabs{li}'):
            return True
        self.ln_d512_apply(li, 'ln1')
        if self.dump(f'y{li}'):
            return True
        self.ph_sconv(li)
        if self.done:
            return True
        if self.dump(f'hmix{li}'):
            return True
        for th in range(2):
            self.ln_d512_stats(li, th, kind='ln2')
        self.ln_d512_apply(li, 'ln2')
        self.ph_w1(li)
        self.ph_w2(li)
        if self.dump(f'hffn{li}'):
            return True
        return False

    def _muS(self, th):
        return self.mdv(th * 2 - th) if False else (
            self.mdv(0) if th == 0 else self.qsb[:, 0:TH])

    def _rstdS(self, th):
        return self.mdv(1) if th == 0 else self.qsb[:, TH:2 * TH]

    def _sqbuf(self, th):
        return (self.bigA[:, 26624:28672] if th == 0
                else self.bigA[:, 22528:24576])

    def _rowf2(self, th):
        if th == 0:
            return self.rowf
        return self.bigA.ap().bitcast(F32)[0:1, 0:1024]

    def _rowb2t(self, th):
        if th == 0:
            return self.rowb
        return self.bigA[0:1, 28672:29696]

    def ln_d512_stats(self, li, th, kind='ln1'):
        kb = self
        b0 = th * 4
        sqb = self._sqbuf(th)

        def vsq(vec):
            i = None
            for g in range(4):
                i = vec.tensor_tensor(sqb[:, g * TH:(g + 1) * TH],
                                      kb.hgt(g, th), kb.hgt(g, th), ALU.mult)
            kb.inc(i, 'v')
        self.blk.vector(vsq)
        sqv = self.n['v']

        def te(t):
            kb.w(t, 'a', 'v')
            i = None
            for g in range(4):
                i = t.matmul(kb.pb[b0][0:1, 0:TH], kb.inv512f[:], kb.hgt(g, th),
                             start=(g == 0), stop=(g == 3))
            kb.wv(t, 'v', sqv)
            for g in range(4):
                i = t.matmul(kb.pb[b0 + 1][0:1, 0:TH], kb.inv512[:],
                             sqb[:, g * TH:(g + 1) * TH],
                             start=(g == 0), stop=(g == 3))
            kb.inc(i, 'pe')
        self.blk.tensor(te)
        pe_stats = self.n['pe']
        rf = self._rowf2(th)
        rb = self._rowb2t(th)
        mu_bf = rb[0:1, 0:TH] if th == 0 else rb[0:1, 0:TH]
        rstd_bf = rb[0:1, TH:2 * TH] if th == 0 else None
        if th != 0:
            rstd_bf = self.bigA[0:1, 29184:29184 + TH]
        musq = rf[0:1, 0:TH]
        varep = rf[0:1, TH:2 * TH]

        def a(sc):
            kb.wv(sc, 'pe', pe_stats)
            sc.activation(musq, kb.pb[b0][0:1, 0:TH], AF.Square)
            i = sc.activation(mu_bf, kb.pb[b0][0:1, 0:TH], AF.Identity)
            kb.inc(i, 'a')
        self.blk.scalar(a)
        av = self.n['a']

        def v(vec):
            kb.wv(vec, 'a', av)
            vec.scalar_tensor_tensor(varep, musq, -1.0, kb.pb[b0 + 1][0:1, 0:TH],
                                     ALU.mult, ALU.add)
            i = vec.tensor_scalar(varep, varep, EPS, None, ALU.add)
            kb.inc(i, 'v')
        self.blk.vector(v)
        vv = self.n['v']

        def a2(sc):
            kb.wv(sc, 'v', vv)
            sc.activation(varep, varep, AF.Ln)
            i = sc.activation(rstd_bf, varep, AF.Exp, scale=kb.cm05[0:1, 0:1])
            kb.inc(i, 'a')
        self.blk.scalar(a2)
        av2 = self.n['a']

        def te2(t):
            kb.wv(t, 'a', av2)
            kb.w(t, 'v', 'p')
            t.matmul(kb.pb[b0 + 2][:, 0:TH], kb.ones1[:], mu_bf,
                     start=True, stop=True)
            i = t.matmul(kb.pb[b0 + 3][:, 0:TH], kb.ones1[:], rstd_bf,
                         start=True, stop=True)
            kb.inc(i, 'pe')
        self.blk.tensor(te2)
        pe_b = self.n['pe']
        if not hasattr(self, '_lnrows'):
            self._lnrows = {}
            self._lnbc = {}
        self._lnbc[th] = pe_b
        if th == 1:
            def vc(vec):
                kb.wv(vec, 'pe', pe_b)
                vec.tensor_copy(kb._muS(th), kb.pb[b0 + 2][:, 0:TH])
                i = vec.tensor_copy(kb._rstdS(th), kb.pb[b0 + 3][:, 0:TH])
                kb.inc(i, 'v')
            self.blk.vector(vc)
        self._lnrows[th] = self.n['v']

    def ln_d512_apply(self, li, kind):
        kb = self
        jg = 0 if kind == 'ln1' else 7
        jb = 1 if kind == 'ln1' else 8
        p0 = self.n['p']
        v0 = self.n['v']
        self._apply_p0 = p0
        self._apply_v0 = v0

        def po(gp):
            for g in range(4):
                kb.wv(gp, 'v', kb._lnrows[1])
                d = kb.yv(g, 1)
                gam = kb.colbc(kb.chps, (li * 9 + jg) * 4 + g, NL * 9 * 4, TH)
                gp.tensor_tensor(d, kb.hgt(g, 1), kb._muS(1), ALU.subtract)
                gp.tensor_tensor(d, d, kb._rstdS(1), ALU.mult)
                i = gp.tensor_tensor(
                    d.rearrange("p (c o) -> p c o", o=1),
                    d.rearrange("p (c o) -> p c o", o=1), gam, ALU.mult)
                kb.inc(i, 'p')
        self.blk.gpsimd(po)

        def vo(vec):
            for g in range(4):
                kb.wv(vec, 'pe', kb._lnbc[0])
                d = kb.yv(g, 0)
                vec.tensor_tensor(kb.sc1[:], kb.hgt(g, 0), kb.pb[2][:, 0:TH],
                                  ALU.subtract)
                i = vec.scalar_tensor_tensor(d, kb.sc1[:], kb.chan(li, jg, g),
                                             kb.pb[3][:, 0:TH], ALU.mult, ALU.mult)
                kb.inc(i, 'v')
        self.blk.vector(vo)

        self._silu_a0 = self.n['a']

        def a(sc):
            for g in range(4):
                for th in range(2):
                    if th == 0:
                        kb.wv(sc, 'v', v0 + g + 1)
                        src = kb.yv(g, 0)
                    else:
                        kb.wv(sc, 'p', p0 + g + 1)
                        src = kb.yv(g, 1)
                    if kind == 'ln1':
                        i = sc.activation(src, src, AF.Silu, bias=kb.chan(li, jb, g))
                    else:
                        i = sc.activation(kb.lv(g, th), src, AF.Identity,
                                          bias=kb.chan(li, jb, g))
                    kb.inc(i, 'a')
        self.blk.scalar(a)

    def ph_psetup_pre(self, li):
        kb = self
        pre, pim = self.t4a, self.t4b
        r2, tmp = self.t4c, self.t4d
        rr = self.sinre
        rsave = self.sinim

        def p1(gp):
            kb.w(gp, 'xs', 'a', 'p')
            gp.tensor_tensor(r2[:], kb.chan(li, 2), kb.chan(li, 2), ALU.mult)
            gp.tensor_tensor(tmp[:], kb.chan(li, 3), kb.chan(li, 3), ALU.mult)
            gp.tensor_tensor(r2[:], r2[:], tmp[:], ALU.add)
            i = gp.tensor_scalar(r2[:], r2[:], 1e-30, None, ALU.add)
            kb.inc(i, 'p')
        self.blk.gpsimd(p1)
        pv = self.n['p']

        def a1(sc):
            kb.wv(sc, 'p', pv)
            sc.activation(tmp[:], r2[:], AF.Ln)
            i = sc.activation(rr[:, 0:4], tmp[:], AF.Exp, scale=kb.cp05[:, 0:1])
            kb.inc(i, 'a')
        self.blk.scalar(a1)
        av = self.n['a']

        def v1(vec):
            kb.wv(vec, 'a', av)
            i = vec.reciprocal(tmp[:], rr[:, 0:4])
            kb.inc(i, 'v')
        self.blk.vector(v1)
        vv = self.n['v']

        def p2(gp):
            kb.wv(gp, 'v', vv)
            gp.tensor_tensor(rsave[:, 0:4], r2[:], tmp[:], ALU.mult)
            gp.tensor_tensor(rsave[:, 0:4], rsave[:, 0:4], rr[:, 0:4], ALU.add)
            i = gp.tensor_scalar(r2[:], rsave[:, 0:4], 0.5, None, ALU.mult)
            kb.inc(i, 'p')
        self.blk.gpsimd(p2)
        pv2 = self.n['p']

        def v2(vec):
            kb.wv(vec, 'p', pv2)
            i = vec.reciprocal(tmp[:], r2[:])
            kb.inc(i, 'v')
        self.blk.vector(v2)
        self._ps_v = self.n['v']

    def ph_psetup_post(self, li):
        kb = self
        pre, pim = self.t4a, self.t4b
        r2, tmp = self.t4c, self.t4d

        def o1(x):
            return x.rearrange("p (c o) -> p c o", o=1)

        def a2(sc):
            kb.wv(sc, 'v', kb._ps_v)
            i = sc.activation(r2[:], r2[:], AF.Tanh)
            kb.inc(i, 'a')
        self.blk.scalar(a2)
        av = self.n['a']

        def p4(gp):
            kb.wv(gp, 'a', av)
            gp.tensor_tensor(tmp[:], r2[:], tmp[:], ALU.mult)
            gp.tensor_tensor(pre[:], kb.chan(li, 2), tmp[:], ALU.mult)
            gp.tensor_tensor(pim[:], kb.chan(li, 3), tmp[:], ALU.mult)
            powre3 = kb.powre[:, :].rearrange("p (g k) -> p g k", g=4)
            powim3 = kb.powim[:, :].rearrange("p (g k) -> p g k", g=4)
            for g in range(4):
                gp.memset(powre3[:, g, 0:1], 1.0)
                gp.memset(powim3[:, g, 0:1], 0.0)
            gp.tensor_copy(powre3[:, :, 1:2], o1(pre[:]))
            gp.tensor_copy(powim3[:, :, 1:2], o1(pim[:]))
            sA = kb.prB[:, :].rearrange("p (g c) -> p g c", g=4)
            sB = kb.piB[:, :].rearrange("p (g c) -> p g c", g=4)
            for m in (1, 2, 4, 8):
                qr = kb.colbc(kb.powre, m, 4 * (C + 1), m, gstep=C + 1, gcnt=4)
                qi = kb.colbc(kb.powim, m, 4 * (C + 1), m, gstep=C + 1, gcnt=4)
                src_r = powre3[:, :, 1:m + 1].rearrange("p g (c o) -> p g c o", o=1)
                src_i = powim3[:, :, 1:m + 1].rearrange("p g (c o) -> p g c o", o=1)
                d_r = powre3[:, :, m + 1:2 * m + 1].rearrange("p g (c o) -> p g c o", o=1)
                d_i = powim3[:, :, m + 1:2 * m + 1].rearrange("p g (c o) -> p g c o", o=1)
                a_ = sA[:, :, 0:m].rearrange("p g (c o) -> p g c o", o=1)
                b_ = sB[:, :, 0:m].rearrange("p g (c o) -> p g c o", o=1)
                gp.tensor_tensor(a_, src_r, qr, ALU.mult)
                gp.tensor_tensor(b_, src_i, qi, ALU.mult)
                gp.tensor_tensor(d_r, a_, b_, ALU.subtract)
                gp.tensor_tensor(a_, src_r, qi, ALU.mult)
                gp.tensor_tensor(b_, src_i, qr, ALU.mult)
                gp.tensor_tensor(d_i, a_, b_, ALU.add)
            qre = kb.qsre[:, :].rearrange("p (g s) -> p g s", g=4)
            qim = kb.qsim[:, :].rearrange("p (g s) -> p g s", g=4)
            gp.tensor_copy(qre[:, :, 0:1], powre3[:, :, C:C + 1])
            gp.tensor_copy(qim[:, :, 0:1], powim3[:, :, C:C + 1])
            for st in range(6):
                a_ = qre[:, :, st:st + 1]
                b_ = qim[:, :, st:st + 1]
                gp.tensor_tensor(o1(kb.sinre[:]), a_, a_, ALU.mult)
                gp.tensor_tensor(o1(kb.sinim[:]), b_, b_, ALU.mult)
                gp.tensor_tensor(qre[:, :, st + 1:st + 2], o1(kb.sinre[:]), o1(kb.sinim[:]), ALU.subtract)
                gp.tensor_tensor(o1(kb.sinre[:]), a_, b_, ALU.mult)
                gp.tensor_scalar(qim[:, :, st + 1:st + 2], o1(kb.sinre[:]), 2.0, None, ALU.mult)
            qcre3 = kb.qcre[:, :].rearrange("p (g c) -> p g c", g=4)
            qcim3 = kb.qcim[:, :].rearrange("p (g c) -> p g c", g=4)
            for g in range(4):
                gp.memset(qcre3[:, g, 0:1], 1.0)
                gp.memset(qcim3[:, g, 0:1], 0.0)
            for st in range(6):
                m = 1 << st
                qr = kb.colbc(kb.qsre, st, 32, m, gstep=8, gcnt=4)
                qi = kb.colbc(kb.qsim, st, 32, m, gstep=8, gcnt=4)
                src_r = qcre3[:, :, 0:m].rearrange("p g (c o) -> p g c o", o=1)
                src_i = qcim3[:, :, 0:m].rearrange("p g (c o) -> p g c o", o=1)
                d_r = qcre3[:, :, m:2 * m].rearrange("p g (c o) -> p g c o", o=1)
                d_i = qcim3[:, :, m:2 * m].rearrange("p g (c o) -> p g c o", o=1)
                a_ = sA[:, :, 0:m].rearrange("p g (c o) -> p g c o", o=1)
                b_ = sB[:, :, 0:m].rearrange("p g (c o) -> p g c o", o=1)
                gp.tensor_tensor(a_, src_r, qr, ALU.mult)
                gp.tensor_tensor(b_, src_i, qi, ALU.mult)
                gp.tensor_tensor(d_r, a_, b_, ALU.subtract)
                gp.tensor_tensor(a_, src_r, qi, ALU.mult)
                gp.tensor_tensor(b_, src_i, qr, ALU.mult)
                gp.tensor_tensor(d_i, a_, b_, ALU.add)
            for g in range(4):
                scl_b = kb.colbc(kb.chps, (li * 9 + 6) * 4 + g, NL * 9 * 4, C)
                gp.tensor_tensor(o1(kb.psclre[:, g * C:(g + 1) * C]),
                                 o1(kb.powre[:, g * (C + 1) + 1:g * (C + 1) + C + 1]),
                                 scl_b, ALU.mult)
                gp.tensor_tensor(o1(kb.psclim[:, g * C:(g + 1) * C]),
                                 o1(kb.powim[:, g * (C + 1) + 1:g * (C + 1) + C + 1]),
                                 scl_b, ALU.mult)
                gp.tensor_scalar(kb.psclim[:, g * C:(g + 1) * C],
                                 kb.psclim[:, g * C:(g + 1) * C], -1.0, None, ALU.mult)
            pp_re = kb.ppre[:, :].rearrange("p (g e) -> p g e", g=4)
            pp_im = kb.ppim[:, :].rearrange("p (g e) -> p g e", g=4)
            Pr = qre[:, :, 6:7]
            Pi = qim[:, :, 6:7]
            for g in range(4):
                gp.memset(pp_re[:, g, 0:1], 1.0)
                gp.memset(pp_im[:, g, 0:1], 0.0)
            gp.tensor_copy(pp_re[:, :, 1:2], Pr)
            gp.tensor_copy(pp_im[:, :, 1:2], Pi)
            i = None
            for e in (2, 3):
                ar = pp_re[:, :, e - 1:e]
                ai = pp_im[:, :, e - 1:e]
                gp.tensor_tensor(o1(kb.sinre[:]), ar, Pr, ALU.mult)
                gp.tensor_tensor(o1(kb.sinim[:]), ai, Pi, ALU.mult)
                gp.tensor_tensor(pp_re[:, :, e:e + 1], o1(kb.sinre[:]), o1(kb.sinim[:]), ALU.subtract)
                gp.tensor_tensor(o1(kb.sinre[:]), ar, Pi, ALU.mult)
                gp.tensor_tensor(o1(kb.sinim[:]), ai, Pr, ALU.mult)
                i = gp.tensor_tensor(pp_im[:, :, e:e + 1], o1(kb.sinre[:]), o1(kb.sinim[:]), ALU.add)
            kb.inc(i, 'p')
        self.blk.gpsimd(p4)
        self._p4p = self.n['p']

    def ph_sconv(self, li):
        kb = self
        wre3 = self.wre[:, :].rearrange("p (g c) -> p g c", g=4)
        wim3 = self.wim[:, :].rearrange("p (g c) -> p g c", g=4)
        Sre3 = self.Sre[:, :].rearrange("p (g c) -> p g c", g=4)
        Sim3 = self.Sim[:, :].rearrange("p (g c) -> p g c", g=4)
        prow = 4 * (C + 1)

        def o1(x):
            return x.rearrange("p (c o) -> p c o", o=1)

        def v(vec):
            kb.wv(vec, 'p', kb._p4p)
            for g in range(4):
                kb.wv(vec, 'a', kb._silu_a0 + 2 * g + 2)
                yg = kb.yv(g)
                y3 = yg.rearrange("p (c i) -> p c i", c=NCH)
                rev_re = bass.AP(kb.powre.ap().tensor, g * (C + 1) + (C - 1),
                                 [[prow, 128], [0, NCH], [-1, C]])
                rev_im = bass.AP(kb.powim.ap().tensor, g * (C + 1) + (C - 1),
                                 [[prow, 128], [0, NCH], [-1, C]])
                vec.tensor_tensor(kb.wtmp3(), y3, rev_re, ALU.mult)
                vec.reduce_sum(wre3[:, g, :], kb.wtmp3(), AX.X)
                vec.tensor_tensor(kb.wtmp3(), y3, rev_im, ALU.mult)
                vec.reduce_sum(wim3[:, g, :], kb.wtmp3(), AX.X)
            N4 = 4 * NCH
            f32v = kb.bigA.ap().bitcast(F32)
            tmp1 = f32v[:, 15360:15360 + N4]
            tmp2 = f32v[:, 15616:15616 + N4]
            cur_r, cur_i, nxt_r, nxt_i = kb.prA, kb.piA, kb.prB, kb.piB

            def g3v(t, lo, hi):
                return t[:].rearrange("p (g c) -> p g c", g=4)[:, :, lo:hi] \
                    .rearrange("p g (c o) -> p g c o", o=1)
            for st in range(6):
                m = 1 << st
                cnt = NCH - m
                qr_b = kb.colbc(kb.qsre, st, 32, cnt, gstep=8, gcnt=4)
                qi_b = kb.colbc(kb.qsim, st, 32, cnt, gstep=8, gcnt=4)
                vec.tensor_copy(g3v(nxt_r, 0, m), g3v(cur_r, 0, m))
                vec.tensor_copy(g3v(nxt_i, 0, m), g3v(cur_i, 0, m))
                tm1 = tmp1.rearrange("p (g c) -> p g c", g=4)[:, :, 0:cnt] \
                    .rearrange("p g (c o) -> p g c o", o=1)
                tm2 = tmp2.rearrange("p (g c) -> p g c", g=4)[:, :, 0:cnt] \
                    .rearrange("p g (c o) -> p g c o", o=1)
                vec.tensor_tensor(tm1, g3v(cur_r, 0, cnt), qr_b, ALU.mult)
                vec.tensor_tensor(tm2, g3v(cur_i, 0, cnt), qi_b, ALU.mult)
                vec.tensor_tensor(g3v(nxt_r, m, NCH), g3v(cur_r, m, NCH), tm1, ALU.add)
                vec.tensor_tensor(g3v(nxt_r, m, NCH), g3v(nxt_r, m, NCH), tm2, ALU.subtract)
                vec.tensor_tensor(tm1, g3v(cur_i, 0, cnt), qr_b, ALU.mult)
                vec.tensor_tensor(tm2, g3v(cur_r, 0, cnt), qi_b, ALU.mult)
                vec.tensor_tensor(g3v(nxt_i, m, NCH), g3v(cur_i, m, NCH), tm1, ALU.add)
                vec.tensor_tensor(g3v(nxt_i, m, NCH), g3v(nxt_i, m, NCH), tm2, ALU.add)
                cur_r, nxt_r = nxt_r, cur_r
                cur_i, nxt_i = nxt_i, cur_i
            cr3 = cur_r[:].rearrange("p (g c) -> p g c", g=4)
            ci3 = cur_i[:].rearrange("p (g c) -> p g c", g=4)
            for g in range(4):
                vec.memset(Sre3[:, g, 0:1], 0.0)
                vec.memset(Sim3[:, g, 0:1], 0.0)
            vec.tensor_copy(Sre3[:, :, 1:NCH], cr3[:, :, 0:NCH - 1])
            vec.tensor_copy(Sim3[:, :, 1:NCH], ci3[:, :, 0:NCH - 1])
            vec.tensor_copy(
                kb.gath[:, 56:64].rearrange("p (g x) -> p g x", g=4)[:, :, 0:1],
                cr3[:, :, NCH - 1:NCH])
            i = vec.tensor_copy(
                kb.gath[:, 56:64].rearrange("p (g x) -> p g x", g=4)[:, :, 1:2],
                ci3[:, :, NCH - 1:NCH])
            kb.inc(i, 'v')
        self.blk.vector(v)

        def g1(gp):
            kb.w(gp, 'v')
            kb.inc(gp.dma_start(kb.cc_in[:, :], kb.gath[:, 56:64]), 'xg', 16)
            kb.w(gp, 'xg')
            i = gp.collective_compute(
                "AllGather", ALU.bypass, replica_groups=[list(range(8))],
                ins=[kb.cc_in.ap().opt()], outs=[kb.cc_out.ap().opt()])
            i.then_inc(kb.ccsems[li])
            gp.wait_ge(kb.ccsems[li], 1)
            src = bass.AP(kb.cc_out.ap().tensor, 0, [[8, 128], [1024, 8], [1, 8]])
            i = gp.dma_start(kb.gath[:].rearrange("p (r c) -> p r c", r=8), src)
            kb.inc(i, 'xg', 16)
        self.blk.gpsimd(g1)

        def vsh(vec):
            for g in range(4):
                kb.wv(vec, 'p', kb._apply_p0 + g + 1)
                yg = kb.yv(g)
                y3 = yg.rearrange("p (c i) -> p c i", c=NCH)
                h3 = kb.hg(g).rearrange("p (c i) -> p c i", c=NCH)
                vec.scalar_tensor_tensor(kb.hg(g), yg, kb.chan(li, 6, g),
                                         kb.hg(g), ALU.mult, ALU.add)
                for k in range(1, C):
                    i = vec.scalar_tensor_tensor(
                        h3[:, :, k:C], y3[:, :, 0:C - k],
                        kb.psclre[:, g * C + k - 1:g * C + k],
                        h3[:, :, k:C], ALU.mult, ALU.add)
                kb.inc(i, 'v')
        self.blk.vector(vsh)

        def pc(gp):
            kb.w(gp, 'xg')
            wjr = kb.t4a
            WJ = kb.prB[:, :].rearrange("p (g c) -> p g c", g=4)
            WJr = WJ[:, :, 0:9]
            WJi = WJ[:, :, 9:18]
            PR = kb.prA[:, :].rearrange("p (g c) -> p g c", g=4)
            prod = PR[:, :, 0:9]
            gp.memset(WJr, 0.0)
            gp.memset(WJi, 0.0)
            for e in range(4):
                cmsb = bass.AP(kb.cms.ap().tensor, e,
                               [[32, 128], [0, 4], [4, 8], [1, 1]])
                ppb_r = bass.AP(kb.ppre.ap().tensor, e,
                                [[16, 128], [4, 4], [0, 8], [1, 1]])
                ppb_i = bass.AP(kb.ppim.ap().tensor, e,
                                [[16, 128], [4, 4], [0, 8], [1, 1]])
                o4 = prod[:, :, 0:8].rearrange("p g (c o) -> p g c o", o=1)
                gp.tensor_tensor(o4, cmsb, ppb_r, ALU.mult)
                gp.tensor_tensor(WJr[:, :, 0:8], WJr[:, :, 0:8], prod[:, :, 0:8], ALU.add)
                gp.tensor_tensor(o4, cmsb, ppb_i, ALU.mult)
                gp.tensor_tensor(WJi[:, :, 0:8], WJi[:, :, 0:8], prod[:, :, 0:8], ALU.add)
                lcb = bass.AP(kb.lcsel.ap().tensor, e,
                              [[4, 128], [0, 4], [0, 1], [1, 1]])
                o41 = prod[:, :, 8:9].rearrange("p g (c o) -> p g c o", o=1)
                gp.tensor_tensor(o41, lcb, ppb_r[:, :, 0:1], ALU.mult)
                gp.tensor_tensor(WJr[:, :, 8:9], WJr[:, :, 8:9], prod[:, :, 8:9], ALU.add)
                gp.tensor_tensor(o41, lcb, ppb_i[:, :, 0:1], ALU.mult)
                gp.tensor_tensor(WJi[:, :, 8:9], WJi[:, :, 8:9], prod[:, :, 8:9], ALU.add)
            SJr = PR[:, :, 16:25]
            SJi = PR[:, :, 25:34]
            gathr = bass.AP(kb.gath.ap().tensor, 0, [[64, 128], [2, 4], [8, 8], [1, 1]])
            gathi = bass.AP(kb.gath.ap().tensor, 1, [[64, 128], [2, 4], [8, 8], [1, 1]])
            gp.tensor_copy(SJr[:, :, 0:8].rearrange("p g (c o) -> p g c o", o=1), gathr)
            gp.tensor_copy(SJi[:, :, 0:8].rearrange("p g (c o) -> p g c o", o=1), gathi)
            gp.tensor_copy(SJr[:, :, 8:9], kb.chan(li, 4).rearrange("p (g o) -> p g o", o=1))
            gp.tensor_copy(SJi[:, :, 8:9], kb.chan(li, 5).rearrange("p (g o) -> p g o", o=1))
            P1 = PR[:, :, 34:43]
            P2 = WJ[:, :, 18:27]
            P3 = PR[:, :, 43:52]
            P4 = WJ[:, :, 27:36]
            gp.tensor_tensor(P1, WJr, SJr, ALU.mult)
            gp.tensor_tensor(P2, WJi, SJi, ALU.mult)
            gp.tensor_tensor(P3, WJr, SJi, ALU.mult)
            i = gp.tensor_tensor(P4, WJi, SJr, ALU.mult)
            kb.inc(i, 'p')
        self.blk.gpsimd(pc)
        pcp = self.n['p']

        def v2(vec):
            kb.wv(vec, 'p', pcp)
            PR = kb.prA[:, :].rearrange("p (g c) -> p g c", g=4)
            WJ = kb.prB[:, :].rearrange("p (g c) -> p g c", g=4)
            vec.reduce_sum(o1(kb.sinre[:]), PR[:, :, 34:43], AX.X)
            vec.reduce_sum(o1(kb.t4c[:]), WJ[:, :, 18:27], AX.X)
            vec.reduce_sum(o1(kb.sinim[:]), PR[:, :, 43:52], AX.X)
            i = vec.reduce_sum(o1(kb.t4d[:]), WJ[:, :, 27:36], AX.X)
            fv = kb.inc(i, 'v')
            vec.wait_ge(kb.sems['v'], fv)
            vec.tensor_tensor(kb.sinre[:], kb.sinre[:], kb.t4c[:], ALU.subtract)
            i = vec.tensor_tensor(kb.sinim[:], kb.sinim[:], kb.t4d[:], ALU.add)
            fv = kb.inc(i, 'v')
            vec.wait_ge(kb.sems['v'], fv)
            qcre3 = kb.qcre[:, :].rearrange("p (g c) -> p g c", g=4)
            qcim3 = kb.qcim[:, :].rearrange("p (g c) -> p g c", g=4)
            i = None
            for g in range(4):
                sr = kb.sinre[:, g:g + 1]
                si = kb.sinim[:, g:g + 1]
                vec.scalar_tensor_tensor(Sre3[:, g, :], qcre3[:, g, :], sr,
                                         Sre3[:, g, :], ALU.mult, ALU.add)
                vec.tensor_scalar(kb.prA[:, 0:NCH], qcim3[:, g, :], si, None, ALU.mult)
                vec.tensor_tensor(Sre3[:, g, :], Sre3[:, g, :], kb.prA[:, 0:NCH], ALU.subtract)
                vec.scalar_tensor_tensor(Sim3[:, g, :], qcim3[:, g, :], sr,
                                         Sim3[:, g, :], ALU.mult, ALU.add)
                vec.tensor_scalar(kb.prA[:, 0:NCH], qcre3[:, g, :], si, None, ALU.mult)
                vec.tensor_tensor(Sim3[:, g, :], Sim3[:, g, :], kb.prA[:, 0:NCH], ALU.add)
                h3 = kb.hg(g).rearrange("p (c i) -> p c i", c=NCH)
                for ii in range(C):
                    vec.scalar_tensor_tensor(
                        h3[:, :, ii:ii + 1],
                        Sre3[:, g, :].rearrange("p (c o) -> p c o", o=1),
                        kb.psclre[:, g * C + ii:g * C + ii + 1],
                        h3[:, :, ii:ii + 1], ALU.mult, ALU.add)
                    i = vec.scalar_tensor_tensor(
                        h3[:, :, ii:ii + 1],
                        Sim3[:, g, :].rearrange("p (c o) -> p c o", o=1),
                        kb.psclim[:, g * C + ii:g * C + ii + 1],
                        h3[:, :, ii:ii + 1], ALU.mult, ALU.add)
            kb.inc(i, 'v')
        self.blk.vector(v2)

    def ph_w1_prefetch(self, li):
        kb = self
        w1s = self.streams['w1']
        w1i = 2 * li

        def g(gp):
            kb.w(gp, 'pe')
            i = gp.dma_start(kb.wsA[:], kb.w1b[li, 0])
            w1s.issue(i, w1i)
            i = gp.dma_start(kb.wsB[:], kb.w1b[li, 1])
            w1s.issue(i, w1i + 1)
        self.blk.gpsimd(g)

    def ph_w1(self, li):
        kb = self
        w1s = self.streams['w1']
        w1i = 2 * li
        a0 = self.n['a']
        pe0 = self.n['pe']
        self._w1_pe0 = pe0

        def te(t):
            kb.w(t, 'a')
            cnt = 0
            for hf in range(2):
                w1s.wait(t, w1i + hf)
                ws = kb.wsA if hf == 0 else kb.wsB
                for mm in range(8):
                    for th in range(2):
                        pbk = kb.pb[4 + (cnt % 2)]
                        if cnt >= 2:
                            kb.wv(t, 'a', a0 + cnt - 1)
                        i = None
                        for kc in range(4):
                            i = t.matmul(
                                pbk[:, 0:TH],
                                ws[:, kc * 1024 + mm * 128: kc * 1024 + (mm + 1) * 128],
                                kb.lv(kc, th), start=(kc == 0), stop=(kc == 3))
                        kb.inc(i, 'pe')
                        cnt += 1
        self.blk.tensor(te)

        def a(sc):
            cnt = 0
            for hf in range(2):
                for mm in range(8):
                    m = hf * 8 + mm
                    for th in range(2):
                        pbk = kb.pb[4 + (cnt % 2)]
                        kb.wv(sc, 'pe', pe0 + cnt + 1)
                        dst = kb.bigB[:, m * T + th * TH: m * T + (th + 1) * TH]
                        i = sc.activation(dst, pbk[:, 0:TH], AF.Silu,
                                          bias=kb.b1s[:, li * 16 + m:li * 16 + m + 1])
                        got = kb.inc(i, 'a')
                        assert got == a0 + cnt + 1
                        cnt += 1
        self.blk.scalar(a)

    def ph_w2(self, li):
        kb = self
        w2s = self.streams['w2']
        w2i = 2 * li
        w1_done = self.n['pe']

        def g(gp):
            kb.wv(gp, 'pe', kb._w1_pe0 + 16)
            i = gp.dma_start(kb.wsA[:], kb.w2b[li, 0])
            w2s.issue(i, w2i)
            kb.wv(gp, 'pe', w1_done)
            i = gp.dma_start(kb.wsB[:], kb.w2b[li, 1])
            w2s.issue(i, w2i + 1)
        self.blk.gpsimd(g)

        v0 = self.n['v']
        pe0 = self.n['pe']

        def te(t):
            kb.w(t, 'a')
            w2s.wait(t, w2i)
            cnt = 0
            for m in range(4):
                for th in range(2):
                    pbk = kb.pb[6 + (cnt % 2)]
                    if cnt >= 2:
                        kb.wv(t, 'v', v0 + cnt - 1)
                    i = None
                    for kc in range(16):
                        if kc == 8 and cnt == 0:
                            w2s.wait(t, w2i + 1)
                        hf = kc // 8
                        kch = kc % 8
                        ws = kb.wsA if hf == 0 else kb.wsB
                        i = t.matmul(
                            pbk[:, 0:TH],
                            ws[:, kch * 512 + m * 128: kch * 512 + (m + 1) * 128],
                            kb.bigB[:, kc * T + th * TH: kc * T + (th + 1) * TH],
                            start=(kc == 0), stop=(kc == 15))
                    kb.inc(i, 'pe')
                    cnt += 1
        self.blk.tensor(te)

        def v(vec):
            cnt = 0
            for m in range(4):
                for th in range(2):
                    pbk = kb.pb[6 + (cnt % 2)]
                    kb.wv(vec, 'pe', pe0 + cnt + 1)
                    i = vec.scalar_tensor_tensor(kb.hgt(m, th), pbk[:, 0:TH],
                                                 kb.b2s[:, li * 4 + m:li * 4 + m + 1],
                                                 kb.hgt(m, th), ALU.add, ALU.add)
                    got = kb.inc(i, 'v')
                    assert got == v0 + cnt + 1
                    cnt += 1
        self.blk.vector(v)

    def ph_lnl_wout(self):
        kb = self
        for th in range(2):
            self.cast_chunks(lambda k: kb.hgt(k, th), 4)
            self.stats_rows(4, kb.inv512[:], self.pb[0], self.pb[1])
            self.rows_and_bcast(self.pb[0], self.pb[1], self.pb[2], self.pb[3])
            v0 = self.n['v']

            def v(vec, th=th):
                kb.w(vec, 'pe')
                for g in range(4):
                    vec.tensor_tensor(kb.sc1[:], kb.hgt(g, th), kb.pb[2][:, 0:TH],
                                      ALU.subtract)
                    i = vec.scalar_tensor_tensor(kb.yv(g, th), kb.sc1[:],
                                                 kb.lnls[:, g:g + 1],
                                                 kb.pb[3][:, 0:TH], ALU.mult, ALU.mult)
                    kb.inc(i, 'v')
            self.blk.vector(v)

            def a(sc, th=th, v0=v0):
                for g in range(4):
                    kb.wv(sc, 'v', v0 + g + 1)
                    i = sc.activation(kb.lv(g, th), kb.yv(g, th), AF.Identity,
                                      bias=kb.lnls[:, 4 + g:5 + g])
                    kb.inc(i, 'a')
            self.blk.scalar(a)

        if self.dump('hL'):
            return
        wow, wox = self.streams['wow'], self.streams['wox']
        base_pe = self.n['pe']
        def grp_pe(cnt):
            return base_pe + (cnt + 1) + max(0, cnt - 2)
        def stats_pe(cnt):
            if cnt <= 61:
                return grp_pe(cnt + 2) + 1
            return base_pe + 126 + (cnt - 61)
        base_v = self.n['v']
        base_a = self.n['a']

        def g(gp):
            kb.w(gp, 'a')

            def wout_load(q):
                if q >= 2:
                    kb.wv(gp, 'pe', grp_pe((q - 1) * 16 - 1))
                i = gp.dma_start((kb.wsA if q % 2 == 0 else kb.wsB)[:], kb.woutb[q])
                wow.issue(i, q)
            wout_load(0)
            wout_load(1)
            for m in range(32):
                if m >= 2:
                    kb.wv(gp, 'v', base_v + (m - 2) * 2 + 2)
                src = bass.AP(kb.xT.tensor, m * 128 * T, [[T, 128], [1, T]])
                i = gp.dma_start(kb.xt(m % 2), src)
                wox.issue(i, m)
                if m == 7:
                    wout_load(2)
                elif m == 15:
                    wout_load(3)
        self.blk.gpsimd(g)

        def te(t):
            kb.w(t, 'v', 'a')
            cnt = 0
            for q in range(4):
                wow.wait(t, q)
                ws = kb.wsA if q % 2 == 0 else kb.wsB
                for mq in range(8):
                    m = q * 8 + mq
                    wox.wait(t, m)
                    for th in range(2):
                        pbk = kb.pb[cnt % 2]
                        if cnt >= 2:
                            kb.wv(t, 'v', base_v + cnt - 1)
                        i = None
                        for kc in range(4):
                            i = t.matmul(
                                pbk[:, 0:TH],
                                ws[:, kc * 1024 + mq * 128: kc * 1024 + (mq + 1) * 128],
                                kb.lv(kc, th), start=(kc == 0), stop=(kc == 3))
                        got = kb.inc(i, 'pe')
                        assert got == grp_pe(cnt)
                        if cnt >= 2:
                            kb._wo_stats_emit(t, cnt - 2, base_a, stats_pe)
                        cnt += 1
            for j in (62, 63):
                kb._wo_stats_emit(t, j, base_a, stats_pe)
        self.blk.tensor(te)

        def v(vec):
            cnt = 0
            for m in range(32):
                for th in range(2):
                    pbk = kb.pb[cnt % 2]
                    kb.wv(vec, 'pe', grp_pe(cnt))
                    wox.wait(vec, m)
                    dst = kb.bigA[:, m * T + th * TH: m * T + (th + 1) * TH]
                    i = vec.scalar_tensor_tensor(
                        dst, pbk[:, 0:TH], kb.bouts[:, m:m + 1],
                        kb.xt(m % 2)[:, th * TH:(th + 1) * TH], ALU.add, ALU.add)
                    got = kb.inc(i, 'v')
                    assert got == base_v + cnt + 1
                    cnt += 1
        self.blk.vector(v)

        def a(sc):
            cnt = 0
            for m in range(32):
                for th in range(2):
                    kb.wv(sc, 'v', base_v + cnt + 1)
                    if cnt >= 2:
                        kb.wv(sc, 'pe', stats_pe(cnt - 2))
                    src = kb.bigA[:, m * T + th * TH: m * T + (th + 1) * TH]
                    i = sc.activation(kb.wsq[:, th * TH:(th + 1) * TH], src, AF.Square)
                    got = kb.inc(i, 'a')
                    assert got == base_a + cnt + 1
                    cnt += 1
        self.blk.scalar(a)

    def _wo_stats_emit(self, t, cnt, base_a, stats_pe):
        kb = self
        m, th = cnt // 2, cnt % 2
        kb.wv(t, 'a', base_a + cnt + 1)
        src = kb.bigA[:, m * T + th * TH: m * T + (th + 1) * TH]
        t.matmul(kb.pb[2 + th][0:1, 0:TH], kb.inv4096[:], src,
                 start=(m == 0), stop=(m == 31))
        i = t.matmul(kb.pb[4 + th][0:1, 0:TH], kb.inv4096[:],
                     kb.wsq[:, th * TH:(th + 1) * TH],
                     start=(m == 0), stop=(m == 31))
        got = kb.inc(i, 'pe')
        assert got == stats_pe(cnt)

    def postln_apply(self, big, to_out):
        kb = self
        for th in range(2):
            self.rows_and_bcast(self.pb[2 + th], self.pb[4 + th],
                                self.pb[6], self.pb[7])

            def ve(vec, th=th):
                kb.w(vec, 'pe', 'p')
                vec.tensor_copy(kb.mub[:], kb.pb[6][:, 0:TH])
                i = vec.tensor_copy(kb.rstdb[:], kb.pb[7][:, 0:TH])
                kb.inc(i, 'v')
            self.blk.vector(ve)

            if not to_out:
                p0 = self.n['p']
                v0 = self.n['v']
                if not hasattr(self, '_pl1'):
                    self._pl1 = {}
                self._pl1[th] = (v0, p0)

                def po(gp, th=th, v0=v0):
                    kb.wv(gp, 'v', v0)
                    for m in range(16, 32):
                        src = big[:, m * T + th * TH: m * T + (th + 1) * TH]
                        gp.tensor_tensor(src, src, kb.mub[:], ALU.subtract)
                        i = gp.tensor_tensor(src, src, kb.rstdb[:], ALU.mult)
                        kb.inc(i, 'p')
                self.blk.gpsimd(po)

                def vo(vec, th=th):
                    for m in range(16):
                        src = big[:, m * T + th * TH: m * T + (th + 1) * TH]
                        vec.tensor_tensor(src, src, kb.mub[:], ALU.subtract)
                        i = vec.tensor_tensor(src, src, kb.rstdb[:], ALU.mult)
                        kb.inc(i, 'v')
                self.blk.vector(vo)
            else:
                outs = self.streams['out']
                outs2 = self.streams['out2']
                p0 = self.n['p']
                v0 = self.n['v']

                def po(gp, th=th, p0=p0, v0=v0):
                    kb.wv(gp, 'v', v0)
                    for j in range(16):
                        jg = th * 16 + j
                        m = 16 + j
                        if jg >= 2:
                            outs2.wait(gp, jg - 2)
                        src = big[:, m * T + th * TH: m * T + (th + 1) * TH]
                        d = kb._oslot(1, jg % 2)
                        gp.tensor_tensor(kb.mdv(1), src, kb.mub[:], ALU.subtract)
                        i = gp.tensor_tensor(d, kb.mdv(1), kb.rstdb[:], ALU.mult)
                        kb.inc(i, 'p')
                self.blk.gpsimd(po)

                def vo(vec, th=th, v0=v0):
                    for j in range(16):
                        jg = th * 16 + j
                        m = j
                        if jg >= 2:
                            outs.wait(vec, jg - 2)
                        src = big[:, m * T + th * TH: m * T + (th + 1) * TH]
                        d = kb._oslot(0, jg % 2)
                        vec.tensor_tensor(kb.mdv(0), src, kb.mub[:], ALU.subtract)
                        i = vec.tensor_tensor(d, kb.mdv(0), kb.rstdb[:], ALU.mult)
                        kb.inc(i, 'v')
                self.blk.vector(vo)

                def so(sync, th=th, p0=p0, v0=v0):
                    for j in range(16):
                        jg = th * 16 + j
                        mv, mp = j, 16 + j
                        kb.wv(sync, 'v', v0 + j + 1)
                        dst = bass.AP(kb.outT.tensor, mv * 128 * T + th * TH,
                                      [[T, 128], [1, TH]])
                        i = sync.dma_start(dst, kb._oslot(0, jg % 2))
                        outs.issue(i, jg)
                        kb.wv(sync, 'p', p0 + j + 1)
                        dst = bass.AP(kb.outT.tensor, mp * 128 * T + th * TH,
                                      [[T, 128], [1, TH]])
                        i = sync.dma_start(dst, kb._oslot(1, jg % 2))
                        outs2.issue(i, jg)
                self.blk.sync(so)

    def ph_postln1(self):
        kb = self
        self._wout_pe_end = self.n['pe']
        des = self.streams['de']

        def g(gp):
            kb.wv(gp, 'pe', kb._wout_pe_end)
            kb.inc(gp.dma_start(kb.cents[:], kb.centb), 'xg', 16)
            asg_i32 = kb.bigB.ap().bitcast(I32)[0:1, 0:V]
            kb.inc(gp.dma_start(asg_i32, kb.assign), 'xg', 16)
            for m0 in range(2):
                i = gp.dma_start((kb.wsA if m0 % 2 == 0 else kb.wsB)[:], kb.fwb[m0])
                des.issue(i, m0)
        self.blk.gpsimd(g)
        self.postln_apply(self.bigA, False)

    def _oslot(self, grp, par):
        outv = self.bigA.ap().bitcast(F32)
        slot = grp * 2 + par
        return outv[:, slot * TH:(slot + 1) * TH]

    def ph_masks(self):
        kb = self

        def v0(vec):
            kb.w(vec, 'xg')
            asg_i32 = kb.bigB.ap().bitcast(I32)[0:1, 0:V]
            asg_f32 = kb.bigB.ap().bitcast(F32)[0:1, 4096:4096 + V]
            asg_bf = kb.bigB[0:1, 16384:16384 + V]
            vec.tensor_copy(asg_f32, asg_i32)
            kb.inc(vec.tensor_copy(asg_bf, asg_f32), 'v')
        self.blk.vector(v0)

        for th in range(2):
            def te(t, th=th):
                kb.w(t, 'xg')
                v0a, p0a = kb._pl1[th]
                i = None
                for kc in range(32):
                    if kc < 16:
                        kb.wv(t, 'v', v0a + kc + 1)
                    else:
                        kb.wv(t, 'p', p0a + (kc - 16) + 1)
                    i = t.matmul(kb.pb[th][0:64, 0:TH],
                                 kb.cents[:, kc * 64:(kc + 1) * 64],
                                 kb.bigA[:, kc * T + th * TH: kc * T + (th + 1) * TH],
                                 start=(kc == 0), stop=(kc == 31))
                kb.inc(i, 'pe')
            self.blk.tensor(te)

            def a(sc, th=th):
                kb.w(sc, 'pe')
                ebf = kb.bigB[0:64, 20480 + th * TH:20480 + (th + 1) * TH]
                kb.inc(sc.activation(ebf, kb.pb[th][0:64, 0:TH], AF.Exp), 'a')
            self.blk.scalar(a)
            if not hasattr(self, '_mask_a_th'):
                self._mask_a_th = {}
            self._mask_a_th[th] = self.n['a']

            def te2(t, th=th):
                kb.w(t, 'a', 'v')
                ebf = kb.bigB[0:64, 20480 + th * TH:20480 + (th + 1) * TH]
                i = t.matmul(kb.pb[2][0:1, 0:TH], kb.ones64[:], ebf,
                             start=True, stop=True)
                kb.inc(i, 'pe')
            self.blk.tensor(te2)

            def v1(vec, th=th):
                kb.w(vec, 'pe')
                i = vec.tensor_scalar(kb.rowb[0:1, 0:TH], kb.pb[2][0:1, 0:TH],
                                      0.01, None, ALU.mult)
                kb.inc(i, 'v')
            self.blk.vector(v1)

            def te3(t, th=th):
                kb.w(t, 'v')
                i = t.matmul(kb.pb[3][0:64, 0:TH], kb.ones1[0:1, 0:64],
                             kb.rowb[0:1, 0:TH], start=True, stop=True)
                kb.inc(i, 'pe')
            self.blk.tensor(te3)

            def v2(vec, th=th):
                kb.w(vec, 'pe')
                ebf = kb.bigB[0:64, 20480 + th * TH:20480 + (th + 1) * TH]
                hits = kb.bigB[0:64, 22528 + th * TH:22528 + (th + 1) * TH]
                vec.tensor_tensor(hits, ebf, kb.pb[3][0:64, 0:TH], ALU.is_gt)
                i = vec.reduce_sum(kb.hcnt[:, th:th + 1], hits, AX.X)
                kb.inc(i, 'v')
            self.blk.vector(v2)

            def te4(t, th=th):
                kb.w(t, 'v')
                hits = kb.bigB[0:64, 22528 + th * TH:22528 + (th + 1) * TH]
                i = t.matmul(kb.pb[4][0:1, 0:TH], kb.ones64[:], hits,
                             start=True, stop=True)
                kb.inc(i, 'pe')
            self.blk.tensor(te4)

            def v3(vec, th=th):
                kb.w(vec, 'pe')
                i = vec.tensor_scalar(kb.rowb2[0:1, th * TH:(th + 1) * TH],
                                      kb.pb[4][0:1, 0:TH], 0.5, None, ALU.is_gt)
                kb.inc(i, 'v')
            self.blk.vector(v3)

        if self.dump('m1'):
            return
        def v4(vec):
            kb.w(vec, 'v')
            i = vec.tensor_tensor(kb.hcnt[:, 0:1], kb.hcnt[:, 0:1],
                                  kb.hcnt[:, 1:2], ALU.add)
            kb.inc(i, 'v')
        self.blk.vector(v4)

        def g2(gp):
            kb.w(gp, 'v')
            kb.inc(gp.dma_start(kb.cl_in[:, :], kb.hcnt[:, 0:1]), 'xg', 16)
            kb.w(gp, 'xg')
            i = gp.collective_compute(
                "AllReduce", ALU.add, replica_groups=[list(range(8))],
                ins=[kb.cl_in.ap().opt()], outs=[kb.cl_out.ap().opt()])
            i.then_inc(kb.ccsems[4])
            gp.wait_ge(kb.ccsems[4], 1)
            kb.inc(gp.dma_start(kb.hcnt[0:64, 10:11], kb.cl_out.ap()[0:64, :]),
                   'xg', 16)
        self.blk.gpsimd(g2)

        if self.dump('m2'):
            return

    def ph_dense(self):
        kb = self
        des = self.streams['de']
        BNK = (0, 1, 6, 7)

        pe0 = self.n['pe']
        kb._oh_v0 = self.n['v']

        def te5(t):
            kb.w(t, 'v')
            for oc in range(8):
                asg_bf = kb.bigB[0:1, 16384 + oc * TH:16384 + (oc + 1) * TH]
                if oc >= 1:
                    kb.wv(t, 'v', kb._oh_v0 + oc)
                i = t.matmul(kb.pb[5][0:64, 0:TH], kb.ones1[0:1, 0:64], asg_bf,
                             start=True, stop=True)
                kb.inc(i, 'pe')
        self.blk.tensor(te5)

        def v5b(vec):
            for oc in range(8):
                kb.wv(vec, 'pe', pe0 + oc + 1)
                oh = kb.bigB[0:64, 24576 + oc * TH:24576 + (oc + 1) * TH]
                i = vec.tensor_scalar(oh, kb.pb[5][0:64, 0:TH], kb.io64[:, 0:1],
                                      None, ALU.is_equal)
                got = kb.inc(i, 'v')
                assert got == kb._oh_v0 + oc + 1
        self.blk.vector(v5b)

        if self.dump('m3'):
            return
        if self.dump('m3b'):
            return

        def te7(t):
            kb.w(t, 'a', 'v')
            t.matmul(kb.pb[2][:, 0:TH], kb.ones1[:], kb.rowb2[0:1, 0:TH],
                     start=True, stop=True)
            i = t.matmul(kb.pb[3][:, 0:TH], kb.ones1[:], kb.rowb2[0:1, TH:2 * TH],
                         start=True, stop=True)
            kb.inc(i, 'pe')
        self.blk.tensor(te7)

        def v7(vec):
            kb.w(vec, 'pe')
            vec.tensor_copy(kb.qsb[:, 0:TH], kb.pb[2][:, 0:TH])
            kb.inc(vec.tensor_copy(kb.qsb[:, TH:T], kb.pb[3][:, 0:TH]), 'v')
        self.blk.vector(v7)

        gmap = {}
        smap = {}

        def te_head(t):
            cnt = 0
            for m in range(2):
                des.wait(t, m)
                ws = kb.wsA if m % 2 == 0 else kb.wsB
                for th in range(2):
                    pbk = kb.pb[BNK[cnt % 4]]
                    if cnt < 2:
                        kb.wv(t, 'v', kb._pl1[th][0] + 16)
                        kb.wv(t, 'p', kb._pl1[th][1] + 16)
                        kb.wv(t, 'a', kb._mask_a_th[th])
                    i = None
                    for kc in range(32):
                        i = t.matmul(
                            pbk[:, 0:TH], ws[:, kc * 128:(kc + 1) * 128],
                            kb.bigA[:, kc * T + th * TH: kc * T + (th + 1) * TH],
                            start=(kc == 0), stop=(kc == 31))
                    gmap[cnt] = kb.inc(i, 'pe')
                    cnt += 1
        self.blk.tensor(te_head)

        def v5(vec):
            kb.w(vec, 'xg')
            i = vec.tensor_scalar(kb.clsel[:, 0:1], kb.hcnt[:, 10:11], 0.5,
                                  None, ALU.is_gt)
            kb.inc(i, 'v')
        self.blk.vector(v5)

        def te6(t):
            kb.w(t, 'v')
            oh = kb.bigB[0:64, 24576:24576 + V]
            i = None
            for m in range(32):
                i = t.matmul(kb.pb[5][:, m:m + 1], oh[:, m * 128:(m + 1) * 128],
                             kb.clsel[:, 0:1], start=True, stop=True)
            kb.inc(i, 'pe')
        self.blk.tensor(te6)

        def v6(vec):
            kb.w(vec, 'pe')
            vec.tensor_tensor(kb.fbrm[:], kb.fbs[:, 0:32], kb.pb[5][:, 0:32],
                              ALU.mult)
            kb.inc(vec.tensor_copy(kb.rsall[:], kb.pb[5][:, 0:32]), 'v')
        self.blk.vector(v6)
        if self.dump('m4'):
            return

        base_v = self.n['v']
        base_a = self.n['a']
        base_p = self.n['p']
        base_dc = self.n['dc']

        def te(t):
            for j in (0, 1):
                kb._de_stats_emit(t, j, base_a, smap)
            cnt = 4
            for m in range(2, 32):
                des.wait(t, m)
                ws = kb.wsA if m % 2 == 0 else kb.wsB
                for th in range(2):
                    pbk = kb.pb[BNK[cnt % 4]]
                    kb.wv(t, 'v', base_v + cnt - 3)
                    i = None
                    for kc in range(32):
                        i = t.matmul(
                            pbk[:, 0:TH], ws[:, kc * 128:(kc + 1) * 128],
                            kb.bigA[:, kc * T + th * TH: kc * T + (th + 1) * TH],
                            start=(kc == 0), stop=(kc == 31))
                    gmap[cnt] = kb.inc(i, 'pe')
                    kb._de_stats_emit(t, cnt - 2, base_a, smap)
                    cnt += 1
            for j in (62, 63):
                kb._de_stats_emit(t, j, base_a, smap)
        self.blk.tensor(te)

        def v(vec):
            cnt = 0
            for m in range(32):
                for th in range(2):
                    kb.wv(vec, 'dc', base_dc + cnt + 1)
                    i = vec.tensor_tensor(kb.mdv(cnt % 2), kb.mdv(cnt % 2),
                                          kb.qsb[:, th * TH:(th + 1) * TH],
                                          ALU.mult)
                    got = kb.inc(i, 'v')
                    assert got == base_v + cnt + 1
                    cnt += 1
        self.blk.vector(v)

        def p(gp):
            kb.wv(gp, 'v', base_v)
            kb.wv(gp, 'p', base_p)
            cnt = 0
            for m in range(32):
                for th in range(2):
                    kb.wv(gp, 'v', base_v + cnt + 1)
                    dst = kb.bigB[:, m * T + th * TH: m * T + (th + 1) * TH]
                    i = gp.tensor_tensor(
                        dst, kb.mdv(cnt % 2),
                        kb.bigA[:, m * T + th * TH: m * T + (th + 1) * TH], ALU.add)
                    got = kb.inc(i, 'p')
                    assert got == base_p + cnt + 1
                    if th == 1 and m + 2 < 32:
                        i = gp.dma_start((kb.wsA if m % 2 == 0 else kb.wsB)[:],
                                         kb.fwb[m + 2])
                        des.issue(i, m + 2)
                    cnt += 1
        self.blk.gpsimd(p)

        def a(sc):
            kb.wv(sc, 'v', base_v)
            cnt = 0
            for m in range(32):
                for th in range(2):
                    pbk = kb.pb[BNK[cnt % 4]]
                    kb.wv(sc, 'pe', gmap[cnt])
                    if cnt >= 2:
                        kb.wv(sc, 'p', base_p + cnt - 1)
                    i = sc.activation(kb.mdv(cnt % 2), pbk[:, 0:TH], AF.Identity,
                                      scale=kb.rsall[:, m:m + 1],
                                      bias=kb.fbrm[:, m:m + 1])
                    got = kb.inc(i, 'dc')
                    assert got == base_dc + cnt + 1
                    kb.wv(sc, 'p', base_p + cnt + 1)
                    if cnt >= 2:
                        kb.wv(sc, 'pe', smap[cnt - 2])
                    src = kb.bigB[:, m * T + th * TH: m * T + (th + 1) * TH]
                    i = sc.activation(kb.wsq[:, th * TH:(th + 1) * TH], src, AF.Square)
                    got = kb.inc(i, 'a')
                    assert got == base_a + cnt + 1
                    cnt += 1
        self.blk.scalar(a)

    def _de_stats_emit(self, t, cnt, base_a, smap):
        kb = self
        m, th = cnt // 2, cnt % 2
        kb.wv(t, 'a', base_a + cnt + 1)
        src = kb.bigB[:, m * T + th * TH: m * T + (th + 1) * TH]
        t.matmul(kb.pb[2 + th][0:1, 0:TH], kb.inv4096[:], src,
                 start=(m == 0), stop=(m == 31))
        i = t.matmul(kb.pb[4 + th][0:1, 0:TH], kb.inv4096[:],
                     kb.wsq[:, th * TH:(th + 1) * TH],
                     start=(m == 0), stop=(m == 31))
        smap[cnt] = kb.inc(i, 'pe')

    def ph_postln2_out(self):
        self.postln_apply(self.bigB, True)
        if self.done:
            return

    def ph_finish(self):
        kb = self

        def s(sync):
            for nm in ('out', 'out2'):
                st = kb.streams[nm]
                for sl in range(st.nbuf):
                    sync.wait_ge(st.sems[sl], st.cnt[sl])
            sync.wait_ge(kb.sems['o'], kb.n['o'])
        self.blk.sync(s)


def _prep_shared(inputs):
    g = {}
    Win = inputs['L0_attn_Win']
    g['winb'] = np.ascontiguousarray(
        Win.T.reshape(4, 8, 128, 512).transpose(0, 2, 1, 3)).reshape(4, 128, 8 * 512)
    W1 = inputs['L0_attn_W1']
    g['w1b'] = np.ascontiguousarray(
        np.stack([W1[i].T.reshape(4, 128, 2, 8, 128).transpose(2, 1, 0, 3, 4)
                  for i in range(NL)])).reshape(NL, 2, 128, 4 * 1024)
    W2 = inputs['L0_attn_W2']
    g['w2b'] = np.ascontiguousarray(
        np.stack([W2[i].T.reshape(2, 8, 128, 512).transpose(0, 2, 1, 3)
                  for i in range(NL)])).reshape(NL, 2, 128, 8 * 512)
    Wout = inputs['L0_attn_Wout']
    g['woutb'] = np.ascontiguousarray(
        Wout.T.reshape(4, 128, 4, 8, 128).transpose(2, 1, 0, 3, 4)).reshape(4, 128, 4096)
    Wf = inputs['L0_ffn_weight']
    g['fwb'] = np.ascontiguousarray(
        Wf.T.reshape(32, 128, 32, 128).transpose(2, 1, 0, 3)).reshape(32, 128, 32 * 128)
    cent = inputs['L0_ffn_centroids']
    g['centb'] = np.ascontiguousarray(
        cent.T.reshape(32, 128, 64).transpose(1, 0, 2)).reshape(128, 32 * 64)
    chp = np.stack([inputs['L0_attn_' + n] for n in
                    ('ln1_w', 'ln1_b', 'phz_re', 'phz_im', 'lc_re', 'lc_im',
                     'sc_lin', 'ln2_w', 'ln2_b')], axis=1)
    g['chp'] = np.ascontiguousarray(
        chp.reshape(NL, 9, 4, 128).transpose(3, 0, 1, 2)).reshape(128, NL * 9 * 4)
    g['binr'] = np.ascontiguousarray(inputs['L0_attn_bin'].reshape(4, 128).T)
    g['b1all'] = np.ascontiguousarray(
        inputs['L0_attn_b1'].reshape(NL, 16, 128).transpose(2, 0, 1)).reshape(128, NL * 16)
    g['b2all'] = np.ascontiguousarray(
        inputs['L0_attn_b2'].reshape(NL, 4, 128).transpose(2, 0, 1)).reshape(128, NL * 4)
    lnl = np.stack([inputs['L0_attn_lnl_w'], inputs['L0_attn_lnl_b']])
    g['lnlwb'] = np.ascontiguousarray(
        lnl.reshape(2, 4, 128).transpose(2, 0, 1)).reshape(128, 8)
    g['boutr'] = np.ascontiguousarray(inputs['L0_attn_bout'].reshape(32, 128).T)
    g['fbr'] = np.ascontiguousarray(inputs['L0_ffn_bias'].reshape(32, 128).T)
    g['iot64'] = np.arange(64, dtype=np.float32).reshape(64, 1)
    g['assign'] = np.ascontiguousarray(
        inputs['L0_ffn_assignments'].reshape(1, V)).astype(np.int32)
    return g


def make_in_maps(inputs):
    x = inputs['L0_attn_x']
    g = _prep_shared(inputs)
    maps = []
    for core in range(8):
        b, pi = core // 4, core % 4
        d = dict(g)
        d['xT'] = np.ascontiguousarray(x[b, pi * T:(pi + 1) * T, :].T)
        cmask = np.zeros((128, 32), dtype=np.float32)
        for j in range(pi):
            e = pi - 1 - j
            cmask[:, (b * 4 + j) * 4 + e] = 1.0
        d['cmask'] = cmask
        ls = np.zeros((128, 4), dtype=np.float32)
        ls[:, pi] = 1.0
        d['lcselr'] = ls
        maps.append(d)
    return maps


def kernel(**inputs):
    kb = KB()
    nc = kb.build()
    maps = make_in_maps(inputs)
    res = run_bass_kernel_spmd(nc, maps, core_ids=list(range(8)))
    out = np.empty((2, 4096, V), dtype=np.float32)
    for core in range(8):
        b, pi = core // 4, core % 4
        out[b, pi * T:(pi + 1) * T, :] = res.results[core]['outT'].T
    return out
